# version 02
# baseline (speedup 1.0000x reference)
import numpy as np
import ml_dtypes
import concourse.bass as bass
import concourse.mybir as mybir
from concourse.ap import AP
from concourse.bass_utils import run_bass_kernel_spmd
from contextlib import ExitStack

F32 = mybir.dt.float32
BF16 = mybir.dt.bfloat16
AF = mybir.ActivationFunctionType
ALU = mybir.AluOpType
BFD = ml_dtypes.bfloat16

T = 2048
D = 1024
LAT = 512
H = 16
HD = 64
DH = 4096
NCORE = 8
STAGES = 4
DEBUG_DUMP = False


def img128(m):
    R, C = m.shape
    assert R % 128 == 0
    return np.ascontiguousarray(
        m.reshape(R // 128, 128, C).transpose(1, 0, 2).reshape(128, (R // 128) * C))


def ogblocks(wt, ogc=128):
    I, O = wt.shape
    n = O // ogc
    a = wt.reshape(I // 128, 128, n, ogc).transpose(2, 1, 0, 3)
    return np.ascontiguousarray(a.reshape(n, 128, (I // 128) * ogc))


def colimg(v):
    return np.ascontiguousarray(v.reshape(-1, 128).T)


class Sem:
    __slots__ = ("h", "n")

    def __init__(self, es, nc, name):
        self.h = es.enter_context(nc.semaphore(name))
        self.n = 0

    def inc(self, instr, k=1):
        instr.then_inc(self.h, k)
        self.n += k
        return self.n


def build(stages):
    nc = bass.Bass("TRN2", target_bir_lowering=False, detect_race_conditions=False)

    def din(name, shape, dtype=F32):
        return nc.dram_tensor(name, list(shape), dtype, kind="ExternalInput").ap()

    xT = din("xT", (128, 8 * T))
    aw_dkv = din("aw_dkv", (128, 4096))
    aw_dq = din("aw_dq", (128, 4096))
    aw_dqn = din("aw_dqn", (128, 4096))
    aw_uq = din("aw_uq", (128, 4096))
    aw_uk = din("aw_uk", (128, 4096))
    aw_uv = din("aw_uv", (128, 4096))
    aw_o = din("aw_o", (128, 8192))
    bw_q = din("bw_q", (128, 8192))
    bw_k = din("bw_k", (128, 8192))
    bw_v = din("bw_v", (128, 8192))
    bw_o = din("bw_o", (128, 8192))
    bq_r = din("bq_r", (1, 1024))
    bk_r = din("bk_r", (1, 1024))
    bo_i = din("bo_i", (128, 8))
    bv_r = din("bv_r", (1, 1024))
    rel2_d = din("rel2", (128, 4095))
    f_w1 = din("f_w1", (32, 128, 1024))
    f_w2 = din("f_w2", (32, 128, 1024))
    f_h1 = din("f_h1", (32, 128, 4096))
    f_h2 = din("f_h2", (32, 128, 4096))
    f_w3 = din("f_w3", (8, 128, 4096))
    s1_i = din("s1_i", (128, 32))
    s2_i = din("s2_i", (128, 32))
    sh1_i = din("sh1_i", (128, 32))
    sh2_i = din("sh2_i", (128, 32))
    s3_i = din("s3_i", (128, 8))

    yO = nc.dram_tensor("y", [128, 8 * T], F32, kind="ExternalOutput").ap()
    parkA = nc.dram_tensor("parkA", [128, 8 * T], F32, kind="Internal").ap()
    parkB = nc.dram_tensor("parkB", [128, 8 * T], F32, kind="Internal").ap()
    osc_d = nc.dram_tensor("oscr", [128, 8 * T], BF16, kind="Internal").ap()
    pexp = nc.dram_tensor("pexp", [H * 2048 * 4095], BF16, kind="Internal").ap()
    cach = {
        "w1": nc.dram_tensor("c_w1", [32, 128, 1024], BF16, kind="Internal").ap(),
        "w2": nc.dram_tensor("c_w2", [32, 128, 1024], BF16, kind="Internal").ap(),
        "h1": nc.dram_tensor("c_h1", [32, 128, 4096], BF16, kind="Internal").ap(),
        "h2": nc.dram_tensor("c_h2", [32, 128, 4096], BF16, kind="Internal").ap(),
        "w3": nc.dram_tensor("c_w3", [8, 128, 4096], BF16, kind="Internal").ap(),
    }

    es = ExitStack()

    def sbuf(name, shape, dtype, side="left"):
        return es.enter_context(nc.sbuf_tensor(name, list(shape), dtype, side=side))

    ones_b = sbuf("ones_b", (128, 512), BF16)
    ones_f = sbuf("ones_f", (128, 128), F32)
    epsT = sbuf("epsT", (128, 1), F32)
    s1s = sbuf("s1s", (128, 32), F32)
    s2s = sbuf("s2s", (128, 32), F32)
    sh1s = sbuf("sh1s", (128, 32), F32)
    sh2s = sbuf("sh2s", (128, 32), F32)
    s3s = sbuf("s3s", (128, 8), F32)
    bq_row_s = sbuf("bq_row_s", (1, 1024), BF16)
    bk_row_s = sbuf("bk_row_s", (1, 1024), BF16)
    bos = sbuf("bos", (128, 8), F32)
    bvs = sbuf("bvs", (1, 1024), BF16)
    V = sbuf("V", (128, 3072), F32)

    dW = Sem(es, nc, "dW")
    dXB = Sem(es, nc, "dXB")
    dPX = Sem(es, nc, "dPX")
    dX = Sem(es, nc, "dX")
    dK = Sem(es, nc, "dK")
    dO = Sem(es, nc, "dO")
    pe = Sem(es, nc, "pe")
    pPV = Sem(es, nc, "pPV")
    pW = Sem(es, nc, "pW")
    aG = Sem(es, nc, "aG")
    aEX = Sem(es, nc, "aEX")
    aEV = Sem(es, nc, "aEV")
    aFE = Sem(es, nc, "aFE")
    vG = Sem(es, nc, "vG")
    vRC = Sem(es, nc, "vRC")
    vNM = Sem(es, nc, "vNM")
    vEM = Sem(es, nc, "vEM")
    vFE = Sem(es, nc, "vFE")
    gMK = Sem(es, nc, "gMK")
    dCW = Sem(es, nc, "dCW")
    dOO = Sem(es, nc, "dOO")
    dXO = Sem(es, nc, "dXO")

    blk = es.enter_context(nc.Block())

    def p0_sync(sy):
        for dst, src in ((s1s, s1_i), (s2s, s2_i), (sh1s, sh1_i), (sh2s, sh2_i),
                         (s3s, s3_i), (bos, bo_i)):
            sy.dma_start(out=dst[:], in_=src).then_inc(dX.h, 16)
            dX.n += 16

    def p0_gp(g):
        for dst, src in ((bq_row_s, bq_r), (bk_row_s, bk_r), (bvs, bv_r)):
            g.dma_start(out=dst[:], in_=src).then_inc(dW.h, 16)
            dW.n += 16

    def p0_vec(v):
        v.memset(ones_b[:], 1.0)
        v.memset(ones_f[:], 1.0)
        vG.inc(v.memset(epsT[:], 1e-5))

    blk.sync(p0_sync)
    blk.gpsimd(p0_gp)
    blk.vector(p0_vec)
    tok_consts_v = vG.n
    tok_consts_w = dW.n
    tok_consts_d = dX.n

    ln_ctr = [0]

    def emit_ln(src_ap, dst, xb, sqb, tb2, gate=None):
        ln_ctr[0] += 1
        u = str(ln_ctr[0])
        ps_es = ExitStack()
        st_sum = ps_es.enter_context(nc.psum_tensor("ln_s" + u, [1, 1024], F32))
        st_sq = ps_es.enter_context(nc.psum_tensor("ln_q" + u, [1, 1024], F32))
        bcR = ps_es.enter_context(nc.psum_tensor("ln_r" + u, [128, 512], F32))
        bcM = ps_es.enter_context(nc.psum_tensor("ln_m" + u, [128, 512], F32))
        prev_affine = [0]
        prev_stats_rd = [0]
        prev_stats = [0]
        for th in (0, 1):
            o = th * 1024
            xb_tok = []

            def w_gp(g, th=th, xb_tok=xb_tok):
                if gate is not None and th == 0:
                    gate(g)
                if th == 1:
                    g.wait_ge(vG.h, prev_affine[0])
                for c in range(8):
                    ins = g.dma_start(
                        out=xb[:, c * 1024:(c + 1) * 1024],
                        in_=src_ap[:, c * 2048 + th * 1024: c * 2048 + th * 1024 + 1024])
                    xb_tok.append(dXB.inc(ins, 16))
            blk.gpsimd(w_gp)
            sq_tok = []

            def w_act(sc, th=th, xb_tok=xb_tok, sq_tok=sq_tok):
                if th == 1:
                    sc.wait_ge(pe.h, prev_stats[0])
                for c in range(8):
                    sc.wait_ge(dXB.h, xb_tok[c])
                    ins = sc.activation(sqb[:, c * 1024:(c + 1) * 1024],
                                        xb[:, c * 1024:(c + 1) * 1024], AF.Square)
                    sq_tok.append(aG.inc(ins))
            blk.scalar(w_act)
            stat_tok = [0]

            def w_pe(t, th=th, xb_tok=xb_tok, sq_tok=sq_tok, stat_tok=stat_tok):
                if th == 1:
                    t.wait_ge(vG.h, prev_stats_rd[0])
                for c in range(8):
                    t.wait_ge(dXB.h, xb_tok[c])
                    t.wait_ge(aG.h, sq_tok[c])
                    for nb in (0, 1):
                        sl = slice(c * 1024 + nb * 512, c * 1024 + nb * 512 + 512)
                        t.matmul(st_sum[0:1, nb * 512:nb * 512 + 512], ones_b[:, 0:1],
                                 xb[:, sl], start=(c == 0), stop=(c == 7))
                        ins = t.matmul(st_sq[0:1, nb * 512:nb * 512 + 512], ones_b[:, 0:1],
                                       sqb[:, sl], start=(c == 0), stop=(c == 7))
                    stat_tok[0] = pe.inc(ins)
            blk.tensor(w_pe)
            prev_stats[0] = pe.n
            va_tok = [0]

            def w_vec(v, stat_tok=stat_tok, va_tok=va_tok):
                v.wait_ge(pe.h, stat_tok[0])
                v.tensor_scalar(V[0:1, 0:1024], st_sum[0:1, :], 1.0 / D, None, ALU.mult)
                v.tensor_scalar(V[0:1, 1024:2048], st_sq[0:1, :], 1.0 / D, None, ALU.mult)
                v.tensor_mul(V[0:1, 2048:3072], V[0:1, 0:1024], V[0:1, 0:1024])
                ins = v.tensor_sub(V[0:1, 1024:2048], V[0:1, 1024:2048], V[0:1, 2048:3072])
                va_tok[0] = vG.inc(ins)
            blk.vector(w_vec)
            prev_stats_rd[0] = vG.n
            sd_tok = [0]

            def w_act2(sc, va_tok=va_tok, sd_tok=sd_tok):
                sc.wait_ge(vG.h, va_tok[0])
                ins = sc.activation(V[0:1, 2048:3072], V[0:1, 1024:2048], AF.Sqrt,
                                    bias=epsT[0:1, 0:1])
                sd_tok[0] = aG.inc(ins)
            blk.scalar(w_act2)
            vb_tok = [0]

            def w_vec2(v, sd_tok=sd_tok, vb_tok=vb_tok):
                v.wait_ge(aG.h, sd_tok[0])
                v.reciprocal(V[0:1, 1024:2048], V[0:1, 2048:3072])
                ins = v.tensor_mul(V[0:1, 2048:3072], V[0:1, 0:1024], V[0:1, 1024:2048])
                vb_tok[0] = vG.inc(ins)
            blk.vector(w_vec2)
            for lb in (0, 1):
                bc_tok = [0]

                def w_pe2(t, lb=lb, vb_tok=vb_tok, bc_tok=bc_tok):
                    t.wait_ge(vG.h, max(vb_tok[0], prev_affine[0]))
                    t.matmul(bcR[:, :], ones_f[0:1, :],
                             V[0:1, 1024 + lb * 512:1024 + lb * 512 + 512],
                             start=True, stop=True)
                    ins = t.matmul(bcM[:, :], ones_f[0:1, :],
                                   V[0:1, 2048 + lb * 512:2048 + lb * 512 + 512],
                                   start=True, stop=True)
                    bc_tok[0] = pe.inc(ins)
                blk.tensor(w_pe2)

                def w_vec3(v, lb=lb, th=th, bc_tok=bc_tok):
                    v.wait_ge(pe.h, bc_tok[0])
                    for c in range(8):
                        sl = slice(c * 1024 + lb * 512, c * 1024 + lb * 512 + 512)
                        v.tensor_tensor(tb2[:, (c % 2) * 512:(c % 2) * 512 + 512],
                                        xb[:, sl], bcR[:, :], ALU.mult)
                        ins = v.tensor_tensor(
                            dst[:, c * 2048 + th * 1024 + lb * 512:
                                c * 2048 + th * 1024 + lb * 512 + 512],
                            tb2[:, (c % 2) * 512:(c % 2) * 512 + 512], bcM[:, :],
                            ALU.subtract)
                        vG.inc(ins)
                blk.vector(w_vec3)
                prev_affine[0] = vG.n
        ps_es.close()
        return vG.n

    blk.tensor(lambda t: t.wait_ge(vG.h, tok_consts_v))
    blk.scalar(lambda sc: (sc.wait_ge(vG.h, tok_consts_v), sc.wait_ge(dX.h, tok_consts_d)))
    blk.vector(lambda v: v.wait_ge(dX.h, tok_consts_d))

    prj_es = [None]

    def emit_proj(nk, nm, nn, nstep, lhsT_fn, rhs_fn, evac_fn, ps_a, ps_b,
                  dma_toks=(), bias_row=None):
        aFE_b = aFE.n
        tiles = [(mi, ni) for mi in range(nm) for ni in range(nn)]
        pe_tok = []

        def f_pe(t):
            for sem, tok in dma_toks:
                t.wait_ge(sem.h, tok)
            for g, (mi, ni) in enumerate(tiles):
                ps = (ps_a, ps_b)[g % 2]
                if aFE_b + g - 1 > 0:
                    t.wait_ge(aFE.h, aFE_b + g - 1)
                last = nk - 1 if bias_row is None else nk
                for kc in range(nk):
                    ins = t.matmul(ps[:, :], lhsT_fn(kc, mi), rhs_fn(kc, ni),
                                   start=(kc == 0), stop=(kc == last))
                if bias_row is not None:
                    ins = t.matmul(ps[:, :], bias_row[0:1, mi * 128:mi * 128 + 128],
                                   ones_b[0:1, 0:nstep], start=False, stop=True)
                pe_tok.append(pe.inc(ins))

        def f_act(sc):
            for g, (mi, ni) in enumerate(tiles):
                ps = (ps_a, ps_b)[g % 2]
                sc.wait_ge(pe.h, pe_tok[g])
                aFE.inc(evac_fn(sc, mi, ni, ps[:, :]))
        blk.tensor(f_pe)
        blk.scalar(f_act)
        return aFE.n

    a_es = ExitStack()

    def rsb(name, shape, dtype):
        return a_es.enter_context(nc.sbuf_tensor(name, list(shape), dtype, side="right"))

    def lsb(name, shape, dtype):
        return a_es.enter_context(nc.sbuf_tensor(name, list(shape), dtype, side="left"))

    ckv_i = lsb("ckv_i", (128, 8192), BF16)
    keT_i = lsb("keT_i", (128, 4096), BF16)
    veT_i = lsb("veT_i", (128, 4096), BF16)

    p2_es = ExitStack()
    q2_i = p2_es.enter_context(nc.sbuf_tensor("q2_i", [128, 8192], BF16, side="right"))
    ln1_es = ExitStack()
    ln1 = ln1_es.enter_context(nc.sbuf_tensor("ln1", [128, 16384], BF16, side="right"))
    ln_es = ExitStack()
    xb1 = ln_es.enter_context(nc.sbuf_tensor("xb1", [128, 8192], BF16, side="right"))
    sqb1 = ln_es.enter_context(nc.sbuf_tensor("sqb1", [128, 8192], BF16, side="right"))
    tb21 = ln_es.enter_context(nc.sbuf_tensor("tb21", [128, 1024], F32, side="right"))
    ln1_tok = emit_ln(xT, ln1, xb1, sqb1, tb21)
    ln_es.close()

    p2ps = ExitStack()
    pr_a = p2ps.enter_context(nc.psum_tensor("pr_a", [128, 512], F32))
    pr_b = p2ps.enter_context(nc.psum_tensor("pr_b", [128, 512], F32))

    def load_w(g, buf_slices, srcs):
        toks = []
        for sl, src in zip(buf_slices, srcs):
            toks.append(dW.inc(g.dma_start(out=sl, in_=src), 16))
        return toks

    aw1_es = ExitStack()
    aw1 = aw1_es.enter_context(nc.sbuf_tensor("aw1", [128, 8192], BF16, side="right"))
    t_aw1 = []
    blk.gpsimd(lambda g: (g.wait_ge(vG.h, ln1_tok),
                          t_aw1.extend(load_w(
                              g, [aw1[:, 0:4096], aw1[:, 4096:8192]], [aw_dkv, aw_dq]))))

    def cp_evac(dst_fn):
        def f(sc, mi, ni, ps):
            return sc.copy(dst_fn(mi, ni), ps)
        return f

    emit_proj(8, 4, 4, 512,
              lambda kc, mi: aw1[:, kc * 512 + mi * 128: kc * 512 + mi * 128 + 128],
              lambda kc, ni: ln1[:, kc * 2048 + ni * 512: kc * 2048 + ni * 512 + 512],
              cp_evac(lambda mi, ni: ckv_i[:, mi * 2048 + ni * 512: mi * 2048 + ni * 512 + 512]),
              pr_a, pr_b, dma_toks=[(dW, t_aw1[0]), (vG, ln1_tok)])
    ckv_done = aFE.n
    emit_proj(8, 4, 4, 512,
              lambda kc, mi: aw1[:, 4096 + kc * 512 + mi * 128: 4096 + kc * 512 + mi * 128 + 128],
              lambda kc, ni: ln1[:, kc * 2048 + ni * 512: kc * 2048 + ni * 512 + 512],
              cp_evac(lambda mi, ni: q2_i[:, mi * 2048 + ni * 512: mi * 2048 + ni * 512 + 512]),
              pr_a, pr_b, dma_toks=[(dW, t_aw1[1]), (vG, ln1_tok)])
    q2_done = aFE.n
    aw1_free_tok = pe.n
    aw1_es.close()
    ln1_es.close()

    aw2_es = ExitStack()
    aw2 = aw2_es.enter_context(nc.sbuf_tensor("aw2", [128, 8192], BF16, side="right"))
    B_es = ExitStack()
    B_i = B_es.enter_context(nc.sbuf_tensor("B_i", [128, 8192], BF16, side="right"))
    t_aw2 = []
    blk.gpsimd(lambda g: (g.wait_ge(pe.h, aw1_free_tok),
                          t_aw2.extend(load_w(
                              g, [aw2[:, 0:4096], aw2[:, 4096:8192]], [aw_uq, aw_dqn]))))
    emit_proj(4, 8, 2, 512,
              lambda kc, mi: aw2[:, kc * 1024 + mi * 128: kc * 1024 + mi * 128 + 128],
              lambda kc, ni: aw2[:, 4096 + kc * 1024 + ni * 512: 4096 + kc * 1024 + ni * 512 + 512],
              cp_evac(lambda mi, ni: B_i[:, mi * 1024 + ni * 512: mi * 1024 + ni * 512 + 512]),
              pr_a, pr_b, dma_toks=[(dW, t_aw2[1])])
    B_done = aFE.n

    aw3_es = ExitStack()
    aw3 = aw3_es.enter_context(nc.sbuf_tensor("aw3", [128, 8192], BF16, side="right"))
    t_aw3 = []
    blk.gpsimd(lambda g: t_aw3.extend(load_w(
        g, [aw3[:, 0:4096], aw3[:, 4096:8192]], [aw_uk, aw_uv])))
    emit_proj(8, 4, 2, 512,
              lambda kc, mi: aw3[:, kc * 512 + mi * 128: kc * 512 + mi * 128 + 128],
              lambda kc, ni: B_i[:, kc * 1024 + ni * 512: kc * 1024 + ni * 512 + 512],
              cp_evac(lambda mi, ni: keT_i[:, mi * 1024 + ni * 512: mi * 1024 + ni * 512 + 512]),
              pr_a, pr_b, dma_toks=[(dW, t_aw3[0]), (aFE, B_done)])
    keT_done = aFE.n
    aw4_es = ExitStack()
    aw4 = aw4_es.enter_context(nc.sbuf_tensor("aw4", [128, 8192], BF16, side="right"))
    t_aw4 = []
    blk.gpsimd(lambda g: t_aw4.extend(load_w(g, [aw4[:, 0:8192]], [aw_o])))
    emit_proj(8, 4, 2, 512,
              lambda kc, mi: aw3[:, 4096 + kc * 512 + mi * 128: 4096 + kc * 512 + mi * 128 + 128],
              lambda kc, ni: aw4[:, kc * 1024 + ni * 512: kc * 1024 + ni * 512 + 512],
              cp_evac(lambda mi, ni: veT_i[:, mi * 1024 + ni * 512: mi * 1024 + ni * 512 + 512]),
              pr_a, pr_b, dma_toks=[(dW, t_aw4[0])])
    veT_done = aFE.n
    aw4_es.close()

    q_i = lsb("q_i", (128, 16384), BF16)
    emit_proj(4, 8, 4, 512,
              lambda kc, mi: aw2[:, kc * 1024 + mi * 128: kc * 1024 + mi * 128 + 128],
              lambda kc, ni: q2_i[:, kc * 2048 + ni * 512: kc * 2048 + ni * 512 + 512],
              cp_evac(lambda mi, ni: q_i[:, mi * 2048 + ni * 512: mi * 2048 + ni * 512 + 512]),
              pr_a, pr_b, dma_toks=[(aFE, q2_done)])
    qT_free = pe.n
    aw3_es.close()
    B_es.close()
    aw2_es.close()
    p2_es.close()

    keys_i = lsb("keys_i", (128, 16384), BF16)
    u_i = lsb("u_i", (128, 16640), BF16)
    emit_proj(4, 8, 4, 512,
              lambda kc, mi: keT_i[:, kc * 1024 + mi * 128: kc * 1024 + mi * 128 + 128],
              lambda kc, ni: ckv_i[:, kc * 2048 + ni * 512: kc * 2048 + ni * 512 + 512],
              cp_evac(lambda mi, ni: keys_i[:, mi * 2048 + ni * 512: mi * 2048 + ni * 512 + 512]),
              pr_a, pr_b, dma_toks=[(aFE, keT_done)])
    keys_done = aFE.n

    u_ones_tok = [0]
    pe_gate_u = pe.n
    blk.vector(lambda v: (v.wait_ge(pe.h, pe_gate_u),
                          u_ones_tok.__setitem__(0, vG.inc(v.memset(
                              u_i.ap().rearrange("p (kc h s) -> p kc h s",
                                                 kc=16, h=16)[:, :, :, 64:65], 1.0)))))
    p2ps.close()
    ups = ExitStack()
    u_ps_a = ups.enter_context(nc.psum_tensor("u_ps_a", [128, 1024], F32))
    u_ps_b = ups.enter_context(nc.psum_tensor("u_ps_b", [128, 1024], F32))
    u_pe_tok = []
    aFE_u_b = aFE.n

    def f_u_pe(t):
        t.wait_ge(aFE.h, max(veT_done, ckv_done, keys_done))
        for kc in range(16):
            ps = (u_ps_a, u_ps_b)[kc % 2]
            if kc >= 2:
                t.wait_ge(aFE.h, aFE_u_b + kc - 1)
            for lc in range(4):
                for nb in (0, 1):
                    ins = t.matmul(ps[:, nb * 512:nb * 512 + 512],
                                   ckv_i[:, lc * 2048 + kc * 128: lc * 2048 + kc * 128 + 128],
                                   veT_i[:, lc * 1024 + nb * 512: lc * 1024 + nb * 512 + 512],
                                   start=(lc == 0), stop=(lc == 3))
            u_pe_tok.append(pe.inc(ins))

    def f_u_act(sc):
        sc.wait_ge(vG.h, u_ones_tok[0])
        for kc in range(16):
            ps = (u_ps_a, u_ps_b)[kc % 2]
            sc.wait_ge(pe.h, u_pe_tok[kc])
            dst = u_i.ap().rearrange("p (kc h s) -> p kc h s", kc=16, h=16)[:, kc, :, 0:64]
            aFE.inc(sc.copy(dst, ps.ap().rearrange("p (h d) -> p h d", h=16)))
    blk.tensor(f_u_pe)
    blk.scalar(f_u_act)
    u_done = aFE.n
    ups.close()

    p3_es = ExitStack()

    def r3(name, shape, dtype):
        return p3_es.enter_context(nc.sbuf_tensor(name, list(shape), dtype, side="right"))

    E2 = r3("E2", (128, 2048), BF16)
    nrm = r3("nrm", (128, 2048), F32)
    o_ev = r3("o_ev", (128, 2048), F32)
    xr = r3("xr", (128, 4096), F32)
    xt = r3("xt", (128, 2048), F32)
    p3ps = ExitStack()
    qk_ps = p3ps.enter_context(nc.psum_tensor("qk_ps", [128, 2048], F32))
    pv_ps = p3ps.enter_context(nc.psum_tensor("pv_ps", [65, 1024], F32))
    bc_ps = p3ps.enter_context(nc.psum_tensor("bc_ps", [64, 1024], F32))

    tiles3 = []
    for h in range(H):
        for lbh in (0, 1):
            for rc in range(8 * (lbh + 1)):
                tiles3.append((h, lbh, rc))
    G3 = len(tiles3)
    masked3 = [(rc >= 8 * lbh) for (h, lbh, rc) in tiles3]
    nmask3 = np.cumsum(masked3).tolist()
    first3 = [g for g in range(G3) if tiles3[g][2] == 0]
    last3 = [g - 1 for g in first3[1:]] + [G3 - 1]
    idx_of = {}
    for i, g in enumerate(first3):
        for gg in range(g, last3[i] + 1):
            idx_of[gg] = i
    aEX_b, aEV_b, vRC_b, vNM_b, pPV_b, gMK_b, dO_b, dX_b = (
        aEX.n, aEV.n, vRC.n, vNM.n, pPV.n, gMK.n, dO.n, dX.n)

    def f3_sync(sy):
        sy.wait_ge(pe.h, qT_free)

        def store(idx):
            h, lbh = idx // 2, idx % 2
            hc, rb = h // 2, (h % 2) * 64
            sy.wait_ge(vNM.h, vNM_b + idx + 1)
            sy.dma_start(
                out=parkA[rb:rb + 64, hc * 2048 + lbh * 1024: hc * 2048 + lbh * 1024 + 1024],
                in_=xt[0:64, (idx % 2) * 1024:(idx % 2) * 1024 + 1024]).then_inc(dO.h, 16)
            dO.n += 16
        for idx in range(2 * H):
            h, lbh = idx // 2, idx % 2
            hc, rb = h // 2, (h % 2) * 64
            if idx >= 4:
                sy.wait_ge(vNM.h, vNM_b + idx - 3)
            sy.dma_start(
                out=xr[0:64, (h % 2) * 2048 + lbh * 1024:(h % 2) * 2048 + lbh * 1024 + 1024],
                in_=xT[rb:rb + 64, hc * 2048 + lbh * 1024: hc * 2048 + lbh * 1024 + 1024]
            ).then_inc(dX.h, 16)
            dX.n += 16
            if idx >= 2:
                store(idx - 2)
        store(2 * H - 2)
        store(2 * H - 1)
    blk.sync(f3_sync)

    qk_tok = [0] * G3
    bc_tok = [0] * (2 * H)

    def f3_pe(t):
        t.wait_ge(aFE.h, max(keys_done, u_done))
        t.wait_ge(vG.h, u_ones_tok[0])

        def emit_qk(g):
            h, lbh, rc = tiles3[g]
            hc, rb = h // 2, (h % 2) * 64
            if aEX_b + g - 1 > 0:
                t.wait_ge(aEX.h, aEX_b + g - 1)
            for cb in (0, 1):
                ins = t.matmul(
                    qk_ps[:, (g % 2) * 1024 + cb * 512:(g % 2) * 1024 + cb * 512 + 512],
                    keys_i[rb:rb + 64, hc * 2048 + rc * 128: hc * 2048 + rc * 128 + 128],
                    q_i[rb:rb + 64, hc * 2048 + lbh * 1024 + cb * 512:
                        hc * 2048 + lbh * 1024 + cb * 512 + 512],
                    start=True, stop=True)
            qk_tok[g] = pe.inc(ins)

        def emit_pv(g):
            h, lbh, rc = tiles3[g]
            idx = idx_of[g]
            t.wait_ge(aEX.h, aEX_b + g + 1)
            if masked3[g]:
                t.wait_ge(gMK.h, gMK_b + nmask3[g])
            if rc == 0 and idx > 0:
                t.wait_ge(aEV.h, aEV_b + idx)
                t.wait_ge(vRC.h, vRC_b + idx)
            for cb in (0, 1):
                ins = t.matmul(
                    pv_ps[:, cb * 512:cb * 512 + 512],
                    u_i[:, rc * 1040 + h * 65: rc * 1040 + h * 65 + 65],
                    E2[:, (g % 2) * 1024 + cb * 512:(g % 2) * 1024 + cb * 512 + 512],
                    start=(rc == 0), stop=(rc == 8 * (lbh + 1) - 1))
            pPV.inc(ins)

        def emit_bc(idx):
            t.wait_ge(vRC.h, vRC_b + idx + 1)
            if idx > 0:
                t.wait_ge(vNM.h, vNM_b + idx)
            for cb in (0, 1):
                ins = t.matmul(
                    bc_ps[:, cb * 512:cb * 512 + 512], ones_f[64:65, 0:64],
                    nrm[64:65, (idx % 2) * 1024 + cb * 512:(idx % 2) * 1024 + cb * 512 + 512],
                    start=True, stop=True)
            bc_tok[idx] = pe.inc(ins)

        for g in range(G3):
            emit_qk(g)
            if g >= 1:
                emit_pv(g - 1)
            if tiles3[g][2] == 1 and idx_of[g] > 0:
                emit_bc(idx_of[g] - 1)
        emit_pv(G3 - 1)
        emit_bc(2 * H - 1)
    blk.tensor(f3_pe)

    ev_tok = [0] * (2 * H)

    def f3_act(sc):
        for g in range(G3):
            h, lbh, rc = tiles3[g]
            idx = idx_of[g]
            sc.wait_ge(pe.h, qk_tok[g])
            if pPV_b + g - 1 > 0:
                sc.wait_ge(pPV.h, pPV_b + g - 1)
            aEX.inc(sc.activation(E2[:, (g % 2) * 1024:(g % 2) * 1024 + 1024],
                                  qk_ps[:, (g % 2) * 1024:(g % 2) * 1024 + 1024],
                                  AF.Exp, scale=0.125))
            if g == last3[idx]:
                sc.wait_ge(pPV.h, pPV_b + g + 1)
                if idx >= 2:
                    sc.wait_ge(vNM.h, vNM_b + idx - 1)
                ins = sc.copy(o_ev[0:64, (idx % 2) * 1024:(idx % 2) * 1024 + 1024],
                              pv_ps[0:64, :])
                ev_tok[idx] = aEV.inc(ins)
    blk.scalar(f3_act)

    def f3_pool(g_eng):
        for g in range(G3):
            if not masked3[g]:
                continue
            h, lbh, rc = tiles3[g]
            g_eng.wait_ge(aEX.h, aEX_b + g + 1)
            ins = g_eng.affine_select(
                E2[:, (g % 2) * 1024:(g % 2) * 1024 + 1024],
                E2[:, (g % 2) * 1024:(g % 2) * 1024 + 1024],
                [[1, 1024]], ALU.is_ge, 0.0,
                base=lbh * 1024 - rc * 128, channel_multiplier=-1)
            gMK.inc(ins)
    blk.gpsimd(f3_pool)

    def f3_vec(v):
        for idx in range(2 * H):
            h, lbh = idx // 2, idx % 2
            v.wait_ge(aEV.h, ev_tok[idx])
            vRC.inc(v.reciprocal(nrm[64:65, (idx % 2) * 1024:(idx % 2) * 1024 + 1024],
                                 pv_ps[64:65, :]))
            v.wait_ge(pe.h, bc_tok[idx])
            v.wait_ge(dX.h, dX_b + 16 * (idx + 1))
            osl = o_ev[0:64, (idx % 2) * 1024:(idx % 2) * 1024 + 1024]
            v.tensor_tensor(osl, osl, bc_ps[0:64, :], ALU.mult)
            if dO_b + 16 * (idx - 1) > 0:
                v.wait_ge(dO.h, dO_b + 16 * (idx - 1))
            ins = v.tensor_tensor(
                xt[0:64, (idx % 2) * 1024:(idx % 2) * 1024 + 1024], osl,
                xr[0:64, (h % 2) * 2048 + lbh * 1024:(h % 2) * 2048 + lbh * 1024 + 1024],
                ALU.add)
            vNM.inc(ins)
    blk.vector(f3_vec)

    x1_tok = dO.n

    p3ps.close()
    p3_es.close()
    a_es.close()

    ffn_ctr = [0]

    def emit_ffn(ln_src, park_src, dst, first, x_gate, ln_tok):
        ffn_ctr[0] += 1
        fu = str(ffn_ctr[0])
        f_es = ExitStack()

        def rf(name, shape, dtype):
            return f_es.enter_context(nc.sbuf_tensor(name + fu, list(shape), dtype, side="right"))

        HA = rf("HA", (128, 32768), BF16)
        HB = rf("HB", (128, 32768), BF16)
        wb = rf("wb", (128, 8192), BF16)
        xold = rf("xold", (128, 1024), F32)
        xn = rf("xn", (128, 1024), F32)
        tAB = rf("tAB", (128, 1024), BF16)
        fps = ExitStack()
        psA = [fps.enter_context(nc.psum_tensor(f"fpsA{i}" + fu, [128, 512], F32)) for i in range(2)]
        psB = [fps.enter_context(nc.psum_tensor(f"fpsB{i}" + fu, [128, 512], F32)) for i in range(2)]

        stages_d = [
            ("A", 32, 8, (f_w1, f_w2), ("w1", "w2"), 1024),
            ("B", 32, 32, (f_h1,), ("h1",), 4096),
            ("C", 32, 32, (f_h2,), ("h2",), 4096),
            ("D", 8, 32, (f_w3,), ("w3",), 4096),
        ]
        prev_stage_done = [pe.n, aFE.n, vFE.n]

        for th in (0, 1):
            for kind, n_og, n_ic, srcs, cks, wcols in stages_d:
                nsrc = len(srcs)
                use_cache = (not first) or th == 1
                dma_sem = dX if use_cache else dW
                dma_b = dma_sem.n
                pW_b = pW.n
                aFE_b = aFE.n
                vFE_b = vFE.n
                dO_b = dO.n
                dX_b = dX.n
                dXO_b_d = dXO.n
                pe_prev, aFE_prev, vFE_prev = prev_stage_done

                pe_tok = {}

                def f_pe(t, kind=kind, n_og=n_og, n_ic=n_ic, nsrc=nsrc, wcols=wcols,
                         dma_sem=dma_sem, dma_b=dma_b, pW_b=pW_b, aFE_b=aFE_b,
                         vFE_b=vFE_b, dX_b=dX_b, th=th, pe_tok=pe_tok,
                         aFE_prev=aFE_prev, vFE_prev=vFE_prev):
                    if kind == "A" and th == 0:
                        t.wait_ge(vG.h, ln_tok)
                    if aFE_prev:
                        t.wait_ge(aFE.h, aFE_prev)
                    if vFE_prev:
                        t.wait_ge(vFE.h, vFE_prev)
                    for og in range(n_og):
                        t.wait_ge(dma_sem.h, dma_b + 16 * nsrc * (og + 1))
                        sl = wb[:, (og % 2) * 4096:(og % 2) * 4096 + 4096]
                        for tb in (0, 1):
                            s = og * 2 + tb
                            rhs_off = th * 1024 + tb * 512
                            if kind == "A":
                                if aFE_b + s - 1 > 0:
                                    t.wait_ge(aFE.h, aFE_b + s - 1)
                                if vFE_b + s - 1 > 0:
                                    t.wait_ge(vFE.h, vFE_b + s - 1)
                                for w_i in (0, 1):
                                    ps = (psA, psB)[w_i][s % 2]
                                    for ic in range(8):
                                        ins = t.matmul(
                                            ps[:, :],
                                            sl[:, w_i * 1024 + ic * 128: w_i * 1024 + ic * 128 + 128],
                                            ln_src[:, ic * 2048 + rhs_off: ic * 2048 + rhs_off + 512],
                                            start=(ic == 0), stop=(ic == 7))
                                    if w_i == 0:
                                        pe_tok[(og, tb, 0)] = pe.inc(ins)
                                if tb == 1:
                                    pW.inc(ins)
                                    pe_tok[(og, tb, 1)] = None
                                else:
                                    pe_tok[(og, tb, 1)] = pe.inc(ins)
                            else:
                                if aFE_b + s - 1 > 0 and kind != "D":
                                    t.wait_ge(aFE.h, aFE_b + s - 1)
                                if vFE_b + s - 1 > 0 and kind == "D":
                                    t.wait_ge(vFE.h, vFE_b + s - 1)
                                src_h = HA if kind in ("B", "D") else HB
                                ps = psA[s % 2]
                                for ic in range(n_ic):
                                    ins = t.matmul(
                                        ps[:, :], sl[:, ic * 128: ic * 128 + 128],
                                        src_h[:, ic * 1024 + tb * 512: ic * 1024 + tb * 512 + 512],
                                        start=(ic == 0), stop=(ic == n_ic - 1))
                                if tb == 1:
                                    pW.inc(ins)
                                    pe_tok[(og, tb)] = None
                                else:
                                    pe_tok[(og, tb)] = pe.inc(ins)
                blk.tensor(f_pe)
                pe_stage_done = pe.n

                if kind in ("A", "B", "C"):
                    def f_act(sc, kind=kind, n_og=n_og, th=th, pe_tok=pe_tok,
                              pW_b=pW_b, vFE_b=vFE_b):
                        for og in range(n_og):
                            for tb in (0, 1):
                                s = og * 2 + tb
                                if kind == "A":
                                    tok = pe_tok[(og, tb, 0)]
                                else:
                                    tok = pe_tok[(og, tb)]
                                if tok is None:
                                    sc.wait_ge(pW.h, pW_b + og + 1)
                                else:
                                    sc.wait_ge(pe.h, tok)
                                if kind == "A":
                                    if vFE_b + s - 1 > 0:
                                        sc.wait_ge(vFE.h, vFE_b + s - 1)
                                    ins = sc.activation(
                                        tAB[:, (s % 2) * 512:(s % 2) * 512 + 512],
                                        psA[s % 2][:, :], AF.Silu, scale=s1s[:, og:og + 1])
                                else:
                                    scl = (sh1s if kind == "B" else sh2s)[:, og:og + 1]
                                    dstH = HB if kind == "B" else HA
                                    ins = sc.activation(
                                        dstH[:, og * 1024 + tb * 512: og * 1024 + tb * 512 + 512],
                                        psA[s % 2][:, :], AF.Gelu, scale=scl)
                                aFE.inc(ins)
                    blk.scalar(f_act)
                aFE_stage_done = aFE.n

                if kind in ("A", "D"):
                    def f_vec(v, kind=kind, n_og=n_og, th=th, pe_tok=pe_tok,
                              pW_b=pW_b, dO_b=dO_b, dX_b=dX_b, aFE_b=aFE_b):
                        for og in range(n_og):
                            for tb in (0, 1):
                                s = og * 2 + tb
                                if kind == "A":
                                    tok = pe_tok[(og, tb, 1)]
                                    if tok is None:
                                        v.wait_ge(pW.h, pW_b + og + 1)
                                    else:
                                        v.wait_ge(pe.h, tok)
                                    v.wait_ge(aFE.h, aFE_b + s + 1)
                                    ins = v.scalar_tensor_tensor(
                                        HA[:, og * 1024 + tb * 512: og * 1024 + tb * 512 + 512],
                                        psB[s % 2][:, :], s2s[:, og:og + 1],
                                        tAB[:, (s % 2) * 512:(s % 2) * 512 + 512],
                                        ALU.mult, ALU.mult)
                                    vFE.inc(ins)
                                else:
                                    tok = pe_tok[(og, tb)]
                                    if tok is None:
                                        v.wait_ge(pW.h, pW_b + og + 1)
                                    else:
                                        v.wait_ge(pe.h, tok)
                                    v.wait_ge(dXO.h, dXO_b_d + 16 * (og + 1))
                                    if dO_b + 16 * (s - 1) > 0:
                                        v.wait_ge(dO.h, dO_b + 16 * (s - 1))
                                    ins = v.scalar_tensor_tensor(
                                        xn[:, (s % 2) * 512:(s % 2) * 512 + 512],
                                        psA[s % 2][:, :], s3s[:, og:og + 1],
                                        xold[:, tb * 512: tb * 512 + 512],
                                        ALU.mult, ALU.add)
                                    vFE.inc(ins)
                    blk.vector(f_vec)
                vFE_stage_done = vFE.n

                wwid = 1024 if nsrc == 2 else 4096
                vFE_stage_b_d = vFE_b

                def emit_store(sy, og, tb, th=th):
                    s = og * 2 + tb
                    sy.wait_ge(vFE.h, vFE_stage_b_d + s + 1)
                    sy.dma_start(
                        out=dst[:, og * 2048 + th * 1024 + tb * 512:
                                og * 2048 + th * 1024 + tb * 512 + 512],
                        in_=xn[:, (s % 2) * 512:(s % 2) * 512 + 512]
                    ).then_inc(dO.h, 16)
                    dO.n += 16

                if not use_cache:
                    dCW_pre = dCW.n

                    def f_gp(g, n_og=n_og, nsrc=nsrc, srcs=srcs, pW_b=pW_b,
                             dCW_pre=dCW_pre, wwid=wwid):
                        g.wait_ge(dO.h, x_gate)
                        for og in range(n_og):
                            g.wait_ge(pW.h, pW_b + max(0, og - 1))
                            if og >= 2:
                                g.wait_ge(dCW.h, dCW_pre + 16 * nsrc * (og - 1))
                            sl = wb[:, (og % 2) * 4096:(og % 2) * 4096 + 4096]
                            for w_i, srcw in enumerate(srcs):
                                ins = g.dma_start(
                                    out=sl[:, w_i * wwid: w_i * wwid + wwid],
                                    in_=srcw[og])
                                dW.inc(ins, 16)

                    def f_syncw(sy, n_og=n_og, nsrc=nsrc, cks=cks, dma_b=dma_b,
                                wwid=wwid, kind=kind):
                        for og in range(n_og):
                            sy.wait_ge(dW.h, dma_b + 16 * nsrc * (og + 1))
                            sl = wb[:, (og % 2) * 4096:(og % 2) * 4096 + 4096]
                            for w_i, ck in enumerate(cks):
                                ins = sy.dma_start(
                                    out=cach[ck][og],
                                    in_=sl[:, w_i * wwid: w_i * wwid + wwid])
                                dCW.inc(ins, 16)
                            if kind == "D" and og >= 1:
                                emit_store(sy, og - 1, 0)
                                emit_store(sy, og - 1, 1)
                        if kind == "D":
                            emit_store(sy, n_og - 1, 0)
                            emit_store(sy, n_og - 1, 1)
                    blk.gpsimd(f_gp)
                    blk.sync(f_syncw)
                else:
                    def f_syncr(sy, n_og=n_og, nsrc=nsrc, cks=cks, pW_b=pW_b,
                                wwid=wwid, kind=kind):
                        sy.wait_ge(dO.h, x_gate)
                        for og in range(n_og):
                            sy.wait_ge(pW.h, pW_b + max(0, og - 1))
                            sl = wb[:, (og % 2) * 4096:(og % 2) * 4096 + 4096]
                            for w_i, ck in enumerate(cks):
                                ins = sy.dma_start(
                                    out=sl[:, w_i * wwid: w_i * wwid + wwid],
                                    in_=cach[ck][og])
                                dX.inc(ins, 16)
                            if kind == "D" and og >= 1:
                                emit_store(sy, og - 1, 0)
                                emit_store(sy, og - 1, 1)
                        if kind == "D":
                            emit_store(sy, n_og - 1, 0)
                            emit_store(sy, n_og - 1, 1)
                    blk.sync(f_syncr)

                if kind == "D":
                    dXO_b = dXO.n

                    def f_actd(sc, n_og=n_og, th=th, x_gate=x_gate):
                        sc.wait_ge(dO.h, x_gate)
                        for og in range(n_og):
                            if og >= 1:
                                sc.wait_ge(vFE.h, vFE_stage_b_d + 2 * og)
                            sc.dma_start(
                                out=xold[:, 0:1024],
                                in_=park_src[:, og * 2048 + th * 1024: og * 2048 + th * 1024 + 1024]
                            ).then_inc(dXO.h, 16)
                            dXO.n += 16
                    blk.scalar(f_actd)

                prev_stage_done = [pe_stage_done, aFE_stage_done, vFE_stage_done]
        fps.close()
        f_es.close()
        return dO.n

    if stages >= 2:
        f1_es = ExitStack()
        ln2 = f1_es.enter_context(nc.sbuf_tensor("ln2", [128, 16384], BF16, side="right"))
        ln2s = ExitStack()
        xb2 = ln2s.enter_context(nc.sbuf_tensor("xb2", [128, 8192], BF16, side="right"))
        sqb2 = ln2s.enter_context(nc.sbuf_tensor("sqb2", [128, 8192], BF16, side="right"))
        tb22 = ln2s.enter_context(nc.sbuf_tensor("tb22", [128, 1024], F32, side="right"))
        ln2_tok = emit_ln(parkA, ln2, xb2, sqb2, tb22,
                          gate=lambda g: g.wait_ge(dO.h, x1_tok))
        ln2s.close()
        x2_tok = emit_ffn(ln2, parkA, parkB, True, x1_tok, ln2_tok)
        f1_es.close()

    if stages >= 3:
        b_es = ExitStack()

        def rb_(name, shape, dtype):
            return b_es.enter_context(nc.sbuf_tensor(name, list(shape), dtype, side="right"))

        qb_i = rb_("qb_i", (128, 16384), BF16)
        kb_i = rb_("kb_i", (128, 16384), BF16)
        v_i = rb_("v_i", (128, 16640), BF16)
        rel2_s = rb_("rel2_s", (128, 4095), BF16)
        ln3_es = ExitStack()
        ln3 = ln3_es.enter_context(nc.sbuf_tensor("ln3", [128, 16384], BF16, side="right"))
        ln3s = ExitStack()
        xb3 = ln3s.enter_context(nc.sbuf_tensor("xb3", [128, 8192], BF16, side="right"))
        sqb3 = ln3s.enter_context(nc.sbuf_tensor("sqb3", [128, 8192], BF16, side="right"))
        tb23 = ln3s.enter_context(nc.sbuf_tensor("tb23", [128, 1024], F32, side="right"))
        dcw_t, dxo_t = dCW.n, dXO.n
        ln3_done = emit_ln(parkB, ln3, xb3, sqb3, tb23,
                           gate=lambda g: (g.wait_ge(dO.h, x2_tok),
                                           g.wait_ge(dCW.h, dcw_t),
                                           g.wait_ge(dXO.h, dxo_t)))
        ln3_stats_pe = pe.n
        ln3s.close()

        t_rel = []
        blk.gpsimd(lambda g: (g.wait_ge(vG.h, ln3_done), g.wait_ge(pe.h, ln3_stats_pe),
                              t_rel.append(dW.inc(
                                  g.dma_start(out=rel2_s[:], in_=rel2_d), 16))))

        bps = ExitStack()
        pb_a = bps.enter_context(nc.psum_tensor("pb_a", [128, 512], F32))
        pb_b = bps.enter_context(nc.psum_tensor("pb_b", [128, 512], F32))
        bw_es = ExitStack()
        bwb = bw_es.enter_context(nc.sbuf_tensor("bwb", [128, 16384], BF16, side="right"))
        t_bw = []
        blk.gpsimd(lambda g: (g.wait_ge(vG.h, ln3_done), g.wait_ge(pe.h, ln3_stats_pe),
                              t_bw.extend(load_w(
                                  g, [bwb[:, 0:8192], bwb[:, 8192:16384]], [bw_q, bw_k]))))

        ln3_tok = vG.n
        emit_proj(8, 8, 4, 512,
                  lambda kc, mi: bwb[:, kc * 1024 + mi * 128: kc * 1024 + mi * 128 + 128],
                  lambda kc, ni: ln3[:, kc * 2048 + ni * 512: kc * 2048 + ni * 512 + 512],
                  cp_evac(lambda mi, ni: qb_i[:, mi * 2048 + ni * 512: mi * 2048 + ni * 512 + 512]),
                  pb_a, pb_b, dma_toks=[(dW, t_bw[0]), (vG, ln3_tok)],
                  bias_row=bq_row_s)
        emit_proj(8, 8, 4, 512,
                  lambda kc, mi: bwb[:, 8192 + kc * 1024 + mi * 128: 8192 + kc * 1024 + mi * 128 + 128],
                  lambda kc, ni: ln3[:, kc * 2048 + ni * 512: kc * 2048 + ni * 512 + 512],
                  cp_evac(lambda mi, ni: kb_i[:, mi * 2048 + ni * 512: mi * 2048 + ni * 512 + 512]),
                  pb_a, pb_b, dma_toks=[(dW, t_bw[1])], bias_row=bk_row_s)
        qk_proj_done = aFE.n
        if DEBUG_DUMP:
            dbgq = nc.dram_tensor("dbgq", [128, 16384], BF16, kind="Internal").ap()
            dbgl = nc.dram_tensor("dbgl", [128, 16384], BF16, kind="Internal").ap()

            def f_dbg(sy):
                sy.wait_ge(aFE.h, qk_proj_done)
                sy.dma_start(out=dbgq, in_=qb_i[:, :]).then_inc(dO.h, 16)
                sy.dma_start(out=dbgl, in_=ln3[:, :]).then_inc(dO.h, 16)
                dO.n += 32
            blk.sync(f_dbg)

        t_bwv = []
        blk.gpsimd(lambda g: (g.wait_ge(aFE.h, qk_proj_done),
                              t_bwv.extend(load_w(g, [bwb[:, 0:8192]], [bw_v]))))
        v_ones_tok = [0]
        pe_gate_v = pe.n
        blk.vector(lambda v: (v.wait_ge(pe.h, pe_gate_v),
                              v_ones_tok.__setitem__(0, vG.inc(v.memset(
                                  v_i.ap().rearrange("p (kc h s) -> p kc h s",
                                                     kc=16, h=16)[:, :, :, 64:65], 1.0)))))
        vps = ExitStack()
        v_ps_a = vps.enter_context(nc.psum_tensor("v_ps_a", [128, 1024], F32))
        v_ps_b = vps.enter_context(nc.psum_tensor("v_ps_b", [128, 1024], F32))
        v_pe_tok = []
        aFE_v_b = aFE.n

        def f_v_pe(t):
            t.wait_ge(dW.h, t_bwv[0])
            t.wait_ge(aFE.h, aFE_v_b)
            for kc in range(16):
                ps = (v_ps_a, v_ps_b)[kc % 2]
                if kc >= 2:
                    t.wait_ge(aFE.h, aFE_v_b + kc - 1)
                for nb in (0, 1):
                    for ic in range(8):
                        t.matmul(ps[:, nb * 512:nb * 512 + 512],
                                 ln3[:, ic * 2048 + kc * 128: ic * 2048 + kc * 128 + 128],
                                 bwb[:, ic * 1024 + nb * 512: ic * 1024 + nb * 512 + 512],
                                 start=(ic == 0), stop=False)
                    ins = t.matmul(ps[:, nb * 512:nb * 512 + 512], ones_b[0:1, 0:128],
                                   bvs[0:1, nb * 512:nb * 512 + 512],
                                   start=False, stop=True)
                v_pe_tok.append(pe.inc(ins))

        def f_v_act(sc):
            sc.wait_ge(vG.h, v_ones_tok[0])
            for kc in range(16):
                ps = (v_ps_a, v_ps_b)[kc % 2]
                sc.wait_ge(pe.h, v_pe_tok[kc])
                dst = v_i.ap().rearrange("p (kc h s) -> p kc h s", kc=16, h=16)[:, kc, :, 0:64]
                aFE.inc(sc.copy(dst, ps.ap().rearrange("p (h d) -> p h d", h=16)))
        blk.tensor(f_v_pe)
        blk.scalar(f_v_act)
        v_done = aFE.n
        vps.close()
        bw_es.close()
        ln3_es.close()
        bps.close()

        p1_es = ExitStack()
        Pst = p1_es.enter_context(nc.sbuf_tensor("Pst", [128, 8704], BF16, side="right"))
        p1ps = ExitStack()
        PA = p1ps.enter_context(nc.psum_tensor("PA", [128, 1088], F32))
        PB = p1ps.enter_context(nc.psum_tensor("PB", [128, 1088], F32))
        aEX_b1 = aEX.n
        dPX_b1 = dPX.n
        p1_pe_tok = {}

        def f1_pe(t):
            t.wait_ge(aFE.h, v_done)
            t.wait_ge(dW.h, t_rel[0])
            for hp in range(8):
                for lc in range(16):
                    for half in (0, 1):
                        s = (hp * 16 + lc) * 2 + half
                        w = 1088 if half == 0 else 1087
                        s0 = 1920 - 128 * lc + half * 1088
                        if s > 0:
                            t.wait_ge(aEX.h, aEX_b1 + 2 * s)
                        for rbv, psx in ((0, PA), (64, PB)):
                            off = 0
                            for n in (512, 512, w - 1024):
                                ins = t.matmul(
                                    psx[:, off:off + n],
                                    qb_i[rbv:rbv + 64, hp * 2048 + lc * 128: hp * 2048 + lc * 128 + 128],
                                    rel2_s[rbv:rbv + 64, s0 + off: s0 + off + n],
                                    start=True, stop=True)
                                off += n
                        p1_pe_tok[s] = pe.inc(ins)
        blk.tensor(f1_pe)

        def f1_act(sc):
            for hp in range(8):
                for lc in range(16):
                    for half in (0, 1):
                        s = (hp * 16 + lc) * 2 + half
                        w = 1088 if half == 0 else 1087
                        sc.wait_ge(pe.h, p1_pe_tok[s])
                        if hp * 16 + lc >= 2 and half == 0:
                            sc.wait_ge(dPX.h, dPX_b1 + 32 * (hp * 16 + lc - 1))
                        for hh, psx in ((0, PA), (1, PB)):
                            ins = sc.activation(
                                Pst[:, hh * 4352 + (lc % 2) * 2176 + half * 1088:
                                    hh * 4352 + (lc % 2) * 2176 + half * 1088 + w],
                                psx[:, 0:w], AF.Exp)
                            aEX.inc(ins)
        blk.scalar(f1_act)

        def f1_gp(g):
            for hp in range(8):
                for lc in range(16):
                    s_last = aEX_b1 + ((hp * 16 + lc) * 2 + 1) * 2 + 2
                    g.wait_ge(aEX.h, s_last)
                    for hh in (0, 1):
                        h = hp * 2 + hh
                        ins = g.dma_start(
                            out=AP(pexp.tensor,
                                   h * 2048 * 4095 + (lc * 128) * 4095 + (1920 - 128 * lc),
                                   [[4095, 128], [1, 2175]]),
                            in_=Pst[:, hh * 4352 + (lc % 2) * 2176:
                                    hh * 4352 + (lc % 2) * 2176 + 2175])
                        dPX.inc(ins, 16)
        blk.gpsimd(f1_gp)
        pexp_all = dPX.n
        p1ps.close()
        p1_es.close()

        p2b_es = ExitStack()
        er = p2b_es.enter_context(nc.sbuf_tensor("er", [128, 8192], BF16, side="right"))
        t1b = p2b_es.enter_context(nc.sbuf_tensor("t1b", [128, 2048], BF16, side="right"))
        Eb = p2b_es.enter_context(nc.sbuf_tensor("Eb", [128, 2048], BF16, side="right"))
        nrm2 = p2b_es.enter_context(nc.sbuf_tensor("nrm2", [128, 2048], F32, side="right"))
        oev2 = p2b_es.enter_context(nc.sbuf_tensor("oev2", [128, 2048], F32, side="right"))
        o_t = p2b_es.enter_context(nc.sbuf_tensor("o_t", [128, 2048], BF16, side="right"))
        p2bps = ExitStack()
        qk2 = p2bps.enter_context(nc.psum_tensor("qk2", [128, 2048], F32))
        pv2 = [p2bps.enter_context(nc.psum_tensor(f"pv2_{i}", [65, 1024], F32)) for i in (0, 1)]
        aEX_b2, aEV_b2, vRC_b2, vNM_b2, vEM_b2, pPV_b2, dK_b2, dOO_b2 = (
            aEX.n, aEV.n, vRC.n, vNM.n, vEM.n, pPV.n, dK.n, dOO.n)

        def f2_syncr(sy):
            sy.wait_ge(dPX.h, pexp_all)

            def store_o(idx):
                h2, lbh2 = idx // 2, idx % 2
                hc2, rbv2 = h2 // 2, (h2 % 2) * 64
                sy.wait_ge(vNM.h, vNM_b2 + idx + 1)
                sy.dma_start(
                    out=osc_d[rbv2:rbv2 + 64,
                              hc2 * 2048 + lbh2 * 1024: hc2 * 2048 + lbh2 * 1024 + 1024],
                    in_=o_t[0:64, (idx % 2) * 1024:(idx % 2) * 1024 + 1024]
                ).then_inc(dOO.h, 16)
                dOO.n += 16
            for h in range(H):
                for rc in range(16):
                    if rc >= 4:
                        e_guard = vEM_b2 + h * 32 + 2 * rc - 6
                    else:
                        e_guard = vEM_b2 + (h - 1) * 32 + 2 * rc + 26
                    if e_guard > vEM_b2:
                        sy.wait_ge(vEM.h, e_guard)
                    src = AP(pexp.tensor, h * 2048 * 4095 + 2047 + 128 * rc,
                             [[4094, 2048], [1, 128]])
                    sy.dma_start(out=er[:, (rc % 4) * 2048:(rc % 4) * 2048 + 2048],
                                 in_=src, transpose=True).then_inc(dK.h, 16)
                    dK.n += 16
                if h >= 1:
                    store_o(2 * (h - 1))
                    store_o(2 * (h - 1) + 1)
            store_o(2 * H - 2)
            store_o(2 * H - 1)
        blk.sync(f2_syncr)

        qk2_tok = {}
        bc2_tok = {}
        units = [(h, rc, lbh) for h in range(H) for rc in range(16) for lbh in (0, 1)]
        NU = len(units)

        def last_j(h, lbh):
            return h * 32 + 30 + lbh

        def f2_pe(t):
            t.wait_ge(aFE.h, v_done)
            t.wait_ge(aEX.h, aEX_b2)
            t.wait_ge(vG.h, v_ones_tok[0])

            def emit_qk(j):
                h, rc, lbh = units[j]
                hc, rbv = h // 2, (h % 2) * 64
                if j > 0:
                    t.wait_ge(aEX.h, aEX_b2 + j - 1)
                for cb in (0, 1):
                    ins = t.matmul(
                        qk2[:, (j % 2) * 1024 + cb * 512:(j % 2) * 1024 + cb * 512 + 512],
                        kb_i[rbv:rbv + 64, hc * 2048 + rc * 128: hc * 2048 + rc * 128 + 128],
                        qb_i[rbv:rbv + 64, hc * 2048 + lbh * 1024 + cb * 512:
                             hc * 2048 + lbh * 1024 + cb * 512 + 512],
                        start=True, stop=True)
                qk2_tok[j] = pe.inc(ins)

            def emit_pv(j):
                h, rc, lbh = units[j]
                idx = 2 * h + lbh
                t.wait_ge(vEM.h, vEM_b2 + j + 1)
                if rc == 0 and h > 0:
                    t.wait_ge(aEV.h, aEV_b2 + 2 * (h - 1) + lbh + 1)
                    t.wait_ge(vRC.h, vRC_b2 + 2 * (h - 1) + lbh + 1)
                    t.wait_ge(vNM.h, vNM_b2 + 2 * (h - 1) + lbh + 1)
                for cb in (0, 1):
                    ins = t.matmul(
                        pv2[lbh][:, cb * 512:cb * 512 + 512],
                        v_i[:, rc * 1040 + h * 65: rc * 1040 + h * 65 + 65],
                        Eb[:, (j % 2) * 1024 + cb * 512:(j % 2) * 1024 + cb * 512 + 512],
                        start=(rc == 0 and cb in (0, 1)), stop=(rc == 15))
                pPV.inc(ins)
                if rc == 15:
                    t.wait_ge(vRC.h, vRC_b2 + idx + 1)
                    for cb in (0, 1):
                        ins = t.matmul(
                            pv2[lbh][0:64, cb * 512:cb * 512 + 512], ones_f[64:65, 0:64],
                            nrm2[64:65, (idx % 2) * 1024 + cb * 512:
                                 (idx % 2) * 1024 + cb * 512 + 512],
                            start=True, stop=True)
                    bc2_tok[idx] = pe.inc(ins)

            for j in range(NU):
                emit_qk(j)
                if j >= 2:
                    emit_pv(j - 2)
            emit_pv(NU - 2)
            emit_pv(NU - 1)
        blk.tensor(f2_pe)

        def f2_act(sc):
            for j in range(NU):
                h, rc, lbh = units[j]
                sc.wait_ge(pe.h, qk2_tok[j])
                if j >= 2:
                    sc.wait_ge(vEM.h, vEM_b2 + j - 1)
                aEX.inc(sc.activation(t1b[:, (j % 2) * 1024:(j % 2) * 1024 + 1024],
                                      qk2[:, (j % 2) * 1024:(j % 2) * 1024 + 1024],
                                      AF.Exp, scale=0.125))
                if rc == 15:
                    idx = 2 * h + lbh
                    sc.wait_ge(pPV.h, pPV_b2 + last_j(h, lbh) + 1)
                    if idx >= 2:
                        sc.wait_ge(vNM.h, vNM_b2 + idx - 1)
                    ins = sc.copy(oev2[0:64, (idx % 2) * 1024:(idx % 2) * 1024 + 1024],
                                  pv2[lbh][0:64, :])
                    aEV.inc(ins)
        blk.scalar(f2_act)

        def f2_vec(v):
            for j in range(NU):
                h, rc, lbh = units[j]
                v.wait_ge(aEX.h, aEX_b2 + j + 1)
                v.wait_ge(dK.h, dK_b2 + 16 * (h * 16 + rc + 1))
                if j >= 2:
                    v.wait_ge(pPV.h, pPV_b2 + j - 1)
                ins = v.tensor_tensor(
                    Eb[:, (j % 2) * 1024:(j % 2) * 1024 + 1024],
                    t1b[:, (j % 2) * 1024:(j % 2) * 1024 + 1024],
                    er[:, (rc % 4) * 2048 + lbh * 1024:(rc % 4) * 2048 + lbh * 1024 + 1024],
                    ALU.mult)
                vEM.inc(ins)
                if rc == 15:
                    idx = 2 * h + lbh
                    v.wait_ge(aEV.h, aEV_b2 + idx + 1)
                    vRC.inc(v.reciprocal(nrm2[64:65, (idx % 2) * 1024:(idx % 2) * 1024 + 1024],
                                         pv2[lbh][64:65, :]))
                    v.wait_ge(pe.h, bc2_tok.get(idx, 0) or 0) if False else None
                    v.wait_ge(pPV.h, pPV_b2 + NU + 2 * idx + 2) if False else None
                    v.wait_ge(pe.h, bc2_tok[idx])
                    if dOO_b2 + 16 * (idx - 1) > dOO_b2:
                        v.wait_ge(dOO.h, dOO_b2 + 16 * (idx - 1))
                    ins = v.tensor_tensor(
                        o_t[0:64, (idx % 2) * 1024:(idx % 2) * 1024 + 1024],
                        oev2[0:64, (idx % 2) * 1024:(idx % 2) * 1024 + 1024],
                        pv2[lbh][0:64, 0:1024], ALU.mult)
                    vNM.inc(ins)
        blk.vector(f2_vec)

        oscr_tok = dOO.n
        p2bps.close()
        p2b_es.close()

        wo_es = ExitStack()
        o_i = wo_es.enter_context(nc.sbuf_tensor("o_i", [128, 16384], BF16, side="right"))
        wo_s = wo_es.enter_context(nc.sbuf_tensor("wo_s", [128, 8192], BF16, side="right"))
        xr2 = wo_es.enter_context(nc.sbuf_tensor("xr2", [128, 4096], F32, side="right"))
        xt2 = wo_es.enter_context(nc.sbuf_tensor("xt2", [128, 1024], F32, side="right"))
        wops = ExitStack()
        wo_a = wops.enter_context(nc.psum_tensor("wo_a", [128, 512], F32))
        wo_b = wops.enter_context(nc.psum_tensor("wo_b", [128, 512], F32))
        t_wo = []
        blk.gpsimd(lambda g: (g.wait_ge(dOO.h, oscr_tok),
                              t_wo.extend(load_w(g, [wo_s[:, :]], [bw_o]))))
        oi_tok = [0]

        def fw_sync_oi(sy):
            sy.wait_ge(dOO.h, oscr_tok)
            oi_tok[0] = dX.inc(sy.dma_start(out=o_i[:, :], in_=osc_d[:, :]), 16)
        blk.sync(fw_sync_oi)

        wo_pe_tok = {}
        vFE_wo_b = vFE.n
        dX_wo_b = dX.n
        dO_wo_b = dO.n

        def fw_pe(t):
            t.wait_ge(dW.h, t_wo[0])
            t.wait_ge(dX.h, oi_tok[0])
            for oc in range(8):
                for tb in range(4):
                    s = oc * 4 + tb
                    if vFE_wo_b + s - 1 > vFE_wo_b - 1 and s >= 2:
                        t.wait_ge(vFE.h, vFE_wo_b + s - 1)
                    ps = (wo_a, wo_b)[s % 2]
                    for ic in range(8):
                        ins = t.matmul(
                            ps[:, :], wo_s[:, ic * 1024 + oc * 128: ic * 1024 + oc * 128 + 128],
                            o_i[:, ic * 2048 + tb * 512: ic * 2048 + tb * 512 + 512],
                            start=(ic == 0), stop=(ic == 7))
                    wo_pe_tok[s] = pe.inc(ins)
        blk.tensor(fw_pe)

        def fw_sync_x(sy):
            sy.wait_ge(dOO.h, oscr_tok)

            def store_x(oc2):
                for tb2 in range(4):
                    s2 = oc2 * 4 + tb2
                    sy.wait_ge(vFE.h, vFE_wo_b + s2 + 1)
                    sy.dma_start(out=parkA[:, oc2 * 2048 + tb2 * 512: oc2 * 2048 + tb2 * 512 + 512],
                                 in_=xt2[:, (s2 % 2) * 512:(s2 % 2) * 512 + 512]
                                 ).then_inc(dO.h, 16)
                    dO.n += 16
            for oc in range(8):
                if oc >= 2:
                    sy.wait_ge(vFE.h, vFE_wo_b + 4 * (oc - 1))
                sy.dma_start(out=xr2[:, (oc % 2) * 2048:(oc % 2) * 2048 + 2048],
                             in_=parkB[:, oc * 2048: oc * 2048 + 2048]).then_inc(dX.h, 16)
                dX.n += 16
                if oc >= 1:
                    store_x(oc - 1)
            store_x(7)
        blk.sync(fw_sync_x)

        def fw_vec(v):
            for oc in range(8):
                v.wait_ge(dX.h, dX_wo_b + 16 * (oc + 1))
                for tb in range(4):
                    s = oc * 4 + tb
                    v.wait_ge(pe.h, wo_pe_tok[s])
                    if dO_wo_b + 16 * (s - 1) > dO_wo_b:
                        v.wait_ge(dO.h, dO_wo_b + 16 * (s - 1))
                    ps = (wo_a, wo_b)[s % 2]
                    ins = v.scalar_tensor_tensor(
                        xt2[:, (s % 2) * 512:(s % 2) * 512 + 512], ps[:, :],
                        bos[:, oc:oc + 1],
                        xr2[:, (oc % 2) * 2048 + tb * 512:(oc % 2) * 2048 + tb * 512 + 512],
                        ALU.add, ALU.add)
                    vFE.inc(ins)
        blk.vector(fw_vec)

        x3_tok = dO.n
        wops.close()
        wo_es.close()
        b_es.close()

    if stages >= 4:
        f2_es = ExitStack()
        ln4 = f2_es.enter_context(nc.sbuf_tensor("ln4", [128, 16384], BF16, side="right"))
        ln4s = ExitStack()
        xb4 = ln4s.enter_context(nc.sbuf_tensor("xb4", [128, 8192], BF16, side="right"))
        sqb4 = ln4s.enter_context(nc.sbuf_tensor("sqb4", [128, 8192], BF16, side="right"))
        tb24 = ln4s.enter_context(nc.sbuf_tensor("tb24", [128, 1024], F32, side="right"))
        dcw_t4, dxo_t4, doo_t4 = dCW.n, dXO.n, dOO.n
        ln4_tok = emit_ln(parkA, ln4, xb4, sqb4, tb24,
                          gate=lambda g: (g.wait_ge(dO.h, x3_tok),
                                          g.wait_ge(dCW.h, dcw_t4),
                                          g.wait_ge(dXO.h, dxo_t4),
                                          g.wait_ge(dOO.h, doo_t4)))
        ln4s.close()
        emit_ffn(ln4, parkA, yO, False, x3_tok, ln4_tok)
        f2_es.close()

    if stages < 4:
        src_park = parkA if stages in (1, 3) else parkB

        def f_tail(sy):
            sy.wait_ge(dO.h, dO.n)
            sy.dma_start(out=yO, in_=src_park).then_inc(dO.h, 16)
            dO.n += 16
        blk.sync(f_tail)

    fins = [(s, s.n) for s in (dW, dXB, dPX, dX, dK, dO, dCW, dOO, dXO)]

    def f_fin(sy):
        for s, n in fins:
            if n:
                sy.wait_ge(s.h, n)
    blk.sync(f_fin)

    es.close()
    nc.finalize()
    return nc


_BUILT = {}


def _prep_inputs(inputs):
    f32 = np.float32
    g = {k: np.asarray(v, dtype=f32) for k, v in inputs.items()}
    d = {}
    d["aw_dkv"] = img128(g["L0_attn_a_W_dkv"].T.copy())
    d["aw_dq"] = img128(g["L0_attn_a_W_dq"].T.copy())
    d["aw_dqn"] = img128(g["L0_attn_a_W_dq"].copy())
    d["aw_uq"] = img128(g["L0_attn_a_W_uq"].T.copy())
    d["aw_uk"] = img128(g["L0_attn_a_W_uk"].copy())
    d["aw_uv"] = img128(g["L0_attn_a_W_uv"].copy())
    d["aw_o"] = img128(g["L0_attn_a_W_o"].T.copy())
    d["bw_q"] = img128(g["L0_attn_b_Wq"].T.copy())
    d["bw_k"] = img128(g["L0_attn_b_Wk"].T.copy())
    d["bw_v"] = img128(g["L0_attn_b_Wv"].T.copy())
    d["bw_o"] = img128(g["L0_attn_b_Wo"].T.copy())
    d["bq_r"] = g["L0_attn_b_bq"].reshape(1, 1024).copy()
    d["bk_r"] = g["L0_attn_b_bk"].reshape(1, 1024).copy()
    d["bo_i"] = colimg(g["L0_attn_b_bo"])
    d["bv_r"] = g["L0_attn_b_bv"].reshape(1, 1024).copy()
    re_t = np.ascontiguousarray(g["L0_attn_b_rel_emb"].T)
    d["rel2"] = np.concatenate([re_t, re_t], axis=0).copy()
    d["f_w1"] = ogblocks(g["L0_ffn_w1_w"].T.copy())
    d["f_w2"] = ogblocks(g["L0_ffn_w2_w"].T.copy())
    d["f_h1"] = ogblocks(g["L0_ffn_h1_w"].T.copy())
    d["f_h2"] = ogblocks(g["L0_ffn_h2_w"].T.copy())
    d["f_w3"] = ogblocks(g["L0_ffn_w3_w"].T.copy())
    d["s1_i"] = colimg(g["L0_ffn_w1_s"])
    d["s2_i"] = colimg(g["L0_ffn_w2_s"])
    d["sh1_i"] = colimg(g["L0_ffn_h1_s"])
    d["sh2_i"] = colimg(g["L0_ffn_h2_s"])
    d["s3_i"] = colimg(g["L0_ffn_w3_s"])
    x = g["L0_attn_a_x"]
    in_maps = []
    for b in range(NCORE):
        m = dict(d)
        m["xT"] = img128(np.ascontiguousarray(x[b].T))
        in_maps.append(m)
    return in_maps


def kernel(**inputs):
    if STAGES not in _BUILT:
        _BUILT[STAGES] = build(STAGES)
    nc = _BUILT[STAGES]
    in_maps = _prep_inputs(inputs)
    res = run_bass_kernel_spmd(nc, in_maps, core_ids=list(range(NCORE)))
    outs = []
    for b in range(NCORE):
        yi = np.asarray(res.results[b]["y"])
        outs.append(yi.reshape(128, 8, 2048).transpose(1, 0, 2).reshape(1024, 2048).T)
    return np.stack(outs, axis=0).astype(np.float32)


# version 03
# speedup vs baseline: 1.1902x; 1.1902x over previous
import numpy as np
import ml_dtypes
import concourse.bass as bass
import concourse.mybir as mybir
from concourse.ap import AP
from concourse.bass_utils import run_bass_kernel_spmd
from contextlib import ExitStack

F32 = mybir.dt.float32
BF16 = mybir.dt.bfloat16
AF = mybir.ActivationFunctionType
ALU = mybir.AluOpType
BFD = ml_dtypes.bfloat16

T = 2048
D = 1024
LAT = 512
H = 16
HD = 64
DH = 4096
NCORE = 8
STAGES = 4
DEBUG_DUMP = False


def img128(m):
    R, C = m.shape
    assert R % 128 == 0
    return np.ascontiguousarray(
        m.reshape(R // 128, 128, C).transpose(1, 0, 2).reshape(128, (R // 128) * C))


def ogblocks(wt, ogc=128):
    I, O = wt.shape
    n = O // ogc
    a = wt.reshape(I // 128, 128, n, ogc).transpose(2, 1, 0, 3)
    return np.ascontiguousarray(a.reshape(n, 128, (I // 128) * ogc))


def colimg(v):
    return np.ascontiguousarray(v.reshape(-1, 128).T)


class Sem:
    __slots__ = ("h", "n")

    def __init__(self, es, nc, name):
        self.h = es.enter_context(nc.semaphore(name))
        self.n = 0

    def inc(self, instr, k=1):
        instr.then_inc(self.h, k)
        self.n += k
        return self.n


def build(stages):
    nc = bass.Bass("TRN2", target_bir_lowering=False, detect_race_conditions=False)

    def din(name, shape, dtype=F32):
        return nc.dram_tensor(name, list(shape), dtype, kind="ExternalInput").ap()

    xT = din("xT", (128, 8 * T))
    aw_dkv = din("aw_dkv", (128, 4096))
    aw_dq = din("aw_dq", (128, 4096))
    aw_dqn = din("aw_dqn", (128, 4096))
    aw_uq = din("aw_uq", (128, 4096))
    aw_uk = din("aw_uk", (128, 4096))
    aw_uv = din("aw_uv", (128, 4096))
    aw_o = din("aw_o", (128, 8192))
    bw_q = din("bw_q", (128, 8192))
    bw_k = din("bw_k", (128, 8192))
    bw_v = din("bw_v", (128, 8192))
    bw_o = din("bw_o", (128, 8192))
    bq_r = din("bq_r", (1, 1024))
    bk_r = din("bk_r", (1, 1024))
    bo_i = din("bo_i", (128, 8))
    bv_r = din("bv_r", (1, 1024))
    rel2_d = din("rel2", (128, 4095))
    f_w1 = din("f_w1", (32, 128, 1024))
    f_w2 = din("f_w2", (32, 128, 1024))
    f_h1 = din("f_h1", (32, 128, 4096))
    f_h2 = din("f_h2", (32, 128, 4096))
    f_w3 = din("f_w3", (8, 128, 4096))
    s1_i = din("s1_i", (128, 32))
    s2_i = din("s2_i", (128, 32))
    sh1_i = din("sh1_i", (128, 32))
    sh2_i = din("sh2_i", (128, 32))
    s3_i = din("s3_i", (128, 8))

    yO = nc.dram_tensor("y", [128, 8 * T], F32, kind="ExternalOutput").ap()
    parkA = nc.dram_tensor("parkA", [128, 8 * T], F32, kind="Internal").ap()
    parkB = nc.dram_tensor("parkB", [128, 8 * T], F32, kind="Internal").ap()
    osc_d = nc.dram_tensor("oscr", [128, 8 * T], BF16, kind="Internal").ap()
    pexp = nc.dram_tensor("pexp", [H * 2048 * 4095], BF16, kind="Internal").ap()
    cach = {
        "w1": nc.dram_tensor("c_w1", [32, 128, 1024], BF16, kind="Internal").ap(),
        "w2": nc.dram_tensor("c_w2", [32, 128, 1024], BF16, kind="Internal").ap(),
        "h1": nc.dram_tensor("c_h1", [32, 128, 4096], BF16, kind="Internal").ap(),
        "h2": nc.dram_tensor("c_h2", [32, 128, 4096], BF16, kind="Internal").ap(),
        "w3": nc.dram_tensor("c_w3", [8, 128, 4096], BF16, kind="Internal").ap(),
    }

    es = ExitStack()

    def sbuf(name, shape, dtype, side="left"):
        return es.enter_context(nc.sbuf_tensor(name, list(shape), dtype, side=side))

    ones_b = sbuf("ones_b", (128, 512), BF16)
    ones_f = sbuf("ones_f", (128, 128), F32)
    epsT = sbuf("epsT", (128, 1), F32)
    s1s = sbuf("s1s", (128, 32), F32)
    s2s = sbuf("s2s", (128, 32), F32)
    sh1s = sbuf("sh1s", (128, 32), F32)
    sh2s = sbuf("sh2s", (128, 32), F32)
    s3s = sbuf("s3s", (128, 8), F32)
    bq_row_s = sbuf("bq_row_s", (1, 1024), BF16)
    bk_row_s = sbuf("bk_row_s", (1, 1024), BF16)
    bos = sbuf("bos", (128, 8), F32)
    bvs = sbuf("bvs", (1, 1024), BF16)
    V = sbuf("V", (128, 3072), F32)

    dW = Sem(es, nc, "dW")
    dXB = Sem(es, nc, "dXB")
    dPX = Sem(es, nc, "dPX")
    dX = Sem(es, nc, "dX")
    dK = Sem(es, nc, "dK")
    dO = Sem(es, nc, "dO")
    pe = Sem(es, nc, "pe")
    pPV = Sem(es, nc, "pPV")
    pW = Sem(es, nc, "pW")
    aG = Sem(es, nc, "aG")
    aEX = Sem(es, nc, "aEX")
    aEV = Sem(es, nc, "aEV")
    aFE = Sem(es, nc, "aFE")
    vG = Sem(es, nc, "vG")
    vRC = Sem(es, nc, "vRC")
    vNM = Sem(es, nc, "vNM")
    vEM = Sem(es, nc, "vEM")
    vFE = Sem(es, nc, "vFE")
    gMK = Sem(es, nc, "gMK")
    dCW = Sem(es, nc, "dCW")
    dOO = Sem(es, nc, "dOO")
    dXO = Sem(es, nc, "dXO")

    blk = es.enter_context(nc.Block())

    def p0_sync(sy):
        for dst, src in ((s1s, s1_i), (s2s, s2_i), (sh1s, sh1_i), (sh2s, sh2_i),
                         (s3s, s3_i), (bos, bo_i)):
            sy.dma_start(out=dst[:], in_=src).then_inc(dX.h, 16)
            dX.n += 16

    def p0_gp(g):
        for dst, src in ((bq_row_s, bq_r), (bk_row_s, bk_r), (bvs, bv_r)):
            g.dma_start(out=dst[:], in_=src).then_inc(dW.h, 16)
            dW.n += 16

    def p0_vec(v):
        v.memset(ones_b[:], 1.0)
        v.memset(ones_f[:], 1.0)
        vG.inc(v.memset(epsT[:], 1e-5))

    blk.sync(p0_sync)
    blk.gpsimd(p0_gp)
    blk.vector(p0_vec)
    tok_consts_v = vG.n
    tok_consts_w = dW.n
    tok_consts_d = dX.n

    ln_ctr = [0]

    def emit_ln(src_ap, dst, xb, sqb, tb2, gate=None):
        ln_ctr[0] += 1
        u = str(ln_ctr[0])
        ps_es = ExitStack()
        st_sum = ps_es.enter_context(nc.psum_tensor("ln_s" + u, [1, 1024], F32))
        st_sq = ps_es.enter_context(nc.psum_tensor("ln_q" + u, [1, 1024], F32))
        bcR = ps_es.enter_context(nc.psum_tensor("ln_r" + u, [128, 512], F32))
        bcM = ps_es.enter_context(nc.psum_tensor("ln_m" + u, [128, 512], F32))
        prev_affine = [0]
        prev_stats_rd = [0]
        prev_stats = [0]
        for th in (0, 1):
            o = th * 1024
            xb_tok = []

            def w_gp(g, th=th, xb_tok=xb_tok):
                if gate is not None and th == 0:
                    gate(g)
                if th == 1:
                    g.wait_ge(vG.h, prev_affine[0])
                for c in range(8):
                    ins = g.dma_start(
                        out=xb[:, c * 1024:(c + 1) * 1024],
                        in_=src_ap[:, c * 2048 + th * 1024: c * 2048 + th * 1024 + 1024])
                    xb_tok.append(dXB.inc(ins, 16))
            blk.gpsimd(w_gp)
            sq_tok = []

            def w_act(sc, th=th, xb_tok=xb_tok, sq_tok=sq_tok):
                if th == 1:
                    sc.wait_ge(pe.h, prev_stats[0])
                for c in range(8):
                    sc.wait_ge(dXB.h, xb_tok[c])
                    ins = sc.activation(sqb[:, c * 1024:(c + 1) * 1024],
                                        xb[:, c * 1024:(c + 1) * 1024], AF.Square)
                    sq_tok.append(aG.inc(ins))
            blk.scalar(w_act)
            stat_tok = [0]

            def w_pe(t, th=th, xb_tok=xb_tok, sq_tok=sq_tok, stat_tok=stat_tok):
                if th == 1:
                    t.wait_ge(vG.h, prev_stats_rd[0])
                for c in range(8):
                    t.wait_ge(dXB.h, xb_tok[c])
                    t.wait_ge(aG.h, sq_tok[c])
                    for nb in (0, 1):
                        sl = slice(c * 1024 + nb * 512, c * 1024 + nb * 512 + 512)
                        t.matmul(st_sum[0:1, nb * 512:nb * 512 + 512], ones_b[:, 0:1],
                                 xb[:, sl], start=(c == 0), stop=(c == 7))
                        ins = t.matmul(st_sq[0:1, nb * 512:nb * 512 + 512], ones_b[:, 0:1],
                                       sqb[:, sl], start=(c == 0), stop=(c == 7))
                    stat_tok[0] = pe.inc(ins)
            blk.tensor(w_pe)
            prev_stats[0] = pe.n
            va_tok = [0]

            def w_vec(v, stat_tok=stat_tok, va_tok=va_tok):
                v.wait_ge(pe.h, stat_tok[0])
                v.tensor_scalar(V[0:1, 0:1024], st_sum[0:1, :], 1.0 / D, None, ALU.mult)
                v.tensor_scalar(V[0:1, 1024:2048], st_sq[0:1, :], 1.0 / D, None, ALU.mult)
                v.tensor_mul(V[0:1, 2048:3072], V[0:1, 0:1024], V[0:1, 0:1024])
                ins = v.tensor_sub(V[0:1, 1024:2048], V[0:1, 1024:2048], V[0:1, 2048:3072])
                va_tok[0] = vG.inc(ins)
            blk.vector(w_vec)
            prev_stats_rd[0] = vG.n
            sd_tok = [0]

            def w_act2(sc, va_tok=va_tok, sd_tok=sd_tok):
                sc.wait_ge(vG.h, va_tok[0])
                ins = sc.activation(V[0:1, 2048:3072], V[0:1, 1024:2048], AF.Sqrt,
                                    bias=epsT[0:1, 0:1])
                sd_tok[0] = aG.inc(ins)
            blk.scalar(w_act2)
            vb_tok = [0]

            def w_vec2(v, sd_tok=sd_tok, vb_tok=vb_tok):
                v.wait_ge(aG.h, sd_tok[0])
                v.reciprocal(V[0:1, 1024:2048], V[0:1, 2048:3072])
                ins = v.tensor_mul(V[0:1, 2048:3072], V[0:1, 0:1024], V[0:1, 1024:2048])
                vb_tok[0] = vG.inc(ins)
            blk.vector(w_vec2)
            for lb in (0, 1):
                bc_tok = [0]

                def w_pe2(t, lb=lb, vb_tok=vb_tok, bc_tok=bc_tok):
                    t.wait_ge(vG.h, max(vb_tok[0], prev_affine[0]))
                    t.matmul(bcR[:, :], ones_f[0:1, :],
                             V[0:1, 1024 + lb * 512:1024 + lb * 512 + 512],
                             start=True, stop=True)
                    ins = t.matmul(bcM[:, :], ones_f[0:1, :],
                                   V[0:1, 2048 + lb * 512:2048 + lb * 512 + 512],
                                   start=True, stop=True)
                    bc_tok[0] = pe.inc(ins)
                blk.tensor(w_pe2)

                def w_vec3(v, lb=lb, th=th, bc_tok=bc_tok):
                    v.wait_ge(pe.h, bc_tok[0])
                    for c in range(8):
                        sl = slice(c * 1024 + lb * 512, c * 1024 + lb * 512 + 512)
                        v.tensor_tensor(tb2[:, (c % 2) * 512:(c % 2) * 512 + 512],
                                        xb[:, sl], bcR[:, :], ALU.mult)
                        ins = v.tensor_tensor(
                            dst[:, c * 2048 + th * 1024 + lb * 512:
                                c * 2048 + th * 1024 + lb * 512 + 512],
                            tb2[:, (c % 2) * 512:(c % 2) * 512 + 512], bcM[:, :],
                            ALU.subtract)
                        vG.inc(ins)
                blk.vector(w_vec3)
                prev_affine[0] = vG.n
        ps_es.close()
        return vG.n

    blk.tensor(lambda t: t.wait_ge(vG.h, tok_consts_v))
    blk.scalar(lambda sc: (sc.wait_ge(vG.h, tok_consts_v), sc.wait_ge(dX.h, tok_consts_d)))
    blk.vector(lambda v: v.wait_ge(dX.h, tok_consts_d))

    prj_es = [None]

    def emit_proj(nk, nm, nn, nstep, lhsT_fn, rhs_fn, evac_fn, ps_a, ps_b,
                  dma_toks=(), bias_row=None):
        aFE_b = aFE.n
        tiles = [(mi, ni) for mi in range(nm) for ni in range(nn)]
        pe_tok = []

        def f_pe(t):
            for sem, tok in dma_toks:
                t.wait_ge(sem.h, tok)
            for g, (mi, ni) in enumerate(tiles):
                ps = (ps_a, ps_b)[g % 2]
                if aFE_b + g - 1 > 0:
                    t.wait_ge(aFE.h, aFE_b + g - 1)
                last = nk - 1 if bias_row is None else nk
                for kc in range(nk):
                    ins = t.matmul(ps[:, :], lhsT_fn(kc, mi), rhs_fn(kc, ni),
                                   start=(kc == 0), stop=(kc == last))
                if bias_row is not None:
                    ins = t.matmul(ps[:, :], bias_row[0:1, mi * 128:mi * 128 + 128],
                                   ones_b[0:1, 0:nstep], start=False, stop=True)
                pe_tok.append(pe.inc(ins))

        def f_act(sc):
            for g, (mi, ni) in enumerate(tiles):
                ps = (ps_a, ps_b)[g % 2]
                sc.wait_ge(pe.h, pe_tok[g])
                aFE.inc(evac_fn(sc, mi, ni, ps[:, :]))
        blk.tensor(f_pe)
        blk.scalar(f_act)
        return aFE.n

    a_es = ExitStack()

    def rsb(name, shape, dtype):
        return a_es.enter_context(nc.sbuf_tensor(name, list(shape), dtype, side="right"))

    def lsb(name, shape, dtype):
        return a_es.enter_context(nc.sbuf_tensor(name, list(shape), dtype, side="left"))

    ckv_i = lsb("ckv_i", (128, 8192), BF16)
    keT_i = lsb("keT_i", (128, 4096), BF16)
    veT_i = lsb("veT_i", (128, 4096), BF16)

    p2_es = ExitStack()
    q2_i = p2_es.enter_context(nc.sbuf_tensor("q2_i", [128, 8192], BF16, side="right"))
    ln1_es = ExitStack()
    ln1 = ln1_es.enter_context(nc.sbuf_tensor("ln1", [128, 16384], BF16, side="right"))
    ln_es = ExitStack()
    xb1 = ln_es.enter_context(nc.sbuf_tensor("xb1", [128, 8192], BF16, side="right"))
    sqb1 = ln_es.enter_context(nc.sbuf_tensor("sqb1", [128, 8192], BF16, side="right"))
    tb21 = ln_es.enter_context(nc.sbuf_tensor("tb21", [128, 1024], F32, side="right"))
    ln1_tok = emit_ln(xT, ln1, xb1, sqb1, tb21)
    ln_es.close()

    p2ps = ExitStack()
    pr_a = p2ps.enter_context(nc.psum_tensor("pr_a", [128, 512], F32))
    pr_b = p2ps.enter_context(nc.psum_tensor("pr_b", [128, 512], F32))

    def load_w(g, buf_slices, srcs):
        toks = []
        for sl, src in zip(buf_slices, srcs):
            toks.append(dW.inc(g.dma_start(out=sl, in_=src), 16))
        return toks

    aw1_es = ExitStack()
    aw1 = aw1_es.enter_context(nc.sbuf_tensor("aw1", [128, 8192], BF16, side="right"))
    t_aw1 = []
    blk.gpsimd(lambda g: (g.wait_ge(vG.h, ln1_tok),
                          t_aw1.extend(load_w(
                              g, [aw1[:, 0:4096], aw1[:, 4096:8192]], [aw_dkv, aw_dq]))))

    def cp_evac(dst_fn):
        def f(sc, mi, ni, ps):
            return sc.copy(dst_fn(mi, ni), ps)
        return f

    emit_proj(8, 4, 4, 512,
              lambda kc, mi: aw1[:, kc * 512 + mi * 128: kc * 512 + mi * 128 + 128],
              lambda kc, ni: ln1[:, kc * 2048 + ni * 512: kc * 2048 + ni * 512 + 512],
              cp_evac(lambda mi, ni: ckv_i[:, mi * 2048 + ni * 512: mi * 2048 + ni * 512 + 512]),
              pr_a, pr_b, dma_toks=[(dW, t_aw1[0]), (vG, ln1_tok)])
    ckv_done = aFE.n
    emit_proj(8, 4, 4, 512,
              lambda kc, mi: aw1[:, 4096 + kc * 512 + mi * 128: 4096 + kc * 512 + mi * 128 + 128],
              lambda kc, ni: ln1[:, kc * 2048 + ni * 512: kc * 2048 + ni * 512 + 512],
              cp_evac(lambda mi, ni: q2_i[:, mi * 2048 + ni * 512: mi * 2048 + ni * 512 + 512]),
              pr_a, pr_b, dma_toks=[(dW, t_aw1[1]), (vG, ln1_tok)])
    q2_done = aFE.n
    aw1_free_tok = pe.n
    aw1_es.close()
    ln1_es.close()

    aw2_es = ExitStack()
    aw2 = aw2_es.enter_context(nc.sbuf_tensor("aw2", [128, 8192], BF16, side="right"))
    B_es = ExitStack()
    B_i = B_es.enter_context(nc.sbuf_tensor("B_i", [128, 8192], BF16, side="right"))
    t_aw2 = []
    blk.gpsimd(lambda g: (g.wait_ge(pe.h, aw1_free_tok),
                          t_aw2.extend(load_w(
                              g, [aw2[:, 0:4096], aw2[:, 4096:8192]], [aw_uq, aw_dqn]))))
    emit_proj(4, 8, 2, 512,
              lambda kc, mi: aw2[:, kc * 1024 + mi * 128: kc * 1024 + mi * 128 + 128],
              lambda kc, ni: aw2[:, 4096 + kc * 1024 + ni * 512: 4096 + kc * 1024 + ni * 512 + 512],
              cp_evac(lambda mi, ni: B_i[:, mi * 1024 + ni * 512: mi * 1024 + ni * 512 + 512]),
              pr_a, pr_b, dma_toks=[(dW, t_aw2[1])])
    B_done = aFE.n

    aw3_es = ExitStack()
    aw3 = aw3_es.enter_context(nc.sbuf_tensor("aw3", [128, 8192], BF16, side="right"))
    t_aw3 = []
    blk.gpsimd(lambda g: t_aw3.extend(load_w(
        g, [aw3[:, 0:4096], aw3[:, 4096:8192]], [aw_uk, aw_uv])))
    emit_proj(8, 4, 2, 512,
              lambda kc, mi: aw3[:, kc * 512 + mi * 128: kc * 512 + mi * 128 + 128],
              lambda kc, ni: B_i[:, kc * 1024 + ni * 512: kc * 1024 + ni * 512 + 512],
              cp_evac(lambda mi, ni: keT_i[:, mi * 1024 + ni * 512: mi * 1024 + ni * 512 + 512]),
              pr_a, pr_b, dma_toks=[(dW, t_aw3[0]), (aFE, B_done)])
    keT_done = aFE.n
    aw4_es = ExitStack()
    aw4 = aw4_es.enter_context(nc.sbuf_tensor("aw4", [128, 8192], BF16, side="right"))
    t_aw4 = []
    blk.gpsimd(lambda g: t_aw4.extend(load_w(g, [aw4[:, 0:8192]], [aw_o])))
    emit_proj(8, 4, 2, 512,
              lambda kc, mi: aw3[:, 4096 + kc * 512 + mi * 128: 4096 + kc * 512 + mi * 128 + 128],
              lambda kc, ni: aw4[:, kc * 1024 + ni * 512: kc * 1024 + ni * 512 + 512],
              cp_evac(lambda mi, ni: veT_i[:, mi * 1024 + ni * 512: mi * 1024 + ni * 512 + 512]),
              pr_a, pr_b, dma_toks=[(dW, t_aw4[0])])
    veT_done = aFE.n
    aw4_es.close()

    q_i = lsb("q_i", (128, 16384), BF16)
    emit_proj(4, 8, 4, 512,
              lambda kc, mi: aw2[:, kc * 1024 + mi * 128: kc * 1024 + mi * 128 + 128],
              lambda kc, ni: q2_i[:, kc * 2048 + ni * 512: kc * 2048 + ni * 512 + 512],
              cp_evac(lambda mi, ni: q_i[:, mi * 2048 + ni * 512: mi * 2048 + ni * 512 + 512]),
              pr_a, pr_b, dma_toks=[(aFE, q2_done)])
    qT_free = pe.n
    aw3_es.close()
    B_es.close()
    aw2_es.close()
    p2_es.close()

    keys_i = lsb("keys_i", (128, 16384), BF16)
    u_i = lsb("u_i", (128, 16640), BF16)
    emit_proj(4, 8, 4, 512,
              lambda kc, mi: keT_i[:, kc * 1024 + mi * 128: kc * 1024 + mi * 128 + 128],
              lambda kc, ni: ckv_i[:, kc * 2048 + ni * 512: kc * 2048 + ni * 512 + 512],
              cp_evac(lambda mi, ni: keys_i[:, mi * 2048 + ni * 512: mi * 2048 + ni * 512 + 512]),
              pr_a, pr_b, dma_toks=[(aFE, keT_done)])
    keys_done = aFE.n

    u_ones_tok = [0]
    pe_gate_u = pe.n
    blk.vector(lambda v: (v.wait_ge(pe.h, pe_gate_u),
                          u_ones_tok.__setitem__(0, vG.inc(v.memset(
                              u_i.ap().rearrange("p (kc h s) -> p kc h s",
                                                 kc=16, h=16)[:, :, :, 64:65], 1.0)))))
    p2ps.close()
    ups = ExitStack()
    u_ps_a = ups.enter_context(nc.psum_tensor("u_ps_a", [128, 1024], F32))
    u_ps_b = ups.enter_context(nc.psum_tensor("u_ps_b", [128, 1024], F32))
    u_pe_tok = []
    aFE_u_b = aFE.n

    def f_u_pe(t):
        t.wait_ge(aFE.h, max(veT_done, ckv_done, keys_done))
        for kc in range(16):
            ps = (u_ps_a, u_ps_b)[kc % 2]
            if kc >= 2:
                t.wait_ge(aFE.h, aFE_u_b + kc - 1)
            for lc in range(4):
                for nb in (0, 1):
                    ins = t.matmul(ps[:, nb * 512:nb * 512 + 512],
                                   ckv_i[:, lc * 2048 + kc * 128: lc * 2048 + kc * 128 + 128],
                                   veT_i[:, lc * 1024 + nb * 512: lc * 1024 + nb * 512 + 512],
                                   start=(lc == 0), stop=(lc == 3))
            u_pe_tok.append(pe.inc(ins))

    def f_u_act(sc):
        sc.wait_ge(vG.h, u_ones_tok[0])
        for kc in range(16):
            ps = (u_ps_a, u_ps_b)[kc % 2]
            sc.wait_ge(pe.h, u_pe_tok[kc])
            dst = u_i.ap().rearrange("p (kc h s) -> p kc h s", kc=16, h=16)[:, kc, :, 0:64]
            aFE.inc(sc.copy(dst, ps.ap().rearrange("p (h d) -> p h d", h=16)))
    blk.tensor(f_u_pe)
    blk.scalar(f_u_act)
    u_done = aFE.n
    ups.close()

    p3_es = ExitStack()

    def r3(name, shape, dtype):
        return p3_es.enter_context(nc.sbuf_tensor(name, list(shape), dtype, side="right"))

    E2 = r3("E2", (128, 2048), BF16)
    nrm = r3("nrm", (128, 2048), F32)
    o_ev = r3("o_ev", (128, 2048), F32)
    xr = r3("xr", (128, 4096), F32)
    xt = r3("xt", (128, 2048), F32)
    p3ps = ExitStack()
    qk_ps = p3ps.enter_context(nc.psum_tensor("qk_ps", [128, 2048], F32))
    pv_ps = p3ps.enter_context(nc.psum_tensor("pv_ps", [65, 1024], F32))
    bc_ps = p3ps.enter_context(nc.psum_tensor("bc_ps", [64, 1024], F32))

    tiles3 = []
    for h in range(H):
        for lbh in (0, 1):
            for rc in range(8 * (lbh + 1)):
                tiles3.append((h, lbh, rc))
    G3 = len(tiles3)
    masked3 = [(rc >= 8 * lbh) for (h, lbh, rc) in tiles3]
    nmask3 = np.cumsum(masked3).tolist()
    first3 = [g for g in range(G3) if tiles3[g][2] == 0]
    last3 = [g - 1 for g in first3[1:]] + [G3 - 1]
    idx_of = {}
    for i, g in enumerate(first3):
        for gg in range(g, last3[i] + 1):
            idx_of[gg] = i
    aEX_b, aEV_b, vRC_b, vNM_b, pPV_b, gMK_b, dO_b, dX_b = (
        aEX.n, aEV.n, vRC.n, vNM.n, pPV.n, gMK.n, dO.n, dX.n)

    def f3_sync(sy):
        sy.wait_ge(pe.h, qT_free)

        def store(idx):
            h, lbh = idx // 2, idx % 2
            hc, rb = h // 2, (h % 2) * 64
            sy.wait_ge(vNM.h, vNM_b + idx + 1)
            sy.dma_start(
                out=parkA[rb:rb + 64, hc * 2048 + lbh * 1024: hc * 2048 + lbh * 1024 + 1024],
                in_=xt[0:64, (idx % 2) * 1024:(idx % 2) * 1024 + 1024]).then_inc(dO.h, 16)
            dO.n += 16
        for idx in range(2 * H):
            h, lbh = idx // 2, idx % 2
            hc, rb = h // 2, (h % 2) * 64
            if idx >= 4:
                sy.wait_ge(vNM.h, vNM_b + idx - 3)
            sy.dma_start(
                out=xr[0:64, (h % 2) * 2048 + lbh * 1024:(h % 2) * 2048 + lbh * 1024 + 1024],
                in_=xT[rb:rb + 64, hc * 2048 + lbh * 1024: hc * 2048 + lbh * 1024 + 1024]
            ).then_inc(dX.h, 16)
            dX.n += 16
            if idx >= 2:
                store(idx - 2)
        store(2 * H - 2)
        store(2 * H - 1)
    blk.sync(f3_sync)

    qk_tok = [0] * G3
    bc_tok = [0] * (2 * H)

    def f3_pe(t):
        t.wait_ge(aFE.h, max(keys_done, u_done))
        t.wait_ge(vG.h, u_ones_tok[0])

        def emit_qk(g):
            h, lbh, rc = tiles3[g]
            hc, rb = h // 2, (h % 2) * 64
            if aEX_b + g - 1 > 0:
                t.wait_ge(aEX.h, aEX_b + g - 1)
            for cb in (0, 1):
                ins = t.matmul(
                    qk_ps[:, (g % 2) * 1024 + cb * 512:(g % 2) * 1024 + cb * 512 + 512],
                    keys_i[rb:rb + 64, hc * 2048 + rc * 128: hc * 2048 + rc * 128 + 128],
                    q_i[rb:rb + 64, hc * 2048 + lbh * 1024 + cb * 512:
                        hc * 2048 + lbh * 1024 + cb * 512 + 512],
                    start=True, stop=True)
            qk_tok[g] = pe.inc(ins)

        def emit_pv(g):
            h, lbh, rc = tiles3[g]
            idx = idx_of[g]
            t.wait_ge(aEX.h, aEX_b + g + 1)
            if masked3[g]:
                t.wait_ge(gMK.h, gMK_b + nmask3[g])
            if rc == 0 and idx > 0:
                t.wait_ge(aEV.h, aEV_b + idx)
                t.wait_ge(vRC.h, vRC_b + idx)
            for cb in (0, 1):
                ins = t.matmul(
                    pv_ps[:, cb * 512:cb * 512 + 512],
                    u_i[:, rc * 1040 + h * 65: rc * 1040 + h * 65 + 65],
                    E2[:, (g % 2) * 1024 + cb * 512:(g % 2) * 1024 + cb * 512 + 512],
                    start=(rc == 0), stop=(rc == 8 * (lbh + 1) - 1))
            pPV.inc(ins)

        def emit_bc(idx):
            t.wait_ge(vRC.h, vRC_b + idx + 1)
            if idx > 0:
                t.wait_ge(vNM.h, vNM_b + idx)
            for cb in (0, 1):
                ins = t.matmul(
                    bc_ps[:, cb * 512:cb * 512 + 512], ones_f[64:65, 0:64],
                    nrm[64:65, (idx % 2) * 1024 + cb * 512:(idx % 2) * 1024 + cb * 512 + 512],
                    start=True, stop=True)
            bc_tok[idx] = pe.inc(ins)

        for g in range(G3):
            emit_qk(g)
            if g >= 1:
                emit_pv(g - 1)
            if tiles3[g][2] == 1 and idx_of[g] > 0:
                emit_bc(idx_of[g] - 1)
        emit_pv(G3 - 1)
        emit_bc(2 * H - 1)
    blk.tensor(f3_pe)

    ev_tok = [0] * (2 * H)

    def f3_act(sc):
        for g in range(G3):
            h, lbh, rc = tiles3[g]
            idx = idx_of[g]
            sc.wait_ge(pe.h, qk_tok[g])
            if pPV_b + g - 1 > 0:
                sc.wait_ge(pPV.h, pPV_b + g - 1)
            aEX.inc(sc.activation(E2[:, (g % 2) * 1024:(g % 2) * 1024 + 1024],
                                  qk_ps[:, (g % 2) * 1024:(g % 2) * 1024 + 1024],
                                  AF.Exp, scale=0.125))
            if g == last3[idx]:
                sc.wait_ge(pPV.h, pPV_b + g + 1)
                if idx >= 2:
                    sc.wait_ge(vNM.h, vNM_b + idx - 1)
                ins = sc.copy(o_ev[0:64, (idx % 2) * 1024:(idx % 2) * 1024 + 1024],
                              pv_ps[0:64, :])
                ev_tok[idx] = aEV.inc(ins)
    blk.scalar(f3_act)

    def f3_pool(g_eng):
        for g in range(G3):
            if not masked3[g]:
                continue
            h, lbh, rc = tiles3[g]
            g_eng.wait_ge(aEX.h, aEX_b + g + 1)
            ins = g_eng.affine_select(
                E2[:, (g % 2) * 1024:(g % 2) * 1024 + 1024],
                E2[:, (g % 2) * 1024:(g % 2) * 1024 + 1024],
                [[1, 1024]], ALU.is_ge, 0.0,
                base=lbh * 1024 - rc * 128, channel_multiplier=-1)
            gMK.inc(ins)
    blk.gpsimd(f3_pool)

    def f3_vec(v):
        for idx in range(2 * H):
            h, lbh = idx // 2, idx % 2
            v.wait_ge(aEV.h, ev_tok[idx])
            vRC.inc(v.reciprocal(nrm[64:65, (idx % 2) * 1024:(idx % 2) * 1024 + 1024],
                                 pv_ps[64:65, :]))
            v.wait_ge(pe.h, bc_tok[idx])
            v.wait_ge(dX.h, dX_b + 16 * (idx + 1))
            osl = o_ev[0:64, (idx % 2) * 1024:(idx % 2) * 1024 + 1024]
            v.tensor_tensor(osl, osl, bc_ps[0:64, :], ALU.mult)
            if dO_b + 16 * (idx - 1) > 0:
                v.wait_ge(dO.h, dO_b + 16 * (idx - 1))
            ins = v.tensor_tensor(
                xt[0:64, (idx % 2) * 1024:(idx % 2) * 1024 + 1024], osl,
                xr[0:64, (h % 2) * 2048 + lbh * 1024:(h % 2) * 2048 + lbh * 1024 + 1024],
                ALU.add)
            vNM.inc(ins)
    blk.vector(f3_vec)

    x1_tok = dO.n

    p3ps.close()
    p3_es.close()
    a_es.close()

    ffn_ctr = [0]

    def emit_ffn(ln_src, park_src, dst, first, x_gate, ln_tok):
        ffn_ctr[0] += 1
        fu = str(ffn_ctr[0])
        f_es = ExitStack()

        def rf(name, shape, dtype):
            return f_es.enter_context(nc.sbuf_tensor(name + fu, list(shape), dtype, side="right"))

        HA = rf("HA", (128, 32768), BF16)
        HB = rf("HB", (128, 32768), BF16)
        wb = rf("wb", (128, 8192), BF16)
        xold = rf("xold", (128, 1024), F32)
        xn = rf("xn", (128, 1024), F32)
        tAB = rf("tAB", (128, 1024), BF16)
        fps = ExitStack()
        psA = [fps.enter_context(nc.psum_tensor(f"fpsA{i}" + fu, [128, 512], F32)) for i in range(2)]
        psB = [fps.enter_context(nc.psum_tensor(f"fpsB{i}" + fu, [128, 512], F32)) for i in range(2)]

        stages_d = [
            ("A", 32, 8, (f_w1, f_w2), ("w1", "w2"), 1024),
            ("B", 32, 32, (f_h1,), ("h1",), 4096),
            ("C", 32, 32, (f_h2,), ("h2",), 4096),
            ("D", 8, 32, (f_w3,), ("w3",), 4096),
        ]
        prev_stage_done = [pe.n, aFE.n, vFE.n]

        for th in (0, 1):
            for kind, n_og, n_ic, srcs, cks, wcols in stages_d:
                nsrc = len(srcs)
                use_cache = (not first) or th == 1
                dma_sem = dX if use_cache else dW
                dma_b = dma_sem.n
                pW_b = pW.n
                aFE_b = aFE.n
                vFE_b = vFE.n
                dO_b = dO.n
                dX_b = dX.n
                dXO_b_d = dXO.n
                pe_prev, aFE_prev, vFE_prev = prev_stage_done

                pe_tok = {}

                def f_pe(t, kind=kind, n_og=n_og, n_ic=n_ic, nsrc=nsrc, wcols=wcols,
                         dma_sem=dma_sem, dma_b=dma_b, pW_b=pW_b, aFE_b=aFE_b,
                         vFE_b=vFE_b, dX_b=dX_b, th=th, pe_tok=pe_tok,
                         aFE_prev=aFE_prev, vFE_prev=vFE_prev):
                    if kind == "A" and th == 0:
                        t.wait_ge(vG.h, ln_tok)
                    if aFE_prev:
                        t.wait_ge(aFE.h, aFE_prev)
                    if vFE_prev:
                        t.wait_ge(vFE.h, vFE_prev)
                    for og in range(n_og):
                        t.wait_ge(dma_sem.h, dma_b + 16 * nsrc * (og + 1))
                        sl = wb[:, (og % 2) * 4096:(og % 2) * 4096 + 4096]
                        for tb in (0, 1):
                            s = og * 2 + tb
                            rhs_off = th * 1024 + tb * 512
                            if kind == "A":
                                if aFE_b + s - 1 > 0:
                                    t.wait_ge(aFE.h, aFE_b + s - 1)
                                if vFE_b + s - 1 > 0:
                                    t.wait_ge(vFE.h, vFE_b + s - 1)
                                for w_i in (0, 1):
                                    ps = (psA, psB)[w_i][s % 2]
                                    for ic in range(8):
                                        ins = t.matmul(
                                            ps[:, :],
                                            sl[:, w_i * 1024 + ic * 128: w_i * 1024 + ic * 128 + 128],
                                            ln_src[:, ic * 2048 + rhs_off: ic * 2048 + rhs_off + 512],
                                            start=(ic == 0), stop=(ic == 7))
                                    if w_i == 0:
                                        pe_tok[(og, tb, 0)] = pe.inc(ins)
                                if tb == 1:
                                    pW.inc(ins)
                                    pe_tok[(og, tb, 1)] = None
                                else:
                                    pe_tok[(og, tb, 1)] = pe.inc(ins)
                            else:
                                if aFE_b + s - 1 > 0 and kind != "D":
                                    t.wait_ge(aFE.h, aFE_b + s - 1)
                                if vFE_b + s - 1 > 0 and kind == "D":
                                    t.wait_ge(vFE.h, vFE_b + s - 1)
                                src_h = HA if kind in ("B", "D") else HB
                                ps = psA[s % 2]
                                for ic in range(n_ic):
                                    ins = t.matmul(
                                        ps[:, :], sl[:, ic * 128: ic * 128 + 128],
                                        src_h[:, ic * 1024 + tb * 512: ic * 1024 + tb * 512 + 512],
                                        start=(ic == 0), stop=(ic == n_ic - 1))
                                if tb == 1:
                                    pW.inc(ins)
                                    pe_tok[(og, tb)] = None
                                else:
                                    pe_tok[(og, tb)] = pe.inc(ins)
                blk.tensor(f_pe)
                pe_stage_done = pe.n

                if kind in ("A", "B", "C"):
                    def f_act(sc, kind=kind, n_og=n_og, th=th, pe_tok=pe_tok,
                              pW_b=pW_b, vFE_b=vFE_b):
                        for og in range(n_og):
                            for tb in (0, 1):
                                s = og * 2 + tb
                                if kind == "A":
                                    tok = pe_tok[(og, tb, 0)]
                                else:
                                    tok = pe_tok[(og, tb)]
                                if tok is None:
                                    sc.wait_ge(pW.h, pW_b + og + 1)
                                else:
                                    sc.wait_ge(pe.h, tok)
                                if kind == "A":
                                    if vFE_b + s - 1 > 0:
                                        sc.wait_ge(vFE.h, vFE_b + s - 1)
                                    ins = sc.activation(
                                        tAB[:, (s % 2) * 512:(s % 2) * 512 + 512],
                                        psA[s % 2][:, :], AF.Silu, scale=s1s[:, og:og + 1])
                                else:
                                    scl = (sh1s if kind == "B" else sh2s)[:, og:og + 1]
                                    dstH = HB if kind == "B" else HA
                                    ins = sc.activation(
                                        dstH[:, og * 1024 + tb * 512: og * 1024 + tb * 512 + 512],
                                        psA[s % 2][:, :], AF.Gelu, scale=scl)
                                aFE.inc(ins)
                    blk.scalar(f_act)
                aFE_stage_done = aFE.n

                if kind in ("A", "D"):
                    def f_vec(v, kind=kind, n_og=n_og, th=th, pe_tok=pe_tok,
                              pW_b=pW_b, dO_b=dO_b, dX_b=dX_b, aFE_b=aFE_b):
                        for og in range(n_og):
                            for tb in (0, 1):
                                s = og * 2 + tb
                                if kind == "A":
                                    tok = pe_tok[(og, tb, 1)]
                                    if tok is None:
                                        v.wait_ge(pW.h, pW_b + og + 1)
                                    else:
                                        v.wait_ge(pe.h, tok)
                                    v.wait_ge(aFE.h, aFE_b + s + 1)
                                    ins = v.scalar_tensor_tensor(
                                        HA[:, og * 1024 + tb * 512: og * 1024 + tb * 512 + 512],
                                        psB[s % 2][:, :], s2s[:, og:og + 1],
                                        tAB[:, (s % 2) * 512:(s % 2) * 512 + 512],
                                        ALU.mult, ALU.mult)
                                    vFE.inc(ins)
                                else:
                                    tok = pe_tok[(og, tb)]
                                    if tok is None:
                                        v.wait_ge(pW.h, pW_b + og + 1)
                                    else:
                                        v.wait_ge(pe.h, tok)
                                    v.wait_ge(dXO.h, dXO_b_d + 16 * (og + 1))
                                    if dO_b + 16 * (s - 1) > 0:
                                        v.wait_ge(dO.h, dO_b + 16 * (s - 1))
                                    ins = v.scalar_tensor_tensor(
                                        xn[:, (s % 2) * 512:(s % 2) * 512 + 512],
                                        psA[s % 2][:, :], s3s[:, og:og + 1],
                                        xold[:, tb * 512: tb * 512 + 512],
                                        ALU.mult, ALU.add)
                                    vFE.inc(ins)
                    blk.vector(f_vec)
                vFE_stage_done = vFE.n

                wwid = 1024 if nsrc == 2 else 4096
                vFE_stage_b_d = vFE_b

                def emit_store(sy, og, tb, th=th):
                    s = og * 2 + tb
                    sy.wait_ge(vFE.h, vFE_stage_b_d + s + 1)
                    sy.dma_start(
                        out=dst[:, og * 2048 + th * 1024 + tb * 512:
                                og * 2048 + th * 1024 + tb * 512 + 512],
                        in_=xn[:, (s % 2) * 512:(s % 2) * 512 + 512]
                    ).then_inc(dO.h, 16)
                    dO.n += 16

                if not use_cache:
                    dCW_pre = dCW.n

                    def f_gp(g, n_og=n_og, nsrc=nsrc, srcs=srcs, pW_b=pW_b,
                             dCW_pre=dCW_pre, wwid=wwid):
                        g.wait_ge(dO.h, x_gate)
                        for og in range(n_og):
                            g.wait_ge(pW.h, pW_b + max(0, og - 1))
                            if og >= 2:
                                g.wait_ge(dCW.h, dCW_pre + 16 * nsrc * (og - 1))
                            sl = wb[:, (og % 2) * 4096:(og % 2) * 4096 + 4096]
                            for w_i, srcw in enumerate(srcs):
                                ins = g.dma_start(
                                    out=sl[:, w_i * wwid: w_i * wwid + wwid],
                                    in_=srcw[og])
                                dW.inc(ins, 16)

                    def f_syncw(sy, n_og=n_og, nsrc=nsrc, cks=cks, dma_b=dma_b,
                                wwid=wwid, kind=kind):
                        for og in range(n_og):
                            sy.wait_ge(dW.h, dma_b + 16 * nsrc * (og + 1))
                            sl = wb[:, (og % 2) * 4096:(og % 2) * 4096 + 4096]
                            for w_i, ck in enumerate(cks):
                                ins = sy.dma_start(
                                    out=cach[ck][og],
                                    in_=sl[:, w_i * wwid: w_i * wwid + wwid])
                                dCW.inc(ins, 16)
                            if kind == "D" and og >= 1:
                                emit_store(sy, og - 1, 0)
                                emit_store(sy, og - 1, 1)
                        if kind == "D":
                            emit_store(sy, n_og - 1, 0)
                            emit_store(sy, n_og - 1, 1)
                    blk.gpsimd(f_gp)
                    blk.sync(f_syncw)
                else:
                    def f_syncr(sy, n_og=n_og, nsrc=nsrc, cks=cks, pW_b=pW_b,
                                wwid=wwid, kind=kind):
                        sy.wait_ge(dO.h, x_gate)
                        for og in range(n_og):
                            sy.wait_ge(pW.h, pW_b + max(0, og - 1))
                            sl = wb[:, (og % 2) * 4096:(og % 2) * 4096 + 4096]
                            for w_i, ck in enumerate(cks):
                                ins = sy.dma_start(
                                    out=sl[:, w_i * wwid: w_i * wwid + wwid],
                                    in_=cach[ck][og])
                                dX.inc(ins, 16)
                            if kind == "D" and og >= 1:
                                emit_store(sy, og - 1, 0)
                                emit_store(sy, og - 1, 1)
                        if kind == "D":
                            emit_store(sy, n_og - 1, 0)
                            emit_store(sy, n_og - 1, 1)
                    blk.sync(f_syncr)

                if kind == "D":
                    dXO_b = dXO.n

                    def f_actd(sc, n_og=n_og, th=th, x_gate=x_gate):
                        sc.wait_ge(dO.h, x_gate)
                        for og in range(n_og):
                            if og >= 1:
                                sc.wait_ge(vFE.h, vFE_stage_b_d + 2 * og)
                            sc.dma_start(
                                out=xold[:, 0:1024],
                                in_=park_src[:, og * 2048 + th * 1024: og * 2048 + th * 1024 + 1024]
                            ).then_inc(dXO.h, 16)
                            dXO.n += 16
                    blk.scalar(f_actd)

                prev_stage_done = [pe_stage_done, aFE_stage_done, vFE_stage_done]
        fps.close()
        f_es.close()
        return dO.n

    if stages >= 2:
        f1_es = ExitStack()
        ln2 = f1_es.enter_context(nc.sbuf_tensor("ln2", [128, 16384], BF16, side="right"))
        ln2s = ExitStack()
        xb2 = ln2s.enter_context(nc.sbuf_tensor("xb2", [128, 8192], BF16, side="right"))
        sqb2 = ln2s.enter_context(nc.sbuf_tensor("sqb2", [128, 8192], BF16, side="right"))
        tb22 = ln2s.enter_context(nc.sbuf_tensor("tb22", [128, 1024], F32, side="right"))
        ln2_tok = emit_ln(parkA, ln2, xb2, sqb2, tb22,
                          gate=lambda g: g.wait_ge(dO.h, x1_tok))
        ln2s.close()
        x2_tok = emit_ffn(ln2, parkA, parkB, True, x1_tok, ln2_tok)
        f1_es.close()

    if stages >= 3:
        b_es = ExitStack()

        def rb_(name, shape, dtype):
            return b_es.enter_context(nc.sbuf_tensor(name, list(shape), dtype, side="right"))

        qb_i = rb_("qb_i", (128, 16384), BF16)
        kb_i = rb_("kb_i", (128, 16384), BF16)
        v_i = rb_("v_i", (128, 16640), BF16)
        rel2_s = rb_("rel2_s", (128, 4095), BF16)
        ln3_es = ExitStack()
        ln3 = ln3_es.enter_context(nc.sbuf_tensor("ln3", [128, 16384], BF16, side="right"))
        ln3s = ExitStack()
        xb3 = ln3s.enter_context(nc.sbuf_tensor("xb3", [128, 8192], BF16, side="right"))
        sqb3 = ln3s.enter_context(nc.sbuf_tensor("sqb3", [128, 8192], BF16, side="right"))
        tb23 = ln3s.enter_context(nc.sbuf_tensor("tb23", [128, 1024], F32, side="right"))
        dcw_t, dxo_t = dCW.n, dXO.n
        ln3_done = emit_ln(parkB, ln3, xb3, sqb3, tb23,
                           gate=lambda g: (g.wait_ge(dO.h, x2_tok),
                                           g.wait_ge(dCW.h, dcw_t),
                                           g.wait_ge(dXO.h, dxo_t)))
        ln3_stats_pe = pe.n
        ln3s.close()

        t_rel = []
        blk.gpsimd(lambda g: (g.wait_ge(vG.h, ln3_done), g.wait_ge(pe.h, ln3_stats_pe),
                              t_rel.append(dW.inc(
                                  g.dma_start(out=rel2_s[:], in_=rel2_d), 16))))

        bps = ExitStack()
        pb_a = bps.enter_context(nc.psum_tensor("pb_a", [128, 512], F32))
        pb_b = bps.enter_context(nc.psum_tensor("pb_b", [128, 512], F32))
        bw_es = ExitStack()
        bwb = bw_es.enter_context(nc.sbuf_tensor("bwb", [128, 16384], BF16, side="right"))
        t_bw = []
        blk.gpsimd(lambda g: (g.wait_ge(vG.h, ln3_done), g.wait_ge(pe.h, ln3_stats_pe),
                              t_bw.extend(load_w(
                                  g, [bwb[:, 0:8192], bwb[:, 8192:16384]], [bw_q, bw_k]))))

        ln3_tok = vG.n
        emit_proj(8, 8, 4, 512,
                  lambda kc, mi: bwb[:, kc * 1024 + mi * 128: kc * 1024 + mi * 128 + 128],
                  lambda kc, ni: ln3[:, kc * 2048 + ni * 512: kc * 2048 + ni * 512 + 512],
                  cp_evac(lambda mi, ni: qb_i[:, mi * 2048 + ni * 512: mi * 2048 + ni * 512 + 512]),
                  pb_a, pb_b, dma_toks=[(dW, t_bw[0]), (vG, ln3_tok)],
                  bias_row=bq_row_s)
        emit_proj(8, 8, 4, 512,
                  lambda kc, mi: bwb[:, 8192 + kc * 1024 + mi * 128: 8192 + kc * 1024 + mi * 128 + 128],
                  lambda kc, ni: ln3[:, kc * 2048 + ni * 512: kc * 2048 + ni * 512 + 512],
                  cp_evac(lambda mi, ni: kb_i[:, mi * 2048 + ni * 512: mi * 2048 + ni * 512 + 512]),
                  pb_a, pb_b, dma_toks=[(dW, t_bw[1])], bias_row=bk_row_s)
        qk_proj_done = aFE.n
        if DEBUG_DUMP:
            dbgq = nc.dram_tensor("dbgq", [128, 16384], BF16, kind="Internal").ap()
            dbgl = nc.dram_tensor("dbgl", [128, 16384], BF16, kind="Internal").ap()

            def f_dbg(sy):
                sy.wait_ge(aFE.h, qk_proj_done)
                sy.dma_start(out=dbgq, in_=qb_i[:, :]).then_inc(dO.h, 16)
                sy.dma_start(out=dbgl, in_=ln3[:, :]).then_inc(dO.h, 16)
                dO.n += 32
            blk.sync(f_dbg)

        t_bwv = []
        blk.gpsimd(lambda g: (g.wait_ge(aFE.h, qk_proj_done),
                              t_bwv.extend(load_w(g, [bwb[:, 0:8192]], [bw_v]))))
        v_ones_tok = [0]
        pe_gate_v = pe.n
        blk.vector(lambda v: (v.wait_ge(pe.h, pe_gate_v),
                              v_ones_tok.__setitem__(0, vG.inc(v.memset(
                                  v_i.ap().rearrange("p (kc h s) -> p kc h s",
                                                     kc=16, h=16)[:, :, :, 64:65], 1.0)))))
        vps = ExitStack()
        v_ps_a = vps.enter_context(nc.psum_tensor("v_ps_a", [128, 1024], F32))
        v_ps_b = vps.enter_context(nc.psum_tensor("v_ps_b", [128, 1024], F32))
        v_pe_tok = []
        aFE_v_b = aFE.n

        def f_v_pe(t):
            t.wait_ge(dW.h, t_bwv[0])
            t.wait_ge(aFE.h, aFE_v_b)
            for kc in range(16):
                ps = (v_ps_a, v_ps_b)[kc % 2]
                if kc >= 2:
                    t.wait_ge(aFE.h, aFE_v_b + kc - 1)
                for nb in (0, 1):
                    for ic in range(8):
                        t.matmul(ps[:, nb * 512:nb * 512 + 512],
                                 ln3[:, ic * 2048 + kc * 128: ic * 2048 + kc * 128 + 128],
                                 bwb[:, ic * 1024 + nb * 512: ic * 1024 + nb * 512 + 512],
                                 start=(ic == 0), stop=False)
                    ins = t.matmul(ps[:, nb * 512:nb * 512 + 512], ones_b[0:1, 0:128],
                                   bvs[0:1, nb * 512:nb * 512 + 512],
                                   start=False, stop=True)
                v_pe_tok.append(pe.inc(ins))

        def f_v_act(sc):
            sc.wait_ge(vG.h, v_ones_tok[0])
            for kc in range(16):
                ps = (v_ps_a, v_ps_b)[kc % 2]
                sc.wait_ge(pe.h, v_pe_tok[kc])
                dst = v_i.ap().rearrange("p (kc h s) -> p kc h s", kc=16, h=16)[:, kc, :, 0:64]
                aFE.inc(sc.copy(dst, ps.ap().rearrange("p (h d) -> p h d", h=16)))
        blk.tensor(f_v_pe)
        blk.scalar(f_v_act)
        v_done = aFE.n
        vps.close()
        bw_es.close()
        ln3_es.close()
        bps.close()

        p2b_es = ExitStack()
        er = p2b_es.enter_context(nc.sbuf_tensor("er", [128, 16384], BF16, side="right"))
        t1b = p2b_es.enter_context(nc.sbuf_tensor("t1b", [128, 2048], BF16, side="right"))
        Eb = p2b_es.enter_context(nc.sbuf_tensor("Eb", [128, 2048], BF16, side="right"))
        nrm2 = p2b_es.enter_context(nc.sbuf_tensor("nrm2", [128, 2048], F32, side="right"))
        oev2 = p2b_es.enter_context(nc.sbuf_tensor("oev2", [128, 2048], F32, side="right"))
        o_t = p2b_es.enter_context(nc.sbuf_tensor("o_t", [128, 2048], BF16, side="right"))
        p1_es = ExitStack()
        Pst = p1_es.enter_context(nc.sbuf_tensor("Pst", [128, 8704], BF16, side="right"))
        p1ps = ExitStack()
        PA = p1ps.enter_context(nc.psum_tensor("PA", [128, 1088], F32))
        PB = p1ps.enter_context(nc.psum_tensor("PB", [128, 1088], F32))
        aEX_b1 = aEX.n
        dPX_b1 = dPX.n
        p1_pe_tok = {}

        def f1_pe(t):
            t.wait_ge(aFE.h, v_done)
            t.wait_ge(dW.h, t_rel[0])
            for hp in range(8):
                for lc in range(16):
                    for half in (0, 1):
                        s = (hp * 16 + lc) * 2 + half
                        w = 1088 if half == 0 else 1087
                        s0 = 1920 - 128 * lc + half * 1088
                        if s > 0:
                            t.wait_ge(aEX.h, aEX_b1 + 2 * s)
                        for rbv, psx in ((0, PA), (64, PB)):
                            off = 0
                            for n in (512, 512, w - 1024):
                                ins = t.matmul(
                                    psx[:, off:off + n],
                                    qb_i[rbv:rbv + 64, hp * 2048 + lc * 128: hp * 2048 + lc * 128 + 128],
                                    rel2_s[rbv:rbv + 64, s0 + off: s0 + off + n],
                                    start=True, stop=True)
                                off += n
                        p1_pe_tok[s] = pe.inc(ins)
        blk.tensor(f1_pe)

        def f1_act(sc):
            for hp in range(8):
                for lc in range(16):
                    for half in (0, 1):
                        s = (hp * 16 + lc) * 2 + half
                        w = 1088 if half == 0 else 1087
                        sc.wait_ge(pe.h, p1_pe_tok[s])
                        if hp * 16 + lc >= 2 and half == 0:
                            sc.wait_ge(dPX.h, dPX_b1 + 32 * (hp * 16 + lc - 1))
                        for hh, psx in ((0, PA), (1, PB)):
                            ins = sc.activation(
                                Pst[:, hh * 4352 + (lc % 2) * 2176 + half * 1088:
                                    hh * 4352 + (lc % 2) * 2176 + half * 1088 + w],
                                psx[:, 0:w], AF.Exp)
                            aEX.inc(ins)
        blk.scalar(f1_act)

        def f1_gp(g):
            for hp in range(8):
                for lc in range(16):
                    s_last = aEX_b1 + ((hp * 16 + lc) * 2 + 1) * 2 + 2
                    g.wait_ge(aEX.h, s_last)
                    for hh in (0, 1):
                        h = hp * 2 + hh
                        ins = g.dma_start(
                            out=AP(pexp.tensor,
                                   h * 2048 * 4095 + (lc * 128) * 4095 + (1920 - 128 * lc),
                                   [[4095, 128], [1, 2175]]),
                            in_=Pst[:, hh * 4352 + (lc % 2) * 2176:
                                    hh * 4352 + (lc % 2) * 2176 + 2175])
                        dPX.inc(ins, 16)
        blk.gpsimd(f1_gp)
        pexp_all = dPX.n
        p1ps.close()
        p1_es.close()
        p2bps = ExitStack()
        qk2 = p2bps.enter_context(nc.psum_tensor("qk2", [128, 2048], F32))
        pv2 = [p2bps.enter_context(nc.psum_tensor(f"pv2_{i}", [65, 1024], F32)) for i in (0, 1)]

        aEX_b2, aEV_b2, vRC_b2, vNM_b2, vEM_b2, pPV_b2, dK_b2, dOO_b2 = (
            aEX.n, aEV.n, vRC.n, vNM.n, vEM.n, pPV.n, dK.n, dOO.n)

        def f2_syncr(sy):

            def store_o(idx):
                h2, lbh2 = idx // 2, idx % 2
                hc2, rbv2 = h2 // 2, (h2 % 2) * 64
                sy.wait_ge(vNM.h, vNM_b2 + idx + 1)
                sy.dma_start(
                    out=osc_d[rbv2:rbv2 + 64,
                              hc2 * 2048 + lbh2 * 1024: hc2 * 2048 + lbh2 * 1024 + 1024],
                    in_=o_t[0:64, (idx % 2) * 1024:(idx % 2) * 1024 + 1024]
                ).then_inc(dOO.h, 16)
                dOO.n += 16
            for h in range(H):
                sy.wait_ge(dPX.h, dPX_b1 + 512 * (h // 2 + 1))
                for rc in range(16):
                    R = h * 16 + rc
                    e_guard = vEM_b2 + 2 * (R - 8) + 2
                    if e_guard > vEM_b2:
                        sy.wait_ge(vEM.h, e_guard)
                    src = AP(pexp.tensor, h * 2048 * 4095 + 2047 + 128 * rc,
                             [[4094, 2048], [1, 128]])
                    sy.dma_start(out=er[:, (R % 8) * 2048:(R % 8) * 2048 + 2048],
                                 in_=src, transpose=True).then_inc(dK.h, 16)
                    dK.n += 16
                if h >= 1:
                    store_o(2 * (h - 1))
                    store_o(2 * (h - 1) + 1)
            store_o(2 * H - 2)
            store_o(2 * H - 1)
        blk.sync(f2_syncr)

        qk2_tok = {}
        bc2_tok = {}
        units = [(h, rc, lbh) for h in range(H) for rc in range(16) for lbh in (0, 1)]
        NU = len(units)

        def last_j(h, lbh):
            return h * 32 + 30 + lbh

        def f2_pe(t):
            t.wait_ge(aFE.h, v_done)
            t.wait_ge(aEX.h, aEX_b2)
            t.wait_ge(vG.h, v_ones_tok[0])

            def emit_qk(j):
                h, rc, lbh = units[j]
                hc, rbv = h // 2, (h % 2) * 64
                if j > 0:
                    t.wait_ge(aEX.h, aEX_b2 + j - 1)
                for cb in (0, 1):
                    ins = t.matmul(
                        qk2[:, (j % 2) * 1024 + cb * 512:(j % 2) * 1024 + cb * 512 + 512],
                        kb_i[rbv:rbv + 64, hc * 2048 + rc * 128: hc * 2048 + rc * 128 + 128],
                        qb_i[rbv:rbv + 64, hc * 2048 + lbh * 1024 + cb * 512:
                             hc * 2048 + lbh * 1024 + cb * 512 + 512],
                        start=True, stop=True)
                qk2_tok[j] = pe.inc(ins)

            def emit_pv(j):
                h, rc, lbh = units[j]
                idx = 2 * h + lbh
                t.wait_ge(vEM.h, vEM_b2 + j + 1)
                if rc == 0 and h > 0:
                    t.wait_ge(aEV.h, aEV_b2 + 2 * (h - 1) + lbh + 1)
                    t.wait_ge(vRC.h, vRC_b2 + 2 * (h - 1) + lbh + 1)
                    t.wait_ge(vNM.h, vNM_b2 + 2 * (h - 1) + lbh + 1)
                for cb in (0, 1):
                    ins = t.matmul(
                        pv2[lbh][:, cb * 512:cb * 512 + 512],
                        v_i[:, rc * 1040 + h * 65: rc * 1040 + h * 65 + 65],
                        Eb[:, (j % 2) * 1024 + cb * 512:(j % 2) * 1024 + cb * 512 + 512],
                        start=(rc == 0 and cb in (0, 1)), stop=(rc == 15))
                pPV.inc(ins)
                if rc == 15:
                    t.wait_ge(vRC.h, vRC_b2 + idx + 1)
                    for cb in (0, 1):
                        ins = t.matmul(
                            pv2[lbh][0:64, cb * 512:cb * 512 + 512], ones_f[64:65, 0:64],
                            nrm2[64:65, (idx % 2) * 1024 + cb * 512:
                                 (idx % 2) * 1024 + cb * 512 + 512],
                            start=True, stop=True)
                    bc2_tok[idx] = pe.inc(ins)

            for j in range(NU):
                emit_qk(j)
                if j >= 2:
                    emit_pv(j - 2)
            emit_pv(NU - 2)
            emit_pv(NU - 1)
        blk.tensor(f2_pe)

        def f2_act(sc):
            for j in range(NU):
                h, rc, lbh = units[j]
                sc.wait_ge(pe.h, qk2_tok[j])
                if j >= 2:
                    sc.wait_ge(vEM.h, vEM_b2 + j - 1)
                aEX.inc(sc.activation(t1b[:, (j % 2) * 1024:(j % 2) * 1024 + 1024],
                                      qk2[:, (j % 2) * 1024:(j % 2) * 1024 + 1024],
                                      AF.Exp, scale=0.125))
                if rc == 15:
                    idx = 2 * h + lbh
                    sc.wait_ge(pPV.h, pPV_b2 + last_j(h, lbh) + 1)
                    if idx >= 2:
                        sc.wait_ge(vNM.h, vNM_b2 + idx - 1)
                    ins = sc.copy(oev2[0:64, (idx % 2) * 1024:(idx % 2) * 1024 + 1024],
                                  pv2[lbh][0:64, :])
                    aEV.inc(ins)
        blk.scalar(f2_act)

        def f2_vec(v):
            for j in range(NU):
                h, rc, lbh = units[j]
                v.wait_ge(aEX.h, aEX_b2 + j + 1)
                v.wait_ge(dK.h, dK_b2 + 16 * (h * 16 + rc + 1))
                if j >= 2:
                    v.wait_ge(pPV.h, pPV_b2 + j - 1)
                R = h * 16 + rc
                ins = v.tensor_tensor(
                    Eb[:, (j % 2) * 1024:(j % 2) * 1024 + 1024],
                    t1b[:, (j % 2) * 1024:(j % 2) * 1024 + 1024],
                    er[:, (R % 8) * 2048 + lbh * 1024:(R % 8) * 2048 + lbh * 1024 + 1024],
                    ALU.mult)
                vEM.inc(ins)
                if rc == 15:
                    idx = 2 * h + lbh
                    v.wait_ge(aEV.h, aEV_b2 + idx + 1)
                    vRC.inc(v.reciprocal(nrm2[64:65, (idx % 2) * 1024:(idx % 2) * 1024 + 1024],
                                         pv2[lbh][64:65, :]))
                    v.wait_ge(pe.h, bc2_tok.get(idx, 0) or 0) if False else None
                    v.wait_ge(pPV.h, pPV_b2 + NU + 2 * idx + 2) if False else None
                    v.wait_ge(pe.h, bc2_tok[idx])
                    if dOO_b2 + 16 * (idx - 1) > dOO_b2:
                        v.wait_ge(dOO.h, dOO_b2 + 16 * (idx - 1))
                    ins = v.tensor_tensor(
                        o_t[0:64, (idx % 2) * 1024:(idx % 2) * 1024 + 1024],
                        oev2[0:64, (idx % 2) * 1024:(idx % 2) * 1024 + 1024],
                        pv2[lbh][0:64, 0:1024], ALU.mult)
                    vNM.inc(ins)
        blk.vector(f2_vec)

        oscr_tok = dOO.n
        p2bps.close()
        p2b_es.close()

        wo_es = ExitStack()
        o_i = wo_es.enter_context(nc.sbuf_tensor("o_i", [128, 16384], BF16, side="right"))
        wo_s = wo_es.enter_context(nc.sbuf_tensor("wo_s", [128, 8192], BF16, side="right"))
        xr2 = wo_es.enter_context(nc.sbuf_tensor("xr2", [128, 4096], F32, side="right"))
        xt2 = wo_es.enter_context(nc.sbuf_tensor("xt2", [128, 1024], F32, side="right"))
        wops = ExitStack()
        wo_a = wops.enter_context(nc.psum_tensor("wo_a", [128, 512], F32))
        wo_b = wops.enter_context(nc.psum_tensor("wo_b", [128, 512], F32))
        t_wo = []
        blk.gpsimd(lambda g: (g.wait_ge(dOO.h, oscr_tok),
                              t_wo.extend(load_w(g, [wo_s[:, :]], [bw_o]))))
        oi_tok = [0]

        def fw_sync_oi(sy):
            sy.wait_ge(dOO.h, oscr_tok)
            oi_tok[0] = dX.inc(sy.dma_start(out=o_i[:, :], in_=osc_d[:, :]), 16)
        blk.sync(fw_sync_oi)

        wo_pe_tok = {}
        vFE_wo_b = vFE.n
        dX_wo_b = dX.n
        dO_wo_b = dO.n

        def fw_pe(t):
            t.wait_ge(dW.h, t_wo[0])
            t.wait_ge(dX.h, oi_tok[0])
            for oc in range(8):
                for tb in range(4):
                    s = oc * 4 + tb
                    if vFE_wo_b + s - 1 > vFE_wo_b - 1 and s >= 2:
                        t.wait_ge(vFE.h, vFE_wo_b + s - 1)
                    ps = (wo_a, wo_b)[s % 2]
                    for ic in range(8):
                        ins = t.matmul(
                            ps[:, :], wo_s[:, ic * 1024 + oc * 128: ic * 1024 + oc * 128 + 128],
                            o_i[:, ic * 2048 + tb * 512: ic * 2048 + tb * 512 + 512],
                            start=(ic == 0), stop=(ic == 7))
                    wo_pe_tok[s] = pe.inc(ins)
        blk.tensor(fw_pe)

        def fw_sync_x(sy):
            sy.wait_ge(dOO.h, oscr_tok)

            def store_x(oc2):
                for tb2 in range(4):
                    s2 = oc2 * 4 + tb2
                    sy.wait_ge(vFE.h, vFE_wo_b + s2 + 1)
                    sy.dma_start(out=parkA[:, oc2 * 2048 + tb2 * 512: oc2 * 2048 + tb2 * 512 + 512],
                                 in_=xt2[:, (s2 % 2) * 512:(s2 % 2) * 512 + 512]
                                 ).then_inc(dO.h, 16)
                    dO.n += 16
            for oc in range(8):
                if oc >= 2:
                    sy.wait_ge(vFE.h, vFE_wo_b + 4 * (oc - 1))
                sy.dma_start(out=xr2[:, (oc % 2) * 2048:(oc % 2) * 2048 + 2048],
                             in_=parkB[:, oc * 2048: oc * 2048 + 2048]).then_inc(dX.h, 16)
                dX.n += 16
                if oc >= 1:
                    store_x(oc - 1)
            store_x(7)
        blk.sync(fw_sync_x)

        def fw_vec(v):
            for oc in range(8):
                v.wait_ge(dX.h, dX_wo_b + 16 * (oc + 1))
                for tb in range(4):
                    s = oc * 4 + tb
                    v.wait_ge(pe.h, wo_pe_tok[s])
                    if dO_wo_b + 16 * (s - 1) > dO_wo_b:
                        v.wait_ge(dO.h, dO_wo_b + 16 * (s - 1))
                    ps = (wo_a, wo_b)[s % 2]
                    ins = v.scalar_tensor_tensor(
                        xt2[:, (s % 2) * 512:(s % 2) * 512 + 512], ps[:, :],
                        bos[:, oc:oc + 1],
                        xr2[:, (oc % 2) * 2048 + tb * 512:(oc % 2) * 2048 + tb * 512 + 512],
                        ALU.add, ALU.add)
                    vFE.inc(ins)
        blk.vector(fw_vec)

        x3_tok = dO.n
        wops.close()
        wo_es.close()
        b_es.close()

    if stages >= 4:
        f2_es = ExitStack()
        ln4 = f2_es.enter_context(nc.sbuf_tensor("ln4", [128, 16384], BF16, side="right"))
        ln4s = ExitStack()
        xb4 = ln4s.enter_context(nc.sbuf_tensor("xb4", [128, 8192], BF16, side="right"))
        sqb4 = ln4s.enter_context(nc.sbuf_tensor("sqb4", [128, 8192], BF16, side="right"))
        tb24 = ln4s.enter_context(nc.sbuf_tensor("tb24", [128, 1024], F32, side="right"))
        dcw_t4, dxo_t4, doo_t4 = dCW.n, dXO.n, dOO.n
        ln4_tok = emit_ln(parkA, ln4, xb4, sqb4, tb24,
                          gate=lambda g: (g.wait_ge(dO.h, x3_tok),
                                          g.wait_ge(dCW.h, dcw_t4),
                                          g.wait_ge(dXO.h, dxo_t4),
                                          g.wait_ge(dOO.h, doo_t4)))
        ln4s.close()
        emit_ffn(ln4, parkA, yO, False, x3_tok, ln4_tok)
        f2_es.close()

    if stages < 4:
        src_park = parkA if stages in (1, 3) else parkB

        def f_tail(sy):
            sy.wait_ge(dO.h, dO.n)
            sy.dma_start(out=yO, in_=src_park).then_inc(dO.h, 16)
            dO.n += 16
        blk.sync(f_tail)

    fins = [(s, s.n) for s in (dW, dXB, dPX, dX, dK, dO, dCW, dOO, dXO)]

    def f_fin(sy):
        for s, n in fins:
            if n:
                sy.wait_ge(s.h, n)
    blk.sync(f_fin)

    es.close()
    nc.finalize()
    return nc


_BUILT = {}


def _prep_inputs(inputs):
    f32 = np.float32
    g = {k: np.asarray(v, dtype=f32) for k, v in inputs.items()}
    d = {}
    d["aw_dkv"] = img128(g["L0_attn_a_W_dkv"].T.copy())
    d["aw_dq"] = img128(g["L0_attn_a_W_dq"].T.copy())
    d["aw_dqn"] = img128(g["L0_attn_a_W_dq"].copy())
    d["aw_uq"] = img128(g["L0_attn_a_W_uq"].T.copy())
    d["aw_uk"] = img128(g["L0_attn_a_W_uk"].copy())
    d["aw_uv"] = img128(g["L0_attn_a_W_uv"].copy())
    d["aw_o"] = img128(g["L0_attn_a_W_o"].T.copy())
    d["bw_q"] = img128(g["L0_attn_b_Wq"].T.copy())
    d["bw_k"] = img128(g["L0_attn_b_Wk"].T.copy())
    d["bw_v"] = img128(g["L0_attn_b_Wv"].T.copy())
    d["bw_o"] = img128(g["L0_attn_b_Wo"].T.copy())
    d["bq_r"] = g["L0_attn_b_bq"].reshape(1, 1024).copy()
    d["bk_r"] = g["L0_attn_b_bk"].reshape(1, 1024).copy()
    d["bo_i"] = colimg(g["L0_attn_b_bo"])
    d["bv_r"] = g["L0_attn_b_bv"].reshape(1, 1024).copy()
    re_t = np.ascontiguousarray(g["L0_attn_b_rel_emb"].T)
    d["rel2"] = np.concatenate([re_t, re_t], axis=0).copy()
    d["f_w1"] = ogblocks(g["L0_ffn_w1_w"].T.copy())
    d["f_w2"] = ogblocks(g["L0_ffn_w2_w"].T.copy())
    d["f_h1"] = ogblocks(g["L0_ffn_h1_w"].T.copy())
    d["f_h2"] = ogblocks(g["L0_ffn_h2_w"].T.copy())
    d["f_w3"] = ogblocks(g["L0_ffn_w3_w"].T.copy())
    d["s1_i"] = colimg(g["L0_ffn_w1_s"])
    d["s2_i"] = colimg(g["L0_ffn_w2_s"])
    d["sh1_i"] = colimg(g["L0_ffn_h1_s"])
    d["sh2_i"] = colimg(g["L0_ffn_h2_s"])
    d["s3_i"] = colimg(g["L0_ffn_w3_s"])
    x = g["L0_attn_a_x"]
    in_maps = []
    for b in range(NCORE):
        m = dict(d)
        m["xT"] = img128(np.ascontiguousarray(x[b].T))
        in_maps.append(m)
    return in_maps


def kernel(**inputs):
    if STAGES not in _BUILT:
        _BUILT[STAGES] = build(STAGES)
    nc = _BUILT[STAGES]
    in_maps = _prep_inputs(inputs)
    res = run_bass_kernel_spmd(nc, in_maps, core_ids=list(range(NCORE)))
    outs = []
    for b in range(NCORE):
        yi = np.asarray(res.results[b]["y"])
        outs.append(yi.reshape(128, 8, 2048).transpose(1, 0, 2).reshape(1024, 2048).T)
    return np.stack(outs, axis=0).astype(np.float32)


# version 06
# speedup vs baseline: 1.2002x; 1.0084x over previous
import numpy as np
import ml_dtypes
import concourse.bass as bass
import concourse.mybir as mybir
from concourse.ap import AP
from concourse.bass_utils import run_bass_kernel_spmd
from contextlib import ExitStack

F32 = mybir.dt.float32
BF16 = mybir.dt.bfloat16
AF = mybir.ActivationFunctionType
ALU = mybir.AluOpType
BFD = ml_dtypes.bfloat16

T = 2048
D = 1024
LAT = 512
H = 16
HD = 64
DH = 4096
NCORE = 8
STAGES = 4
DEBUG_DUMP = False


def img128(m):
    R, C = m.shape
    assert R % 128 == 0
    return np.ascontiguousarray(
        m.reshape(R // 128, 128, C).transpose(1, 0, 2).reshape(128, (R // 128) * C))


def ogblocks(wt, ogc=128):
    I, O = wt.shape
    n = O // ogc
    a = wt.reshape(I // 128, 128, n, ogc).transpose(2, 1, 0, 3)
    return np.ascontiguousarray(a.reshape(n, 128, (I // 128) * ogc))


def colimg(v):
    return np.ascontiguousarray(v.reshape(-1, 128).T)


class Sem:
    __slots__ = ("h", "n")

    def __init__(self, es, nc, name):
        self.h = es.enter_context(nc.semaphore(name))
        self.n = 0

    def inc(self, instr, k=1):
        instr.then_inc(self.h, k)
        self.n += k
        return self.n


def build(stages):
    nc = bass.Bass("TRN2", target_bir_lowering=False, detect_race_conditions=False)

    def din(name, shape, dtype=F32):
        return nc.dram_tensor(name, list(shape), dtype, kind="ExternalInput").ap()

    xT = din("xT", (128, 8 * T))
    aw_dkv = din("aw_dkv", (128, 4096))
    aw_dq = din("aw_dq", (128, 4096))
    aw_dqn = din("aw_dqn", (128, 4096))
    aw_uq = din("aw_uq", (128, 4096))
    aw_uk = din("aw_uk", (128, 4096))
    aw_uv = din("aw_uv", (128, 4096))
    aw_o = din("aw_o", (128, 8192))
    bw_q = din("bw_q", (128, 8192))
    bw_k = din("bw_k", (128, 8192))
    bw_v = din("bw_v", (128, 8192))
    bw_o = din("bw_o", (128, 8192))
    bq_r = din("bq_r", (1, 1024))
    bk_r = din("bk_r", (1, 1024))
    bo_i = din("bo_i", (128, 8))
    bv_r = din("bv_r", (1, 1024))
    rel2_d = din("rel2", (128, 4095))
    f_w1 = din("f_w1", (32, 128, 1024))
    f_w2 = din("f_w2", (32, 128, 1024))
    f_h1 = din("f_h1", (32, 128, 4096))
    f_h2 = din("f_h2", (32, 128, 4096))
    f_w3 = din("f_w3", (8, 128, 4096))
    s1_i = din("s1_i", (128, 32))
    s2_i = din("s2_i", (128, 32))
    sh1_i = din("sh1_i", (128, 32))
    sh2_i = din("sh2_i", (128, 32))
    s3_i = din("s3_i", (128, 8))

    yO = nc.dram_tensor("y", [128, 8 * T], F32, kind="ExternalOutput").ap()
    parkA = nc.dram_tensor("parkA", [128, 8 * T], F32, kind="Internal").ap()
    parkB = nc.dram_tensor("parkB", [128, 8 * T], F32, kind="Internal").ap()
    osc_d = nc.dram_tensor("oscr", [128, 8 * T], BF16, kind="Internal").ap()
    ln3st = nc.dram_tensor("ln3st", [128, 8 * T], BF16, kind="Internal").ap()
    pexp = nc.dram_tensor("pexp", [H * 2048 * 4095], BF16, kind="Internal").ap()
    cach = {
        "w1": nc.dram_tensor("c_w1", [32, 128, 1024], BF16, kind="Internal").ap(),
        "w2": nc.dram_tensor("c_w2", [32, 128, 1024], BF16, kind="Internal").ap(),
        "h1": nc.dram_tensor("c_h1", [32, 128, 4096], BF16, kind="Internal").ap(),
        "h2": nc.dram_tensor("c_h2", [32, 128, 4096], BF16, kind="Internal").ap(),
        "w3": nc.dram_tensor("c_w3", [8, 128, 4096], BF16, kind="Internal").ap(),
    }

    es = ExitStack()

    def sbuf(name, shape, dtype, side="left"):
        return es.enter_context(nc.sbuf_tensor(name, list(shape), dtype, side=side))

    ones_b = sbuf("ones_b", (128, 512), BF16)
    ones_f = sbuf("ones_f", (128, 128), F32)
    epsT = sbuf("epsT", (128, 1), F32)
    s1s = sbuf("s1s", (128, 32), F32)
    s2s = sbuf("s2s", (128, 32), F32)
    sh1s = sbuf("sh1s", (128, 32), F32)
    sh2s = sbuf("sh2s", (128, 32), F32)
    s3s = sbuf("s3s", (128, 8), F32)
    bq_row_s = sbuf("bq_row_s", (1, 1024), BF16)
    bk_row_s = sbuf("bk_row_s", (1, 1024), BF16)
    bos = sbuf("bos", (128, 8), F32)
    bvs = sbuf("bvs", (1, 1024), BF16)
    V = sbuf("V", (128, 3072), F32)

    dW = Sem(es, nc, "dW")
    dXB = Sem(es, nc, "dXB")
    dPX = Sem(es, nc, "dPX")
    dX = Sem(es, nc, "dX")
    dK = Sem(es, nc, "dK")
    dO = Sem(es, nc, "dO")
    pe = Sem(es, nc, "pe")
    pPV = Sem(es, nc, "pPV")
    pW = Sem(es, nc, "pW")
    aG = Sem(es, nc, "aG")
    aEX = Sem(es, nc, "aEX")
    aEV = Sem(es, nc, "aEV")
    aFE = Sem(es, nc, "aFE")
    vG = Sem(es, nc, "vG")
    vRC = Sem(es, nc, "vRC")
    vNM = Sem(es, nc, "vNM")
    vEM = Sem(es, nc, "vEM")
    vFE = Sem(es, nc, "vFE")
    gMK = Sem(es, nc, "gMK")
    pLN = Sem(es, nc, "pLN")
    dCW = Sem(es, nc, "dCW")
    dOO = Sem(es, nc, "dOO")
    dXO = Sem(es, nc, "dXO")

    blk = es.enter_context(nc.Block())

    def p0_sync(sy):
        for dst, src in ((s1s, s1_i), (s2s, s2_i), (sh1s, sh1_i), (sh2s, sh2_i),
                         (s3s, s3_i), (bos, bo_i)):
            sy.dma_start(out=dst[:], in_=src).then_inc(dX.h, 16)
            dX.n += 16

    def p0_gp(g):
        for dst, src in ((bq_row_s, bq_r), (bk_row_s, bk_r), (bvs, bv_r)):
            g.dma_start(out=dst[:], in_=src).then_inc(dW.h, 16)
            dW.n += 16

    def p0_vec(v):
        v.memset(ones_b[:], 1.0)
        v.memset(ones_f[:], 1.0)
        vG.inc(v.memset(epsT[:], 1e-5))

    blk.sync(p0_sync)
    blk.gpsimd(p0_gp)
    blk.vector(p0_vec)
    tok_consts_v = vG.n
    tok_consts_w = dW.n
    tok_consts_d = dX.n

    ln_ctr = [0]

    def emit_ln(src_ap, dst, xb, sqb, tb2, gate=None):
        ln_ctr[0] += 1
        u = str(ln_ctr[0])
        ps_es = ExitStack()
        st_sum = ps_es.enter_context(nc.psum_tensor("ln_s" + u, [1, 1024], F32))
        st_sq = ps_es.enter_context(nc.psum_tensor("ln_q" + u, [1, 1024], F32))
        bcR = ps_es.enter_context(nc.psum_tensor("ln_r" + u, [128, 512], F32))
        bcM = ps_es.enter_context(nc.psum_tensor("ln_m" + u, [128, 512], F32))
        half_tok = [0]
        pLN_b = [pLN.n]
        prev_affine = [0]
        prev_stats_rd = [0]
        prev_stats = [0]
        for th in (0, 1):
            o = th * 1024
            xb_tok = []

            def w_gp(g, th=th, xb_tok=xb_tok):
                if gate is not None and th == 0:
                    gate(g)
                if th == 1:
                    g.wait_ge(vG.h, prev_affine[0])
                for c in range(8):
                    ins = g.dma_start(
                        out=xb[:, c * 1024:(c + 1) * 1024],
                        in_=src_ap[:, c * 2048 + th * 1024: c * 2048 + th * 1024 + 1024])
                    xb_tok.append(dXB.inc(ins, 16))
            blk.gpsimd(w_gp)
            sq_tok = []

            def w_act(sc, th=th, xb_tok=xb_tok, sq_tok=sq_tok):
                for c in range(8):
                    sc.wait_ge(dXB.h, xb_tok[c])
                    gi = th * 8 + c
                    if gi >= 2:
                        sc.wait_ge(pLN.h, pLN_b[0] + gi - 1)
                    ins = sc.activation(sqb[:, (c % 2) * 1024:(c % 2) * 1024 + 1024],
                                        xb[:, c * 1024:(c + 1) * 1024], AF.Square)
                    sq_tok.append(aG.inc(ins))
            blk.scalar(w_act)
            stat_tok = [0]

            def w_pe(t, th=th, xb_tok=xb_tok, sq_tok=sq_tok, stat_tok=stat_tok):
                if th == 1:
                    t.wait_ge(vG.h, prev_stats_rd[0])
                for c in range(8):
                    t.wait_ge(dXB.h, xb_tok[c])
                    t.wait_ge(aG.h, sq_tok[c])
                    for nb in (0, 1):
                        sl = slice(c * 1024 + nb * 512, c * 1024 + nb * 512 + 512)
                        sq_sl = slice((c % 2) * 1024 + nb * 512, (c % 2) * 1024 + nb * 512 + 512)
                        t.matmul(st_sum[0:1, nb * 512:nb * 512 + 512], ones_b[:, 0:1],
                                 xb[:, sl], start=(c == 0), stop=(c == 7))
                        ins = t.matmul(st_sq[0:1, nb * 512:nb * 512 + 512], ones_b[:, 0:1],
                                       sqb[:, sq_sl], start=(c == 0), stop=(c == 7))
                    pLN.inc(ins)
                stat_tok[0] = pLN.n
            blk.tensor(w_pe)
            prev_stats[0] = pe.n
            va_tok = [0]

            def w_vec(v, stat_tok=stat_tok, va_tok=va_tok):
                v.wait_ge(pLN.h, stat_tok[0])
                v.tensor_scalar(V[0:1, 0:1024], st_sum[0:1, :], 1.0 / D, None, ALU.mult)
                v.tensor_scalar(V[0:1, 1024:2048], st_sq[0:1, :], 1.0 / D, None, ALU.mult)
                v.tensor_mul(V[0:1, 2048:3072], V[0:1, 0:1024], V[0:1, 0:1024])
                ins = v.tensor_sub(V[0:1, 1024:2048], V[0:1, 1024:2048], V[0:1, 2048:3072])
                va_tok[0] = vG.inc(ins)
            blk.vector(w_vec)
            prev_stats_rd[0] = vG.n
            sd_tok = [0]

            def w_act2(sc, va_tok=va_tok, sd_tok=sd_tok):
                sc.wait_ge(vG.h, va_tok[0])
                ins = sc.activation(V[0:1, 2048:3072], V[0:1, 1024:2048], AF.Sqrt,
                                    bias=epsT[0:1, 0:1])
                sd_tok[0] = aG.inc(ins)
            blk.scalar(w_act2)
            vb_tok = [0]

            def w_vec2(v, sd_tok=sd_tok, vb_tok=vb_tok):
                v.wait_ge(aG.h, sd_tok[0])
                v.reciprocal(V[0:1, 1024:2048], V[0:1, 2048:3072])
                ins = v.tensor_mul(V[0:1, 2048:3072], V[0:1, 0:1024], V[0:1, 1024:2048])
                vb_tok[0] = vG.inc(ins)
            blk.vector(w_vec2)
            for lb in (0, 1):
                bc_tok = [0]

                def w_pe2(t, lb=lb, vb_tok=vb_tok, bc_tok=bc_tok):
                    t.wait_ge(vG.h, max(vb_tok[0], prev_affine[0]))
                    t.matmul(bcR[:, :], ones_f[0:1, :],
                             V[0:1, 1024 + lb * 512:1024 + lb * 512 + 512],
                             start=True, stop=True)
                    ins = t.matmul(bcM[:, :], ones_f[0:1, :],
                                   V[0:1, 2048 + lb * 512:2048 + lb * 512 + 512],
                                   start=True, stop=True)
                    bc_tok[0] = pe.inc(ins)
                blk.tensor(w_pe2)

                def w_vec3(v, lb=lb, th=th, bc_tok=bc_tok):
                    v.wait_ge(pe.h, bc_tok[0])
                    for c in range(8):
                        sl = slice(c * 1024 + lb * 512, c * 1024 + lb * 512 + 512)
                        v.tensor_tensor(tb2[:, (c % 2) * 512:(c % 2) * 512 + 512],
                                        xb[:, sl], bcR[:, :], ALU.mult)
                        ins = v.tensor_tensor(
                            dst[:, c * 2048 + th * 1024 + lb * 512:
                                c * 2048 + th * 1024 + lb * 512 + 512],
                            tb2[:, (c % 2) * 512:(c % 2) * 512 + 512], bcM[:, :],
                            ALU.subtract)
                        vG.inc(ins)
                blk.vector(w_vec3)
                prev_affine[0] = vG.n
            if th == 0:
                half_tok[0] = vG.n
        ps_es.close()
        return half_tok[0], vG.n

    blk.tensor(lambda t: t.wait_ge(vG.h, tok_consts_v))
    blk.scalar(lambda sc: (sc.wait_ge(vG.h, tok_consts_v), sc.wait_ge(dX.h, tok_consts_d)))
    blk.vector(lambda v: v.wait_ge(dX.h, tok_consts_d))

    prj_es = [None]

    def emit_proj(nk, nm, nn, nstep, lhsT_fn, rhs_fn, evac_fn, ps_a, ps_b,
                  dma_toks=(), bias_row=None):
        aFE_b = aFE.n
        tiles = [(mi, ni) for mi in range(nm) for ni in range(nn)]
        pe_tok = []

        def f_pe(t):
            for sem, tok in dma_toks:
                t.wait_ge(sem.h, tok)
            for g, (mi, ni) in enumerate(tiles):
                ps = (ps_a, ps_b)[g % 2]
                if aFE_b + g - 1 > 0:
                    t.wait_ge(aFE.h, aFE_b + g - 1)
                last = nk - 1 if bias_row is None else nk
                for kc in range(nk):
                    ins = t.matmul(ps[:, :], lhsT_fn(kc, mi), rhs_fn(kc, ni),
                                   start=(kc == 0), stop=(kc == last))
                if bias_row is not None:
                    ins = t.matmul(ps[:, :], bias_row[0:1, mi * 128:mi * 128 + 128],
                                   ones_b[0:1, 0:nstep], start=False, stop=True)
                pe_tok.append(pe.inc(ins))

        def f_act(sc):
            for g, (mi, ni) in enumerate(tiles):
                ps = (ps_a, ps_b)[g % 2]
                sc.wait_ge(pe.h, pe_tok[g])
                aFE.inc(evac_fn(sc, mi, ni, ps[:, :]))
        blk.tensor(f_pe)
        blk.scalar(f_act)
        return aFE.n

    a_es = ExitStack()

    def rsb(name, shape, dtype):
        return a_es.enter_context(nc.sbuf_tensor(name, list(shape), dtype, side="right"))

    def lsb(name, shape, dtype):
        return a_es.enter_context(nc.sbuf_tensor(name, list(shape), dtype, side="left"))

    ckv_i = lsb("ckv_i", (128, 8192), BF16)
    keT_i = lsb("keT_i", (128, 4096), BF16)
    veT_i = lsb("veT_i", (128, 4096), BF16)

    p2_es = ExitStack()
    q2_i = p2_es.enter_context(nc.sbuf_tensor("q2_i", [128, 8192], BF16, side="right"))
    ln1_es = ExitStack()
    ln1 = ln1_es.enter_context(nc.sbuf_tensor("ln1", [128, 16384], BF16, side="right"))
    ln_es = ExitStack()
    xb1 = ln_es.enter_context(nc.sbuf_tensor("xb1", [128, 8192], BF16, side="right"))
    sqb1 = ln_es.enter_context(nc.sbuf_tensor("sqb1", [128, 2048], BF16, side="right"))
    tb21 = ln_es.enter_context(nc.sbuf_tensor("tb21", [128, 1024], F32, side="right"))
    ln1_tok = emit_ln(xT, ln1, xb1, sqb1, tb21)[1]
    ln_es.close()

    p2ps = ExitStack()
    pr_a = p2ps.enter_context(nc.psum_tensor("pr_a", [128, 512], F32))
    pr_b = p2ps.enter_context(nc.psum_tensor("pr_b", [128, 512], F32))

    def load_w(g, buf_slices, srcs):
        toks = []
        for sl, src in zip(buf_slices, srcs):
            toks.append(dW.inc(g.dma_start(out=sl, in_=src), 16))
        return toks

    aw1_es = ExitStack()
    aw1 = aw1_es.enter_context(nc.sbuf_tensor("aw1", [128, 8192], BF16, side="right"))
    t_aw1 = []
    blk.gpsimd(lambda g: (g.wait_ge(vG.h, ln1_tok),
                          t_aw1.extend(load_w(
                              g, [aw1[:, 0:4096], aw1[:, 4096:8192]], [aw_dkv, aw_dq]))))

    def cp_evac(dst_fn):
        def f(sc, mi, ni, ps):
            return sc.copy(dst_fn(mi, ni), ps)
        return f

    emit_proj(8, 4, 4, 512,
              lambda kc, mi: aw1[:, kc * 512 + mi * 128: kc * 512 + mi * 128 + 128],
              lambda kc, ni: ln1[:, kc * 2048 + ni * 512: kc * 2048 + ni * 512 + 512],
              cp_evac(lambda mi, ni: ckv_i[:, mi * 2048 + ni * 512: mi * 2048 + ni * 512 + 512]),
              pr_a, pr_b, dma_toks=[(dW, t_aw1[0]), (vG, ln1_tok)])
    ckv_done = aFE.n
    emit_proj(8, 4, 4, 512,
              lambda kc, mi: aw1[:, 4096 + kc * 512 + mi * 128: 4096 + kc * 512 + mi * 128 + 128],
              lambda kc, ni: ln1[:, kc * 2048 + ni * 512: kc * 2048 + ni * 512 + 512],
              cp_evac(lambda mi, ni: q2_i[:, mi * 2048 + ni * 512: mi * 2048 + ni * 512 + 512]),
              pr_a, pr_b, dma_toks=[(dW, t_aw1[1]), (vG, ln1_tok)])
    q2_done = aFE.n
    aw1_free_tok = pe.n
    aw1_es.close()
    ln1_es.close()

    aw2_es = ExitStack()
    aw2 = aw2_es.enter_context(nc.sbuf_tensor("aw2", [128, 8192], BF16, side="right"))
    B_es = ExitStack()
    B_i = B_es.enter_context(nc.sbuf_tensor("B_i", [128, 8192], BF16, side="right"))
    t_aw2 = []
    blk.gpsimd(lambda g: (g.wait_ge(pe.h, aw1_free_tok),
                          t_aw2.extend(load_w(
                              g, [aw2[:, 0:4096], aw2[:, 4096:8192]], [aw_uq, aw_dqn]))))
    emit_proj(4, 8, 2, 512,
              lambda kc, mi: aw2[:, kc * 1024 + mi * 128: kc * 1024 + mi * 128 + 128],
              lambda kc, ni: aw2[:, 4096 + kc * 1024 + ni * 512: 4096 + kc * 1024 + ni * 512 + 512],
              cp_evac(lambda mi, ni: B_i[:, mi * 1024 + ni * 512: mi * 1024 + ni * 512 + 512]),
              pr_a, pr_b, dma_toks=[(dW, t_aw2[1])])
    B_done = aFE.n

    aw3_es = ExitStack()
    aw3 = aw3_es.enter_context(nc.sbuf_tensor("aw3", [128, 8192], BF16, side="right"))
    t_aw3 = []
    blk.gpsimd(lambda g: t_aw3.extend(load_w(
        g, [aw3[:, 0:4096], aw3[:, 4096:8192]], [aw_uk, aw_uv])))
    emit_proj(8, 4, 2, 512,
              lambda kc, mi: aw3[:, kc * 512 + mi * 128: kc * 512 + mi * 128 + 128],
              lambda kc, ni: B_i[:, kc * 1024 + ni * 512: kc * 1024 + ni * 512 + 512],
              cp_evac(lambda mi, ni: keT_i[:, mi * 1024 + ni * 512: mi * 1024 + ni * 512 + 512]),
              pr_a, pr_b, dma_toks=[(dW, t_aw3[0]), (aFE, B_done)])
    keT_done = aFE.n
    aw4_es = ExitStack()
    aw4 = aw4_es.enter_context(nc.sbuf_tensor("aw4", [128, 8192], BF16, side="right"))
    t_aw4 = []
    blk.gpsimd(lambda g: t_aw4.extend(load_w(g, [aw4[:, 0:8192]], [aw_o])))
    emit_proj(8, 4, 2, 512,
              lambda kc, mi: aw3[:, 4096 + kc * 512 + mi * 128: 4096 + kc * 512 + mi * 128 + 128],
              lambda kc, ni: aw4[:, kc * 1024 + ni * 512: kc * 1024 + ni * 512 + 512],
              cp_evac(lambda mi, ni: veT_i[:, mi * 1024 + ni * 512: mi * 1024 + ni * 512 + 512]),
              pr_a, pr_b, dma_toks=[(dW, t_aw4[0])])
    veT_done = aFE.n
    aw4_es.close()

    q_i = lsb("q_i", (128, 16384), BF16)
    emit_proj(4, 8, 4, 512,
              lambda kc, mi: aw2[:, kc * 1024 + mi * 128: kc * 1024 + mi * 128 + 128],
              lambda kc, ni: q2_i[:, kc * 2048 + ni * 512: kc * 2048 + ni * 512 + 512],
              cp_evac(lambda mi, ni: q_i[:, mi * 2048 + ni * 512: mi * 2048 + ni * 512 + 512]),
              pr_a, pr_b, dma_toks=[(aFE, q2_done)])
    qT_free = pe.n
    aw3_es.close()
    B_es.close()
    aw2_es.close()
    p2_es.close()

    keys_i = lsb("keys_i", (128, 16384), BF16)
    u_i = lsb("u_i", (128, 16640), BF16)
    emit_proj(4, 8, 4, 512,
              lambda kc, mi: keT_i[:, kc * 1024 + mi * 128: kc * 1024 + mi * 128 + 128],
              lambda kc, ni: ckv_i[:, kc * 2048 + ni * 512: kc * 2048 + ni * 512 + 512],
              cp_evac(lambda mi, ni: keys_i[:, mi * 2048 + ni * 512: mi * 2048 + ni * 512 + 512]),
              pr_a, pr_b, dma_toks=[(aFE, keT_done)])
    keys_done = aFE.n

    u_ones_tok = [0]
    pe_gate_u = pe.n
    blk.vector(lambda v: (v.wait_ge(pe.h, pe_gate_u),
                          u_ones_tok.__setitem__(0, vG.inc(v.memset(
                              u_i.ap().rearrange("p (kc h s) -> p kc h s",
                                                 kc=16, h=16)[:, :, :, 64:65], 1.0)))))
    p2ps.close()
    ups = ExitStack()
    u_ps_a = ups.enter_context(nc.psum_tensor("u_ps_a", [128, 1024], F32))
    u_ps_b = ups.enter_context(nc.psum_tensor("u_ps_b", [128, 1024], F32))
    u_pe_tok = []
    aFE_u_b = aFE.n

    def f_u_pe(t):
        t.wait_ge(aFE.h, max(veT_done, ckv_done, keys_done))
        for kc in range(16):
            ps = (u_ps_a, u_ps_b)[kc % 2]
            if kc >= 2:
                t.wait_ge(aFE.h, aFE_u_b + kc - 1)
            for lc in range(4):
                for nb in (0, 1):
                    ins = t.matmul(ps[:, nb * 512:nb * 512 + 512],
                                   ckv_i[:, lc * 2048 + kc * 128: lc * 2048 + kc * 128 + 128],
                                   veT_i[:, lc * 1024 + nb * 512: lc * 1024 + nb * 512 + 512],
                                   start=(lc == 0), stop=(lc == 3))
            u_pe_tok.append(pe.inc(ins))

    def f_u_act(sc):
        sc.wait_ge(vG.h, u_ones_tok[0])
        for kc in range(16):
            ps = (u_ps_a, u_ps_b)[kc % 2]
            sc.wait_ge(pe.h, u_pe_tok[kc])
            dst = u_i.ap().rearrange("p (kc h s) -> p kc h s", kc=16, h=16)[:, kc, :, 0:64]
            aFE.inc(sc.copy(dst, ps.ap().rearrange("p (h d) -> p h d", h=16)))
    blk.tensor(f_u_pe)
    blk.scalar(f_u_act)
    u_done = aFE.n
    ups.close()

    p3_es = ExitStack()

    def r3(name, shape, dtype):
        return p3_es.enter_context(nc.sbuf_tensor(name, list(shape), dtype, side="right"))

    E2 = r3("E2", (128, 2048), BF16)
    nrm = r3("nrm", (128, 2048), F32)
    o_ev = r3("o_ev", (128, 2048), F32)
    xr = r3("xr", (128, 4096), F32)
    xt = r3("xt", (128, 2048), F32)
    p3ps = ExitStack()
    qk_ps = p3ps.enter_context(nc.psum_tensor("qk_ps", [128, 2048], F32))
    pv_ps = p3ps.enter_context(nc.psum_tensor("pv_ps", [65, 1024], F32))
    bc_ps = p3ps.enter_context(nc.psum_tensor("bc_ps", [64, 1024], F32))

    tiles3 = []
    for h in range(H):
        for lbh in (0, 1):
            for rc in range(8 * (lbh + 1)):
                tiles3.append((h, lbh, rc))
    G3 = len(tiles3)
    masked3 = [(rc >= 8 * lbh) for (h, lbh, rc) in tiles3]
    nmask3 = np.cumsum(masked3).tolist()
    first3 = [g for g in range(G3) if tiles3[g][2] == 0]
    last3 = [g - 1 for g in first3[1:]] + [G3 - 1]
    idx_of = {}
    for i, g in enumerate(first3):
        for gg in range(g, last3[i] + 1):
            idx_of[gg] = i
    aEX_b, aEV_b, vRC_b, vNM_b, pPV_b, gMK_b, dO_b, dX_b = (
        aEX.n, aEV.n, vRC.n, vNM.n, pPV.n, gMK.n, dO.n, dX.n)

    def f3_sync(sy):
        sy.wait_ge(pe.h, qT_free)

        def store(idx):
            h, lbh = idx // 2, idx % 2
            hc, rb = h // 2, (h % 2) * 64
            sy.wait_ge(vNM.h, vNM_b + idx + 1)
            sy.dma_start(
                out=parkA[rb:rb + 64, hc * 2048 + lbh * 1024: hc * 2048 + lbh * 1024 + 1024],
                in_=xt[0:64, (idx % 2) * 1024:(idx % 2) * 1024 + 1024]).then_inc(dO.h, 16)
            dO.n += 16
        for idx in range(2 * H):
            h, lbh = idx // 2, idx % 2
            hc, rb = h // 2, (h % 2) * 64
            if idx >= 4:
                sy.wait_ge(vNM.h, vNM_b + idx - 3)
            sy.dma_start(
                out=xr[0:64, (h % 2) * 2048 + lbh * 1024:(h % 2) * 2048 + lbh * 1024 + 1024],
                in_=xT[rb:rb + 64, hc * 2048 + lbh * 1024: hc * 2048 + lbh * 1024 + 1024]
            ).then_inc(dX.h, 16)
            dX.n += 16
            if idx >= 2:
                store(idx - 2)
        store(2 * H - 2)
        store(2 * H - 1)
    blk.sync(f3_sync)

    qk_tok = [0] * G3
    bc_tok = [0] * (2 * H)

    def f3_pe(t):
        t.wait_ge(aFE.h, max(keys_done, u_done))
        t.wait_ge(vG.h, u_ones_tok[0])

        def emit_qk(g):
            h, lbh, rc = tiles3[g]
            hc, rb = h // 2, (h % 2) * 64
            if aEX_b + g - 1 > 0:
                t.wait_ge(aEX.h, aEX_b + g - 1)
            for cb in (0, 1):
                ins = t.matmul(
                    qk_ps[:, (g % 2) * 1024 + cb * 512:(g % 2) * 1024 + cb * 512 + 512],
                    keys_i[rb:rb + 64, hc * 2048 + rc * 128: hc * 2048 + rc * 128 + 128],
                    q_i[rb:rb + 64, hc * 2048 + lbh * 1024 + cb * 512:
                        hc * 2048 + lbh * 1024 + cb * 512 + 512],
                    start=True, stop=True)
            qk_tok[g] = pe.inc(ins)

        def emit_pv(g):
            h, lbh, rc = tiles3[g]
            idx = idx_of[g]
            t.wait_ge(aEX.h, aEX_b + g + 1)
            if masked3[g]:
                t.wait_ge(gMK.h, gMK_b + nmask3[g])
            if rc == 0 and idx > 0:
                t.wait_ge(aEV.h, aEV_b + idx)
            for cb in (0, 1):
                ins = t.matmul(
                    pv_ps[:, cb * 512:cb * 512 + 512],
                    u_i[:, rc * 1040 + h * 65: rc * 1040 + h * 65 + 65],
                    E2[:, (g % 2) * 1024 + cb * 512:(g % 2) * 1024 + cb * 512 + 512],
                    start=(rc == 0), stop=(rc == 8 * (lbh + 1) - 1))
            pPV.inc(ins)

        def emit_bc(idx):
            t.wait_ge(aEV.h, aEV_b + idx + 1)
            if idx > 0:
                t.wait_ge(vNM.h, vNM_b + idx)
            for cb in (0, 1):
                ins = t.matmul(
                    bc_ps[:, cb * 512:cb * 512 + 512], ones_f[64:65, 0:64],
                    nrm[64:65, (idx % 2) * 1024 + cb * 512:(idx % 2) * 1024 + cb * 512 + 512],
                    start=True, stop=True)
            bc_tok[idx] = pe.inc(ins)

        for g in range(G3):
            emit_qk(g)
            if g >= 1:
                emit_pv(g - 1)
            if tiles3[g][2] == 1 and idx_of[g] > 0:
                emit_bc(idx_of[g] - 1)
        emit_pv(G3 - 1)
        emit_bc(2 * H - 1)
    blk.tensor(f3_pe)

    ev_tok = [0] * (2 * H)

    def f3_act(sc):
        for g in range(G3):
            h, lbh, rc = tiles3[g]
            idx = idx_of[g]
            sc.wait_ge(pe.h, qk_tok[g])
            if pPV_b + g - 1 > 0:
                sc.wait_ge(pPV.h, pPV_b + g - 1)
            aEX.inc(sc.activation(E2[:, (g % 2) * 1024:(g % 2) * 1024 + 1024],
                                  qk_ps[:, (g % 2) * 1024:(g % 2) * 1024 + 1024],
                                  AF.Exp, scale=0.125))
            if g == last3[idx]:
                sc.wait_ge(pPV.h, pPV_b + g + 1)
                if idx >= 2:
                    sc.wait_ge(vNM.h, vNM_b + idx - 1)
                sc.copy(o_ev[0:64, (idx % 2) * 1024:(idx % 2) * 1024 + 1024],
                        pv_ps[0:64, :])
                sc.activation(nrm[64:65, (idx % 2) * 1024:(idx % 2) * 1024 + 1024],
                              pv_ps[64:65, :], AF.Ln)
                ins = sc.activation(nrm[64:65, (idx % 2) * 1024:(idx % 2) * 1024 + 1024],
                                    nrm[64:65, (idx % 2) * 1024:(idx % 2) * 1024 + 1024],
                                    AF.Exp, scale=-1.0)
                ev_tok[idx] = aEV.inc(ins)
    blk.scalar(f3_act)

    def f3_pool(g_eng):
        for g in range(G3):
            if not masked3[g]:
                continue
            h, lbh, rc = tiles3[g]
            g_eng.wait_ge(aEX.h, aEX_b + g + 1)
            ins = g_eng.affine_select(
                E2[:, (g % 2) * 1024:(g % 2) * 1024 + 1024],
                E2[:, (g % 2) * 1024:(g % 2) * 1024 + 1024],
                [[1, 1024]], ALU.is_ge, 0.0,
                base=lbh * 1024 - rc * 128, channel_multiplier=-1)
            gMK.inc(ins)
    blk.gpsimd(f3_pool)

    def f3_vec(v):
        for idx in range(2 * H):
            h, lbh = idx // 2, idx % 2
            v.wait_ge(pe.h, bc_tok[idx])
            v.wait_ge(dX.h, dX_b + 16 * (idx + 1))
            osl = o_ev[0:64, (idx % 2) * 1024:(idx % 2) * 1024 + 1024]
            v.tensor_tensor(osl, osl, bc_ps[0:64, :], ALU.mult)
            if dO_b + 16 * (idx - 1) > 0:
                v.wait_ge(dO.h, dO_b + 16 * (idx - 1))
            ins = v.tensor_tensor(
                xt[0:64, (idx % 2) * 1024:(idx % 2) * 1024 + 1024], osl,
                xr[0:64, (h % 2) * 2048 + lbh * 1024:(h % 2) * 2048 + lbh * 1024 + 1024],
                ALU.add)
            vNM.inc(ins)
    blk.vector(f3_vec)

    x1_tok = dO.n

    p3ps.close()
    p3_es.close()
    a_es.close()

    ffn_ctr = [0]

    def emit_ffn(ln_src, park_src, dst, first, x_gate, ln_tok):
        ffn_ctr[0] += 1
        fu = str(ffn_ctr[0])
        f_es = ExitStack()

        def rf(name, shape, dtype):
            return f_es.enter_context(nc.sbuf_tensor(name + fu, list(shape), dtype, side="right"))

        HA = rf("HA", (128, 32768), BF16)
        HB = rf("HB", (128, 32768), BF16)
        wb = rf("wb", (128, 8192), BF16)
        xold = rf("xold", (128, 1024), F32)
        xn = rf("xn", (128, 1024), F32)
        tAB = rf("tAB", (128, 1024), BF16)
        fps = ExitStack()
        psA = [fps.enter_context(nc.psum_tensor(f"fpsA{i}" + fu, [128, 512], F32)) for i in range(2)]
        psB = [fps.enter_context(nc.psum_tensor(f"fpsB{i}" + fu, [128, 512], F32)) for i in range(2)]

        stages_d = [
            ("A", 32, 8, (f_w1, f_w2), ("w1", "w2"), 1024),
            ("B", 32, 32, (f_h1,), ("h1",), 4096),
            ("C", 32, 32, (f_h2,), ("h2",), 4096),
            ("D", 8, 32, (f_w3,), ("w3",), 4096),
        ]
        prev_stage_done = [pe.n, aFE.n, vFE.n]

        for th in (0, 1):
            for kind, n_og, n_ic, srcs, cks, wcols in stages_d:
                nsrc = len(srcs)
                use_cache = (not first) or th == 1
                dma_sem = dX if use_cache else dW
                dma_b = dma_sem.n
                pW_b = pW.n
                aFE_b = aFE.n
                vFE_b = vFE.n
                dO_b = dO.n
                dX_b = dX.n
                dXO_b_d = dXO.n
                pe_prev, aFE_prev, vFE_prev = prev_stage_done

                pe_tok = {}

                def f_pe(t, kind=kind, n_og=n_og, n_ic=n_ic, nsrc=nsrc, wcols=wcols,
                         dma_sem=dma_sem, dma_b=dma_b, pW_b=pW_b, aFE_b=aFE_b,
                         vFE_b=vFE_b, dX_b=dX_b, th=th, pe_tok=pe_tok,
                         aFE_prev=aFE_prev, vFE_prev=vFE_prev):
                    if kind == "A":
                        t.wait_ge(vG.h, ln_tok[th])
                    if aFE_prev:
                        t.wait_ge(aFE.h, aFE_prev)
                    if vFE_prev:
                        t.wait_ge(vFE.h, vFE_prev)
                    for og in range(n_og):
                        t.wait_ge(dma_sem.h, dma_b + 16 * nsrc * (og + 1))
                        sl = wb[:, (og % 2) * 4096:(og % 2) * 4096 + 4096]
                        for tb in (0, 1):
                            s = og * 2 + tb
                            rhs_off = th * 1024 + tb * 512
                            if kind == "A":
                                if aFE_b + s - 1 > 0:
                                    t.wait_ge(aFE.h, aFE_b + s - 1)
                                if vFE_b + s - 1 > 0:
                                    t.wait_ge(vFE.h, vFE_b + s - 1)
                                for w_i in (0, 1):
                                    ps = (psA, psB)[w_i][s % 2]
                                    for ic in range(8):
                                        ins = t.matmul(
                                            ps[:, :],
                                            sl[:, w_i * 1024 + ic * 128: w_i * 1024 + ic * 128 + 128],
                                            ln_src[:, ic * 2048 + rhs_off: ic * 2048 + rhs_off + 512],
                                            start=(ic == 0), stop=(ic == 7))
                                    if w_i == 0:
                                        pe_tok[(og, tb, 0)] = pe.inc(ins)
                                if tb == 1:
                                    pW.inc(ins)
                                    pe_tok[(og, tb, 1)] = None
                                else:
                                    pe_tok[(og, tb, 1)] = pe.inc(ins)
                            else:
                                if aFE_b + s - 1 > 0 and kind != "D":
                                    t.wait_ge(aFE.h, aFE_b + s - 1)
                                if vFE_b + s - 1 > 0 and kind == "D":
                                    t.wait_ge(vFE.h, vFE_b + s - 1)
                                src_h = HA if kind in ("B", "D") else HB
                                ps = psA[s % 2]
                                for ic in range(n_ic):
                                    ins = t.matmul(
                                        ps[:, :], sl[:, ic * 128: ic * 128 + 128],
                                        src_h[:, ic * 1024 + tb * 512: ic * 1024 + tb * 512 + 512],
                                        start=(ic == 0), stop=(ic == n_ic - 1))
                                if tb == 1:
                                    pW.inc(ins)
                                    pe_tok[(og, tb)] = None
                                else:
                                    pe_tok[(og, tb)] = pe.inc(ins)
                blk.tensor(f_pe)
                pe_stage_done = pe.n

                if kind in ("A", "B", "C"):
                    def f_act(sc, kind=kind, n_og=n_og, th=th, pe_tok=pe_tok,
                              pW_b=pW_b, vFE_b=vFE_b):
                        for og in range(n_og):
                            for tb in (0, 1):
                                s = og * 2 + tb
                                if kind == "A":
                                    tok = pe_tok[(og, tb, 0)]
                                else:
                                    tok = pe_tok[(og, tb)]
                                if tok is None:
                                    sc.wait_ge(pW.h, pW_b + og + 1)
                                else:
                                    sc.wait_ge(pe.h, tok)
                                if kind == "A":
                                    if vFE_b + s - 1 > 0:
                                        sc.wait_ge(vFE.h, vFE_b + s - 1)
                                    ins = sc.activation(
                                        tAB[:, (s % 2) * 512:(s % 2) * 512 + 512],
                                        psA[s % 2][:, :], AF.Silu, scale=s1s[:, og:og + 1])
                                else:
                                    scl = (sh1s if kind == "B" else sh2s)[:, og:og + 1]
                                    dstH = HB if kind == "B" else HA
                                    ins = sc.activation(
                                        dstH[:, og * 1024 + tb * 512: og * 1024 + tb * 512 + 512],
                                        psA[s % 2][:, :], AF.Gelu, scale=scl)
                                aFE.inc(ins)
                    blk.scalar(f_act)
                aFE_stage_done = aFE.n

                if kind in ("A", "D"):
                    def f_vec(v, kind=kind, n_og=n_og, th=th, pe_tok=pe_tok,
                              pW_b=pW_b, dO_b=dO_b, dX_b=dX_b, aFE_b=aFE_b):
                        for og in range(n_og):
                            for tb in (0, 1):
                                s = og * 2 + tb
                                if kind == "A":
                                    tok = pe_tok[(og, tb, 1)]
                                    if tok is None:
                                        v.wait_ge(pW.h, pW_b + og + 1)
                                    else:
                                        v.wait_ge(pe.h, tok)
                                    v.wait_ge(aFE.h, aFE_b + s + 1)
                                    ins = v.scalar_tensor_tensor(
                                        HA[:, og * 1024 + tb * 512: og * 1024 + tb * 512 + 512],
                                        psB[s % 2][:, :], s2s[:, og:og + 1],
                                        tAB[:, (s % 2) * 512:(s % 2) * 512 + 512],
                                        ALU.mult, ALU.mult)
                                    vFE.inc(ins)
                                else:
                                    tok = pe_tok[(og, tb)]
                                    if tok is None:
                                        v.wait_ge(pW.h, pW_b + og + 1)
                                    else:
                                        v.wait_ge(pe.h, tok)
                                    v.wait_ge(dXO.h, dXO_b_d + 16 * (og + 1))
                                    if dO_b + 16 * (s - 1) > 0:
                                        v.wait_ge(dO.h, dO_b + 16 * (s - 1))
                                    ins = v.scalar_tensor_tensor(
                                        xn[:, (s % 2) * 512:(s % 2) * 512 + 512],
                                        psA[s % 2][:, :], s3s[:, og:og + 1],
                                        xold[:, tb * 512: tb * 512 + 512],
                                        ALU.mult, ALU.add)
                                    vFE.inc(ins)
                    blk.vector(f_vec)
                vFE_stage_done = vFE.n

                wwid = 1024 if nsrc == 2 else 4096
                vFE_stage_b_d = vFE_b

                def emit_store(sy, og, tb, th=th):
                    s = og * 2 + tb
                    sy.wait_ge(vFE.h, vFE_stage_b_d + s + 1)
                    sy.dma_start(
                        out=dst[:, og * 2048 + th * 1024 + tb * 512:
                                og * 2048 + th * 1024 + tb * 512 + 512],
                        in_=xn[:, (s % 2) * 512:(s % 2) * 512 + 512]
                    ).then_inc(dO.h, 16)
                    dO.n += 16

                if not use_cache:
                    dCW_pre = dCW.n

                    def f_gp(g, n_og=n_og, nsrc=nsrc, srcs=srcs, pW_b=pW_b,
                             dCW_pre=dCW_pre, wwid=wwid):
                        g.wait_ge(dO.h, x_gate)
                        for og in range(n_og):
                            g.wait_ge(pW.h, pW_b + max(0, og - 1))
                            if og >= 2:
                                g.wait_ge(dCW.h, dCW_pre + 16 * nsrc * (og - 1))
                            sl = wb[:, (og % 2) * 4096:(og % 2) * 4096 + 4096]
                            for w_i, srcw in enumerate(srcs):
                                ins = g.dma_start(
                                    out=sl[:, w_i * wwid: w_i * wwid + wwid],
                                    in_=srcw[og])
                                dW.inc(ins, 16)

                    def f_syncw(sy, n_og=n_og, nsrc=nsrc, cks=cks, dma_b=dma_b,
                                wwid=wwid, kind=kind):
                        for og in range(n_og):
                            sy.wait_ge(dW.h, dma_b + 16 * nsrc * (og + 1))
                            sl = wb[:, (og % 2) * 4096:(og % 2) * 4096 + 4096]
                            for w_i, ck in enumerate(cks):
                                ins = sy.dma_start(
                                    out=cach[ck][og],
                                    in_=sl[:, w_i * wwid: w_i * wwid + wwid])
                                dCW.inc(ins, 16)
                            if kind == "D" and og >= 1:
                                emit_store(sy, og - 1, 0)
                                emit_store(sy, og - 1, 1)
                        if kind == "D":
                            emit_store(sy, n_og - 1, 0)
                            emit_store(sy, n_og - 1, 1)
                    blk.gpsimd(f_gp)
                    blk.sync(f_syncw)
                else:
                    def f_syncr(sy, n_og=n_og, nsrc=nsrc, cks=cks, pW_b=pW_b,
                                wwid=wwid, kind=kind):
                        sy.wait_ge(dO.h, x_gate)
                        for og in range(n_og):
                            sy.wait_ge(pW.h, pW_b + max(0, og - 1))
                            sl = wb[:, (og % 2) * 4096:(og % 2) * 4096 + 4096]
                            for w_i, ck in enumerate(cks):
                                ins = sy.dma_start(
                                    out=sl[:, w_i * wwid: w_i * wwid + wwid],
                                    in_=cach[ck][og])
                                dX.inc(ins, 16)
                            if kind == "D" and og >= 1:
                                emit_store(sy, og - 1, 0)
                                emit_store(sy, og - 1, 1)
                        if kind == "D":
                            emit_store(sy, n_og - 1, 0)
                            emit_store(sy, n_og - 1, 1)
                    blk.sync(f_syncr)

                if kind == "D":
                    dXO_b = dXO.n

                    def f_actd(sc, n_og=n_og, th=th, x_gate=x_gate):
                        sc.wait_ge(dO.h, x_gate)
                        for og in range(n_og):
                            if og >= 1:
                                sc.wait_ge(vFE.h, vFE_stage_b_d + 2 * og)
                            sc.dma_start(
                                out=xold[:, 0:1024],
                                in_=park_src[:, og * 2048 + th * 1024: og * 2048 + th * 1024 + 1024]
                            ).then_inc(dXO.h, 16)
                            dXO.n += 16
                    blk.scalar(f_actd)

                prev_stage_done = [pe_stage_done, aFE_stage_done, vFE_stage_done]
        fps.close()
        f_es.close()
        return dO.n

    if stages >= 2:
        f1_es = ExitStack()
        ln2 = f1_es.enter_context(nc.sbuf_tensor("ln2", [128, 16384], BF16, side="right"))
        ln2s = ExitStack()
        xb2 = ln2s.enter_context(nc.sbuf_tensor("xb2", [128, 8192], BF16, side="right"))
        sqb2 = ln2s.enter_context(nc.sbuf_tensor("sqb2", [128, 2048], BF16, side="right"))
        tb22 = ln2s.enter_context(nc.sbuf_tensor("tb22", [128, 1024], F32, side="right"))
        ln2_tok = emit_ln(parkA, ln2, xb2, sqb2, tb22,
                          gate=lambda g: g.wait_ge(dO.h, x1_tok))
        ln2s.close()
        x2_tok = emit_ffn(ln2, parkA, parkB, True, x1_tok, ln2_tok)
        f1_es.close()

    if stages >= 3:
        b_es = ExitStack()

        def rb_(name, shape, dtype):
            return b_es.enter_context(nc.sbuf_tensor(name, list(shape), dtype, side="right"))

        qb_i = rb_("qb_i", (128, 16384), BF16)
        kb_i = rb_("kb_i", (128, 16384), BF16)
        v_i = rb_("v_i", (128, 16640), BF16)
        rel2_s = rb_("rel2_s", (128, 4095), BF16)
        bwk_es = ExitStack()
        bwk = bwk_es.enter_context(nc.sbuf_tensor("bwk", [128, 8192], BF16, side="right"))
        ln3_es = ExitStack()
        ln3 = ln3_es.enter_context(nc.sbuf_tensor("ln3", [128, 16384], BF16, side="right"))
        ln3s = ExitStack()
        xb3 = ln3s.enter_context(nc.sbuf_tensor("xb3", [128, 8192], BF16, side="right"))
        sqb3 = ln3s.enter_context(nc.sbuf_tensor("sqb3", [128, 2048], BF16, side="right"))
        tb23 = ln3s.enter_context(nc.sbuf_tensor("tb23", [128, 1024], F32, side="right"))
        dcw_t, dxo_t = dCW.n, dXO.n
        ln3_done_t = emit_ln(parkB, ln3, xb3, sqb3, tb23,
                           gate=lambda g: (g.wait_ge(dO.h, x2_tok),
                                           g.wait_ge(dCW.h, dcw_t),
                                           g.wait_ge(dXO.h, dxo_t)))
        ln3_done = ln3_done_t[1]
        ln3_stats_pe = pe.n
        ln3s.close()
        vg_ln3 = vG.n

        def f_ln3st(sy):
            sy.wait_ge(vG.h, vg_ln3)
            sy.dma_start(out=ln3st, in_=ln3[:, :]).then_inc(dCW.h, 16)
            dCW.n += 16
        blk.sync(f_ln3st)
        ln3st_tok = dCW.n

        t_rel = []
        blk.gpsimd(lambda g: (g.wait_ge(vG.h, ln3_done), g.wait_ge(pe.h, ln3_stats_pe),
                              t_rel.append(dW.inc(
                                  g.dma_start(out=rel2_s[:], in_=rel2_d), 16))))

        bps = ExitStack()
        pb_a = bps.enter_context(nc.psum_tensor("pb_a", [128, 512], F32))
        pb_b = bps.enter_context(nc.psum_tensor("pb_b", [128, 512], F32))
        bw_es = ExitStack()
        bwb = bw_es.enter_context(nc.sbuf_tensor("bwb", [128, 8192], BF16, side="right"))
        t_bw = []
        blk.gpsimd(lambda g: (g.wait_ge(vG.h, ln3_done), g.wait_ge(pe.h, ln3_stats_pe),
                              t_bw.extend(load_w(
                                  g, [bwb[:, 0:8192], bwk[:, 0:8192]], [bw_q, bw_k]))))

        ln3_tok = vG.n
        emit_proj(8, 8, 4, 512,
                  lambda kc, mi: bwb[:, kc * 1024 + mi * 128: kc * 1024 + mi * 128 + 128],
                  lambda kc, ni: ln3[:, kc * 2048 + ni * 512: kc * 2048 + ni * 512 + 512],
                  cp_evac(lambda mi, ni: qb_i[:, mi * 2048 + ni * 512: mi * 2048 + ni * 512 + 512]),
                  pb_a, pb_b, dma_toks=[(dW, t_bw[0]), (vG, ln3_tok)],
                  bias_row=bq_row_s)
        qk_proj_done = aFE.n
        qb_pe = pe.n

        t_bwv = []
        blk.gpsimd(lambda g: (g.wait_ge(pe.h, qb_pe),
                              t_bwv.extend(load_w(g, [bwb[:, 0:8192]], [bw_v]))))
        v_ones_tok = [0]
        pe_gate_v = pe.n
        blk.vector(lambda v: (v.wait_ge(pe.h, pe_gate_v),
                              v_ones_tok.__setitem__(0, vG.inc(v.memset(
                                  v_i.ap().rearrange("p (kc h s) -> p kc h s",
                                                     kc=16, h=16)[:, :, :, 64:65], 1.0)))))
        v_pe_tok = []
        aFE_v_b = aFE.n

        def f_v_pe(t):
            t.wait_ge(dW.h, t_bwv[0])
            for kc in range(16):
                for nb in (0, 1):
                    kt = kc * 2 + nb
                    ps = (pb_a, pb_b)[kt % 2]
                    if aFE_v_b + kt - 1 > 0:
                        t.wait_ge(aFE.h, aFE_v_b + kt - 1)
                    for ic in range(8):
                        t.matmul(ps[:, :],
                                 ln3[:, ic * 2048 + kc * 128: ic * 2048 + kc * 128 + 128],
                                 bwb[:, ic * 1024 + nb * 512: ic * 1024 + nb * 512 + 512],
                                 start=(ic == 0), stop=False)
                    ins = t.matmul(ps[:, :], ones_b[0:1, 0:128],
                                   bvs[0:1, nb * 512:nb * 512 + 512],
                                   start=False, stop=True)
                    v_pe_tok.append(pe.inc(ins))

        def f_v_act(sc):
            sc.wait_ge(vG.h, v_ones_tok[0])
            for kc in range(16):
                for nb in (0, 1):
                    kt = kc * 2 + nb
                    ps = (pb_a, pb_b)[kt % 2]
                    sc.wait_ge(pe.h, v_pe_tok[kt])
                    dst = v_i.ap().rearrange("p (kc h s) -> p kc h s",
                                             kc=16, h=16)[:, kc, nb * 8:nb * 8 + 8, 0:64]
                    aFE.inc(sc.copy(dst, ps.ap().rearrange("p (h d) -> p h d", h=8)))
        blk.tensor(f_v_pe)
        blk.scalar(f_v_act)
        v_proj_done = aFE.n
        bw_es.close()
        ln3_es.close()
        if DEBUG_DUMP:
            dbgq = nc.dram_tensor("dbgq", [128, 16384], BF16, kind="Internal").ap()
            dbgl = nc.dram_tensor("dbgl", [128, 16384], BF16, kind="Internal").ap()

            def f_dbg(sy):
                sy.wait_ge(aFE.h, qk_proj_done)
                sy.dma_start(out=dbgq, in_=qb_i[:, :]).then_inc(dO.h, 16)
                sy.dma_start(out=dbgl, in_=ln3[:, :]).then_inc(dO.h, 16)
                dO.n += 32
            blk.sync(f_dbg)

        p2b_es = ExitStack()
        er = p2b_es.enter_context(nc.sbuf_tensor("er", [128, 6144], BF16, side="right"))
        t1b = p2b_es.enter_context(nc.sbuf_tensor("t1b", [128, 2048], BF16, side="right"))
        Eb = p2b_es.enter_context(nc.sbuf_tensor("Eb", [128, 2048], BF16, side="right"))
        nrm2 = p2b_es.enter_context(nc.sbuf_tensor("nrm2", [128, 2048], F32, side="right"))
        oev2 = p2b_es.enter_context(nc.sbuf_tensor("oev2", [128, 2048], F32, side="right"))
        o_t = p2b_es.enter_context(nc.sbuf_tensor("o_t", [128, 2048], BF16, side="right"))
        p1_es = ExitStack()
        Pst = p1_es.enter_context(nc.sbuf_tensor("Pst", [128, 8704], BF16, side="right"))
        lnst = p1_es.enter_context(nc.sbuf_tensor("lnst", [128, 4096], BF16, side="right"))
        p1ps = ExitStack()
        PA = p1ps.enter_context(nc.psum_tensor("PA", [128, 1088], F32))
        PB = p1ps.enter_context(nc.psum_tensor("PB", [128, 1088], F32))
        aEX_b1 = aEX.n
        dPX_b1 = dPX.n

        p1_items = [(hp, lc, half) for hp in range(8) for lc in range(16) for half in (0, 1)]
        proj_tiles = [("kb", mi, ni) for ni in range(4) for mi in range(8)]
        sched = []
        pi = 0
        for i, it in enumerate(p1_items):
            sched.append(("p1",) + it)
            if i % 8 == 7 and pi < len(proj_tiles):
                sched.append(proj_tiles[pi])
                pi += 1
        while pi < len(proj_tiles):
            sched.append(proj_tiles[pi])
            pi += 1

        p1_pe_tok = {}
        pt_pe_tok = {}
        dX_st_b = dX.n

        def f_lnst(sy):
            sy.wait_ge(dCW.h, ln3st_tok)
            for ni in range(4):
                if ni >= 1:
                    sy.wait_ge(aFE.h, v_proj_done + 8 * ni)
                sy.dma_start(
                    out=lnst[:, :],
                    in_=AP(ln3st.tensor, ni * 512, [[16384, 128], [2048, 8], [1, 512]])
                ).then_inc(dX.h, 16)
                dX.n += 16
        blk.sync(f_lnst)

        def f1_pe(t):
            t.wait_ge(aFE.h, v_proj_done)
            t.wait_ge(dCW.h, ln3st_tok)
            t.wait_ge(dW.h, t_rel[0])
            pt = 0
            for item in sched:
                if item[0] == "p1":
                    _, hp, lc, half = item
                    s = (hp * 16 + lc) * 2 + half
                    w = 1088 if half == 0 else 1087
                    s0 = 1920 - 128 * lc + half * 1088
                    if s > 0:
                        t.wait_ge(aEX.h, aEX_b1 + 2 * s)
                    for rbv, psx in ((0, PA), (64, PB)):
                        off = 0
                        for n in (512, 512, w - 1024):
                            ins = t.matmul(
                                psx[:, off:off + n],
                                qb_i[rbv:rbv + 64, hp * 2048 + lc * 128: hp * 2048 + lc * 128 + 128],
                                rel2_s[rbv:rbv + 64, s0 + off: s0 + off + n],
                                start=True, stop=True)
                            off += n
                    p1_pe_tok[s] = pe.inc(ins)
                else:
                    _, mi, ni = item
                    ps = (pb_a, pb_b)[pt % 2]
                    if v_proj_done + pt - 1 > 0:
                        t.wait_ge(aFE.h, v_proj_done + pt - 1)
                    t.wait_ge(dX.h, dX_st_b + 16 * (ni + 1))
                    for kc in range(8):
                        t.matmul(ps[:, :],
                                 bwk[:, kc * 1024 + mi * 128: kc * 1024 + mi * 128 + 128],
                                 lnst[:, kc * 512: kc * 512 + 512],
                                 start=(kc == 0), stop=False)
                    ins = t.matmul(ps[:, :], bk_row_s[0:1, mi * 128:mi * 128 + 128],
                                   ones_b[0:1, 0:512], start=False, stop=True)
                    pt_pe_tok[pt] = pe.inc(ins)
                    pt += 1
        blk.tensor(f1_pe)

        def f1_act(sc):
            pt = 0
            for item in sched:
                if item[0] == "p1":
                    _, hp, lc, half = item
                    s = (hp * 16 + lc) * 2 + half
                    w = 1088 if half == 0 else 1087
                    sc.wait_ge(pe.h, p1_pe_tok[s])
                    if hp * 16 + lc >= 2 and half == 0:
                        sc.wait_ge(dPX.h, dPX_b1 + 32 * (hp * 16 + lc - 1))
                    for hh, psx in ((0, PA), (1, PB)):
                        ins = sc.activation(
                            Pst[:, hh * 4352 + (lc % 2) * 2176 + half * 1088:
                                hh * 4352 + (lc % 2) * 2176 + half * 1088 + w],
                            psx[:, 0:w], AF.Exp)
                        aEX.inc(ins)
                else:
                    _, mi, ni = item
                    ps = (pb_a, pb_b)[pt % 2]
                    sc.wait_ge(pe.h, pt_pe_tok[pt])
                    ins = sc.copy(kb_i[:, mi * 2048 + ni * 512: mi * 2048 + ni * 512 + 512],
                                  ps[:, :])
                    aFE.inc(ins)
                    pt += 1
        blk.scalar(f1_act)
        v_done = aFE.n

        def f1_gp(g):
            for hp in range(8):
                for lc in range(16):
                    s_last = aEX_b1 + ((hp * 16 + lc) * 2 + 1) * 2 + 2
                    g.wait_ge(aEX.h, s_last)
                    for hh in (0, 1):
                        h = hp * 2 + hh
                        ins = g.dma_start(
                            out=AP(pexp.tensor,
                                   h * 2048 * 4095 + (lc * 128) * 4095 + (1920 - 128 * lc),
                                   [[4095, 128], [1, 2175]]),
                            in_=Pst[:, hh * 4352 + (lc % 2) * 2176:
                                    hh * 4352 + (lc % 2) * 2176 + 2175])
                        dPX.inc(ins, 16)
        blk.gpsimd(f1_gp)
        pexp_all = dPX.n
        p1ps.close()
        p1_es.close()
        bps.close()
        p2bps = ExitStack()
        qk2 = p2bps.enter_context(nc.psum_tensor("qk2", [128, 2048], F32))
        pv2 = [p2bps.enter_context(nc.psum_tensor(f"pv2_{i}", [65, 1024], F32)) for i in (0, 1)]

        aEX_b2, aEV_b2, vRC_b2, vNM_b2, vEM_b2, pPV_b2, dK_b2, dOO_b2 = (
            aEX.n, aEV.n, vRC.n, vNM.n, vEM.n, pPV.n, dK.n, dOO.n)

        def f2_syncr(sy):

            def store_o(idx):
                h2, lbh2 = idx // 2, idx % 2
                hc2, rbv2 = h2 // 2, (h2 % 2) * 64
                sy.wait_ge(vNM.h, vNM_b2 + idx + 1)
                sy.dma_start(
                    out=osc_d[rbv2:rbv2 + 64,
                              hc2 * 2048 + lbh2 * 1024: hc2 * 2048 + lbh2 * 1024 + 1024],
                    in_=o_t[0:64, (idx % 2) * 1024:(idx % 2) * 1024 + 1024]
                ).then_inc(dOO.h, 16)
                dOO.n += 16
            for h in range(H):
                sy.wait_ge(dPX.h, dPX_b1 + 512 * (h // 2 + 1))
                for rc in range(16):
                    R = h * 16 + rc
                    e_guard = vEM_b2 + 2 * (R - 3) + 2
                    if e_guard > vEM_b2:
                        sy.wait_ge(vEM.h, e_guard)
                    src = AP(pexp.tensor, h * 2048 * 4095 + 2047 + 128 * rc,
                             [[4094, 2048], [1, 128]])
                    sy.dma_start(out=er[:, (R % 3) * 2048:(R % 3) * 2048 + 2048],
                                 in_=src, transpose=True).then_inc(dK.h, 16)
                    dK.n += 16
                if h >= 1:
                    store_o(2 * (h - 1))
                    store_o(2 * (h - 1) + 1)
            store_o(2 * H - 2)
            store_o(2 * H - 1)
        blk.sync(f2_syncr)

        qk2_tok = {}
        bc2_tok = {}
        units = [(h, rc, lbh) for h in range(H) for rc in range(16) for lbh in (0, 1)]
        NU = len(units)

        def last_j(h, lbh):
            return h * 32 + 30 + lbh

        def f2_pe(t):
            t.wait_ge(aFE.h, v_done)
            t.wait_ge(aEX.h, aEX_b2)
            t.wait_ge(vG.h, v_ones_tok[0])

            def emit_qk(j):
                h, rc, lbh = units[j]
                hc, rbv = h // 2, (h % 2) * 64
                if j > 0:
                    t.wait_ge(aEX.h, aEX_b2 + j - 1)
                for cb in (0, 1):
                    ins = t.matmul(
                        qk2[:, (j % 2) * 1024 + cb * 512:(j % 2) * 1024 + cb * 512 + 512],
                        kb_i[rbv:rbv + 64, hc * 2048 + rc * 128: hc * 2048 + rc * 128 + 128],
                        qb_i[rbv:rbv + 64, hc * 2048 + lbh * 1024 + cb * 512:
                             hc * 2048 + lbh * 1024 + cb * 512 + 512],
                        start=True, stop=True)
                qk2_tok[j] = pe.inc(ins)

            def emit_pv(j):
                h, rc, lbh = units[j]
                idx = 2 * h + lbh
                t.wait_ge(vEM.h, vEM_b2 + j + 1)
                if rc == 0 and h > 0:
                    t.wait_ge(aEV.h, aEV_b2 + 2 * (h - 1) + lbh + 1)
                    t.wait_ge(vNM.h, vNM_b2 + 2 * (h - 1) + lbh + 1)
                for cb in (0, 1):
                    ins = t.matmul(
                        pv2[lbh][:, cb * 512:cb * 512 + 512],
                        v_i[:, rc * 1040 + h * 65: rc * 1040 + h * 65 + 65],
                        Eb[:, (j % 2) * 1024 + cb * 512:(j % 2) * 1024 + cb * 512 + 512],
                        start=(rc == 0 and cb in (0, 1)), stop=(rc == 15))
                pPV.inc(ins)
                if rc == 15:
                    t.wait_ge(aEV.h, aEV_b2 + idx + 1)
                    for cb in (0, 1):
                        ins = t.matmul(
                            pv2[lbh][0:64, cb * 512:cb * 512 + 512], ones_f[64:65, 0:64],
                            nrm2[64:65, (idx % 2) * 1024 + cb * 512:
                                 (idx % 2) * 1024 + cb * 512 + 512],
                            start=True, stop=True)
                    bc2_tok[idx] = pe.inc(ins)

            for j in range(NU):
                emit_qk(j)
                if j >= 2:
                    emit_pv(j - 2)
            emit_pv(NU - 2)
            emit_pv(NU - 1)
        blk.tensor(f2_pe)

        def f2_act(sc):
            for j in range(NU):
                h, rc, lbh = units[j]
                sc.wait_ge(pe.h, qk2_tok[j])
                if j >= 2:
                    sc.wait_ge(vEM.h, vEM_b2 + j - 1)
                aEX.inc(sc.activation(t1b[:, (j % 2) * 1024:(j % 2) * 1024 + 1024],
                                      qk2[:, (j % 2) * 1024:(j % 2) * 1024 + 1024],
                                      AF.Exp, scale=0.125))
                if rc == 15:
                    idx = 2 * h + lbh
                    sc.wait_ge(pPV.h, pPV_b2 + last_j(h, lbh) + 1)
                    if idx >= 2:
                        sc.wait_ge(vNM.h, vNM_b2 + idx - 1)
                    sc.copy(oev2[0:64, (idx % 2) * 1024:(idx % 2) * 1024 + 1024],
                            pv2[lbh][0:64, :])
                    sc.activation(nrm2[64:65, (idx % 2) * 1024:(idx % 2) * 1024 + 1024],
                                  pv2[lbh][64:65, :], AF.Ln)
                    ins = sc.activation(nrm2[64:65, (idx % 2) * 1024:(idx % 2) * 1024 + 1024],
                                        nrm2[64:65, (idx % 2) * 1024:(idx % 2) * 1024 + 1024],
                                        AF.Exp, scale=-1.0)
                    aEV.inc(ins)
        blk.scalar(f2_act)

        def f2_vec(v):
            for j in range(NU):
                h, rc, lbh = units[j]
                v.wait_ge(aEX.h, aEX_b2 + j + 1)
                v.wait_ge(dK.h, dK_b2 + 16 * (h * 16 + rc + 1))
                if j >= 2:
                    v.wait_ge(pPV.h, pPV_b2 + j - 1)
                R = h * 16 + rc
                ins = v.tensor_tensor(
                    Eb[:, (j % 2) * 1024:(j % 2) * 1024 + 1024],
                    t1b[:, (j % 2) * 1024:(j % 2) * 1024 + 1024],
                    er[:, (R % 3) * 2048 + lbh * 1024:(R % 3) * 2048 + lbh * 1024 + 1024],
                    ALU.mult)
                vEM.inc(ins)
                if rc == 15:
                    idx = 2 * h + lbh
                    v.wait_ge(pe.h, bc2_tok[idx])
                    if dOO_b2 + 16 * (idx - 1) > dOO_b2:
                        v.wait_ge(dOO.h, dOO_b2 + 16 * (idx - 1))
                    ins = v.tensor_tensor(
                        o_t[0:64, (idx % 2) * 1024:(idx % 2) * 1024 + 1024],
                        oev2[0:64, (idx % 2) * 1024:(idx % 2) * 1024 + 1024],
                        pv2[lbh][0:64, 0:1024], ALU.mult)
                    vNM.inc(ins)
        blk.vector(f2_vec)

        oscr_tok = dOO.n
        p2bps.close()
        p2b_es.close()
        bwk_es.close()

        wo_es = ExitStack()
        o_i = wo_es.enter_context(nc.sbuf_tensor("o_i", [128, 16384], BF16, side="right"))
        wo_s = wo_es.enter_context(nc.sbuf_tensor("wo_s", [128, 8192], BF16, side="right"))
        xr2 = wo_es.enter_context(nc.sbuf_tensor("xr2", [128, 4096], F32, side="right"))
        xt2 = wo_es.enter_context(nc.sbuf_tensor("xt2", [128, 1024], F32, side="right"))
        wops = ExitStack()
        wo_a = wops.enter_context(nc.psum_tensor("wo_a", [128, 512], F32))
        wo_b = wops.enter_context(nc.psum_tensor("wo_b", [128, 512], F32))
        t_wo = []
        blk.gpsimd(lambda g: (g.wait_ge(dOO.h, oscr_tok),
                              t_wo.extend(load_w(g, [wo_s[:, :]], [bw_o]))))
        oi_tok = [0]

        def fw_sync_oi(sy):
            sy.wait_ge(dOO.h, oscr_tok)
            oi_tok[0] = dX.inc(sy.dma_start(out=o_i[:, :], in_=osc_d[:, :]), 16)
        blk.sync(fw_sync_oi)

        wo_pe_tok = {}
        vFE_wo_b = vFE.n
        dX_wo_b = dX.n
        dO_wo_b = dO.n

        def fw_pe(t):
            t.wait_ge(dW.h, t_wo[0])
            t.wait_ge(dX.h, oi_tok[0])
            for oc in range(8):
                for tb in range(4):
                    s = oc * 4 + tb
                    if vFE_wo_b + s - 1 > vFE_wo_b - 1 and s >= 2:
                        t.wait_ge(vFE.h, vFE_wo_b + s - 1)
                    ps = (wo_a, wo_b)[s % 2]
                    for ic in range(8):
                        ins = t.matmul(
                            ps[:, :], wo_s[:, ic * 1024 + oc * 128: ic * 1024 + oc * 128 + 128],
                            o_i[:, ic * 2048 + tb * 512: ic * 2048 + tb * 512 + 512],
                            start=(ic == 0), stop=(ic == 7))
                    wo_pe_tok[s] = pe.inc(ins)
        blk.tensor(fw_pe)

        def fw_sync_x(sy):
            sy.wait_ge(dOO.h, oscr_tok)

            def store_x(oc2):
                for tb2 in range(4):
                    s2 = oc2 * 4 + tb2
                    sy.wait_ge(vFE.h, vFE_wo_b + s2 + 1)
                    sy.dma_start(out=parkA[:, oc2 * 2048 + tb2 * 512: oc2 * 2048 + tb2 * 512 + 512],
                                 in_=xt2[:, (s2 % 2) * 512:(s2 % 2) * 512 + 512]
                                 ).then_inc(dO.h, 16)
                    dO.n += 16
            for oc in range(8):
                if oc >= 2:
                    sy.wait_ge(vFE.h, vFE_wo_b + 4 * (oc - 1))
                sy.dma_start(out=xr2[:, (oc % 2) * 2048:(oc % 2) * 2048 + 2048],
                             in_=parkB[:, oc * 2048: oc * 2048 + 2048]).then_inc(dX.h, 16)
                dX.n += 16
                if oc >= 1:
                    store_x(oc - 1)
            store_x(7)
        blk.sync(fw_sync_x)

        def fw_vec(v):
            for oc in range(8):
                v.wait_ge(dX.h, dX_wo_b + 16 * (oc + 1))
                for tb in range(4):
                    s = oc * 4 + tb
                    v.wait_ge(pe.h, wo_pe_tok[s])
                    if dO_wo_b + 16 * (s - 1) > dO_wo_b:
                        v.wait_ge(dO.h, dO_wo_b + 16 * (s - 1))
                    ps = (wo_a, wo_b)[s % 2]
                    ins = v.scalar_tensor_tensor(
                        xt2[:, (s % 2) * 512:(s % 2) * 512 + 512], ps[:, :],
                        bos[:, oc:oc + 1],
                        xr2[:, (oc % 2) * 2048 + tb * 512:(oc % 2) * 2048 + tb * 512 + 512],
                        ALU.add, ALU.add)
                    vFE.inc(ins)
        blk.vector(fw_vec)

        x3_tok = dO.n
        wops.close()
        wo_es.close()
        b_es.close()

    if stages >= 4:
        f2_es = ExitStack()
        ln4 = f2_es.enter_context(nc.sbuf_tensor("ln4", [128, 16384], BF16, side="right"))
        ln4s = ExitStack()
        xb4 = ln4s.enter_context(nc.sbuf_tensor("xb4", [128, 8192], BF16, side="right"))
        sqb4 = ln4s.enter_context(nc.sbuf_tensor("sqb4", [128, 2048], BF16, side="right"))
        tb24 = ln4s.enter_context(nc.sbuf_tensor("tb24", [128, 1024], F32, side="right"))
        dcw_t4, dxo_t4, doo_t4 = dCW.n, dXO.n, dOO.n
        ln4_tok = emit_ln(parkA, ln4, xb4, sqb4, tb24,
                          gate=lambda g: (g.wait_ge(dO.h, x3_tok),
                                          g.wait_ge(dCW.h, dcw_t4),
                                          g.wait_ge(dXO.h, dxo_t4),
                                          g.wait_ge(dOO.h, doo_t4)))
        ln4s.close()
        emit_ffn(ln4, parkA, yO, False, x3_tok, ln4_tok)
        f2_es.close()

    if stages < 4:
        src_park = parkA if stages in (1, 3) else parkB

        def f_tail(sy):
            sy.wait_ge(dO.h, dO.n)
            sy.dma_start(out=yO, in_=src_park).then_inc(dO.h, 16)
            dO.n += 16
        blk.sync(f_tail)

    fins = [(s, s.n) for s in (dW, dXB, dPX, dX, dK, dO, dCW, dOO, dXO)]

    def f_fin(sy):
        for s, n in fins:
            if n:
                sy.wait_ge(s.h, n)
    blk.sync(f_fin)

    es.close()
    nc.finalize()
    return nc


_BUILT = {}


def _prep_inputs(inputs):
    f32 = np.float32
    g = {k: np.asarray(v, dtype=f32) for k, v in inputs.items()}
    d = {}
    d["aw_dkv"] = img128(g["L0_attn_a_W_dkv"].T.copy())
    d["aw_dq"] = img128(g["L0_attn_a_W_dq"].T.copy())
    d["aw_dqn"] = img128(g["L0_attn_a_W_dq"].copy())
    d["aw_uq"] = img128(g["L0_attn_a_W_uq"].T.copy())
    d["aw_uk"] = img128(g["L0_attn_a_W_uk"].copy())
    d["aw_uv"] = img128(g["L0_attn_a_W_uv"].copy())
    d["aw_o"] = img128(g["L0_attn_a_W_o"].T.copy())
    d["bw_q"] = img128(g["L0_attn_b_Wq"].T.copy())
    d["bw_k"] = img128(g["L0_attn_b_Wk"].T.copy())
    d["bw_v"] = img128(g["L0_attn_b_Wv"].T.copy())
    d["bw_o"] = img128(g["L0_attn_b_Wo"].T.copy())
    d["bq_r"] = g["L0_attn_b_bq"].reshape(1, 1024).copy()
    d["bk_r"] = g["L0_attn_b_bk"].reshape(1, 1024).copy()
    d["bo_i"] = colimg(g["L0_attn_b_bo"])
    d["bv_r"] = g["L0_attn_b_bv"].reshape(1, 1024).copy()
    re_t = np.ascontiguousarray(g["L0_attn_b_rel_emb"].T)
    d["rel2"] = np.concatenate([re_t, re_t], axis=0).copy()
    d["f_w1"] = ogblocks(g["L0_ffn_w1_w"].T.copy())
    d["f_w2"] = ogblocks(g["L0_ffn_w2_w"].T.copy())
    d["f_h1"] = ogblocks(g["L0_ffn_h1_w"].T.copy())
    d["f_h2"] = ogblocks(g["L0_ffn_h2_w"].T.copy())
    d["f_w3"] = ogblocks(g["L0_ffn_w3_w"].T.copy())
    d["s1_i"] = colimg(g["L0_ffn_w1_s"])
    d["s2_i"] = colimg(g["L0_ffn_w2_s"])
    d["sh1_i"] = colimg(g["L0_ffn_h1_s"])
    d["sh2_i"] = colimg(g["L0_ffn_h2_s"])
    d["s3_i"] = colimg(g["L0_ffn_w3_s"])
    x = g["L0_attn_a_x"]
    in_maps = []
    for b in range(NCORE):
        m = dict(d)
        m["xT"] = img128(np.ascontiguousarray(x[b].T))
        in_maps.append(m)
    return in_maps


def kernel(**inputs):
    if STAGES not in _BUILT:
        _BUILT[STAGES] = build(STAGES)
    nc = _BUILT[STAGES]
    in_maps = _prep_inputs(inputs)
    res = run_bass_kernel_spmd(nc, in_maps, core_ids=list(range(NCORE)))
    outs = []
    for b in range(NCORE):
        yi = np.asarray(res.results[b]["y"])
        outs.append(yi.reshape(128, 8, 2048).transpose(1, 0, 2).reshape(1024, 2048).T)
    return np.stack(outs, axis=0).astype(np.float32)


# version 07
# speedup vs baseline: 1.2213x; 1.0176x over previous
import numpy as np
import ml_dtypes
import concourse.bass as bass
import concourse.mybir as mybir
from concourse.ap import AP
from concourse.bass_utils import run_bass_kernel_spmd
from contextlib import ExitStack

F32 = mybir.dt.float32
BF16 = mybir.dt.bfloat16
AF = mybir.ActivationFunctionType
ALU = mybir.AluOpType
BFD = ml_dtypes.bfloat16

T = 2048
D = 1024
LAT = 512
H = 16
HD = 64
DH = 4096
NCORE = 8
STAGES = 4
DEBUG_DUMP = False


def img128(m):
    R, C = m.shape
    assert R % 128 == 0
    return np.ascontiguousarray(
        m.reshape(R // 128, 128, C).transpose(1, 0, 2).reshape(128, (R // 128) * C))


def ogblocks(wt, ogc=128):
    I, O = wt.shape
    n = O // ogc
    a = wt.reshape(I // 128, 128, n, ogc).transpose(2, 1, 0, 3)
    return np.ascontiguousarray(a.reshape(n, 128, (I // 128) * ogc))


def colimg(v):
    return np.ascontiguousarray(v.reshape(-1, 128).T)


class Sem:
    __slots__ = ("h", "n")

    def __init__(self, es, nc, name):
        self.h = es.enter_context(nc.semaphore(name))
        self.n = 0

    def inc(self, instr, k=1):
        instr.then_inc(self.h, k)
        self.n += k
        return self.n


def build(stages):
    nc = bass.Bass("TRN2", target_bir_lowering=False, detect_race_conditions=False)

    def din(name, shape, dtype=F32):
        return nc.dram_tensor(name, list(shape), dtype, kind="ExternalInput").ap()

    xT = din("xT", (128, 8 * T))
    aw_dkv = din("aw_dkv", (128, 4096))
    aw_dq = din("aw_dq", (128, 4096))
    aw_dqn = din("aw_dqn", (128, 4096))
    aw_uq = din("aw_uq", (128, 4096))
    aw_uk = din("aw_uk", (128, 4096))
    aw_uv = din("aw_uv", (128, 4096))
    aw_o = din("aw_o", (128, 8192))
    bw_q = din("bw_q", (128, 8192))
    bw_k = din("bw_k", (128, 8192))
    bw_v = din("bw_v", (128, 8192))
    bw_o = din("bw_o", (128, 8192))
    bq_r = din("bq_r", (1, 1024))
    bk_r = din("bk_r", (1, 1024))
    bo_i = din("bo_i", (128, 8))
    bv_r = din("bv_r", (1, 1024))
    rel2_d = din("rel2", (128, 4095))
    f_w1 = din("f_w1", (32, 128, 1024))
    f_w2 = din("f_w2", (32, 128, 1024))
    f_h1 = din("f_h1", (32, 128, 4096))
    f_h2 = din("f_h2", (32, 128, 4096))
    f_w3 = din("f_w3", (8, 128, 4096))
    s1_i = din("s1_i", (128, 32))
    s2_i = din("s2_i", (128, 32))
    sh1_i = din("sh1_i", (128, 32))
    sh2_i = din("sh2_i", (128, 32))
    s3_i = din("s3_i", (128, 8))

    yO = nc.dram_tensor("y", [128, 8 * T], F32, kind="ExternalOutput").ap()
    parkA = nc.dram_tensor("parkA", [128, 8 * T], F32, kind="Internal").ap()
    parkB = nc.dram_tensor("parkB", [128, 8 * T], F32, kind="Internal").ap()
    osc_d = nc.dram_tensor("oscr", [128, 8 * T], BF16, kind="Internal").ap()
    ln3st = nc.dram_tensor("ln3st", [128, 8 * T], BF16, kind="Internal").ap()
    pexp = nc.dram_tensor("pexp", [H * 2048 * 4095], BF16, kind="Internal").ap()
    cach = {
        "w1": nc.dram_tensor("c_w1", [32, 128, 1024], BF16, kind="Internal").ap(),
        "w2": nc.dram_tensor("c_w2", [32, 128, 1024], BF16, kind="Internal").ap(),
        "h1": nc.dram_tensor("c_h1", [32, 128, 4096], BF16, kind="Internal").ap(),
        "h2": nc.dram_tensor("c_h2", [32, 128, 4096], BF16, kind="Internal").ap(),
        "w3": nc.dram_tensor("c_w3", [8, 128, 4096], BF16, kind="Internal").ap(),
    }

    es = ExitStack()

    def sbuf(name, shape, dtype, side="left"):
        return es.enter_context(nc.sbuf_tensor(name, list(shape), dtype, side=side))

    ones_b = sbuf("ones_b", (128, 512), BF16)
    ones_f = sbuf("ones_f", (128, 128), F32)
    epsT = sbuf("epsT", (128, 1), F32)
    s1s = sbuf("s1s", (128, 32), F32)
    s2s = sbuf("s2s", (128, 32), F32)
    sh1s = sbuf("sh1s", (128, 32), F32)
    sh2s = sbuf("sh2s", (128, 32), F32)
    s3s = sbuf("s3s", (128, 8), F32)
    bq_row_s = sbuf("bq_row_s", (1, 1024), BF16)
    bk_row_s = sbuf("bk_row_s", (1, 1024), BF16)
    bos = sbuf("bos", (128, 8), F32)
    bvs = sbuf("bvs", (1, 1024), BF16)
    V = sbuf("V", (128, 3072), F32)

    dW = Sem(es, nc, "dW")
    dXB = Sem(es, nc, "dXB")
    dPX = Sem(es, nc, "dPX")
    dX = Sem(es, nc, "dX")
    dK = Sem(es, nc, "dK")
    dO = Sem(es, nc, "dO")
    pe = Sem(es, nc, "pe")
    pPV = Sem(es, nc, "pPV")
    pW = Sem(es, nc, "pW")
    aG = Sem(es, nc, "aG")
    aEX = Sem(es, nc, "aEX")
    aEV = Sem(es, nc, "aEV")
    aFE = Sem(es, nc, "aFE")
    vG = Sem(es, nc, "vG")
    vRC = Sem(es, nc, "vRC")
    vNM = Sem(es, nc, "vNM")
    vEM = Sem(es, nc, "vEM")
    vFE = Sem(es, nc, "vFE")
    gMK = Sem(es, nc, "gMK")
    pLN = Sem(es, nc, "pLN")
    dCW = Sem(es, nc, "dCW")
    dOO = Sem(es, nc, "dOO")
    dXO = Sem(es, nc, "dXO")

    blk = es.enter_context(nc.Block())

    def p0_sync(sy):
        for dst, src in ((s1s, s1_i), (s2s, s2_i), (sh1s, sh1_i), (sh2s, sh2_i),
                         (s3s, s3_i), (bos, bo_i)):
            sy.dma_start(out=dst[:], in_=src).then_inc(dX.h, 16)
            dX.n += 16

    def p0_gp(g):
        for dst, src in ((bq_row_s, bq_r), (bk_row_s, bk_r), (bvs, bv_r)):
            g.dma_start(out=dst[:], in_=src).then_inc(dW.h, 16)
            dW.n += 16

    def p0_vec(v):
        v.memset(ones_b[:], 1.0)
        v.memset(ones_f[:], 1.0)
        vG.inc(v.memset(epsT[:], 1e-5))

    blk.sync(p0_sync)
    blk.gpsimd(p0_gp)
    blk.vector(p0_vec)
    tok_consts_v = vG.n
    tok_consts_w = dW.n
    tok_consts_d = dX.n

    ln_ctr = [0]

    def emit_ln(src_ap, dst, xb, sqb, tb2, gate=None):
        ln_ctr[0] += 1
        u = str(ln_ctr[0])
        ps_es = ExitStack()
        st_sum = ps_es.enter_context(nc.psum_tensor("ln_s" + u, [1, 1024], F32))
        st_sq = ps_es.enter_context(nc.psum_tensor("ln_q" + u, [1, 1024], F32))
        bcR = ps_es.enter_context(nc.psum_tensor("ln_r" + u, [128, 512], F32))
        bcM = ps_es.enter_context(nc.psum_tensor("ln_m" + u, [128, 512], F32))
        half_tok = [0]
        pLN_b = [pLN.n]
        prev_affine = [0]
        prev_stats_rd = [0]
        prev_stats = [0]
        for th in (0, 1):
            o = th * 1024
            xb_tok = []

            def w_gp(g, th=th, xb_tok=xb_tok):
                if gate is not None and th == 0:
                    gate(g)
                if th == 1:
                    g.wait_ge(vG.h, prev_affine[0])
                for c in range(8):
                    ins = g.dma_start(
                        out=xb[:, c * 1024:(c + 1) * 1024],
                        in_=src_ap[:, c * 2048 + th * 1024: c * 2048 + th * 1024 + 1024])
                    xb_tok.append(dXB.inc(ins, 16))
            blk.gpsimd(w_gp)
            sq_tok = []

            def w_act(sc, th=th, xb_tok=xb_tok, sq_tok=sq_tok):
                for c in range(8):
                    sc.wait_ge(dXB.h, xb_tok[c])
                    gi = th * 8 + c
                    if gi >= 2:
                        sc.wait_ge(pLN.h, pLN_b[0] + gi - 1)
                    ins = sc.activation(sqb[:, (c % 2) * 1024:(c % 2) * 1024 + 1024],
                                        xb[:, c * 1024:(c + 1) * 1024], AF.Square)
                    sq_tok.append(aG.inc(ins))
            blk.scalar(w_act)
            stat_tok = [0]

            def w_pe(t, th=th, xb_tok=xb_tok, sq_tok=sq_tok, stat_tok=stat_tok):
                if th == 1:
                    t.wait_ge(vG.h, prev_stats_rd[0])
                for c in range(8):
                    t.wait_ge(dXB.h, xb_tok[c])
                    t.wait_ge(aG.h, sq_tok[c])
                    for nb in (0, 1):
                        sl = slice(c * 1024 + nb * 512, c * 1024 + nb * 512 + 512)
                        sq_sl = slice((c % 2) * 1024 + nb * 512, (c % 2) * 1024 + nb * 512 + 512)
                        t.matmul(st_sum[0:1, nb * 512:nb * 512 + 512], ones_b[:, 0:1],
                                 xb[:, sl], start=(c == 0), stop=(c == 7))
                        ins = t.matmul(st_sq[0:1, nb * 512:nb * 512 + 512], ones_b[:, 0:1],
                                       sqb[:, sq_sl], start=(c == 0), stop=(c == 7))
                    pLN.inc(ins)
                stat_tok[0] = pLN.n
            blk.tensor(w_pe)
            prev_stats[0] = pe.n
            va_tok = [0]

            def w_vec(v, stat_tok=stat_tok, va_tok=va_tok):
                v.wait_ge(pLN.h, stat_tok[0])
                v.tensor_scalar(V[0:1, 0:1024], st_sum[0:1, :], 1.0 / D, None, ALU.mult)
                v.tensor_scalar(V[0:1, 1024:2048], st_sq[0:1, :], 1.0 / D, None, ALU.mult)
                v.tensor_mul(V[0:1, 2048:3072], V[0:1, 0:1024], V[0:1, 0:1024])
                ins = v.tensor_sub(V[0:1, 1024:2048], V[0:1, 1024:2048], V[0:1, 2048:3072])
                va_tok[0] = vG.inc(ins)
            blk.vector(w_vec)
            prev_stats_rd[0] = vG.n
            sd_tok = [0]

            def w_act2(sc, va_tok=va_tok, sd_tok=sd_tok):
                sc.wait_ge(vG.h, va_tok[0])
                ins = sc.activation(V[0:1, 2048:3072], V[0:1, 1024:2048], AF.Sqrt,
                                    bias=epsT[0:1, 0:1])
                sd_tok[0] = aG.inc(ins)
            blk.scalar(w_act2)
            vb_tok = [0]

            def w_vec2(v, sd_tok=sd_tok, vb_tok=vb_tok):
                v.wait_ge(aG.h, sd_tok[0])
                v.reciprocal(V[0:1, 1024:2048], V[0:1, 2048:3072])
                ins = v.tensor_mul(V[0:1, 2048:3072], V[0:1, 0:1024], V[0:1, 1024:2048])
                vb_tok[0] = vG.inc(ins)
            blk.vector(w_vec2)
            for lb in (0, 1):
                bc_tok = [0]

                def w_pe2(t, lb=lb, vb_tok=vb_tok, bc_tok=bc_tok):
                    t.wait_ge(vG.h, max(vb_tok[0], prev_affine[0]))
                    t.matmul(bcR[:, :], ones_f[0:1, :],
                             V[0:1, 1024 + lb * 512:1024 + lb * 512 + 512],
                             start=True, stop=True)
                    ins = t.matmul(bcM[:, :], ones_f[0:1, :],
                                   V[0:1, 2048 + lb * 512:2048 + lb * 512 + 512],
                                   start=True, stop=True)
                    bc_tok[0] = pe.inc(ins)
                blk.tensor(w_pe2)

                def w_vec3(v, lb=lb, th=th, bc_tok=bc_tok):
                    v.wait_ge(pe.h, bc_tok[0])
                    for c in range(8):
                        sl = slice(c * 1024 + lb * 512, c * 1024 + lb * 512 + 512)
                        v.tensor_tensor(tb2[:, (c % 2) * 512:(c % 2) * 512 + 512],
                                        xb[:, sl], bcR[:, :], ALU.mult)
                        ins = v.tensor_tensor(
                            dst[:, c * 2048 + th * 1024 + lb * 512:
                                c * 2048 + th * 1024 + lb * 512 + 512],
                            tb2[:, (c % 2) * 512:(c % 2) * 512 + 512], bcM[:, :],
                            ALU.subtract)
                        vG.inc(ins)
                blk.vector(w_vec3)
                prev_affine[0] = vG.n
            if th == 0:
                half_tok[0] = vG.n
        ps_es.close()
        return half_tok[0], vG.n

    blk.tensor(lambda t: t.wait_ge(vG.h, tok_consts_v))
    blk.scalar(lambda sc: (sc.wait_ge(vG.h, tok_consts_v), sc.wait_ge(dX.h, tok_consts_d)))
    blk.vector(lambda v: v.wait_ge(dX.h, tok_consts_d))

    prj_es = [None]

    def emit_proj(nk, nm, nn, nstep, lhsT_fn, rhs_fn, evac_fn, ps_a, ps_b,
                  dma_toks=(), bias_row=None):
        aFE_b = aFE.n
        tiles = [(mi, ni) for mi in range(nm) for ni in range(nn)]
        pe_tok = []

        def f_pe(t):
            for sem, tok in dma_toks:
                t.wait_ge(sem.h, tok)
            for g, (mi, ni) in enumerate(tiles):
                ps = (ps_a, ps_b)[g % 2]
                if aFE_b + g - 1 > 0:
                    t.wait_ge(aFE.h, aFE_b + g - 1)
                last = nk - 1 if bias_row is None else nk
                for kc in range(nk):
                    ins = t.matmul(ps[:, :], lhsT_fn(kc, mi), rhs_fn(kc, ni),
                                   start=(kc == 0), stop=(kc == last))
                if bias_row is not None:
                    ins = t.matmul(ps[:, :], bias_row[0:1, mi * 128:mi * 128 + 128],
                                   ones_b[0:1, 0:nstep], start=False, stop=True)
                pe_tok.append(pe.inc(ins))

        def f_act(sc):
            for g, (mi, ni) in enumerate(tiles):
                ps = (ps_a, ps_b)[g % 2]
                sc.wait_ge(pe.h, pe_tok[g])
                aFE.inc(evac_fn(sc, mi, ni, ps[:, :]))
        blk.tensor(f_pe)
        blk.scalar(f_act)
        return aFE.n

    a_es = ExitStack()

    def rsb(name, shape, dtype):
        return a_es.enter_context(nc.sbuf_tensor(name, list(shape), dtype, side="right"))

    def lsb(name, shape, dtype):
        return a_es.enter_context(nc.sbuf_tensor(name, list(shape), dtype, side="left"))

    ckv_i = lsb("ckv_i", (128, 8192), BF16)
    keT_i = lsb("keT_i", (128, 4096), BF16)
    veT_i = lsb("veT_i", (128, 4096), BF16)

    p2_es = ExitStack()
    q2_i = p2_es.enter_context(nc.sbuf_tensor("q2_i", [128, 8192], BF16, side="right"))
    ln1_es = ExitStack()
    ln1 = ln1_es.enter_context(nc.sbuf_tensor("ln1", [128, 16384], BF16, side="right"))
    ln_es = ExitStack()
    xb1 = ln_es.enter_context(nc.sbuf_tensor("xb1", [128, 8192], BF16, side="right"))
    sqb1 = ln_es.enter_context(nc.sbuf_tensor("sqb1", [128, 2048], BF16, side="right"))
    tb21 = ln_es.enter_context(nc.sbuf_tensor("tb21", [128, 1024], F32, side="right"))
    ln1_tok = emit_ln(xT, ln1, xb1, sqb1, tb21)[1]
    ln_es.close()

    p2ps = ExitStack()
    pr_a = p2ps.enter_context(nc.psum_tensor("pr_a", [128, 512], F32))
    pr_b = p2ps.enter_context(nc.psum_tensor("pr_b", [128, 512], F32))

    def load_w(g, buf_slices, srcs):
        toks = []
        for sl, src in zip(buf_slices, srcs):
            toks.append(dW.inc(g.dma_start(out=sl, in_=src), 16))
        return toks

    aw1_es = ExitStack()
    aw1 = aw1_es.enter_context(nc.sbuf_tensor("aw1", [128, 8192], BF16, side="right"))
    t_aw1 = []
    blk.gpsimd(lambda g: (g.wait_ge(vG.h, ln1_tok),
                          t_aw1.extend(load_w(
                              g, [aw1[:, 0:4096], aw1[:, 4096:8192]], [aw_dkv, aw_dq]))))

    def cp_evac(dst_fn):
        def f(sc, mi, ni, ps):
            return sc.copy(dst_fn(mi, ni), ps)
        return f

    emit_proj(8, 4, 4, 512,
              lambda kc, mi: aw1[:, kc * 512 + mi * 128: kc * 512 + mi * 128 + 128],
              lambda kc, ni: ln1[:, kc * 2048 + ni * 512: kc * 2048 + ni * 512 + 512],
              cp_evac(lambda mi, ni: ckv_i[:, mi * 2048 + ni * 512: mi * 2048 + ni * 512 + 512]),
              pr_a, pr_b, dma_toks=[(dW, t_aw1[0]), (vG, ln1_tok)])
    ckv_done = aFE.n
    emit_proj(8, 4, 4, 512,
              lambda kc, mi: aw1[:, 4096 + kc * 512 + mi * 128: 4096 + kc * 512 + mi * 128 + 128],
              lambda kc, ni: ln1[:, kc * 2048 + ni * 512: kc * 2048 + ni * 512 + 512],
              cp_evac(lambda mi, ni: q2_i[:, mi * 2048 + ni * 512: mi * 2048 + ni * 512 + 512]),
              pr_a, pr_b, dma_toks=[(dW, t_aw1[1]), (vG, ln1_tok)])
    q2_done = aFE.n
    aw1_free_tok = pe.n
    aw1_es.close()
    ln1_es.close()

    aw2_es = ExitStack()
    aw2 = aw2_es.enter_context(nc.sbuf_tensor("aw2", [128, 8192], BF16, side="right"))
    B_es = ExitStack()
    B_i = B_es.enter_context(nc.sbuf_tensor("B_i", [128, 8192], BF16, side="right"))
    t_aw2 = []
    blk.gpsimd(lambda g: (g.wait_ge(pe.h, aw1_free_tok),
                          t_aw2.extend(load_w(
                              g, [aw2[:, 0:4096], aw2[:, 4096:8192]], [aw_uq, aw_dqn]))))
    emit_proj(4, 8, 2, 512,
              lambda kc, mi: aw2[:, kc * 1024 + mi * 128: kc * 1024 + mi * 128 + 128],
              lambda kc, ni: aw2[:, 4096 + kc * 1024 + ni * 512: 4096 + kc * 1024 + ni * 512 + 512],
              cp_evac(lambda mi, ni: B_i[:, mi * 1024 + ni * 512: mi * 1024 + ni * 512 + 512]),
              pr_a, pr_b, dma_toks=[(dW, t_aw2[1])])
    B_done = aFE.n

    aw3_es = ExitStack()
    aw3 = aw3_es.enter_context(nc.sbuf_tensor("aw3", [128, 8192], BF16, side="right"))
    t_aw3 = []
    blk.gpsimd(lambda g: t_aw3.extend(load_w(
        g, [aw3[:, 0:4096], aw3[:, 4096:8192]], [aw_uk, aw_uv])))
    emit_proj(8, 4, 2, 512,
              lambda kc, mi: aw3[:, kc * 512 + mi * 128: kc * 512 + mi * 128 + 128],
              lambda kc, ni: B_i[:, kc * 1024 + ni * 512: kc * 1024 + ni * 512 + 512],
              cp_evac(lambda mi, ni: keT_i[:, mi * 1024 + ni * 512: mi * 1024 + ni * 512 + 512]),
              pr_a, pr_b, dma_toks=[(dW, t_aw3[0]), (aFE, B_done)])
    keT_done = aFE.n
    aw4_es = ExitStack()
    aw4 = aw4_es.enter_context(nc.sbuf_tensor("aw4", [128, 8192], BF16, side="right"))
    t_aw4 = []
    blk.gpsimd(lambda g: t_aw4.extend(load_w(g, [aw4[:, 0:8192]], [aw_o])))
    emit_proj(8, 4, 2, 512,
              lambda kc, mi: aw3[:, 4096 + kc * 512 + mi * 128: 4096 + kc * 512 + mi * 128 + 128],
              lambda kc, ni: aw4[:, kc * 1024 + ni * 512: kc * 1024 + ni * 512 + 512],
              cp_evac(lambda mi, ni: veT_i[:, mi * 1024 + ni * 512: mi * 1024 + ni * 512 + 512]),
              pr_a, pr_b, dma_toks=[(dW, t_aw4[0])])
    veT_done = aFE.n
    aw4_es.close()

    q_i = lsb("q_i", (128, 16384), BF16)
    emit_proj(4, 8, 4, 512,
              lambda kc, mi: aw2[:, kc * 1024 + mi * 128: kc * 1024 + mi * 128 + 128],
              lambda kc, ni: q2_i[:, kc * 2048 + ni * 512: kc * 2048 + ni * 512 + 512],
              cp_evac(lambda mi, ni: q_i[:, mi * 2048 + ni * 512: mi * 2048 + ni * 512 + 512]),
              pr_a, pr_b, dma_toks=[(aFE, q2_done)])
    qT_free = pe.n
    aw3_es.close()
    B_es.close()
    aw2_es.close()
    p2_es.close()

    keys_i = lsb("keys_i", (128, 16384), BF16)
    u_i = lsb("u_i", (128, 16640), BF16)
    emit_proj(4, 8, 4, 512,
              lambda kc, mi: keT_i[:, kc * 1024 + mi * 128: kc * 1024 + mi * 128 + 128],
              lambda kc, ni: ckv_i[:, kc * 2048 + ni * 512: kc * 2048 + ni * 512 + 512],
              cp_evac(lambda mi, ni: keys_i[:, mi * 2048 + ni * 512: mi * 2048 + ni * 512 + 512]),
              pr_a, pr_b, dma_toks=[(aFE, keT_done)])
    keys_done = aFE.n

    u_ones_tok = [0]
    pe_gate_u = pe.n
    blk.vector(lambda v: (v.wait_ge(pe.h, pe_gate_u),
                          u_ones_tok.__setitem__(0, vG.inc(v.memset(
                              u_i.ap().rearrange("p (kc h s) -> p kc h s",
                                                 kc=16, h=16)[:, :, :, 64:65], 1.0)))))
    p2ps.close()
    ups = ExitStack()
    u_ps_a = ups.enter_context(nc.psum_tensor("u_ps_a", [128, 1024], F32))
    u_ps_b = ups.enter_context(nc.psum_tensor("u_ps_b", [128, 1024], F32))
    u_pe_tok = []
    aFE_u_b = aFE.n

    def f_u_pe(t):
        t.wait_ge(aFE.h, max(veT_done, ckv_done, keys_done))
        for kc in range(16):
            ps = (u_ps_a, u_ps_b)[kc % 2]
            if kc >= 2:
                t.wait_ge(aFE.h, aFE_u_b + kc - 1)
            for lc in range(4):
                for nb in (0, 1):
                    ins = t.matmul(ps[:, nb * 512:nb * 512 + 512],
                                   ckv_i[:, lc * 2048 + kc * 128: lc * 2048 + kc * 128 + 128],
                                   veT_i[:, lc * 1024 + nb * 512: lc * 1024 + nb * 512 + 512],
                                   start=(lc == 0), stop=(lc == 3))
            u_pe_tok.append(pe.inc(ins))

    def f_u_act(sc):
        sc.wait_ge(vG.h, u_ones_tok[0])
        for kc in range(16):
            ps = (u_ps_a, u_ps_b)[kc % 2]
            sc.wait_ge(pe.h, u_pe_tok[kc])
            dst = u_i.ap().rearrange("p (kc h s) -> p kc h s", kc=16, h=16)[:, kc, :, 0:64]
            aFE.inc(sc.copy(dst, ps.ap().rearrange("p (h d) -> p h d", h=16)))
    blk.tensor(f_u_pe)
    blk.scalar(f_u_act)
    u_done = aFE.n
    ups.close()

    p3_es = ExitStack()

    def r3(name, shape, dtype):
        return p3_es.enter_context(nc.sbuf_tensor(name, list(shape), dtype, side="right"))

    E2 = r3("E2", (128, 2048), BF16)
    nrm = r3("nrm", (128, 2048), F32)
    o_ev = r3("o_ev", (128, 2048), F32)
    xr = r3("xr", (128, 4096), F32)
    xt = r3("xt", (128, 2048), F32)
    p3ps = ExitStack()
    qk_ps = p3ps.enter_context(nc.psum_tensor("qk_ps", [128, 2048], F32))
    pv_ps = p3ps.enter_context(nc.psum_tensor("pv_ps", [65, 1024], F32))
    bc_ps = p3ps.enter_context(nc.psum_tensor("bc_ps", [64, 1024], F32))

    tiles3 = []
    for h in range(H):
        for lbh in (0, 1):
            for rc in range(8 * (lbh + 1)):
                tiles3.append((h, lbh, rc))
    G3 = len(tiles3)
    masked3 = [(rc >= 8 * lbh) for (h, lbh, rc) in tiles3]
    nmask3 = np.cumsum(masked3).tolist()
    first3 = [g for g in range(G3) if tiles3[g][2] == 0]
    last3 = [g - 1 for g in first3[1:]] + [G3 - 1]
    idx_of = {}
    for i, g in enumerate(first3):
        for gg in range(g, last3[i] + 1):
            idx_of[gg] = i
    aEX_b, aEV_b, vRC_b, vNM_b, pPV_b, gMK_b, dO_b, dX_b = (
        aEX.n, aEV.n, vRC.n, vNM.n, pPV.n, gMK.n, dO.n, dX.n)

    def f3_sync(sy):
        sy.wait_ge(pe.h, qT_free)

        def store(idx):
            h, lbh = idx // 2, idx % 2
            hc, rb = h // 2, (h % 2) * 64
            sy.wait_ge(vNM.h, vNM_b + idx + 1)
            sy.dma_start(
                out=parkA[rb:rb + 64, hc * 2048 + lbh * 1024: hc * 2048 + lbh * 1024 + 1024],
                in_=xt[0:64, (idx % 2) * 1024:(idx % 2) * 1024 + 1024]).then_inc(dO.h, 16)
            dO.n += 16
        for idx in range(2 * H):
            h, lbh = idx // 2, idx % 2
            hc, rb = h // 2, (h % 2) * 64
            if idx >= 4:
                sy.wait_ge(vNM.h, vNM_b + idx - 3)
            sy.dma_start(
                out=xr[0:64, (h % 2) * 2048 + lbh * 1024:(h % 2) * 2048 + lbh * 1024 + 1024],
                in_=xT[rb:rb + 64, hc * 2048 + lbh * 1024: hc * 2048 + lbh * 1024 + 1024]
            ).then_inc(dX.h, 16)
            dX.n += 16
            if idx >= 2:
                store(idx - 2)
        store(2 * H - 2)
        store(2 * H - 1)
    blk.sync(f3_sync)

    qk_tok = [0] * G3
    bc_tok = [0] * (2 * H)

    def f3_pe(t):
        t.wait_ge(aFE.h, max(keys_done, u_done))
        t.wait_ge(vG.h, u_ones_tok[0])

        def emit_qk(g):
            h, lbh, rc = tiles3[g]
            hc, rb = h // 2, (h % 2) * 64
            if aEX_b + g - 1 > 0:
                t.wait_ge(aEX.h, aEX_b + g - 1)
            for cb in (0, 1):
                ins = t.matmul(
                    qk_ps[:, (g % 2) * 1024 + cb * 512:(g % 2) * 1024 + cb * 512 + 512],
                    keys_i[rb:rb + 64, hc * 2048 + rc * 128: hc * 2048 + rc * 128 + 128],
                    q_i[rb:rb + 64, hc * 2048 + lbh * 1024 + cb * 512:
                        hc * 2048 + lbh * 1024 + cb * 512 + 512],
                    start=True, stop=True)
            qk_tok[g] = pe.inc(ins)

        def emit_pv(g):
            h, lbh, rc = tiles3[g]
            idx = idx_of[g]
            t.wait_ge(aEX.h, aEX_b + g + 1)
            if masked3[g]:
                t.wait_ge(gMK.h, gMK_b + nmask3[g])
            if rc == 0 and idx > 0:
                t.wait_ge(aEV.h, aEV_b + idx)
            for cb in (0, 1):
                ins = t.matmul(
                    pv_ps[:, cb * 512:cb * 512 + 512],
                    u_i[:, rc * 1040 + h * 65: rc * 1040 + h * 65 + 65],
                    E2[:, (g % 2) * 1024 + cb * 512:(g % 2) * 1024 + cb * 512 + 512],
                    start=(rc == 0), stop=(rc == 8 * (lbh + 1) - 1))
            pPV.inc(ins)

        def emit_bc(idx):
            t.wait_ge(aEV.h, aEV_b + idx + 1)
            if idx > 0:
                t.wait_ge(vNM.h, vNM_b + idx)
            for cb in (0, 1):
                ins = t.matmul(
                    bc_ps[:, cb * 512:cb * 512 + 512], ones_f[64:65, 0:64],
                    nrm[64:65, (idx % 2) * 1024 + cb * 512:(idx % 2) * 1024 + cb * 512 + 512],
                    start=True, stop=True)
            bc_tok[idx] = pe.inc(ins)

        for g in range(G3):
            emit_qk(g)
            if g >= 1:
                emit_pv(g - 1)
            if tiles3[g][2] == 1 and idx_of[g] > 0:
                emit_bc(idx_of[g] - 1)
        emit_pv(G3 - 1)
        emit_bc(2 * H - 1)
    blk.tensor(f3_pe)

    ev_tok = [0] * (2 * H)

    def f3_act(sc):
        for g in range(G3):
            h, lbh, rc = tiles3[g]
            idx = idx_of[g]
            sc.wait_ge(pe.h, qk_tok[g])
            if pPV_b + g - 1 > 0:
                sc.wait_ge(pPV.h, pPV_b + g - 1)
            aEX.inc(sc.activation(E2[:, (g % 2) * 1024:(g % 2) * 1024 + 1024],
                                  qk_ps[:, (g % 2) * 1024:(g % 2) * 1024 + 1024],
                                  AF.Exp, scale=0.125))
            if g == last3[idx]:
                sc.wait_ge(pPV.h, pPV_b + g + 1)
                if idx >= 2:
                    sc.wait_ge(vNM.h, vNM_b + idx - 1)
                sc.copy(o_ev[0:64, (idx % 2) * 1024:(idx % 2) * 1024 + 1024],
                        pv_ps[0:64, :])
                sc.activation(nrm[64:65, (idx % 2) * 1024:(idx % 2) * 1024 + 1024],
                              pv_ps[64:65, :], AF.Ln)
                ins = sc.activation(nrm[64:65, (idx % 2) * 1024:(idx % 2) * 1024 + 1024],
                                    nrm[64:65, (idx % 2) * 1024:(idx % 2) * 1024 + 1024],
                                    AF.Exp, scale=-1.0)
                ev_tok[idx] = aEV.inc(ins)
    blk.scalar(f3_act)

    def f3_pool(g_eng):
        for g in range(G3):
            if not masked3[g]:
                continue
            h, lbh, rc = tiles3[g]
            g_eng.wait_ge(aEX.h, aEX_b + g + 1)
            ins = g_eng.affine_select(
                E2[:, (g % 2) * 1024:(g % 2) * 1024 + 1024],
                E2[:, (g % 2) * 1024:(g % 2) * 1024 + 1024],
                [[1, 1024]], ALU.is_ge, 0.0,
                base=lbh * 1024 - rc * 128, channel_multiplier=-1)
            gMK.inc(ins)
    blk.gpsimd(f3_pool)

    def f3_vec(v):
        for idx in range(2 * H):
            h, lbh = idx // 2, idx % 2
            v.wait_ge(pe.h, bc_tok[idx])
            v.wait_ge(dX.h, dX_b + 16 * (idx + 1))
            osl = o_ev[0:64, (idx % 2) * 1024:(idx % 2) * 1024 + 1024]
            v.tensor_tensor(osl, osl, bc_ps[0:64, :], ALU.mult)
            if dO_b + 16 * (idx - 1) > 0:
                v.wait_ge(dO.h, dO_b + 16 * (idx - 1))
            ins = v.tensor_tensor(
                xt[0:64, (idx % 2) * 1024:(idx % 2) * 1024 + 1024], osl,
                xr[0:64, (h % 2) * 2048 + lbh * 1024:(h % 2) * 2048 + lbh * 1024 + 1024],
                ALU.add)
            vNM.inc(ins)
    blk.vector(f3_vec)

    x1_tok = dO.n

    p3ps.close()
    p3_es.close()
    a_es.close()

    ffn_ctr = [0]

    def emit_ffn(ln_src, park_src, dst, first, x_gate, ln_tok):
        ffn_ctr[0] += 1
        fu = str(ffn_ctr[0])
        f_es = ExitStack()

        def rf(name, shape, dtype):
            return f_es.enter_context(nc.sbuf_tensor(name + fu, list(shape), dtype, side="right"))

        HA = rf("HA", (128, 32768), BF16)
        HB = rf("HB", (128, 32768), BF16)
        wb = rf("wb", (128, 8192), BF16)
        xold = rf("xold", (128, 1024), F32)
        xn = rf("xn", (128, 1024), F32)
        tAB = rf("tAB", (128, 1024), BF16)
        fps = ExitStack()
        psA = [fps.enter_context(nc.psum_tensor(f"fpsA{i}" + fu, [128, 512], F32)) for i in range(2)]
        psB = [fps.enter_context(nc.psum_tensor(f"fpsB{i}" + fu, [128, 512], F32)) for i in range(2)]

        stages_d = [
            ("A", 32, 8, (f_w1, f_w2), ("w1", "w2"), 1024),
            ("B", 32, 32, (f_h1,), ("h1",), 4096),
            ("C", 32, 32, (f_h2,), ("h2",), 4096),
            ("D", 8, 32, (f_w3,), ("w3",), 4096),
        ]
        prev_stage_done = [pe.n, aFE.n, vFE.n]

        for th in (0, 1):
            for kind, n_og, n_ic, srcs, cks, wcols in stages_d:
                nsrc = len(srcs)
                use_cache = (not first) or th == 1
                dma_sem = dX if use_cache else dW
                dma_b = dma_sem.n
                pW_b = pW.n
                aFE_b = aFE.n
                vFE_b = vFE.n
                dO_b = dO.n
                dX_b = dX.n
                dXO_b_d = dXO.n
                pe_prev, aFE_prev, vFE_prev = prev_stage_done

                pe_tok = {}

                def f_pe(t, kind=kind, n_og=n_og, n_ic=n_ic, nsrc=nsrc, wcols=wcols,
                         dma_sem=dma_sem, dma_b=dma_b, pW_b=pW_b, aFE_b=aFE_b,
                         vFE_b=vFE_b, dX_b=dX_b, th=th, pe_tok=pe_tok,
                         aFE_prev=aFE_prev, vFE_prev=vFE_prev):
                    if kind == "A":
                        t.wait_ge(vG.h, ln_tok[th])
                    if aFE_prev:
                        t.wait_ge(aFE.h, aFE_prev)
                    if vFE_prev:
                        t.wait_ge(vFE.h, vFE_prev)
                    for og in range(n_og):
                        t.wait_ge(dma_sem.h, dma_b + 16 * nsrc * (og + 1))
                        sl = wb[:, (og % 2) * 4096:(og % 2) * 4096 + 4096]
                        for tb in (0, 1):
                            s = og * 2 + tb
                            rhs_off = th * 1024 + tb * 512
                            if kind == "A":
                                if aFE_b + s - 1 > 0:
                                    t.wait_ge(aFE.h, aFE_b + s - 1)
                                if vFE_b + s - 1 > 0:
                                    t.wait_ge(vFE.h, vFE_b + s - 1)
                                for w_i in (0, 1):
                                    ps = (psA, psB)[w_i][s % 2]
                                    for ic in range(8):
                                        ins = t.matmul(
                                            ps[:, :],
                                            sl[:, w_i * 1024 + ic * 128: w_i * 1024 + ic * 128 + 128],
                                            ln_src[:, ic * 2048 + rhs_off: ic * 2048 + rhs_off + 512],
                                            start=(ic == 0), stop=(ic == 7))
                                    if w_i == 0:
                                        pe_tok[(og, tb, 0)] = pe.inc(ins)
                                if tb == 1:
                                    pW.inc(ins)
                                    pe_tok[(og, tb, 1)] = None
                                else:
                                    pe_tok[(og, tb, 1)] = pe.inc(ins)
                            else:
                                if aFE_b + s - 1 > 0 and kind != "D":
                                    t.wait_ge(aFE.h, aFE_b + s - 1)
                                if vFE_b + s - 1 > 0 and kind == "D":
                                    t.wait_ge(vFE.h, vFE_b + s - 1)
                                src_h = HA if kind in ("B", "D") else HB
                                ps = psA[s % 2]
                                for ic in range(n_ic):
                                    ins = t.matmul(
                                        ps[:, :], sl[:, ic * 128: ic * 128 + 128],
                                        src_h[:, ic * 1024 + tb * 512: ic * 1024 + tb * 512 + 512],
                                        start=(ic == 0), stop=(ic == n_ic - 1))
                                if tb == 1:
                                    pW.inc(ins)
                                    pe_tok[(og, tb)] = None
                                else:
                                    pe_tok[(og, tb)] = pe.inc(ins)
                blk.tensor(f_pe)
                pe_stage_done = pe.n

                if kind in ("A", "B", "C"):
                    def f_act(sc, kind=kind, n_og=n_og, th=th, pe_tok=pe_tok,
                              pW_b=pW_b, vFE_b=vFE_b):
                        for og in range(n_og):
                            for tb in (0, 1):
                                s = og * 2 + tb
                                if kind == "A":
                                    tok = pe_tok[(og, tb, 0)]
                                else:
                                    tok = pe_tok[(og, tb)]
                                if tok is None:
                                    sc.wait_ge(pW.h, pW_b + og + 1)
                                else:
                                    sc.wait_ge(pe.h, tok)
                                if kind == "A":
                                    if vFE_b + s - 1 > 0:
                                        sc.wait_ge(vFE.h, vFE_b + s - 1)
                                    ins = sc.activation(
                                        tAB[:, (s % 2) * 512:(s % 2) * 512 + 512],
                                        psA[s % 2][:, :], AF.Silu, scale=s1s[:, og:og + 1])
                                else:
                                    scl = (sh1s if kind == "B" else sh2s)[:, og:og + 1]
                                    dstH = HB if kind == "B" else HA
                                    ins = sc.activation(
                                        dstH[:, og * 1024 + tb * 512: og * 1024 + tb * 512 + 512],
                                        psA[s % 2][:, :], AF.Gelu, scale=scl)
                                aFE.inc(ins)
                    blk.scalar(f_act)
                aFE_stage_done = aFE.n

                if kind in ("A", "D"):
                    def f_vec(v, kind=kind, n_og=n_og, th=th, pe_tok=pe_tok,
                              pW_b=pW_b, dO_b=dO_b, dX_b=dX_b, aFE_b=aFE_b):
                        for og in range(n_og):
                            for tb in (0, 1):
                                s = og * 2 + tb
                                if kind == "A":
                                    tok = pe_tok[(og, tb, 1)]
                                    if tok is None:
                                        v.wait_ge(pW.h, pW_b + og + 1)
                                    else:
                                        v.wait_ge(pe.h, tok)
                                    v.wait_ge(aFE.h, aFE_b + s + 1)
                                    ins = v.scalar_tensor_tensor(
                                        HA[:, og * 1024 + tb * 512: og * 1024 + tb * 512 + 512],
                                        psB[s % 2][:, :], s2s[:, og:og + 1],
                                        tAB[:, (s % 2) * 512:(s % 2) * 512 + 512],
                                        ALU.mult, ALU.mult)
                                    vFE.inc(ins)
                                else:
                                    tok = pe_tok[(og, tb)]
                                    if tok is None:
                                        v.wait_ge(pW.h, pW_b + og + 1)
                                    else:
                                        v.wait_ge(pe.h, tok)
                                    v.wait_ge(dXO.h, dXO_b_d + 16 * (og + 1))
                                    if dO_b + 16 * (s - 1) > 0:
                                        v.wait_ge(dO.h, dO_b + 16 * (s - 1))
                                    ins = v.scalar_tensor_tensor(
                                        xn[:, (s % 2) * 512:(s % 2) * 512 + 512],
                                        psA[s % 2][:, :], s3s[:, og:og + 1],
                                        xold[:, tb * 512: tb * 512 + 512],
                                        ALU.mult, ALU.add)
                                    vFE.inc(ins)
                    blk.vector(f_vec)
                vFE_stage_done = vFE.n

                wwid = 1024 if nsrc == 2 else 4096
                vFE_stage_b_d = vFE_b

                def emit_store(sy, og, tb, th=th):
                    s = og * 2 + tb
                    sy.wait_ge(vFE.h, vFE_stage_b_d + s + 1)
                    sy.dma_start(
                        out=dst[:, og * 2048 + th * 1024 + tb * 512:
                                og * 2048 + th * 1024 + tb * 512 + 512],
                        in_=xn[:, (s % 2) * 512:(s % 2) * 512 + 512]
                    ).then_inc(dO.h, 16)
                    dO.n += 16

                if not use_cache:
                    dCW_pre = dCW.n

                    def f_gp(g, n_og=n_og, nsrc=nsrc, srcs=srcs, pW_b=pW_b,
                             dCW_pre=dCW_pre, wwid=wwid):
                        g.wait_ge(dO.h, x_gate)
                        for og in range(n_og):
                            g.wait_ge(pW.h, pW_b + max(0, og - 1))
                            if og >= 2:
                                g.wait_ge(dCW.h, dCW_pre + 16 * nsrc * (og - 1))
                            sl = wb[:, (og % 2) * 4096:(og % 2) * 4096 + 4096]
                            for w_i, srcw in enumerate(srcs):
                                ins = g.dma_start(
                                    out=sl[:, w_i * wwid: w_i * wwid + wwid],
                                    in_=srcw[og])
                                dW.inc(ins, 16)

                    def f_syncw(sy, n_og=n_og, nsrc=nsrc, cks=cks, dma_b=dma_b,
                                wwid=wwid, kind=kind):
                        for og in range(n_og):
                            sy.wait_ge(dW.h, dma_b + 16 * nsrc * (og + 1))
                            sl = wb[:, (og % 2) * 4096:(og % 2) * 4096 + 4096]
                            for w_i, ck in enumerate(cks):
                                ins = sy.dma_start(
                                    out=cach[ck][og],
                                    in_=sl[:, w_i * wwid: w_i * wwid + wwid])
                                dCW.inc(ins, 16)
                            if kind == "D" and og >= 1:
                                emit_store(sy, og - 1, 0)
                                emit_store(sy, og - 1, 1)
                        if kind == "D":
                            emit_store(sy, n_og - 1, 0)
                            emit_store(sy, n_og - 1, 1)
                    blk.gpsimd(f_gp)
                    blk.sync(f_syncw)
                else:
                    def f_syncr(sy, n_og=n_og, nsrc=nsrc, cks=cks, pW_b=pW_b,
                                wwid=wwid, kind=kind):
                        sy.wait_ge(dO.h, x_gate)
                        for og in range(n_og):
                            sy.wait_ge(pW.h, pW_b + max(0, og - 1))
                            sl = wb[:, (og % 2) * 4096:(og % 2) * 4096 + 4096]
                            for w_i, ck in enumerate(cks):
                                ins = sy.dma_start(
                                    out=sl[:, w_i * wwid: w_i * wwid + wwid],
                                    in_=cach[ck][og])
                                dX.inc(ins, 16)
                            if kind == "D" and og >= 1:
                                emit_store(sy, og - 1, 0)
                                emit_store(sy, og - 1, 1)
                        if kind == "D":
                            emit_store(sy, n_og - 1, 0)
                            emit_store(sy, n_og - 1, 1)
                    blk.sync(f_syncr)

                if kind == "D":
                    dXO_b = dXO.n

                    def f_actd(sc, n_og=n_og, th=th, x_gate=x_gate):
                        sc.wait_ge(dO.h, x_gate)
                        for og in range(n_og):
                            if og >= 1:
                                sc.wait_ge(vFE.h, vFE_stage_b_d + 2 * og)
                            sc.dma_start(
                                out=xold[:, 0:1024],
                                in_=park_src[:, og * 2048 + th * 1024: og * 2048 + th * 1024 + 1024]
                            ).then_inc(dXO.h, 16)
                            dXO.n += 16
                    blk.scalar(f_actd)

                prev_stage_done = [pe_stage_done, aFE_stage_done, vFE_stage_done]
        fps.close()
        f_es.close()
        return dO.n

    if stages >= 2:
        f1_es = ExitStack()
        ln2 = f1_es.enter_context(nc.sbuf_tensor("ln2", [128, 16384], BF16, side="right"))
        ln2s = ExitStack()
        xb2 = ln2s.enter_context(nc.sbuf_tensor("xb2", [128, 8192], BF16, side="right"))
        sqb2 = ln2s.enter_context(nc.sbuf_tensor("sqb2", [128, 2048], BF16, side="right"))
        tb22 = ln2s.enter_context(nc.sbuf_tensor("tb22", [128, 1024], F32, side="right"))
        ln2_tok = emit_ln(parkA, ln2, xb2, sqb2, tb22,
                          gate=lambda g: g.wait_ge(dO.h, x1_tok))
        ln2s.close()
        x2_tok = emit_ffn(ln2, parkA, parkB, True, x1_tok, ln2_tok)
        f1_es.close()

    if stages >= 3:
        b_es = ExitStack()

        def rb_(name, shape, dtype):
            return b_es.enter_context(nc.sbuf_tensor(name, list(shape), dtype, side="right"))

        qb_i = rb_("qb_i", (128, 16384), BF16)
        kb_i = rb_("kb_i", (128, 16384), BF16)
        v_i = rb_("v_i", (128, 16640), BF16)
        rel2_s = rb_("rel2_s", (128, 4095), BF16)
        bwk_es = ExitStack()
        bwk = bwk_es.enter_context(nc.sbuf_tensor("bwk", [128, 8192], BF16, side="right"))
        ln3_es = ExitStack()
        ln3 = ln3_es.enter_context(nc.sbuf_tensor("ln3", [128, 16384], BF16, side="right"))
        ln3s = ExitStack()
        xb3 = ln3s.enter_context(nc.sbuf_tensor("xb3", [128, 8192], BF16, side="right"))
        sqb3 = ln3s.enter_context(nc.sbuf_tensor("sqb3", [128, 2048], BF16, side="right"))
        tb23 = ln3s.enter_context(nc.sbuf_tensor("tb23", [128, 1024], F32, side="right"))
        dcw_t, dxo_t = dCW.n, dXO.n
        ln3_done_t = emit_ln(parkB, ln3, xb3, sqb3, tb23,
                           gate=lambda g: (g.wait_ge(dO.h, x2_tok),
                                           g.wait_ge(dCW.h, dcw_t),
                                           g.wait_ge(dXO.h, dxo_t)))
        ln3_done = ln3_done_t[1]
        ln3_stats_pe = pe.n
        ln3s.close()
        vg_ln3 = vG.n

        def f_ln3st(sy):
            sy.wait_ge(vG.h, vg_ln3)
            sy.dma_start(out=ln3st, in_=ln3[:, :]).then_inc(dCW.h, 16)
            dCW.n += 16
        blk.sync(f_ln3st)
        ln3st_tok = dCW.n

        t_rel = []
        blk.gpsimd(lambda g: (g.wait_ge(vG.h, ln3_done), g.wait_ge(pe.h, ln3_stats_pe),
                              t_rel.append(dW.inc(
                                  g.dma_start(out=rel2_s[:], in_=rel2_d), 16))))

        bps = ExitStack()
        pb_a = bps.enter_context(nc.psum_tensor("pb_a", [128, 512], F32))
        pb_b = bps.enter_context(nc.psum_tensor("pb_b", [128, 512], F32))
        bw_es = ExitStack()
        bwb = bw_es.enter_context(nc.sbuf_tensor("bwb", [128, 8192], BF16, side="right"))
        t_bw = []
        blk.gpsimd(lambda g: (g.wait_ge(vG.h, ln3_done), g.wait_ge(pe.h, ln3_stats_pe),
                              t_bw.extend(load_w(
                                  g, [bwb[:, 0:8192], bwk[:, 0:8192]], [bw_q, bw_k]))))

        ln3_tok = vG.n
        emit_proj(8, 8, 4, 512,
                  lambda kc, mi: bwb[:, kc * 1024 + mi * 128: kc * 1024 + mi * 128 + 128],
                  lambda kc, ni: ln3[:, kc * 2048 + ni * 512: kc * 2048 + ni * 512 + 512],
                  cp_evac(lambda mi, ni: qb_i[:, mi * 2048 + ni * 512: mi * 2048 + ni * 512 + 512]),
                  pb_a, pb_b, dma_toks=[(dW, t_bw[0]), (vG, ln3_tok)],
                  bias_row=bq_row_s)
        qk_proj_done = aFE.n
        qb_pe = pe.n

        t_bwv = []
        blk.gpsimd(lambda g: (g.wait_ge(pe.h, qb_pe),
                              t_bwv.extend(load_w(g, [bwb[:, 0:8192]], [bw_v]))))
        v_ones_tok = [0]
        pe_gate_v = pe.n
        blk.vector(lambda v: (v.wait_ge(pe.h, pe_gate_v),
                              v_ones_tok.__setitem__(0, vG.inc(v.memset(
                                  v_i.ap().rearrange("p (kc h s) -> p kc h s",
                                                     kc=16, h=16)[:, :, :, 64:65], 1.0)))))
        v_pe_tok = []
        aFE_v_b = aFE.n

        def f_v_pe(t):
            t.wait_ge(dW.h, t_bwv[0])
            for kc in range(16):
                for nb in (0, 1):
                    kt = kc * 2 + nb
                    ps = (pb_a, pb_b)[kt % 2]
                    if aFE_v_b + kt - 1 > 0:
                        t.wait_ge(aFE.h, aFE_v_b + kt - 1)
                    for ic in range(8):
                        t.matmul(ps[:, :],
                                 ln3[:, ic * 2048 + kc * 128: ic * 2048 + kc * 128 + 128],
                                 bwb[:, ic * 1024 + nb * 512: ic * 1024 + nb * 512 + 512],
                                 start=(ic == 0), stop=False)
                    ins = t.matmul(ps[:, :], ones_b[0:1, 0:128],
                                   bvs[0:1, nb * 512:nb * 512 + 512],
                                   start=False, stop=True)
                    v_pe_tok.append(pe.inc(ins))

        def f_v_act(sc):
            sc.wait_ge(vG.h, v_ones_tok[0])
            for kc in range(16):
                for nb in (0, 1):
                    kt = kc * 2 + nb
                    ps = (pb_a, pb_b)[kt % 2]
                    sc.wait_ge(pe.h, v_pe_tok[kt])
                    dst = v_i.ap().rearrange("p (kc h s) -> p kc h s",
                                             kc=16, h=16)[:, kc, nb * 8:nb * 8 + 8, 0:64]
                    aFE.inc(sc.copy(dst, ps.ap().rearrange("p (h d) -> p h d", h=8)))
        blk.tensor(f_v_pe)
        blk.scalar(f_v_act)
        v_proj_done = aFE.n
        bw_es.close()
        ln3_es.close()
        if DEBUG_DUMP:
            dbgq = nc.dram_tensor("dbgq", [128, 16384], BF16, kind="Internal").ap()
            dbgl = nc.dram_tensor("dbgl", [128, 16384], BF16, kind="Internal").ap()

            def f_dbg(sy):
                sy.wait_ge(aFE.h, qk_proj_done)
                sy.dma_start(out=dbgq, in_=qb_i[:, :]).then_inc(dO.h, 16)
                sy.dma_start(out=dbgl, in_=ln3[:, :]).then_inc(dO.h, 16)
                dO.n += 32
            blk.sync(f_dbg)

        p2b_es = ExitStack()
        er = p2b_es.enter_context(nc.sbuf_tensor("er", [128, 6144], BF16, side="right"))
        t1b = p2b_es.enter_context(nc.sbuf_tensor("t1b", [128, 2048], BF16, side="right"))
        Eb = p2b_es.enter_context(nc.sbuf_tensor("Eb", [128, 2048], BF16, side="right"))
        nrm2 = p2b_es.enter_context(nc.sbuf_tensor("nrm2", [128, 2048], F32, side="right"))
        oev2 = p2b_es.enter_context(nc.sbuf_tensor("oev2", [128, 2048], F32, side="right"))
        o_t = p2b_es.enter_context(nc.sbuf_tensor("o_t", [128, 2048], BF16, side="right"))
        p1_es = ExitStack()
        Pst = p1_es.enter_context(nc.sbuf_tensor("Pst", [128, 8704], BF16, side="right"))
        lnst = p1_es.enter_context(nc.sbuf_tensor("lnst", [128, 4096], BF16, side="right"))
        p1ps = ExitStack()
        P01 = [p1ps.enter_context(nc.psum_tensor(f"P0{i}", [128, 1088], F32)) for i in (0, 1)]
        aEX_b1 = aEX.n
        dPX_b1 = dPX.n

        p1_items = [(h, lc, half) for h in range(H) for lc in range(16) for half in (0, 1)]
        proj_tiles = [("kb", mi, ni) for ni in range(4) for mi in range(8)]
        sched = []
        pi = 0
        for i, it in enumerate(p1_items):
            sched.append(("p1",) + it)
            if i % 16 == 15 and pi < len(proj_tiles):
                sched.append(proj_tiles[pi])
                pi += 1
        while pi < len(proj_tiles):
            sched.append(proj_tiles[pi])
            pi += 1

        p1_pe_tok = {}
        pt_pe_tok = {}
        dX_st_b = dX.n

        def f_lnst(sy):
            sy.wait_ge(dCW.h, ln3st_tok)
            for ni in range(4):
                if ni >= 1:
                    sy.wait_ge(aFE.h, v_proj_done + 8 * ni)
                sy.dma_start(
                    out=lnst[:, :],
                    in_=AP(ln3st.tensor, ni * 512, [[16384, 128], [2048, 8], [1, 512]])
                ).then_inc(dX.h, 16)
                dX.n += 16
        blk.sync(f_lnst)

        def f1_pe(t):
            t.wait_ge(aFE.h, v_proj_done)
            t.wait_ge(dCW.h, ln3st_tok)
            t.wait_ge(dW.h, t_rel[0])
            pt = 0
            for item in sched:
                if item[0] == "p1":
                    _, h, lc, half = item
                    hc, rbv = h // 2, (h % 2) * 64
                    s = (h * 16 + lc) * 2 + half
                    w = 1088 if half == 0 else 1087
                    s0 = 1920 - 128 * lc + half * 1088
                    if s > 1:
                        t.wait_ge(aEX.h, aEX_b1 + s - 1)
                    psx = P01[s % 2]
                    off = 0
                    for n in (512, 512, w - 1024):
                        ins = t.matmul(
                            psx[:, off:off + n],
                            qb_i[rbv:rbv + 64, hc * 2048 + lc * 128: hc * 2048 + lc * 128 + 128],
                            rel2_s[rbv:rbv + 64, s0 + off: s0 + off + n],
                            start=True, stop=True)
                        off += n
                    p1_pe_tok[s] = pe.inc(ins)
                else:
                    _, mi, ni = item
                    ps = (pb_a, pb_b)[pt % 2]
                    if v_proj_done + pt - 1 > 0:
                        t.wait_ge(aFE.h, v_proj_done + pt - 1)
                    t.wait_ge(dX.h, dX_st_b + 16 * (ni + 1))
                    for kc in range(8):
                        t.matmul(ps[:, :],
                                 bwk[:, kc * 1024 + mi * 128: kc * 1024 + mi * 128 + 128],
                                 lnst[:, kc * 512: kc * 512 + 512],
                                 start=(kc == 0), stop=False)
                    ins = t.matmul(ps[:, :], bk_row_s[0:1, mi * 128:mi * 128 + 128],
                                   ones_b[0:1, 0:512], start=False, stop=True)
                    pt_pe_tok[pt] = pe.inc(ins)
                    pt += 1
        blk.tensor(f1_pe)

        def f1_act(sc):
            pt = 0
            for item in sched:
                if item[0] == "p1":
                    _, h, lc, half = item
                    hl = h * 16 + lc
                    s = hl * 2 + half
                    w = 1088 if half == 0 else 1087
                    sc.wait_ge(pe.h, p1_pe_tok[s])
                    if hl >= 4 and half == 0:
                        sc.wait_ge(dPX.h, dPX_b1 + 16 * (hl - 3))
                    ins = sc.activation(
                        Pst[:, (hl % 4) * 2176 + half * 1088:
                            (hl % 4) * 2176 + half * 1088 + w],
                        P01[s % 2][:, 0:w], AF.Exp)
                    aEX.inc(ins)
                else:
                    _, mi, ni = item
                    ps = (pb_a, pb_b)[pt % 2]
                    sc.wait_ge(pe.h, pt_pe_tok[pt])
                    ins = sc.copy(kb_i[:, mi * 2048 + ni * 512: mi * 2048 + ni * 512 + 512],
                                  ps[:, :])
                    aFE.inc(ins)
                    pt += 1
        blk.scalar(f1_act)
        v_done = aFE.n

        def f1_gp(g):
            for h in range(H):
                for lc in range(16):
                    hl = h * 16 + lc
                    g.wait_ge(aEX.h, aEX_b1 + 2 * hl + 2)
                    ins = g.dma_start(
                        out=AP(pexp.tensor,
                               h * 2048 * 4095 + (lc * 128) * 4095 + (1920 - 128 * lc),
                               [[4095, 128], [1, 2175]]),
                        in_=Pst[:, (hl % 4) * 2176: (hl % 4) * 2176 + 2175])
                    dPX.inc(ins, 16)
        blk.gpsimd(f1_gp)
        pexp_all = dPX.n
        p1ps.close()
        p1_es.close()
        bps.close()
        p2bps = ExitStack()
        qk2 = p2bps.enter_context(nc.psum_tensor("qk2", [128, 2048], F32))
        pv2 = [p2bps.enter_context(nc.psum_tensor(f"pv2_{i}", [65, 1024], F32)) for i in (0, 1)]

        aEX_b2, aEV_b2, vRC_b2, vNM_b2, vEM_b2, pPV_b2, dK_b2, dOO_b2 = (
            aEX.n, aEV.n, vRC.n, vNM.n, vEM.n, pPV.n, dK.n, dOO.n)

        def f2_syncr(sy):

            def store_o(idx):
                h2, lbh2 = idx // 2, idx % 2
                hc2, rbv2 = h2 // 2, (h2 % 2) * 64
                sy.wait_ge(vNM.h, vNM_b2 + idx + 1)
                sy.dma_start(
                    out=osc_d[rbv2:rbv2 + 64,
                              hc2 * 2048 + lbh2 * 1024: hc2 * 2048 + lbh2 * 1024 + 1024],
                    in_=o_t[0:64, (idx % 2) * 1024:(idx % 2) * 1024 + 1024]
                ).then_inc(dOO.h, 16)
                dOO.n += 16
            for h in range(H):
                sy.wait_ge(dPX.h, dPX_b1 + 256 * (h + 1))
                for rc in range(16):
                    R = h * 16 + rc
                    e_guard = vEM_b2 + 2 * (R - 3) + 2
                    if e_guard > vEM_b2:
                        sy.wait_ge(vEM.h, e_guard)
                    src = AP(pexp.tensor, h * 2048 * 4095 + 2047 + 128 * rc,
                             [[4094, 2048], [1, 128]])
                    sy.dma_start(out=er[:, (R % 3) * 2048:(R % 3) * 2048 + 2048],
                                 in_=src, transpose=True).then_inc(dK.h, 16)
                    dK.n += 16
                if h >= 1:
                    store_o(2 * (h - 1))
                    store_o(2 * (h - 1) + 1)
            store_o(2 * H - 2)
            store_o(2 * H - 1)
        blk.sync(f2_syncr)

        qk2_tok = {}
        bc2_tok = {}
        units = [(h, rc, lbh) for h in range(H) for rc in range(16) for lbh in (0, 1)]
        NU = len(units)

        def last_j(h, lbh):
            return h * 32 + 30 + lbh

        def f2_pe(t):
            t.wait_ge(aFE.h, v_done)
            t.wait_ge(aEX.h, aEX_b2)
            t.wait_ge(vG.h, v_ones_tok[0])

            def emit_qk(j):
                h, rc, lbh = units[j]
                hc, rbv = h // 2, (h % 2) * 64
                if j > 0:
                    t.wait_ge(aEX.h, aEX_b2 + j - 1)
                for cb in (0, 1):
                    ins = t.matmul(
                        qk2[:, (j % 2) * 1024 + cb * 512:(j % 2) * 1024 + cb * 512 + 512],
                        kb_i[rbv:rbv + 64, hc * 2048 + rc * 128: hc * 2048 + rc * 128 + 128],
                        qb_i[rbv:rbv + 64, hc * 2048 + lbh * 1024 + cb * 512:
                             hc * 2048 + lbh * 1024 + cb * 512 + 512],
                        start=True, stop=True)
                qk2_tok[j] = pe.inc(ins)

            def emit_pv(j):
                h, rc, lbh = units[j]
                idx = 2 * h + lbh
                t.wait_ge(vEM.h, vEM_b2 + j + 1)
                if rc == 0 and h > 0:
                    t.wait_ge(aEV.h, aEV_b2 + 2 * (h - 1) + lbh + 1)
                    t.wait_ge(vNM.h, vNM_b2 + 2 * (h - 1) + lbh + 1)
                for cb in (0, 1):
                    ins = t.matmul(
                        pv2[lbh][:, cb * 512:cb * 512 + 512],
                        v_i[:, rc * 1040 + h * 65: rc * 1040 + h * 65 + 65],
                        Eb[:, (j % 2) * 1024 + cb * 512:(j % 2) * 1024 + cb * 512 + 512],
                        start=(rc == 0 and cb in (0, 1)), stop=(rc == 15))
                pPV.inc(ins)
                if rc == 15:
                    t.wait_ge(aEV.h, aEV_b2 + idx + 1)
                    for cb in (0, 1):
                        ins = t.matmul(
                            pv2[lbh][0:64, cb * 512:cb * 512 + 512], ones_f[64:65, 0:64],
                            nrm2[64:65, (idx % 2) * 1024 + cb * 512:
                                 (idx % 2) * 1024 + cb * 512 + 512],
                            start=True, stop=True)
                    bc2_tok[idx] = pe.inc(ins)

            for j in range(NU):
                emit_qk(j)
                if j >= 2:
                    emit_pv(j - 2)
            emit_pv(NU - 2)
            emit_pv(NU - 1)
        blk.tensor(f2_pe)

        def f2_act(sc):
            for j in range(NU):
                h, rc, lbh = units[j]
                sc.wait_ge(pe.h, qk2_tok[j])
                if j >= 2:
                    sc.wait_ge(vEM.h, vEM_b2 + j - 1)
                aEX.inc(sc.activation(t1b[:, (j % 2) * 1024:(j % 2) * 1024 + 1024],
                                      qk2[:, (j % 2) * 1024:(j % 2) * 1024 + 1024],
                                      AF.Exp, scale=0.125))
                if rc == 15:
                    idx = 2 * h + lbh
                    sc.wait_ge(pPV.h, pPV_b2 + last_j(h, lbh) + 1)
                    if idx >= 2:
                        sc.wait_ge(vNM.h, vNM_b2 + idx - 1)
                    sc.copy(oev2[0:64, (idx % 2) * 1024:(idx % 2) * 1024 + 1024],
                            pv2[lbh][0:64, :])
                    sc.activation(nrm2[64:65, (idx % 2) * 1024:(idx % 2) * 1024 + 1024],
                                  pv2[lbh][64:65, :], AF.Ln)
                    ins = sc.activation(nrm2[64:65, (idx % 2) * 1024:(idx % 2) * 1024 + 1024],
                                        nrm2[64:65, (idx % 2) * 1024:(idx % 2) * 1024 + 1024],
                                        AF.Exp, scale=-1.0)
                    aEV.inc(ins)
        blk.scalar(f2_act)

        def f2_vec(v):
            for j in range(NU):
                h, rc, lbh = units[j]
                v.wait_ge(aEX.h, aEX_b2 + j + 1)
                v.wait_ge(dK.h, dK_b2 + 16 * (h * 16 + rc + 1))
                if j >= 2:
                    v.wait_ge(pPV.h, pPV_b2 + j - 1)
                R = h * 16 + rc
                ins = v.tensor_tensor(
                    Eb[:, (j % 2) * 1024:(j % 2) * 1024 + 1024],
                    t1b[:, (j % 2) * 1024:(j % 2) * 1024 + 1024],
                    er[:, (R % 3) * 2048 + lbh * 1024:(R % 3) * 2048 + lbh * 1024 + 1024],
                    ALU.mult)
                vEM.inc(ins)
                if rc == 15:
                    idx = 2 * h + lbh
                    v.wait_ge(pe.h, bc2_tok[idx])
                    if dOO_b2 + 16 * (idx - 1) > dOO_b2:
                        v.wait_ge(dOO.h, dOO_b2 + 16 * (idx - 1))
                    ins = v.tensor_tensor(
                        o_t[0:64, (idx % 2) * 1024:(idx % 2) * 1024 + 1024],
                        oev2[0:64, (idx % 2) * 1024:(idx % 2) * 1024 + 1024],
                        pv2[lbh][0:64, 0:1024], ALU.mult)
                    vNM.inc(ins)
        blk.vector(f2_vec)

        oscr_tok = dOO.n
        p2bps.close()
        p2b_es.close()
        bwk_es.close()

        wo_es = ExitStack()
        o_i = wo_es.enter_context(nc.sbuf_tensor("o_i", [128, 16384], BF16, side="right"))
        wo_s = wo_es.enter_context(nc.sbuf_tensor("wo_s", [128, 8192], BF16, side="right"))
        xr2 = wo_es.enter_context(nc.sbuf_tensor("xr2", [128, 4096], F32, side="right"))
        xt2 = wo_es.enter_context(nc.sbuf_tensor("xt2", [128, 1024], F32, side="right"))
        wops = ExitStack()
        wo_a = wops.enter_context(nc.psum_tensor("wo_a", [128, 512], F32))
        wo_b = wops.enter_context(nc.psum_tensor("wo_b", [128, 512], F32))
        t_wo = []
        blk.gpsimd(lambda g: (g.wait_ge(dOO.h, oscr_tok),
                              t_wo.extend(load_w(g, [wo_s[:, :]], [bw_o]))))
        oi_tok = [0]

        def fw_sync_oi(sy):
            sy.wait_ge(dOO.h, oscr_tok)
            oi_tok[0] = dX.inc(sy.dma_start(out=o_i[:, :], in_=osc_d[:, :]), 16)
        blk.sync(fw_sync_oi)

        wo_pe_tok = {}
        vFE_wo_b = vFE.n
        dX_wo_b = dX.n
        dO_wo_b = dO.n

        def fw_pe(t):
            t.wait_ge(dW.h, t_wo[0])
            t.wait_ge(dX.h, oi_tok[0])
            for oc in range(8):
                for tb in range(4):
                    s = oc * 4 + tb
                    if vFE_wo_b + s - 1 > vFE_wo_b - 1 and s >= 2:
                        t.wait_ge(vFE.h, vFE_wo_b + s - 1)
                    ps = (wo_a, wo_b)[s % 2]
                    for ic in range(8):
                        ins = t.matmul(
                            ps[:, :], wo_s[:, ic * 1024 + oc * 128: ic * 1024 + oc * 128 + 128],
                            o_i[:, ic * 2048 + tb * 512: ic * 2048 + tb * 512 + 512],
                            start=(ic == 0), stop=(ic == 7))
                    wo_pe_tok[s] = pe.inc(ins)
        blk.tensor(fw_pe)

        def fw_sync_x(sy):
            sy.wait_ge(dOO.h, oscr_tok)

            def store_x(oc2):
                for tb2 in range(4):
                    s2 = oc2 * 4 + tb2
                    sy.wait_ge(vFE.h, vFE_wo_b + s2 + 1)
                    sy.dma_start(out=parkA[:, oc2 * 2048 + tb2 * 512: oc2 * 2048 + tb2 * 512 + 512],
                                 in_=xt2[:, (s2 % 2) * 512:(s2 % 2) * 512 + 512]
                                 ).then_inc(dO.h, 16)
                    dO.n += 16
            for oc in range(8):
                if oc >= 2:
                    sy.wait_ge(vFE.h, vFE_wo_b + 4 * (oc - 1))
                sy.dma_start(out=xr2[:, (oc % 2) * 2048:(oc % 2) * 2048 + 2048],
                             in_=parkB[:, oc * 2048: oc * 2048 + 2048]).then_inc(dX.h, 16)
                dX.n += 16
                if oc >= 1:
                    store_x(oc - 1)
            store_x(7)
        blk.sync(fw_sync_x)

        def fw_vec(v):
            for oc in range(8):
                v.wait_ge(dX.h, dX_wo_b + 16 * (oc + 1))
                for tb in range(4):
                    s = oc * 4 + tb
                    v.wait_ge(pe.h, wo_pe_tok[s])
                    if dO_wo_b + 16 * (s - 1) > dO_wo_b:
                        v.wait_ge(dO.h, dO_wo_b + 16 * (s - 1))
                    ps = (wo_a, wo_b)[s % 2]
                    ins = v.scalar_tensor_tensor(
                        xt2[:, (s % 2) * 512:(s % 2) * 512 + 512], ps[:, :],
                        bos[:, oc:oc + 1],
                        xr2[:, (oc % 2) * 2048 + tb * 512:(oc % 2) * 2048 + tb * 512 + 512],
                        ALU.add, ALU.add)
                    vFE.inc(ins)
        blk.vector(fw_vec)

        x3_tok = dO.n
        wops.close()
        wo_es.close()
        b_es.close()

    if stages >= 4:
        f2_es = ExitStack()
        ln4 = f2_es.enter_context(nc.sbuf_tensor("ln4", [128, 16384], BF16, side="right"))
        ln4s = ExitStack()
        xb4 = ln4s.enter_context(nc.sbuf_tensor("xb4", [128, 8192], BF16, side="right"))
        sqb4 = ln4s.enter_context(nc.sbuf_tensor("sqb4", [128, 2048], BF16, side="right"))
        tb24 = ln4s.enter_context(nc.sbuf_tensor("tb24", [128, 1024], F32, side="right"))
        dcw_t4, dxo_t4, doo_t4 = dCW.n, dXO.n, dOO.n
        ln4_tok = emit_ln(parkA, ln4, xb4, sqb4, tb24,
                          gate=lambda g: (g.wait_ge(dO.h, x3_tok),
                                          g.wait_ge(dCW.h, dcw_t4),
                                          g.wait_ge(dXO.h, dxo_t4),
                                          g.wait_ge(dOO.h, doo_t4)))
        ln4s.close()
        emit_ffn(ln4, parkA, yO, False, x3_tok, ln4_tok)
        f2_es.close()

    if stages < 4:
        src_park = parkA if stages in (1, 3) else parkB

        def f_tail(sy):
            sy.wait_ge(dO.h, dO.n)
            sy.dma_start(out=yO, in_=src_park).then_inc(dO.h, 16)
            dO.n += 16
        blk.sync(f_tail)

    fins = [(s, s.n) for s in (dW, dXB, dPX, dX, dK, dO, dCW, dOO, dXO)]

    def f_fin(sy):
        for s, n in fins:
            if n:
                sy.wait_ge(s.h, n)
    blk.sync(f_fin)

    es.close()
    nc.finalize()
    return nc


_BUILT = {}


def _prep_inputs(inputs):
    f32 = np.float32
    g = {k: np.asarray(v, dtype=f32) for k, v in inputs.items()}
    d = {}
    d["aw_dkv"] = img128(g["L0_attn_a_W_dkv"].T.copy())
    d["aw_dq"] = img128(g["L0_attn_a_W_dq"].T.copy())
    d["aw_dqn"] = img128(g["L0_attn_a_W_dq"].copy())
    d["aw_uq"] = img128(g["L0_attn_a_W_uq"].T.copy())
    d["aw_uk"] = img128(g["L0_attn_a_W_uk"].copy())
    d["aw_uv"] = img128(g["L0_attn_a_W_uv"].copy())
    d["aw_o"] = img128(g["L0_attn_a_W_o"].T.copy())
    d["bw_q"] = img128(g["L0_attn_b_Wq"].T.copy())
    d["bw_k"] = img128(g["L0_attn_b_Wk"].T.copy())
    d["bw_v"] = img128(g["L0_attn_b_Wv"].T.copy())
    d["bw_o"] = img128(g["L0_attn_b_Wo"].T.copy())
    d["bq_r"] = g["L0_attn_b_bq"].reshape(1, 1024).copy()
    d["bk_r"] = g["L0_attn_b_bk"].reshape(1, 1024).copy()
    d["bo_i"] = colimg(g["L0_attn_b_bo"])
    d["bv_r"] = g["L0_attn_b_bv"].reshape(1, 1024).copy()
    re_t = np.ascontiguousarray(g["L0_attn_b_rel_emb"].T)
    d["rel2"] = np.concatenate([re_t, re_t], axis=0).copy()
    d["f_w1"] = ogblocks(g["L0_ffn_w1_w"].T.copy())
    d["f_w2"] = ogblocks(g["L0_ffn_w2_w"].T.copy())
    d["f_h1"] = ogblocks(g["L0_ffn_h1_w"].T.copy())
    d["f_h2"] = ogblocks(g["L0_ffn_h2_w"].T.copy())
    d["f_w3"] = ogblocks(g["L0_ffn_w3_w"].T.copy())
    d["s1_i"] = colimg(g["L0_ffn_w1_s"])
    d["s2_i"] = colimg(g["L0_ffn_w2_s"])
    d["sh1_i"] = colimg(g["L0_ffn_h1_s"])
    d["sh2_i"] = colimg(g["L0_ffn_h2_s"])
    d["s3_i"] = colimg(g["L0_ffn_w3_s"])
    x = g["L0_attn_a_x"]
    in_maps = []
    for b in range(NCORE):
        m = dict(d)
        m["xT"] = img128(np.ascontiguousarray(x[b].T))
        in_maps.append(m)
    return in_maps


def kernel(**inputs):
    if STAGES not in _BUILT:
        _BUILT[STAGES] = build(STAGES)
    nc = _BUILT[STAGES]
    in_maps = _prep_inputs(inputs)
    res = run_bass_kernel_spmd(nc, in_maps, core_ids=list(range(NCORE)))
    outs = []
    for b in range(NCORE):
        yi = np.asarray(res.results[b]["y"])
        outs.append(yi.reshape(128, 8, 2048).transpose(1, 0, 2).reshape(1024, 2048).T)
    return np.stack(outs, axis=0).astype(np.float32)


# version 08
# speedup vs baseline: 1.2663x; 1.0368x over previous
import numpy as np
import ml_dtypes
import concourse.bass as bass
import concourse.mybir as mybir
from concourse.ap import AP
from concourse.bass_utils import run_bass_kernel_spmd
from contextlib import ExitStack

F32 = mybir.dt.float32
BF16 = mybir.dt.bfloat16
AF = mybir.ActivationFunctionType
ALU = mybir.AluOpType
BFD = ml_dtypes.bfloat16

T = 2048
D = 1024
LAT = 512
H = 16
HD = 64
DH = 4096
NCORE = 8
STAGES = 4
DEBUG_DUMP = False


def img128(m):
    R, C = m.shape
    assert R % 128 == 0
    return np.ascontiguousarray(
        m.reshape(R // 128, 128, C).transpose(1, 0, 2).reshape(128, (R // 128) * C))


def ogblocks(wt, ogc=128):
    I, O = wt.shape
    n = O // ogc
    a = wt.reshape(I // 128, 128, n, ogc).transpose(2, 1, 0, 3)
    return np.ascontiguousarray(a.reshape(n, 128, (I // 128) * ogc))


def colimg(v):
    return np.ascontiguousarray(v.reshape(-1, 128).T)


class Sem:
    __slots__ = ("h", "n")

    def __init__(self, es, nc, name):
        self.h = es.enter_context(nc.semaphore(name))
        self.n = 0

    def inc(self, instr, k=1):
        instr.then_inc(self.h, k)
        self.n += k
        return self.n


def build(stages):
    nc = bass.Bass("TRN2", target_bir_lowering=False, detect_race_conditions=False)

    def din(name, shape, dtype=F32):
        return nc.dram_tensor(name, list(shape), dtype, kind="ExternalInput").ap()

    xT = din("xT", (128, 8 * T))
    aw_dkv = din("aw_dkv", (128, 4096))
    aw_dq = din("aw_dq", (128, 4096))
    aw_dqn = din("aw_dqn", (128, 4096))
    aw_uq = din("aw_uq", (128, 4096))
    aw_uk = din("aw_uk", (128, 4096))
    aw_uv = din("aw_uv", (128, 4096))
    aw_o = din("aw_o", (128, 8192))
    bw_q = din("bw_q", (128, 8192))
    bw_k = din("bw_k", (128, 8192))
    bw_v = din("bw_v", (128, 8192))
    bw_o = din("bw_o", (128, 8192))
    bq_r = din("bq_r", (1, 1024))
    bk_r = din("bk_r", (1, 1024))
    bo_i = din("bo_i", (128, 8))
    bv_r = din("bv_r", (1, 1024))
    rel2_d = din("rel2", (128, 4095))
    f_w1 = din("f_w1", (32, 128, 1024))
    f_w2 = din("f_w2", (32, 128, 1024))
    f_h1 = din("f_h1", (32, 128, 4096))
    f_h2 = din("f_h2", (32, 128, 4096))
    f_w3 = din("f_w3", (8, 128, 4096))
    s1_i = din("s1_i", (128, 32))
    s2_i = din("s2_i", (128, 32))
    sh1_i = din("sh1_i", (128, 32))
    sh2_i = din("sh2_i", (128, 32))
    s3_i = din("s3_i", (128, 8))

    yO = nc.dram_tensor("y", [128, 8 * T], F32, kind="ExternalOutput").ap()
    parkA = nc.dram_tensor("parkA", [128, 8 * T], F32, kind="Internal").ap()
    parkB = nc.dram_tensor("parkB", [128, 8 * T], F32, kind="Internal").ap()
    osc_d = nc.dram_tensor("oscr", [128, 8 * T], BF16, kind="Internal").ap()
    ln3st = nc.dram_tensor("ln3st", [128, 8 * T], BF16, kind="Internal").ap()
    pexp = nc.dram_tensor("pexp", [H * 2048 * 4095], BF16, kind="Internal").ap()
    cach = {
        "w1": nc.dram_tensor("c_w1", [32, 128, 1024], BF16, kind="Internal").ap(),
        "w2": nc.dram_tensor("c_w2", [32, 128, 1024], BF16, kind="Internal").ap(),
        "h1": nc.dram_tensor("c_h1", [32, 128, 4096], BF16, kind="Internal").ap(),
        "h2": nc.dram_tensor("c_h2", [32, 128, 4096], BF16, kind="Internal").ap(),
        "w3": nc.dram_tensor("c_w3", [8, 128, 4096], BF16, kind="Internal").ap(),
    }

    es = ExitStack()

    def sbuf(name, shape, dtype, side="left"):
        return es.enter_context(nc.sbuf_tensor(name, list(shape), dtype, side=side))

    ones_b = sbuf("ones_b", (128, 512), BF16)
    ones_f = sbuf("ones_f", (128, 128), F32)
    epsT = sbuf("epsT", (128, 1), F32)
    s1s = sbuf("s1s", (128, 32), F32)
    s2s = sbuf("s2s", (128, 32), F32)
    sh1s = sbuf("sh1s", (128, 32), F32)
    sh2s = sbuf("sh2s", (128, 32), F32)
    s3s = sbuf("s3s", (128, 8), F32)
    bq_row_s = sbuf("bq_row_s", (1, 1024), BF16)
    bk_row_s = sbuf("bk_row_s", (1, 1024), BF16)
    bos = sbuf("bos", (128, 8), F32)
    bvs = sbuf("bvs", (1, 1024), BF16)
    V = sbuf("V", (128, 3072), F32)

    dW = Sem(es, nc, "dW")
    dXB = Sem(es, nc, "dXB")
    dPX = Sem(es, nc, "dPX")
    dX = Sem(es, nc, "dX")
    dK = Sem(es, nc, "dK")
    dO = Sem(es, nc, "dO")
    pe = Sem(es, nc, "pe")
    pPV = Sem(es, nc, "pPV")
    pW = Sem(es, nc, "pW")
    aG = Sem(es, nc, "aG")
    aEX = Sem(es, nc, "aEX")
    aEV = Sem(es, nc, "aEV")
    aFE = Sem(es, nc, "aFE")
    vG = Sem(es, nc, "vG")
    vRC = Sem(es, nc, "vRC")
    vNM = Sem(es, nc, "vNM")
    vEM = Sem(es, nc, "vEM")
    vFE = Sem(es, nc, "vFE")
    gMK = Sem(es, nc, "gMK")
    pLN = Sem(es, nc, "pLN")
    dCW = Sem(es, nc, "dCW")
    dOO = Sem(es, nc, "dOO")
    dXO = Sem(es, nc, "dXO")

    blk = es.enter_context(nc.Block())

    def p0_sync(sy):
        for dst, src in ((s1s, s1_i), (s2s, s2_i), (sh1s, sh1_i), (sh2s, sh2_i),
                         (s3s, s3_i), (bos, bo_i)):
            sy.dma_start(out=dst[:], in_=src).then_inc(dX.h, 16)
            dX.n += 16

    def p0_gp(g):
        for dst, src in ((bq_row_s, bq_r), (bk_row_s, bk_r), (bvs, bv_r)):
            g.dma_start(out=dst[:], in_=src).then_inc(dW.h, 16)
            dW.n += 16

    def p0_vec(v):
        v.memset(ones_b[:], 1.0)
        v.memset(ones_f[:], 1.0)
        vG.inc(v.memset(epsT[:], 1e-5))

    blk.sync(p0_sync)
    blk.gpsimd(p0_gp)
    blk.vector(p0_vec)
    tok_consts_v = vG.n
    tok_consts_w = dW.n
    tok_consts_d = dX.n

    ln_ctr = [0]

    def emit_ln(src_ap, dst, xb, sqb, tb2, gate=None):
        ln_ctr[0] += 1
        u = str(ln_ctr[0])
        ps_es = ExitStack()
        st_sum = ps_es.enter_context(nc.psum_tensor("ln_s" + u, [1, 1024], F32))
        st_sq = ps_es.enter_context(nc.psum_tensor("ln_q" + u, [1, 1024], F32))
        bcR = ps_es.enter_context(nc.psum_tensor("ln_r" + u, [128, 512], F32))
        bcM = ps_es.enter_context(nc.psum_tensor("ln_m" + u, [128, 512], F32))
        half_tok = [0]
        pLN_b = [pLN.n]
        prev_affine = [0]
        prev_stats_rd = [0]
        prev_stats = [0]
        for th in (0, 1):
            o = th * 1024
            xb_tok = []

            def w_gp(g, th=th, xb_tok=xb_tok):
                if gate is not None and th == 0:
                    gate(g)
                if th == 1:
                    g.wait_ge(vG.h, prev_affine[0])
                for c in range(8):
                    ins = g.dma_start(
                        out=xb[:, c * 1024:(c + 1) * 1024],
                        in_=src_ap[:, c * 2048 + th * 1024: c * 2048 + th * 1024 + 1024])
                    xb_tok.append(dXB.inc(ins, 16))
            blk.gpsimd(w_gp)
            sq_tok = []

            def w_act(sc, th=th, xb_tok=xb_tok, sq_tok=sq_tok):
                for c in range(8):
                    sc.wait_ge(dXB.h, xb_tok[c])
                    gi = th * 8 + c
                    if gi >= 2:
                        sc.wait_ge(pLN.h, pLN_b[0] + gi - 1)
                    ins = sc.activation(sqb[:, (c % 2) * 1024:(c % 2) * 1024 + 1024],
                                        xb[:, c * 1024:(c + 1) * 1024], AF.Square)
                    sq_tok.append(aG.inc(ins))
            blk.scalar(w_act)
            stat_tok = [0]

            def w_pe(t, th=th, xb_tok=xb_tok, sq_tok=sq_tok, stat_tok=stat_tok):
                if th == 1:
                    t.wait_ge(vG.h, prev_stats_rd[0])
                for c in range(8):
                    t.wait_ge(dXB.h, xb_tok[c])
                    t.wait_ge(aG.h, sq_tok[c])
                    for nb in (0, 1):
                        sl = slice(c * 1024 + nb * 512, c * 1024 + nb * 512 + 512)
                        sq_sl = slice((c % 2) * 1024 + nb * 512, (c % 2) * 1024 + nb * 512 + 512)
                        t.matmul(st_sum[0:1, nb * 512:nb * 512 + 512], ones_b[:, 0:1],
                                 xb[:, sl], start=(c == 0), stop=(c == 7))
                        ins = t.matmul(st_sq[0:1, nb * 512:nb * 512 + 512], ones_b[:, 0:1],
                                       sqb[:, sq_sl], start=(c == 0), stop=(c == 7))
                    pLN.inc(ins)
                stat_tok[0] = pLN.n
            blk.tensor(w_pe)
            prev_stats[0] = pe.n
            va_tok = [0]

            def w_vec(v, stat_tok=stat_tok, va_tok=va_tok):
                v.wait_ge(pLN.h, stat_tok[0])
                v.tensor_scalar(V[0:1, 0:1024], st_sum[0:1, :], 1.0 / D, None, ALU.mult)
                v.tensor_scalar(V[0:1, 1024:2048], st_sq[0:1, :], 1.0 / D, None, ALU.mult)
                v.tensor_mul(V[0:1, 2048:3072], V[0:1, 0:1024], V[0:1, 0:1024])
                ins = v.tensor_sub(V[0:1, 1024:2048], V[0:1, 1024:2048], V[0:1, 2048:3072])
                va_tok[0] = vG.inc(ins)
            blk.vector(w_vec)
            prev_stats_rd[0] = vG.n
            sd_tok = [0]

            def w_act2(sc, va_tok=va_tok, sd_tok=sd_tok):
                sc.wait_ge(vG.h, va_tok[0])
                ins = sc.activation(V[0:1, 2048:3072], V[0:1, 1024:2048], AF.Sqrt,
                                    bias=epsT[0:1, 0:1])
                sd_tok[0] = aG.inc(ins)
            blk.scalar(w_act2)
            vb_tok = [0]

            def w_vec2(v, sd_tok=sd_tok, vb_tok=vb_tok):
                v.wait_ge(aG.h, sd_tok[0])
                v.reciprocal(V[0:1, 1024:2048], V[0:1, 2048:3072])
                ins = v.tensor_mul(V[0:1, 2048:3072], V[0:1, 0:1024], V[0:1, 1024:2048])
                vb_tok[0] = vG.inc(ins)
            blk.vector(w_vec2)
            for lb in (0, 1):
                bc_tok = [0]

                def w_pe2(t, lb=lb, vb_tok=vb_tok, bc_tok=bc_tok):
                    t.wait_ge(vG.h, max(vb_tok[0], prev_affine[0]))
                    t.matmul(bcR[:, :], ones_f[0:1, :],
                             V[0:1, 1024 + lb * 512:1024 + lb * 512 + 512],
                             start=True, stop=True)
                    ins = t.matmul(bcM[:, :], ones_f[0:1, :],
                                   V[0:1, 2048 + lb * 512:2048 + lb * 512 + 512],
                                   start=True, stop=True)
                    bc_tok[0] = pe.inc(ins)
                blk.tensor(w_pe2)

                def w_vec3(v, lb=lb, th=th, bc_tok=bc_tok):
                    v.wait_ge(pe.h, bc_tok[0])
                    for c in range(8):
                        sl = slice(c * 1024 + lb * 512, c * 1024 + lb * 512 + 512)
                        v.tensor_tensor(tb2[:, (c % 2) * 512:(c % 2) * 512 + 512],
                                        xb[:, sl], bcR[:, :], ALU.mult)
                        ins = v.tensor_tensor(
                            dst[:, c * 2048 + th * 1024 + lb * 512:
                                c * 2048 + th * 1024 + lb * 512 + 512],
                            tb2[:, (c % 2) * 512:(c % 2) * 512 + 512], bcM[:, :],
                            ALU.subtract)
                        vG.inc(ins)
                blk.vector(w_vec3)
                prev_affine[0] = vG.n
            if th == 0:
                half_tok[0] = vG.n
        ps_es.close()
        return half_tok[0], vG.n

    blk.tensor(lambda t: t.wait_ge(vG.h, tok_consts_v))
    blk.scalar(lambda sc: (sc.wait_ge(vG.h, tok_consts_v), sc.wait_ge(dX.h, tok_consts_d)))
    blk.vector(lambda v: v.wait_ge(dX.h, tok_consts_d))

    prj_es = [None]

    def emit_proj(nk, nm, nn, nstep, lhsT_fn, rhs_fn, evac_fn, ps_a, ps_b,
                  dma_toks=(), bias_row=None):
        aFE_b = aFE.n
        tiles = [(mi, ni) for mi in range(nm) for ni in range(nn)]
        pe_tok = []

        def f_pe(t):
            for sem, tok in dma_toks:
                t.wait_ge(sem.h, tok)
            for g, (mi, ni) in enumerate(tiles):
                ps = (ps_a, ps_b)[g % 2]
                if aFE_b + g - 1 > 0:
                    t.wait_ge(aFE.h, aFE_b + g - 1)
                last = nk - 1 if bias_row is None else nk
                for kc in range(nk):
                    ins = t.matmul(ps[:, :], lhsT_fn(kc, mi), rhs_fn(kc, ni),
                                   start=(kc == 0), stop=(kc == last))
                if bias_row is not None:
                    ins = t.matmul(ps[:, :], bias_row[0:1, mi * 128:mi * 128 + 128],
                                   ones_b[0:1, 0:nstep], start=False, stop=True)
                pe_tok.append(pe.inc(ins))

        def f_act(sc):
            for g, (mi, ni) in enumerate(tiles):
                ps = (ps_a, ps_b)[g % 2]
                sc.wait_ge(pe.h, pe_tok[g])
                aFE.inc(evac_fn(sc, mi, ni, ps[:, :]))
        blk.tensor(f_pe)
        blk.scalar(f_act)
        return aFE.n

    a_es = ExitStack()

    def rsb(name, shape, dtype):
        return a_es.enter_context(nc.sbuf_tensor(name, list(shape), dtype, side="right"))

    def lsb(name, shape, dtype):
        return a_es.enter_context(nc.sbuf_tensor(name, list(shape), dtype, side="left"))

    ckv_i = lsb("ckv_i", (128, 8192), BF16)
    keT_i = lsb("keT_i", (128, 4096), BF16)
    veT_i = lsb("veT_i", (128, 4096), BF16)

    p2_es = ExitStack()
    q2_i = p2_es.enter_context(nc.sbuf_tensor("q2_i", [128, 8192], BF16, side="right"))
    ln1_es = ExitStack()
    ln1 = ln1_es.enter_context(nc.sbuf_tensor("ln1", [128, 16384], BF16, side="right"))
    ln_es = ExitStack()
    xb1 = ln_es.enter_context(nc.sbuf_tensor("xb1", [128, 8192], BF16, side="right"))
    sqb1 = ln_es.enter_context(nc.sbuf_tensor("sqb1", [128, 2048], BF16, side="right"))
    tb21 = ln_es.enter_context(nc.sbuf_tensor("tb21", [128, 1024], F32, side="right"))
    ln1_tok = emit_ln(xT, ln1, xb1, sqb1, tb21)[1]
    ln_es.close()

    p2ps = ExitStack()
    pr_a = p2ps.enter_context(nc.psum_tensor("pr_a", [128, 512], F32))
    pr_b = p2ps.enter_context(nc.psum_tensor("pr_b", [128, 512], F32))

    def load_w(g, buf_slices, srcs):
        toks = []
        for sl, src in zip(buf_slices, srcs):
            toks.append(dW.inc(g.dma_start(out=sl, in_=src), 16))
        return toks

    aw1_es = ExitStack()
    aw1 = aw1_es.enter_context(nc.sbuf_tensor("aw1", [128, 8192], BF16, side="right"))
    t_aw1 = []
    blk.gpsimd(lambda g: (g.wait_ge(vG.h, ln1_tok),
                          t_aw1.extend(load_w(
                              g, [aw1[:, 0:4096], aw1[:, 4096:8192]], [aw_dkv, aw_dq]))))

    def cp_evac(dst_fn):
        def f(sc, mi, ni, ps):
            return sc.copy(dst_fn(mi, ni), ps)
        return f

    emit_proj(8, 4, 4, 512,
              lambda kc, mi: aw1[:, kc * 512 + mi * 128: kc * 512 + mi * 128 + 128],
              lambda kc, ni: ln1[:, kc * 2048 + ni * 512: kc * 2048 + ni * 512 + 512],
              cp_evac(lambda mi, ni: ckv_i[:, mi * 2048 + ni * 512: mi * 2048 + ni * 512 + 512]),
              pr_a, pr_b, dma_toks=[(dW, t_aw1[0]), (vG, ln1_tok)])
    ckv_done = aFE.n
    emit_proj(8, 4, 4, 512,
              lambda kc, mi: aw1[:, 4096 + kc * 512 + mi * 128: 4096 + kc * 512 + mi * 128 + 128],
              lambda kc, ni: ln1[:, kc * 2048 + ni * 512: kc * 2048 + ni * 512 + 512],
              cp_evac(lambda mi, ni: q2_i[:, mi * 2048 + ni * 512: mi * 2048 + ni * 512 + 512]),
              pr_a, pr_b, dma_toks=[(dW, t_aw1[1]), (vG, ln1_tok)])
    q2_done = aFE.n
    aw1_free_tok = pe.n
    aw1_es.close()
    ln1_es.close()

    aw2_es = ExitStack()
    aw2 = aw2_es.enter_context(nc.sbuf_tensor("aw2", [128, 8192], BF16, side="right"))
    B_es = ExitStack()
    B_i = B_es.enter_context(nc.sbuf_tensor("B_i", [128, 8192], BF16, side="right"))
    t_aw2 = []
    blk.gpsimd(lambda g: (g.wait_ge(pe.h, aw1_free_tok),
                          t_aw2.extend(load_w(
                              g, [aw2[:, 0:4096], aw2[:, 4096:8192]], [aw_uq, aw_dqn]))))
    emit_proj(4, 8, 2, 512,
              lambda kc, mi: aw2[:, kc * 1024 + mi * 128: kc * 1024 + mi * 128 + 128],
              lambda kc, ni: aw2[:, 4096 + kc * 1024 + ni * 512: 4096 + kc * 1024 + ni * 512 + 512],
              cp_evac(lambda mi, ni: B_i[:, mi * 1024 + ni * 512: mi * 1024 + ni * 512 + 512]),
              pr_a, pr_b, dma_toks=[(dW, t_aw2[1])])
    B_done = aFE.n

    aw3_es = ExitStack()
    aw3 = aw3_es.enter_context(nc.sbuf_tensor("aw3", [128, 8192], BF16, side="right"))
    t_aw3 = []
    blk.gpsimd(lambda g: t_aw3.extend(load_w(
        g, [aw3[:, 0:4096], aw3[:, 4096:8192]], [aw_uk, aw_uv])))
    emit_proj(8, 4, 2, 512,
              lambda kc, mi: aw3[:, kc * 512 + mi * 128: kc * 512 + mi * 128 + 128],
              lambda kc, ni: B_i[:, kc * 1024 + ni * 512: kc * 1024 + ni * 512 + 512],
              cp_evac(lambda mi, ni: keT_i[:, mi * 1024 + ni * 512: mi * 1024 + ni * 512 + 512]),
              pr_a, pr_b, dma_toks=[(dW, t_aw3[0]), (aFE, B_done)])
    keT_done = aFE.n
    aw4_es = ExitStack()
    aw4 = aw4_es.enter_context(nc.sbuf_tensor("aw4", [128, 8192], BF16, side="right"))
    t_aw4 = []
    blk.gpsimd(lambda g: t_aw4.extend(load_w(g, [aw4[:, 0:8192]], [aw_o])))
    emit_proj(8, 4, 2, 512,
              lambda kc, mi: aw3[:, 4096 + kc * 512 + mi * 128: 4096 + kc * 512 + mi * 128 + 128],
              lambda kc, ni: aw4[:, kc * 1024 + ni * 512: kc * 1024 + ni * 512 + 512],
              cp_evac(lambda mi, ni: veT_i[:, mi * 1024 + ni * 512: mi * 1024 + ni * 512 + 512]),
              pr_a, pr_b, dma_toks=[(dW, t_aw4[0])])
    veT_done = aFE.n
    aw4_es.close()

    q_i = lsb("q_i", (128, 16384), BF16)
    emit_proj(4, 8, 4, 512,
              lambda kc, mi: aw2[:, kc * 1024 + mi * 128: kc * 1024 + mi * 128 + 128],
              lambda kc, ni: q2_i[:, kc * 2048 + ni * 512: kc * 2048 + ni * 512 + 512],
              cp_evac(lambda mi, ni: q_i[:, mi * 2048 + ni * 512: mi * 2048 + ni * 512 + 512]),
              pr_a, pr_b, dma_toks=[(aFE, q2_done)])
    qT_free = pe.n
    aw3_es.close()
    B_es.close()
    aw2_es.close()
    p2_es.close()

    keys_i = lsb("keys_i", (128, 16384), BF16)
    u_i = lsb("u_i", (128, 16640), BF16)
    emit_proj(4, 8, 4, 512,
              lambda kc, mi: keT_i[:, kc * 1024 + mi * 128: kc * 1024 + mi * 128 + 128],
              lambda kc, ni: ckv_i[:, kc * 2048 + ni * 512: kc * 2048 + ni * 512 + 512],
              cp_evac(lambda mi, ni: keys_i[:, mi * 2048 + ni * 512: mi * 2048 + ni * 512 + 512]),
              pr_a, pr_b, dma_toks=[(aFE, keT_done)])
    keys_done = aFE.n

    u_ones_tok = [0]
    pe_gate_u = pe.n
    blk.vector(lambda v: (v.wait_ge(pe.h, pe_gate_u),
                          u_ones_tok.__setitem__(0, vG.inc(v.memset(
                              u_i.ap().rearrange("p (kc h s) -> p kc h s",
                                                 kc=16, h=16)[:, :, :, 64:65], 1.0)))))
    p2ps.close()
    ups = ExitStack()
    u_ps_a = ups.enter_context(nc.psum_tensor("u_ps_a", [128, 1024], F32))
    u_ps_b = ups.enter_context(nc.psum_tensor("u_ps_b", [128, 1024], F32))
    u_pe_tok = []
    aFE_u_b = aFE.n

    def f_u_pe(t):
        t.wait_ge(aFE.h, max(veT_done, ckv_done, keys_done))
        for kc in range(16):
            ps = (u_ps_a, u_ps_b)[kc % 2]
            if kc >= 2:
                t.wait_ge(aFE.h, aFE_u_b + kc - 1)
            for lc in range(4):
                for nb in (0, 1):
                    ins = t.matmul(ps[:, nb * 512:nb * 512 + 512],
                                   ckv_i[:, lc * 2048 + kc * 128: lc * 2048 + kc * 128 + 128],
                                   veT_i[:, lc * 1024 + nb * 512: lc * 1024 + nb * 512 + 512],
                                   start=(lc == 0), stop=(lc == 3))
            u_pe_tok.append(pe.inc(ins))

    def f_u_act(sc):
        sc.wait_ge(vG.h, u_ones_tok[0])
        for kc in range(16):
            ps = (u_ps_a, u_ps_b)[kc % 2]
            sc.wait_ge(pe.h, u_pe_tok[kc])
            dst = u_i.ap().rearrange("p (kc h s) -> p kc h s", kc=16, h=16)[:, kc, :, 0:64]
            aFE.inc(sc.copy(dst, ps.ap().rearrange("p (h d) -> p h d", h=16)))
    blk.tensor(f_u_pe)
    blk.scalar(f_u_act)
    u_done = aFE.n
    ups.close()

    p3_es = ExitStack()

    def r3(name, shape, dtype):
        return p3_es.enter_context(nc.sbuf_tensor(name, list(shape), dtype, side="right"))

    E2 = r3("E2", (128, 2048), BF16)
    nrm = r3("nrm", (128, 2048), F32)
    o_ev = r3("o_ev", (128, 2048), F32)
    xr = r3("xr", (128, 4096), F32)
    xt = r3("xt", (128, 2048), F32)
    p3ps = ExitStack()
    qk_ps = p3ps.enter_context(nc.psum_tensor("qk_ps", [128, 2048], F32))
    pv_ps = p3ps.enter_context(nc.psum_tensor("pv_ps", [65, 1024], F32))
    bc_ps = p3ps.enter_context(nc.psum_tensor("bc_ps", [64, 1024], F32))

    tiles3 = []
    for h in range(H):
        for lbh in (0, 1):
            for rc in range(8 * (lbh + 1)):
                tiles3.append((h, lbh, rc))
    G3 = len(tiles3)
    masked3 = [(rc >= 8 * lbh) for (h, lbh, rc) in tiles3]
    nmask3 = np.cumsum(masked3).tolist()
    first3 = [g for g in range(G3) if tiles3[g][2] == 0]
    last3 = [g - 1 for g in first3[1:]] + [G3 - 1]
    idx_of = {}
    for i, g in enumerate(first3):
        for gg in range(g, last3[i] + 1):
            idx_of[gg] = i
    aEX_b, aEV_b, vRC_b, vNM_b, pPV_b, gMK_b, dO_b, dX_b = (
        aEX.n, aEV.n, vRC.n, vNM.n, pPV.n, gMK.n, dO.n, dX.n)

    def acc_ranges(lo, lo_next, last):
        rs = []
        if lo < 512:
            rs.append((lo, 512, last or lo_next >= 512))
        rs.append((max(lo, 512), 1024, last))
        return rs

    def f3_sync(sy):
        sy.wait_ge(pe.h, qT_free)

        def store(idx):
            h, lbh = idx // 2, idx % 2
            hc, rb = h // 2, (h % 2) * 64
            sy.wait_ge(vNM.h, vNM_b + idx + 1)
            sy.dma_start(
                out=parkA[rb:rb + 64, hc * 2048 + lbh * 1024: hc * 2048 + lbh * 1024 + 1024],
                in_=xt[0:64, (idx % 2) * 1024:(idx % 2) * 1024 + 1024]).then_inc(dO.h, 16)
            dO.n += 16
        for idx in range(2 * H):
            h, lbh = idx // 2, idx % 2
            hc, rb = h // 2, (h % 2) * 64
            if idx >= 4:
                sy.wait_ge(vNM.h, vNM_b + idx - 3)
            sy.dma_start(
                out=xr[0:64, (h % 2) * 2048 + lbh * 1024:(h % 2) * 2048 + lbh * 1024 + 1024],
                in_=xT[rb:rb + 64, hc * 2048 + lbh * 1024: hc * 2048 + lbh * 1024 + 1024]
            ).then_inc(dX.h, 16)
            dX.n += 16
            if idx >= 2:
                store(idx - 2)
        store(2 * H - 2)
        store(2 * H - 1)
    blk.sync(f3_sync)

    qk_tok = [0] * G3
    bc_tok = [0] * (2 * H)

    def f3_pe(t):
        t.wait_ge(aFE.h, max(keys_done, u_done))
        t.wait_ge(vG.h, u_ones_tok[0])

        def emit_qk(g):
            h, lbh, rc = tiles3[g]
            hc, rb = h // 2, (h % 2) * 64
            lo = max(0, 128 * (rc - 8 * lbh))
            if aEX_b + g - 1 > 0:
                t.wait_ge(aEX.h, aEX_b + g - 1)
            for c0, c1 in (((lo, 512), (512, 1024)) if lo < 512 else ((lo, 1024),)):
                ins = t.matmul(
                    qk_ps[:, (g % 2) * 1024 + c0:(g % 2) * 1024 + c1],
                    keys_i[rb:rb + 64, hc * 2048 + rc * 128: hc * 2048 + rc * 128 + 128],
                    q_i[rb:rb + 64, hc * 2048 + lbh * 1024 + c0:
                        hc * 2048 + lbh * 1024 + c1],
                    start=True, stop=True)
            qk_tok[g] = pe.inc(ins)

        def emit_pv(g):
            h, lbh, rc = tiles3[g]
            idx = idx_of[g]
            lo = max(0, 128 * (rc - 8 * lbh))
            t.wait_ge(aEX.h, aEX_b + g + 1)
            if masked3[g]:
                t.wait_ge(gMK.h, gMK_b + nmask3[g])
            if rc == 0 and idx > 0:
                t.wait_ge(aEV.h, aEV_b + idx)
            lo_next = max(0, 128 * (rc + 1 - 8 * lbh))
            for c0, c1, st in acc_ranges(lo, lo_next, rc == 8 * (lbh + 1) - 1):
                ins = t.matmul(
                    pv_ps[:, c0:c1],
                    u_i[:, rc * 1040 + h * 65: rc * 1040 + h * 65 + 65],
                    E2[:, (g % 2) * 1024 + c0:(g % 2) * 1024 + c1],
                    start=(rc == 0), stop=st)
            pPV.inc(ins)

        def emit_bc(idx):
            t.wait_ge(aEV.h, aEV_b + idx + 1)
            if idx > 0:
                t.wait_ge(vNM.h, vNM_b + idx)
            for cb in (0, 1):
                ins = t.matmul(
                    bc_ps[:, cb * 512:cb * 512 + 512], ones_f[64:65, 0:64],
                    nrm[64:65, (idx % 2) * 1024 + cb * 512:(idx % 2) * 1024 + cb * 512 + 512],
                    start=True, stop=True)
            bc_tok[idx] = pe.inc(ins)

        for g in range(G3):
            emit_qk(g)
            if g >= 1:
                emit_pv(g - 1)
            if tiles3[g][2] == 1 and idx_of[g] > 0:
                emit_bc(idx_of[g] - 1)
        emit_pv(G3 - 1)
        emit_bc(2 * H - 1)
    blk.tensor(f3_pe)

    ev_tok = [0] * (2 * H)

    def f3_act(sc):
        for g in range(G3):
            h, lbh, rc = tiles3[g]
            idx = idx_of[g]
            lo = max(0, 128 * (rc - 8 * lbh))
            sc.wait_ge(pe.h, qk_tok[g])
            if pPV_b + g - 1 > 0:
                sc.wait_ge(pPV.h, pPV_b + g - 1)
            aEX.inc(sc.activation(E2[:, (g % 2) * 1024 + lo:(g % 2) * 1024 + 1024],
                                  qk_ps[:, (g % 2) * 1024 + lo:(g % 2) * 1024 + 1024],
                                  AF.Exp, scale=0.125))
            if g == last3[idx]:
                sc.wait_ge(pPV.h, pPV_b + g + 1)
                if idx >= 2:
                    sc.wait_ge(vNM.h, vNM_b + idx - 1)
                sc.copy(o_ev[0:64, (idx % 2) * 1024:(idx % 2) * 1024 + 1024],
                        pv_ps[0:64, :])
                sc.activation(nrm[64:65, (idx % 2) * 1024:(idx % 2) * 1024 + 1024],
                              pv_ps[64:65, :], AF.Ln)
                ins = sc.activation(nrm[64:65, (idx % 2) * 1024:(idx % 2) * 1024 + 1024],
                                    nrm[64:65, (idx % 2) * 1024:(idx % 2) * 1024 + 1024],
                                    AF.Exp, scale=-1.0)
                ev_tok[idx] = aEV.inc(ins)
    blk.scalar(f3_act)

    def f3_pool(g_eng):
        for g in range(G3):
            if not masked3[g]:
                continue
            h, lbh, rc = tiles3[g]
            lo = max(0, 128 * (rc - 8 * lbh))
            g_eng.wait_ge(aEX.h, aEX_b + g + 1)
            ins = g_eng.affine_select(
                E2[:, (g % 2) * 1024 + lo:(g % 2) * 1024 + 1024],
                E2[:, (g % 2) * 1024 + lo:(g % 2) * 1024 + 1024],
                [[1, 1024 - lo]], ALU.is_ge, 0.0,
                base=lbh * 1024 - rc * 128 + lo, channel_multiplier=-1)
            gMK.inc(ins)
    blk.gpsimd(f3_pool)

    def f3_vec(v):
        for idx in range(2 * H):
            h, lbh = idx // 2, idx % 2
            v.wait_ge(pe.h, bc_tok[idx])
            v.wait_ge(dX.h, dX_b + 16 * (idx + 1))
            osl = o_ev[0:64, (idx % 2) * 1024:(idx % 2) * 1024 + 1024]
            v.tensor_tensor(osl, osl, bc_ps[0:64, :], ALU.mult)
            if dO_b + 16 * (idx - 1) > 0:
                v.wait_ge(dO.h, dO_b + 16 * (idx - 1))
            ins = v.tensor_tensor(
                xt[0:64, (idx % 2) * 1024:(idx % 2) * 1024 + 1024], osl,
                xr[0:64, (h % 2) * 2048 + lbh * 1024:(h % 2) * 2048 + lbh * 1024 + 1024],
                ALU.add)
            vNM.inc(ins)
    blk.vector(f3_vec)

    x1_tok = dO.n

    p3ps.close()
    p3_es.close()
    a_es.close()

    ffn_ctr = [0]

    def emit_ffn(ln_src, park_src, dst, first, x_gate, ln_tok):
        ffn_ctr[0] += 1
        fu = str(ffn_ctr[0])
        f_es = ExitStack()

        def rf(name, shape, dtype):
            return f_es.enter_context(nc.sbuf_tensor(name + fu, list(shape), dtype, side="right"))

        HA = rf("HA", (128, 32768), BF16)
        HB = rf("HB", (128, 32768), BF16)
        wb = rf("wb", (128, 8192), BF16)
        xold = rf("xold", (128, 1024), F32)
        xn = rf("xn", (128, 1024), F32)
        tAB = rf("tAB", (128, 1024), BF16)
        fps = ExitStack()
        psA = [fps.enter_context(nc.psum_tensor(f"fpsA{i}" + fu, [128, 512], F32)) for i in range(2)]
        psB = [fps.enter_context(nc.psum_tensor(f"fpsB{i}" + fu, [128, 512], F32)) for i in range(2)]

        stages_d = [
            ("A", 32, 8, (f_w1, f_w2), ("w1", "w2"), 1024),
            ("B", 32, 32, (f_h1,), ("h1",), 4096),
            ("C", 32, 32, (f_h2,), ("h2",), 4096),
            ("D", 8, 32, (f_w3,), ("w3",), 4096),
        ]
        prev_stage_done = [pe.n, aFE.n, vFE.n]

        for th in (0, 1):
            for kind, n_og, n_ic, srcs, cks, wcols in stages_d:
                nsrc = len(srcs)
                use_cache = (not first) or th == 1
                dma_sem = dX if use_cache else dW
                dma_b = dma_sem.n
                pW_b = pW.n
                aFE_b = aFE.n
                vFE_b = vFE.n
                dO_b = dO.n
                dX_b = dX.n
                dXO_b_d = dXO.n
                pe_prev, aFE_prev, vFE_prev = prev_stage_done

                pe_tok = {}

                def f_pe(t, kind=kind, n_og=n_og, n_ic=n_ic, nsrc=nsrc, wcols=wcols,
                         dma_sem=dma_sem, dma_b=dma_b, pW_b=pW_b, aFE_b=aFE_b,
                         vFE_b=vFE_b, dX_b=dX_b, th=th, pe_tok=pe_tok,
                         aFE_prev=aFE_prev, vFE_prev=vFE_prev):
                    if kind == "A":
                        t.wait_ge(vG.h, ln_tok[th])
                    if aFE_prev:
                        t.wait_ge(aFE.h, aFE_prev)
                    if vFE_prev:
                        t.wait_ge(vFE.h, vFE_prev)
                    for og in range(n_og):
                        t.wait_ge(dma_sem.h, dma_b + 16 * nsrc * (og + 1))
                        sl = wb[:, (og % 2) * 4096:(og % 2) * 4096 + 4096]
                        for tb in (0, 1):
                            s = og * 2 + tb
                            rhs_off = th * 1024 + tb * 512
                            if kind == "A":
                                if aFE_b + s - 1 > 0:
                                    t.wait_ge(aFE.h, aFE_b + s - 1)
                                if vFE_b + s - 1 > 0:
                                    t.wait_ge(vFE.h, vFE_b + s - 1)
                                for w_i in (0, 1):
                                    ps = (psA, psB)[w_i][s % 2]
                                    for ic in range(8):
                                        ins = t.matmul(
                                            ps[:, :],
                                            sl[:, w_i * 1024 + ic * 128: w_i * 1024 + ic * 128 + 128],
                                            ln_src[:, ic * 2048 + rhs_off: ic * 2048 + rhs_off + 512],
                                            start=(ic == 0), stop=(ic == 7))
                                    if w_i == 0:
                                        pe_tok[(og, tb, 0)] = pe.inc(ins)
                                if tb == 1:
                                    pW.inc(ins)
                                    pe_tok[(og, tb, 1)] = None
                                else:
                                    pe_tok[(og, tb, 1)] = pe.inc(ins)
                            else:
                                if aFE_b + s - 1 > 0 and kind != "D":
                                    t.wait_ge(aFE.h, aFE_b + s - 1)
                                if vFE_b + s - 1 > 0 and kind == "D":
                                    t.wait_ge(vFE.h, vFE_b + s - 1)
                                src_h = HA if kind in ("B", "D") else HB
                                ps = psA[s % 2]
                                for ic in range(n_ic):
                                    ins = t.matmul(
                                        ps[:, :], sl[:, ic * 128: ic * 128 + 128],
                                        src_h[:, ic * 1024 + tb * 512: ic * 1024 + tb * 512 + 512],
                                        start=(ic == 0), stop=(ic == n_ic - 1))
                                if tb == 1:
                                    pW.inc(ins)
                                    pe_tok[(og, tb)] = None
                                else:
                                    pe_tok[(og, tb)] = pe.inc(ins)
                blk.tensor(f_pe)
                pe_stage_done = pe.n

                if kind in ("A", "B", "C"):
                    def f_act(sc, kind=kind, n_og=n_og, th=th, pe_tok=pe_tok,
                              pW_b=pW_b, vFE_b=vFE_b):
                        for og in range(n_og):
                            for tb in (0, 1):
                                s = og * 2 + tb
                                if kind == "A":
                                    tok = pe_tok[(og, tb, 0)]
                                else:
                                    tok = pe_tok[(og, tb)]
                                if tok is None:
                                    sc.wait_ge(pW.h, pW_b + og + 1)
                                else:
                                    sc.wait_ge(pe.h, tok)
                                if kind == "A":
                                    if vFE_b + s - 1 > 0:
                                        sc.wait_ge(vFE.h, vFE_b + s - 1)
                                    ins = sc.activation(
                                        tAB[:, (s % 2) * 512:(s % 2) * 512 + 512],
                                        psA[s % 2][:, :], AF.Silu, scale=s1s[:, og:og + 1])
                                else:
                                    scl = (sh1s if kind == "B" else sh2s)[:, og:og + 1]
                                    dstH = HB if kind == "B" else HA
                                    ins = sc.activation(
                                        dstH[:, og * 1024 + tb * 512: og * 1024 + tb * 512 + 512],
                                        psA[s % 2][:, :], AF.Gelu, scale=scl)
                                aFE.inc(ins)
                    blk.scalar(f_act)
                aFE_stage_done = aFE.n

                if kind in ("A", "D"):
                    def f_vec(v, kind=kind, n_og=n_og, th=th, pe_tok=pe_tok,
                              pW_b=pW_b, dO_b=dO_b, dX_b=dX_b, aFE_b=aFE_b):
                        for og in range(n_og):
                            for tb in (0, 1):
                                s = og * 2 + tb
                                if kind == "A":
                                    tok = pe_tok[(og, tb, 1)]
                                    if tok is None:
                                        v.wait_ge(pW.h, pW_b + og + 1)
                                    else:
                                        v.wait_ge(pe.h, tok)
                                    v.wait_ge(aFE.h, aFE_b + s + 1)
                                    ins = v.scalar_tensor_tensor(
                                        HA[:, og * 1024 + tb * 512: og * 1024 + tb * 512 + 512],
                                        psB[s % 2][:, :], s2s[:, og:og + 1],
                                        tAB[:, (s % 2) * 512:(s % 2) * 512 + 512],
                                        ALU.mult, ALU.mult)
                                    vFE.inc(ins)
                                else:
                                    tok = pe_tok[(og, tb)]
                                    if tok is None:
                                        v.wait_ge(pW.h, pW_b + og + 1)
                                    else:
                                        v.wait_ge(pe.h, tok)
                                    v.wait_ge(dXO.h, dXO_b_d + 16 * (og + 1))
                                    if dO_b + 16 * (s - 1) > 0:
                                        v.wait_ge(dO.h, dO_b + 16 * (s - 1))
                                    ins = v.scalar_tensor_tensor(
                                        xn[:, (s % 2) * 512:(s % 2) * 512 + 512],
                                        psA[s % 2][:, :], s3s[:, og:og + 1],
                                        xold[:, tb * 512: tb * 512 + 512],
                                        ALU.mult, ALU.add)
                                    vFE.inc(ins)
                    blk.vector(f_vec)
                vFE_stage_done = vFE.n

                wwid = 1024 if nsrc == 2 else 4096
                vFE_stage_b_d = vFE_b

                def emit_store(sy, og, tb, th=th):
                    s = og * 2 + tb
                    sy.wait_ge(vFE.h, vFE_stage_b_d + s + 1)
                    sy.dma_start(
                        out=dst[:, og * 2048 + th * 1024 + tb * 512:
                                og * 2048 + th * 1024 + tb * 512 + 512],
                        in_=xn[:, (s % 2) * 512:(s % 2) * 512 + 512]
                    ).then_inc(dO.h, 16)
                    dO.n += 16

                if not use_cache:
                    dCW_pre = dCW.n

                    def f_gp(g, n_og=n_og, nsrc=nsrc, srcs=srcs, pW_b=pW_b,
                             dCW_pre=dCW_pre, wwid=wwid):
                        g.wait_ge(dO.h, x_gate)
                        for og in range(n_og):
                            g.wait_ge(pW.h, pW_b + max(0, og - 1))
                            if og >= 2:
                                g.wait_ge(dCW.h, dCW_pre + 16 * nsrc * (og - 1))
                            sl = wb[:, (og % 2) * 4096:(og % 2) * 4096 + 4096]
                            for w_i, srcw in enumerate(srcs):
                                ins = g.dma_start(
                                    out=sl[:, w_i * wwid: w_i * wwid + wwid],
                                    in_=srcw[og])
                                dW.inc(ins, 16)

                    def f_syncw(sy, n_og=n_og, nsrc=nsrc, cks=cks, dma_b=dma_b,
                                wwid=wwid, kind=kind):
                        for og in range(n_og):
                            sy.wait_ge(dW.h, dma_b + 16 * nsrc * (og + 1))
                            sl = wb[:, (og % 2) * 4096:(og % 2) * 4096 + 4096]
                            for w_i, ck in enumerate(cks):
                                ins = sy.dma_start(
                                    out=cach[ck][og],
                                    in_=sl[:, w_i * wwid: w_i * wwid + wwid])
                                dCW.inc(ins, 16)
                            if kind == "D" and og >= 1:
                                emit_store(sy, og - 1, 0)
                                emit_store(sy, og - 1, 1)
                        if kind == "D":
                            emit_store(sy, n_og - 1, 0)
                            emit_store(sy, n_og - 1, 1)
                    blk.gpsimd(f_gp)
                    blk.sync(f_syncw)
                else:
                    def f_syncr(sy, n_og=n_og, nsrc=nsrc, cks=cks, pW_b=pW_b,
                                wwid=wwid, kind=kind):
                        sy.wait_ge(dO.h, x_gate)
                        for og in range(n_og):
                            sy.wait_ge(pW.h, pW_b + max(0, og - 1))
                            sl = wb[:, (og % 2) * 4096:(og % 2) * 4096 + 4096]
                            for w_i, ck in enumerate(cks):
                                ins = sy.dma_start(
                                    out=sl[:, w_i * wwid: w_i * wwid + wwid],
                                    in_=cach[ck][og])
                                dX.inc(ins, 16)
                            if kind == "D" and og >= 1:
                                emit_store(sy, og - 1, 0)
                                emit_store(sy, og - 1, 1)
                        if kind == "D":
                            emit_store(sy, n_og - 1, 0)
                            emit_store(sy, n_og - 1, 1)
                    blk.sync(f_syncr)

                if kind == "D":
                    dXO_b = dXO.n

                    def f_actd(sc, n_og=n_og, th=th, x_gate=x_gate):
                        sc.wait_ge(dO.h, x_gate)
                        for og in range(n_og):
                            if og >= 1:
                                sc.wait_ge(vFE.h, vFE_stage_b_d + 2 * og)
                            sc.dma_start(
                                out=xold[:, 0:1024],
                                in_=park_src[:, og * 2048 + th * 1024: og * 2048 + th * 1024 + 1024]
                            ).then_inc(dXO.h, 16)
                            dXO.n += 16
                    blk.scalar(f_actd)

                prev_stage_done = [pe_stage_done, aFE_stage_done, vFE_stage_done]
        fps.close()
        f_es.close()
        return dO.n

    if stages >= 2:
        f1_es = ExitStack()
        ln2 = f1_es.enter_context(nc.sbuf_tensor("ln2", [128, 16384], BF16, side="right"))
        ln2s = ExitStack()
        xb2 = ln2s.enter_context(nc.sbuf_tensor("xb2", [128, 8192], BF16, side="right"))
        sqb2 = ln2s.enter_context(nc.sbuf_tensor("sqb2", [128, 2048], BF16, side="right"))
        tb22 = ln2s.enter_context(nc.sbuf_tensor("tb22", [128, 1024], F32, side="right"))
        ln2_tok = emit_ln(parkA, ln2, xb2, sqb2, tb22,
                          gate=lambda g: g.wait_ge(dO.h, x1_tok))
        ln2s.close()
        x2_tok = emit_ffn(ln2, parkA, parkB, True, x1_tok, ln2_tok)
        f1_es.close()

    if stages >= 3:
        b_es = ExitStack()

        def rb_(name, shape, dtype):
            return b_es.enter_context(nc.sbuf_tensor(name, list(shape), dtype, side="right"))

        qb_i = rb_("qb_i", (128, 16384), BF16)
        kb_i = rb_("kb_i", (128, 16384), BF16)
        v_i = rb_("v_i", (128, 16640), BF16)
        rel2_s = rb_("rel2_s", (128, 4095), BF16)
        bwk_es = ExitStack()
        bwk = bwk_es.enter_context(nc.sbuf_tensor("bwk", [128, 8192], BF16, side="right"))
        ln3_es = ExitStack()
        ln3 = ln3_es.enter_context(nc.sbuf_tensor("ln3", [128, 16384], BF16, side="right"))
        ln3s = ExitStack()
        xb3 = ln3s.enter_context(nc.sbuf_tensor("xb3", [128, 8192], BF16, side="right"))
        sqb3 = ln3s.enter_context(nc.sbuf_tensor("sqb3", [128, 2048], BF16, side="right"))
        tb23 = ln3s.enter_context(nc.sbuf_tensor("tb23", [128, 1024], F32, side="right"))
        dcw_t, dxo_t = dCW.n, dXO.n
        ln3_done_t = emit_ln(parkB, ln3, xb3, sqb3, tb23,
                           gate=lambda g: (g.wait_ge(dO.h, x2_tok),
                                           g.wait_ge(dCW.h, dcw_t),
                                           g.wait_ge(dXO.h, dxo_t)))
        ln3_done = ln3_done_t[1]
        ln3_stats_pe = pe.n
        ln3s.close()
        vg_ln3 = vG.n

        def f_ln3st(sy):
            sy.wait_ge(vG.h, vg_ln3)
            sy.dma_start(out=ln3st, in_=ln3[:, :]).then_inc(dCW.h, 16)
            dCW.n += 16
        blk.sync(f_ln3st)
        ln3st_tok = dCW.n

        t_rel = []
        blk.gpsimd(lambda g: (g.wait_ge(vG.h, ln3_done), g.wait_ge(pe.h, ln3_stats_pe),
                              t_rel.append(dW.inc(
                                  g.dma_start(out=rel2_s[:], in_=rel2_d), 16))))

        bps = ExitStack()
        pb_a = bps.enter_context(nc.psum_tensor("pb_a", [128, 512], F32))
        pb_b = bps.enter_context(nc.psum_tensor("pb_b", [128, 512], F32))
        bw_es = ExitStack()
        bwb = bw_es.enter_context(nc.sbuf_tensor("bwb", [128, 8192], BF16, side="right"))
        t_bw = []
        blk.gpsimd(lambda g: (g.wait_ge(vG.h, ln3_done), g.wait_ge(pe.h, ln3_stats_pe),
                              t_bw.extend(load_w(
                                  g, [bwb[:, 0:8192], bwk[:, 0:8192]], [bw_q, bw_k]))))

        ln3_tok = vG.n
        emit_proj(8, 8, 4, 512,
                  lambda kc, mi: bwb[:, kc * 1024 + mi * 128: kc * 1024 + mi * 128 + 128],
                  lambda kc, ni: ln3[:, kc * 2048 + ni * 512: kc * 2048 + ni * 512 + 512],
                  cp_evac(lambda mi, ni: qb_i[:, mi * 2048 + ni * 512: mi * 2048 + ni * 512 + 512]),
                  pb_a, pb_b, dma_toks=[(dW, t_bw[0]), (vG, ln3_tok)],
                  bias_row=bq_row_s)
        qk_proj_done = aFE.n
        qb_pe = pe.n

        t_bwv = []
        blk.gpsimd(lambda g: (g.wait_ge(pe.h, qb_pe),
                              t_bwv.extend(load_w(g, [bwb[:, 0:8192]], [bw_v]))))
        v_ones_tok = [0]
        pe_gate_v = pe.n
        blk.vector(lambda v: (v.wait_ge(pe.h, pe_gate_v),
                              v_ones_tok.__setitem__(0, vG.inc(v.memset(
                                  v_i.ap().rearrange("p (kc h s) -> p kc h s",
                                                     kc=16, h=16)[:, :, :, 64:65], 1.0)))))
        v_pe_tok = []
        aFE_v_b = aFE.n

        def f_v_pe(t):
            t.wait_ge(dW.h, t_bwv[0])
            for kc in range(16):
                for nb in (0, 1):
                    kt = kc * 2 + nb
                    ps = (pb_a, pb_b)[kt % 2]
                    if aFE_v_b + kt - 1 > 0:
                        t.wait_ge(aFE.h, aFE_v_b + kt - 1)
                    for ic in range(8):
                        t.matmul(ps[:, :],
                                 ln3[:, ic * 2048 + kc * 128: ic * 2048 + kc * 128 + 128],
                                 bwb[:, ic * 1024 + nb * 512: ic * 1024 + nb * 512 + 512],
                                 start=(ic == 0), stop=False)
                    ins = t.matmul(ps[:, :], ones_b[0:1, 0:128],
                                   bvs[0:1, nb * 512:nb * 512 + 512],
                                   start=False, stop=True)
                    v_pe_tok.append(pe.inc(ins))

        def f_v_act(sc):
            sc.wait_ge(vG.h, v_ones_tok[0])
            for kc in range(16):
                for nb in (0, 1):
                    kt = kc * 2 + nb
                    ps = (pb_a, pb_b)[kt % 2]
                    sc.wait_ge(pe.h, v_pe_tok[kt])
                    dst = v_i.ap().rearrange("p (kc h s) -> p kc h s",
                                             kc=16, h=16)[:, kc, nb * 8:nb * 8 + 8, 0:64]
                    aFE.inc(sc.copy(dst, ps.ap().rearrange("p (h d) -> p h d", h=8)))
        blk.tensor(f_v_pe)
        blk.scalar(f_v_act)
        v_proj_done = aFE.n
        bw_es.close()
        ln3_es.close()
        if DEBUG_DUMP:
            dbgq = nc.dram_tensor("dbgq", [128, 16384], BF16, kind="Internal").ap()
            dbgl = nc.dram_tensor("dbgl", [128, 16384], BF16, kind="Internal").ap()

            def f_dbg(sy):
                sy.wait_ge(aFE.h, qk_proj_done)
                sy.dma_start(out=dbgq, in_=qb_i[:, :]).then_inc(dO.h, 16)
                sy.dma_start(out=dbgl, in_=ln3[:, :]).then_inc(dO.h, 16)
                dO.n += 32
            blk.sync(f_dbg)

        p2b_es = ExitStack()
        er = p2b_es.enter_context(nc.sbuf_tensor("er", [128, 6144], BF16, side="right"))
        t1b = p2b_es.enter_context(nc.sbuf_tensor("t1b", [128, 2048], BF16, side="right"))
        Eb = p2b_es.enter_context(nc.sbuf_tensor("Eb", [128, 2048], BF16, side="right"))
        nrm2 = p2b_es.enter_context(nc.sbuf_tensor("nrm2", [128, 2048], F32, side="right"))
        oev2 = p2b_es.enter_context(nc.sbuf_tensor("oev2", [128, 2048], F32, side="right"))
        o_t = p2b_es.enter_context(nc.sbuf_tensor("o_t", [128, 2048], BF16, side="right"))
        p1_es = ExitStack()
        Pst = p1_es.enter_context(nc.sbuf_tensor("Pst", [128, 8704], BF16, side="right"))
        lnst = p1_es.enter_context(nc.sbuf_tensor("lnst", [128, 4096], BF16, side="right"))
        p1ps = ExitStack()
        P01 = [p1ps.enter_context(nc.psum_tensor(f"P0{i}", [128, 1088], F32)) for i in (0, 1)]
        aEX_b1 = aEX.n
        dPX_b1 = dPX.n

        p1_items = [(h, lc, half) for h in range(H) for lc in range(16) for half in (0, 1)]
        proj_tiles = [("kb", mi, ni) for ni in range(4) for mi in range(8)]
        sched = []
        pi = 0
        for i, it in enumerate(p1_items):
            sched.append(("p1",) + it)
            if i % 16 == 15 and pi < len(proj_tiles):
                sched.append(proj_tiles[pi])
                pi += 1
        while pi < len(proj_tiles):
            sched.append(proj_tiles[pi])
            pi += 1

        p1_pe_tok = {}
        pt_pe_tok = {}
        dX_st_b = dX.n

        def f_lnst(sy):
            sy.wait_ge(dCW.h, ln3st_tok)
            for ni in range(4):
                if ni >= 1:
                    sy.wait_ge(aFE.h, v_proj_done + 8 * ni)
                sy.dma_start(
                    out=lnst[:, :],
                    in_=AP(ln3st.tensor, ni * 512, [[16384, 128], [2048, 8], [1, 512]])
                ).then_inc(dX.h, 16)
                dX.n += 16
        blk.sync(f_lnst)

        def f1_pe(t):
            t.wait_ge(aFE.h, v_proj_done)
            t.wait_ge(dCW.h, ln3st_tok)
            t.wait_ge(dW.h, t_rel[0])
            pt = 0
            for item in sched:
                if item[0] == "p1":
                    _, h, lc, half = item
                    hc, rbv = h // 2, (h % 2) * 64
                    s = (h * 16 + lc) * 2 + half
                    w = 1088 if half == 0 else 1087
                    s0 = 1920 - 128 * lc + half * 1088
                    if s > 1:
                        t.wait_ge(aEX.h, aEX_b1 + s - 1)
                    psx = P01[s % 2]
                    off = 0
                    for n in (512, 512, w - 1024):
                        ins = t.matmul(
                            psx[:, off:off + n],
                            qb_i[rbv:rbv + 64, hc * 2048 + lc * 128: hc * 2048 + lc * 128 + 128],
                            rel2_s[rbv:rbv + 64, s0 + off: s0 + off + n],
                            start=True, stop=True)
                        off += n
                    p1_pe_tok[s] = pe.inc(ins)
                else:
                    _, mi, ni = item
                    ps = (pb_a, pb_b)[pt % 2]
                    if v_proj_done + pt - 1 > 0:
                        t.wait_ge(aFE.h, v_proj_done + pt - 1)
                    t.wait_ge(dX.h, dX_st_b + 16 * (ni + 1))
                    for kc in range(8):
                        t.matmul(ps[:, :],
                                 bwk[:, kc * 1024 + mi * 128: kc * 1024 + mi * 128 + 128],
                                 lnst[:, kc * 512: kc * 512 + 512],
                                 start=(kc == 0), stop=False)
                    ins = t.matmul(ps[:, :], bk_row_s[0:1, mi * 128:mi * 128 + 128],
                                   ones_b[0:1, 0:512], start=False, stop=True)
                    pt_pe_tok[pt] = pe.inc(ins)
                    pt += 1
        blk.tensor(f1_pe)

        def f1_act(sc):
            pt = 0
            for item in sched:
                if item[0] == "p1":
                    _, h, lc, half = item
                    hl = h * 16 + lc
                    s = hl * 2 + half
                    w = 1088 if half == 0 else 1087
                    sc.wait_ge(pe.h, p1_pe_tok[s])
                    if hl >= 4 and half == 0:
                        sc.wait_ge(dPX.h, dPX_b1 + 16 * (hl - 3))
                    ins = sc.activation(
                        Pst[:, (hl % 4) * 2176 + half * 1088:
                            (hl % 4) * 2176 + half * 1088 + w],
                        P01[s % 2][:, 0:w], AF.Exp)
                    aEX.inc(ins)
                else:
                    _, mi, ni = item
                    ps = (pb_a, pb_b)[pt % 2]
                    sc.wait_ge(pe.h, pt_pe_tok[pt])
                    ins = sc.copy(kb_i[:, mi * 2048 + ni * 512: mi * 2048 + ni * 512 + 512],
                                  ps[:, :])
                    aFE.inc(ins)
                    pt += 1
        blk.scalar(f1_act)
        v_done = aFE.n

        def f1_gp(g):
            for h in range(H):
                for lc in range(16):
                    hl = h * 16 + lc
                    g.wait_ge(aEX.h, aEX_b1 + 2 * hl + 2)
                    ins = g.dma_start(
                        out=AP(pexp.tensor,
                               h * 2048 * 4095 + (lc * 128) * 4095 + (1920 - 128 * lc),
                               [[4095, 128], [1, 2175]]),
                        in_=Pst[:, (hl % 4) * 2176: (hl % 4) * 2176 + 2175])
                    dPX.inc(ins, 16)
        blk.gpsimd(f1_gp)
        pexp_all = dPX.n
        p1ps.close()
        p1_es.close()
        bps.close()
        p2bps = ExitStack()
        qk2 = p2bps.enter_context(nc.psum_tensor("qk2", [128, 2048], F32))
        pv2 = [p2bps.enter_context(nc.psum_tensor(f"pv2_{i}", [65, 1024], F32)) for i in (0, 1)]

        aEX_b2, aEV_b2, vRC_b2, vNM_b2, vEM_b2, pPV_b2, dK_b2, dOO_b2 = (
            aEX.n, aEV.n, vRC.n, vNM.n, vEM.n, pPV.n, dK.n, dOO.n)

        qk2_tok = {}
        bc2_tok = {}
        units = [(h, rc, lbh) for h in range(H) for rc in range(16)
                 for lbh in (0, 1)]
        NU = len(units)
        u_lo = [0] * NU
        last_of = {}
        vem_after_R = {}
        for j, (h, rc, lbh) in enumerate(units):
            last_of[(h, lbh)] = j
            vem_after_R[h * 16 + rc] = j + 1

        def f2_syncr(sy):

            def store_o(idx):
                h2, lbh2 = idx // 2, idx % 2
                hc2, rbv2 = h2 // 2, (h2 % 2) * 64
                sy.wait_ge(vNM.h, vNM_b2 + idx + 1)
                sy.dma_start(
                    out=osc_d[rbv2:rbv2 + 64,
                              hc2 * 2048 + lbh2 * 1024: hc2 * 2048 + lbh2 * 1024 + 1024],
                    in_=o_t[0:64, (idx % 2) * 1024:(idx % 2) * 1024 + 1024]
                ).then_inc(dOO.h, 16)
                dOO.n += 16
            for h in range(H):
                if h >= 1:
                    store_o(2 * (h - 1))
                    store_o(2 * (h - 1) + 1)
                sy.wait_ge(dPX.h, dPX_b1 + 256 * (h + 1))
                for rc in range(16):
                    R = h * 16 + rc
                    if R >= 3:
                        sy.wait_ge(vEM.h, vEM_b2 + vem_after_R[R - 3])
                    src = AP(pexp.tensor, h * 2048 * 4095 + 2047 + 128 * rc,
                             [[4094, 2048], [1, 128]])
                    sy.dma_start(out=er[:, (R % 3) * 2048:(R % 3) * 2048 + 2048],
                                 in_=src, transpose=True).then_inc(dK.h, 16)
                    dK.n += 16
            store_o(2 * H - 2)
            store_o(2 * H - 1)
        blk.sync(f2_syncr)


        def f2_pe(t):
            t.wait_ge(aFE.h, v_done)
            t.wait_ge(aEX.h, aEX_b2)
            t.wait_ge(vG.h, v_ones_tok[0])

            def emit_qk(j):
                h, rc, lbh = units[j]
                hc, rbv = h // 2, (h % 2) * 64
                lo = u_lo[j]
                if j > 0:
                    t.wait_ge(aEX.h, aEX_b2 + j - 1)
                for c0, c1 in (((lo, 512), (512, 1024)) if lo < 512 else ((lo, 1024),)):
                    ins = t.matmul(
                        qk2[:, (j % 2) * 1024 + c0:(j % 2) * 1024 + c1],
                        kb_i[rbv:rbv + 64, hc * 2048 + rc * 128: hc * 2048 + rc * 128 + 128],
                        qb_i[rbv:rbv + 64, hc * 2048 + lbh * 1024 + c0:
                             hc * 2048 + lbh * 1024 + c1],
                        start=True, stop=True)
                qk2_tok[j] = pe.inc(ins)

            def emit_pv(j):
                h, rc, lbh = units[j]
                idx = 2 * h + lbh
                lo = u_lo[j]
                t.wait_ge(vEM.h, vEM_b2 + j + 1)
                if rc == 0 and h > 0:
                    t.wait_ge(aEV.h, aEV_b2 + 2 * (h - 1) + lbh + 1)
                    t.wait_ge(vNM.h, vNM_b2 + 2 * (h - 1) + lbh + 1)
                for c0, c1, st in acc_ranges(lo, 0, j == last_of[(h, lbh)]):
                    ins = t.matmul(
                        pv2[lbh][:, c0:c1],
                        v_i[:, rc * 1040 + h * 65: rc * 1040 + h * 65 + 65],
                        Eb[:, (j % 2) * 1024 + c0:(j % 2) * 1024 + c1],
                        start=(rc == 0), stop=st)
                pPV.inc(ins)
                if j == last_of[(h, lbh)]:
                    t.wait_ge(aEV.h, aEV_b2 + idx + 1)
                    for cb in (0, 1):
                        ins = t.matmul(
                            pv2[lbh][0:64, cb * 512:cb * 512 + 512], ones_f[64:65, 0:64],
                            nrm2[64:65, (idx % 2) * 1024 + cb * 512:
                                 (idx % 2) * 1024 + cb * 512 + 512],
                            start=True, stop=True)
                    bc2_tok[idx] = pe.inc(ins)

            for j in range(NU):
                emit_qk(j)
                if j >= 2:
                    emit_pv(j - 2)
            emit_pv(NU - 2)
            emit_pv(NU - 1)
        blk.tensor(f2_pe)

        def f2_act(sc):
            for j in range(NU):
                h, rc, lbh = units[j]
                lo = u_lo[j]
                sc.wait_ge(pe.h, qk2_tok[j])
                if j >= 2:
                    sc.wait_ge(vEM.h, vEM_b2 + j - 1)
                aEX.inc(sc.activation(t1b[:, (j % 2) * 1024 + lo:(j % 2) * 1024 + 1024],
                                      qk2[:, (j % 2) * 1024 + lo:(j % 2) * 1024 + 1024],
                                      AF.Exp, scale=0.125))
                if j == last_of[(h, lbh)]:
                    idx = 2 * h + lbh
                    sc.wait_ge(pPV.h, pPV_b2 + j + 1)
                    if idx >= 2:
                        sc.wait_ge(vNM.h, vNM_b2 + idx - 1)
                    sc.copy(oev2[0:64, (idx % 2) * 1024:(idx % 2) * 1024 + 1024],
                            pv2[lbh][0:64, :])
                    sc.activation(nrm2[64:65, (idx % 2) * 1024:(idx % 2) * 1024 + 1024],
                                  pv2[lbh][64:65, :], AF.Ln)
                    ins = sc.activation(nrm2[64:65, (idx % 2) * 1024:(idx % 2) * 1024 + 1024],
                                        nrm2[64:65, (idx % 2) * 1024:(idx % 2) * 1024 + 1024],
                                        AF.Exp, scale=-1.0)
                    aEV.inc(ins)
        blk.scalar(f2_act)

        def f2_vec(v):
            for j in range(NU):
                h, rc, lbh = units[j]
                lo = u_lo[j]
                v.wait_ge(aEX.h, aEX_b2 + j + 1)
                v.wait_ge(dK.h, dK_b2 + 16 * (h * 16 + rc + 1))
                if j >= 2:
                    v.wait_ge(pPV.h, pPV_b2 + j - 1)
                R = h * 16 + rc
                ins = v.tensor_tensor(
                    Eb[:, (j % 2) * 1024 + lo:(j % 2) * 1024 + 1024],
                    t1b[:, (j % 2) * 1024 + lo:(j % 2) * 1024 + 1024],
                    er[:, (R % 3) * 2048 + lbh * 1024 + lo:
                       (R % 3) * 2048 + lbh * 1024 + 1024],
                    ALU.mult)
                vEM.inc(ins)
                if j == last_of[(h, lbh)]:
                    idx = 2 * h + lbh
                    v.wait_ge(pe.h, bc2_tok[idx])
                    if dOO_b2 + 16 * (idx - 1) > dOO_b2:
                        v.wait_ge(dOO.h, dOO_b2 + 16 * (idx - 1))
                    ins = v.tensor_tensor(
                        o_t[0:64, (idx % 2) * 1024:(idx % 2) * 1024 + 1024],
                        oev2[0:64, (idx % 2) * 1024:(idx % 2) * 1024 + 1024],
                        pv2[lbh][0:64, 0:1024], ALU.mult)
                    vNM.inc(ins)
        blk.vector(f2_vec)

        oscr_tok = dOO.n
        p2bps.close()
        p2b_es.close()
        bwk_es.close()

        wo_es = ExitStack()
        o_i = wo_es.enter_context(nc.sbuf_tensor("o_i", [128, 16384], BF16, side="right"))
        wo_s = wo_es.enter_context(nc.sbuf_tensor("wo_s", [128, 8192], BF16, side="right"))
        xr2 = wo_es.enter_context(nc.sbuf_tensor("xr2", [128, 4096], F32, side="right"))
        xt2 = wo_es.enter_context(nc.sbuf_tensor("xt2", [128, 1024], F32, side="right"))
        wops = ExitStack()
        wo_a = wops.enter_context(nc.psum_tensor("wo_a", [128, 512], F32))
        wo_b = wops.enter_context(nc.psum_tensor("wo_b", [128, 512], F32))
        t_wo = []
        blk.gpsimd(lambda g: (g.wait_ge(dOO.h, oscr_tok),
                              t_wo.extend(load_w(g, [wo_s[:, :]], [bw_o]))))
        oi_tok = [0]

        def fw_sync_oi(sy):
            sy.wait_ge(dOO.h, oscr_tok)
            oi_tok[0] = dX.inc(sy.dma_start(out=o_i[:, :], in_=osc_d[:, :]), 16)
        blk.sync(fw_sync_oi)

        wo_pe_tok = {}
        vFE_wo_b = vFE.n
        dX_wo_b = dX.n
        dO_wo_b = dO.n

        def fw_pe(t):
            t.wait_ge(dW.h, t_wo[0])
            t.wait_ge(dX.h, oi_tok[0])
            for oc in range(8):
                for tb in range(4):
                    s = oc * 4 + tb
                    if vFE_wo_b + s - 1 > vFE_wo_b - 1 and s >= 2:
                        t.wait_ge(vFE.h, vFE_wo_b + s - 1)
                    ps = (wo_a, wo_b)[s % 2]
                    for ic in range(8):
                        ins = t.matmul(
                            ps[:, :], wo_s[:, ic * 1024 + oc * 128: ic * 1024 + oc * 128 + 128],
                            o_i[:, ic * 2048 + tb * 512: ic * 2048 + tb * 512 + 512],
                            start=(ic == 0), stop=(ic == 7))
                    wo_pe_tok[s] = pe.inc(ins)
        blk.tensor(fw_pe)

        def fw_sync_x(sy):
            sy.wait_ge(dOO.h, oscr_tok)

            def store_x(oc2):
                for tb2 in range(4):
                    s2 = oc2 * 4 + tb2
                    sy.wait_ge(vFE.h, vFE_wo_b + s2 + 1)
                    sy.dma_start(out=parkA[:, oc2 * 2048 + tb2 * 512: oc2 * 2048 + tb2 * 512 + 512],
                                 in_=xt2[:, (s2 % 2) * 512:(s2 % 2) * 512 + 512]
                                 ).then_inc(dO.h, 16)
                    dO.n += 16
            for oc in range(8):
                if oc >= 2:
                    sy.wait_ge(vFE.h, vFE_wo_b + 4 * (oc - 1))
                sy.dma_start(out=xr2[:, (oc % 2) * 2048:(oc % 2) * 2048 + 2048],
                             in_=parkB[:, oc * 2048: oc * 2048 + 2048]).then_inc(dX.h, 16)
                dX.n += 16
                if oc >= 1:
                    store_x(oc - 1)
            store_x(7)
        blk.sync(fw_sync_x)

        def fw_vec(v):
            for oc in range(8):
                v.wait_ge(dX.h, dX_wo_b + 16 * (oc + 1))
                for tb in range(4):
                    s = oc * 4 + tb
                    v.wait_ge(pe.h, wo_pe_tok[s])
                    if dO_wo_b + 16 * (s - 1) > dO_wo_b:
                        v.wait_ge(dO.h, dO_wo_b + 16 * (s - 1))
                    ps = (wo_a, wo_b)[s % 2]
                    ins = v.scalar_tensor_tensor(
                        xt2[:, (s % 2) * 512:(s % 2) * 512 + 512], ps[:, :],
                        bos[:, oc:oc + 1],
                        xr2[:, (oc % 2) * 2048 + tb * 512:(oc % 2) * 2048 + tb * 512 + 512],
                        ALU.add, ALU.add)
                    vFE.inc(ins)
        blk.vector(fw_vec)

        x3_tok = dO.n
        wops.close()
        wo_es.close()
        b_es.close()

    if stages >= 4:
        f2_es = ExitStack()
        ln4 = f2_es.enter_context(nc.sbuf_tensor("ln4", [128, 16384], BF16, side="right"))
        ln4s = ExitStack()
        xb4 = ln4s.enter_context(nc.sbuf_tensor("xb4", [128, 8192], BF16, side="right"))
        sqb4 = ln4s.enter_context(nc.sbuf_tensor("sqb4", [128, 2048], BF16, side="right"))
        tb24 = ln4s.enter_context(nc.sbuf_tensor("tb24", [128, 1024], F32, side="right"))
        dcw_t4, dxo_t4, doo_t4 = dCW.n, dXO.n, dOO.n
        ln4_tok = emit_ln(parkA, ln4, xb4, sqb4, tb24,
                          gate=lambda g: (g.wait_ge(dO.h, x3_tok),
                                          g.wait_ge(dCW.h, dcw_t4),
                                          g.wait_ge(dXO.h, dxo_t4),
                                          g.wait_ge(dOO.h, doo_t4)))
        ln4s.close()
        emit_ffn(ln4, parkA, yO, False, x3_tok, ln4_tok)
        f2_es.close()

    if stages < 4:
        src_park = parkA if stages in (1, 3) else parkB

        def f_tail(sy):
            sy.wait_ge(dO.h, dO.n)
            sy.dma_start(out=yO, in_=src_park).then_inc(dO.h, 16)
            dO.n += 16
        blk.sync(f_tail)

    fins = [(s, s.n) for s in (dW, dXB, dPX, dX, dK, dO, dCW, dOO, dXO)]

    def f_fin(sy):
        for s, n in fins:
            if n:
                sy.wait_ge(s.h, n)
    blk.sync(f_fin)

    es.close()
    nc.finalize()
    return nc


_BUILT = {}


def _prep_inputs(inputs):
    f32 = np.float32
    g = {k: np.asarray(v, dtype=f32) for k, v in inputs.items()}
    d = {}
    d["aw_dkv"] = img128(g["L0_attn_a_W_dkv"].T.copy())
    d["aw_dq"] = img128(g["L0_attn_a_W_dq"].T.copy())
    d["aw_dqn"] = img128(g["L0_attn_a_W_dq"].copy())
    d["aw_uq"] = img128(g["L0_attn_a_W_uq"].T.copy())
    d["aw_uk"] = img128(g["L0_attn_a_W_uk"].copy())
    d["aw_uv"] = img128(g["L0_attn_a_W_uv"].copy())
    d["aw_o"] = img128(g["L0_attn_a_W_o"].T.copy())
    d["bw_q"] = img128(g["L0_attn_b_Wq"].T.copy())
    d["bw_k"] = img128(g["L0_attn_b_Wk"].T.copy())
    d["bw_v"] = img128(g["L0_attn_b_Wv"].T.copy())
    d["bw_o"] = img128(g["L0_attn_b_Wo"].T.copy())
    d["bq_r"] = g["L0_attn_b_bq"].reshape(1, 1024).copy()
    d["bk_r"] = g["L0_attn_b_bk"].reshape(1, 1024).copy()
    d["bo_i"] = colimg(g["L0_attn_b_bo"])
    d["bv_r"] = g["L0_attn_b_bv"].reshape(1, 1024).copy()
    re_t = np.ascontiguousarray(g["L0_attn_b_rel_emb"].T)
    d["rel2"] = np.concatenate([re_t, re_t], axis=0).copy()
    d["f_w1"] = ogblocks(g["L0_ffn_w1_w"].T.copy())
    d["f_w2"] = ogblocks(g["L0_ffn_w2_w"].T.copy())
    d["f_h1"] = ogblocks(g["L0_ffn_h1_w"].T.copy())
    d["f_h2"] = ogblocks(g["L0_ffn_h2_w"].T.copy())
    d["f_w3"] = ogblocks(g["L0_ffn_w3_w"].T.copy())
    d["s1_i"] = colimg(g["L0_ffn_w1_s"])
    d["s2_i"] = colimg(g["L0_ffn_w2_s"])
    d["sh1_i"] = colimg(g["L0_ffn_h1_s"])
    d["sh2_i"] = colimg(g["L0_ffn_h2_s"])
    d["s3_i"] = colimg(g["L0_ffn_w3_s"])
    x = g["L0_attn_a_x"]
    in_maps = []
    for b in range(NCORE):
        m = dict(d)
        m["xT"] = img128(np.ascontiguousarray(x[b].T))
        in_maps.append(m)
    return in_maps


def kernel(**inputs):
    if STAGES not in _BUILT:
        _BUILT[STAGES] = build(STAGES)
    nc = _BUILT[STAGES]
    in_maps = _prep_inputs(inputs)
    res = run_bass_kernel_spmd(nc, in_maps, core_ids=list(range(NCORE)))
    outs = []
    for b in range(NCORE):
        yi = np.asarray(res.results[b]["y"])
        outs.append(yi.reshape(128, 8, 2048).transpose(1, 0, 2).reshape(1024, 2048).T)
    return np.stack(outs, axis=0).astype(np.float32)


# version 09
# speedup vs baseline: 1.2699x; 1.0028x over previous
import numpy as np
import ml_dtypes
import concourse.bass as bass
import concourse.mybir as mybir
from concourse.ap import AP
from concourse.bass_utils import run_bass_kernel_spmd
from contextlib import ExitStack

F32 = mybir.dt.float32
BF16 = mybir.dt.bfloat16
AF = mybir.ActivationFunctionType
ALU = mybir.AluOpType
BFD = ml_dtypes.bfloat16

T = 2048
D = 1024
LAT = 512
H = 16
HD = 64
DH = 4096
NCORE = 8
STAGES = 4
DEBUG_DUMP = False


def img128(m):
    R, C = m.shape
    assert R % 128 == 0
    return np.ascontiguousarray(
        m.reshape(R // 128, 128, C).transpose(1, 0, 2).reshape(128, (R // 128) * C))


def ogblocks(wt, ogc=128):
    I, O = wt.shape
    n = O // ogc
    a = wt.reshape(I // 128, 128, n, ogc).transpose(2, 1, 0, 3)
    return np.ascontiguousarray(a.reshape(n, 128, (I // 128) * ogc))


def colimg(v):
    return np.ascontiguousarray(v.reshape(-1, 128).T)


class Sem:
    __slots__ = ("h", "n")

    def __init__(self, es, nc, name):
        self.h = es.enter_context(nc.semaphore(name))
        self.n = 0

    def inc(self, instr, k=1):
        instr.then_inc(self.h, k)
        self.n += k
        return self.n


def build(stages):
    nc = bass.Bass("TRN2", target_bir_lowering=False, detect_race_conditions=False)

    def din(name, shape, dtype=F32):
        return nc.dram_tensor(name, list(shape), dtype, kind="ExternalInput").ap()

    xT = din("xT", (128, 8 * T))
    aw_dkv = din("aw_dkv", (128, 4096))
    aw_dq = din("aw_dq", (128, 4096))
    aw_dqn = din("aw_dqn", (128, 4096))
    aw_uq = din("aw_uq", (128, 4096))
    aw_uk = din("aw_uk", (128, 4096))
    aw_uv = din("aw_uv", (128, 4096))
    aw_o = din("aw_o", (128, 8192))
    bw_q = din("bw_q", (128, 8192))
    bw_k = din("bw_k", (128, 8192))
    bw_v = din("bw_v", (128, 8192))
    bw_o = din("bw_o", (128, 8192))
    bq_r = din("bq_r", (1, 1024))
    bk_r = din("bk_r", (1, 1024))
    bo_i = din("bo_i", (128, 8))
    bv_r = din("bv_r", (1, 1024))
    rel2_d = din("rel2", (128, 4095))
    f_w1 = din("f_w1", (32, 128, 1024))
    f_w2 = din("f_w2", (32, 128, 1024))
    f_h1 = din("f_h1", (32, 128, 4096))
    f_h2 = din("f_h2", (32, 128, 4096))
    f_w3 = din("f_w3", (8, 128, 4096))
    s1_i = din("s1_i", (128, 32))
    s2_i = din("s2_i", (128, 32))
    sh1_i = din("sh1_i", (128, 32))
    sh2_i = din("sh2_i", (128, 32))
    s3_i = din("s3_i", (128, 8))

    yO = nc.dram_tensor("y", [128, 8 * T], F32, kind="ExternalOutput").ap()
    parkA = nc.dram_tensor("parkA", [128, 8 * T], F32, kind="Internal").ap()
    parkB = nc.dram_tensor("parkB", [128, 8 * T], F32, kind="Internal").ap()
    osc_d = nc.dram_tensor("oscr", [128, 8 * T], BF16, kind="Internal").ap()
    ln3st = nc.dram_tensor("ln3st", [128, 8 * T], BF16, kind="Internal").ap()
    pexp = nc.dram_tensor("pexp", [H * 2048 * 4095], BF16, kind="Internal").ap()
    cach = {
        "w1": nc.dram_tensor("c_w1", [32, 128, 1024], BF16, kind="Internal").ap(),
        "w2": nc.dram_tensor("c_w2", [32, 128, 1024], BF16, kind="Internal").ap(),
        "h1": nc.dram_tensor("c_h1", [32, 128, 4096], BF16, kind="Internal").ap(),
        "h2": nc.dram_tensor("c_h2", [32, 128, 4096], BF16, kind="Internal").ap(),
        "w3": nc.dram_tensor("c_w3", [8, 128, 4096], BF16, kind="Internal").ap(),
    }

    es = ExitStack()

    def sbuf(name, shape, dtype, side="left"):
        return es.enter_context(nc.sbuf_tensor(name, list(shape), dtype, side=side))

    ones_b = sbuf("ones_b", (128, 512), BF16)
    ones_f = sbuf("ones_f", (128, 128), F32)
    epsT = sbuf("epsT", (128, 1), F32)
    s1s = sbuf("s1s", (128, 32), F32)
    s2s = sbuf("s2s", (128, 32), F32)
    sh1s = sbuf("sh1s", (128, 32), F32)
    sh2s = sbuf("sh2s", (128, 32), F32)
    s3s = sbuf("s3s", (128, 8), F32)
    bq_row_s = sbuf("bq_row_s", (1, 1024), BF16)
    bk_row_s = sbuf("bk_row_s", (1, 1024), BF16)
    bos = sbuf("bos", (128, 8), F32)
    bvs = sbuf("bvs", (1, 1024), BF16)
    V = sbuf("V", (128, 3072), F32)

    dW = Sem(es, nc, "dW")
    dXB = Sem(es, nc, "dXB")
    dPX = Sem(es, nc, "dPX")
    dX = Sem(es, nc, "dX")
    dK = Sem(es, nc, "dK")
    dO = Sem(es, nc, "dO")
    pe = Sem(es, nc, "pe")
    pPV = Sem(es, nc, "pPV")
    pW = Sem(es, nc, "pW")
    aG = Sem(es, nc, "aG")
    aEX = Sem(es, nc, "aEX")
    aEV = Sem(es, nc, "aEV")
    aFE = Sem(es, nc, "aFE")
    vG = Sem(es, nc, "vG")
    vRC = Sem(es, nc, "vRC")
    vNM = Sem(es, nc, "vNM")
    vEM = Sem(es, nc, "vEM")
    vFE = Sem(es, nc, "vFE")
    gMK = Sem(es, nc, "gMK")
    pLN = Sem(es, nc, "pLN")
    dCW = Sem(es, nc, "dCW")
    dOO = Sem(es, nc, "dOO")
    dXO = Sem(es, nc, "dXO")

    blk = es.enter_context(nc.Block())

    def p0_sync(sy):
        for dst, src in ((s1s, s1_i), (s2s, s2_i), (sh1s, sh1_i), (sh2s, sh2_i),
                         (s3s, s3_i), (bos, bo_i)):
            sy.dma_start(out=dst[:], in_=src).then_inc(dX.h, 16)
            dX.n += 16

    def p0_gp(g):
        for dst, src in ((bq_row_s, bq_r), (bk_row_s, bk_r), (bvs, bv_r)):
            g.dma_start(out=dst[:], in_=src).then_inc(dW.h, 16)
            dW.n += 16

    def p0_vec(v):
        v.memset(ones_b[:], 1.0)
        v.memset(ones_f[:], 1.0)
        vG.inc(v.memset(epsT[:], 1e-5))

    blk.sync(p0_sync)
    blk.gpsimd(p0_gp)
    blk.vector(p0_vec)
    tok_consts_v = vG.n
    tok_consts_w = dW.n
    tok_consts_d = dX.n

    ln_ctr = [0]

    def emit_ln(src_ap, dst, xb, sqb, tb2, gate=None):
        ln_ctr[0] += 1
        u = str(ln_ctr[0])
        ps_es = ExitStack()
        st_sum = ps_es.enter_context(nc.psum_tensor("ln_s" + u, [1, 1024], F32))
        st_sq = ps_es.enter_context(nc.psum_tensor("ln_q" + u, [1, 1024], F32))
        bcR = ps_es.enter_context(nc.psum_tensor("ln_r" + u, [128, 512], F32))
        bcM = ps_es.enter_context(nc.psum_tensor("ln_m" + u, [128, 512], F32))
        half_tok = [0]
        pLN_b = [pLN.n]
        prev_affine = [0]
        prev_stats_rd = [0]
        prev_stats = [0]
        for th in (0, 1):
            o = th * 1024
            xb_tok = []

            def w_gp(g, th=th, xb_tok=xb_tok):
                if gate is not None and th == 0:
                    gate(g)
                if th == 1:
                    g.wait_ge(vG.h, prev_affine[0])
                for c in range(8):
                    ins = g.dma_start(
                        out=xb[:, c * 1024:(c + 1) * 1024],
                        in_=src_ap[:, c * 2048 + th * 1024: c * 2048 + th * 1024 + 1024])
                    xb_tok.append(dXB.inc(ins, 16))
            blk.gpsimd(w_gp)
            sq_tok = []

            def w_act(sc, th=th, xb_tok=xb_tok, sq_tok=sq_tok):
                for c in range(8):
                    sc.wait_ge(dXB.h, xb_tok[c])
                    gi = th * 8 + c
                    if gi >= 2:
                        sc.wait_ge(pLN.h, pLN_b[0] + gi - 1)
                    ins = sc.activation(sqb[:, (c % 2) * 1024:(c % 2) * 1024 + 1024],
                                        xb[:, c * 1024:(c + 1) * 1024], AF.Square)
                    sq_tok.append(aG.inc(ins))
            blk.scalar(w_act)
            stat_tok = [0]

            def w_pe(t, th=th, xb_tok=xb_tok, sq_tok=sq_tok, stat_tok=stat_tok):
                if th == 1:
                    t.wait_ge(vG.h, prev_stats_rd[0])
                for c in range(8):
                    t.wait_ge(dXB.h, xb_tok[c])
                    t.wait_ge(aG.h, sq_tok[c])
                    for nb in (0, 1):
                        sl = slice(c * 1024 + nb * 512, c * 1024 + nb * 512 + 512)
                        sq_sl = slice((c % 2) * 1024 + nb * 512, (c % 2) * 1024 + nb * 512 + 512)
                        t.matmul(st_sum[0:1, nb * 512:nb * 512 + 512], ones_b[:, 0:1],
                                 xb[:, sl], start=(c == 0), stop=(c == 7))
                        ins = t.matmul(st_sq[0:1, nb * 512:nb * 512 + 512], ones_b[:, 0:1],
                                       sqb[:, sq_sl], start=(c == 0), stop=(c == 7))
                    pLN.inc(ins)
                stat_tok[0] = pLN.n
            blk.tensor(w_pe)
            prev_stats[0] = pe.n
            va_tok = [0]

            def w_vec(v, stat_tok=stat_tok, va_tok=va_tok):
                v.wait_ge(pLN.h, stat_tok[0])
                v.tensor_scalar(V[0:1, 0:1024], st_sum[0:1, :], 1.0 / D, None, ALU.mult)
                v.tensor_scalar(V[0:1, 1024:2048], st_sq[0:1, :], 1.0 / D, None, ALU.mult)
                v.tensor_mul(V[0:1, 2048:3072], V[0:1, 0:1024], V[0:1, 0:1024])
                ins = v.tensor_sub(V[0:1, 1024:2048], V[0:1, 1024:2048], V[0:1, 2048:3072])
                va_tok[0] = vG.inc(ins)
            blk.vector(w_vec)
            prev_stats_rd[0] = vG.n
            sd_tok = [0]

            def w_act2(sc, va_tok=va_tok, sd_tok=sd_tok):
                sc.wait_ge(vG.h, va_tok[0])
                ins = sc.activation(V[0:1, 2048:3072], V[0:1, 1024:2048], AF.Sqrt,
                                    bias=epsT[0:1, 0:1])
                sd_tok[0] = aG.inc(ins)
            blk.scalar(w_act2)
            vb_tok = [0]

            def w_vec2(v, sd_tok=sd_tok, vb_tok=vb_tok):
                v.wait_ge(aG.h, sd_tok[0])
                v.reciprocal(V[0:1, 1024:2048], V[0:1, 2048:3072])
                ins = v.tensor_mul(V[0:1, 2048:3072], V[0:1, 0:1024], V[0:1, 1024:2048])
                vb_tok[0] = vG.inc(ins)
            blk.vector(w_vec2)
            for lb in (0, 1):
                bc_tok = [0]

                def w_pe2(t, lb=lb, vb_tok=vb_tok, bc_tok=bc_tok):
                    t.wait_ge(vG.h, max(vb_tok[0], prev_affine[0]))
                    t.matmul(bcR[:, :], ones_f[0:1, :],
                             V[0:1, 1024 + lb * 512:1024 + lb * 512 + 512],
                             start=True, stop=True)
                    ins = t.matmul(bcM[:, :], ones_f[0:1, :],
                                   V[0:1, 2048 + lb * 512:2048 + lb * 512 + 512],
                                   start=True, stop=True)
                    bc_tok[0] = pe.inc(ins)
                blk.tensor(w_pe2)

                def w_vec3(v, lb=lb, th=th, bc_tok=bc_tok):
                    v.wait_ge(pe.h, bc_tok[0])
                    for c in range(8):
                        sl = slice(c * 1024 + lb * 512, c * 1024 + lb * 512 + 512)
                        v.tensor_tensor(tb2[:, (c % 2) * 512:(c % 2) * 512 + 512],
                                        xb[:, sl], bcR[:, :], ALU.mult)
                        ins = v.tensor_tensor(
                            dst[:, c * 2048 + th * 1024 + lb * 512:
                                c * 2048 + th * 1024 + lb * 512 + 512],
                            tb2[:, (c % 2) * 512:(c % 2) * 512 + 512], bcM[:, :],
                            ALU.subtract)
                        vG.inc(ins)
                blk.vector(w_vec3)
                prev_affine[0] = vG.n
            if th == 0:
                half_tok[0] = vG.n
        ps_es.close()
        return half_tok[0], vG.n

    blk.tensor(lambda t: t.wait_ge(vG.h, tok_consts_v))
    blk.scalar(lambda sc: (sc.wait_ge(vG.h, tok_consts_v), sc.wait_ge(dX.h, tok_consts_d)))
    blk.vector(lambda v: v.wait_ge(dX.h, tok_consts_d))

    prj_es = [None]

    def emit_proj(nk, nm, nn, nstep, lhsT_fn, rhs_fn, evac_fn, ps_a, ps_b,
                  dma_toks=(), bias_row=None):
        aFE_b = aFE.n
        tiles = [(mi, ni) for mi in range(nm) for ni in range(nn)]
        pe_tok = []

        def f_pe(t):
            for sem, tok in dma_toks:
                t.wait_ge(sem.h, tok)
            for g, (mi, ni) in enumerate(tiles):
                ps = (ps_a, ps_b)[g % 2]
                if aFE_b + g - 1 > 0:
                    t.wait_ge(aFE.h, aFE_b + g - 1)
                last = nk - 1 if bias_row is None else nk
                for kc in range(nk):
                    ins = t.matmul(ps[:, :], lhsT_fn(kc, mi), rhs_fn(kc, ni),
                                   start=(kc == 0), stop=(kc == last))
                if bias_row is not None:
                    ins = t.matmul(ps[:, :], bias_row[0:1, mi * 128:mi * 128 + 128],
                                   ones_b[0:1, 0:nstep], start=False, stop=True)
                pe_tok.append(pe.inc(ins))

        def f_act(sc):
            for g, (mi, ni) in enumerate(tiles):
                ps = (ps_a, ps_b)[g % 2]
                sc.wait_ge(pe.h, pe_tok[g])
                aFE.inc(evac_fn(sc, mi, ni, ps[:, :]))
        blk.tensor(f_pe)
        blk.scalar(f_act)
        return aFE.n

    a_es = ExitStack()

    def rsb(name, shape, dtype):
        return a_es.enter_context(nc.sbuf_tensor(name, list(shape), dtype, side="right"))

    def lsb(name, shape, dtype):
        return a_es.enter_context(nc.sbuf_tensor(name, list(shape), dtype, side="left"))

    ckv_i = lsb("ckv_i", (128, 8192), BF16)
    keT_i = lsb("keT_i", (128, 4096), BF16)
    veT_i = lsb("veT_i", (128, 4096), BF16)

    p2_es = ExitStack()
    q2_i = p2_es.enter_context(nc.sbuf_tensor("q2_i", [128, 8192], BF16, side="right"))
    ln1_es = ExitStack()
    ln1 = ln1_es.enter_context(nc.sbuf_tensor("ln1", [128, 16384], BF16, side="right"))
    ln_es = ExitStack()
    xb1 = ln_es.enter_context(nc.sbuf_tensor("xb1", [128, 8192], BF16, side="right"))
    sqb1 = ln_es.enter_context(nc.sbuf_tensor("sqb1", [128, 2048], BF16, side="right"))
    tb21 = ln_es.enter_context(nc.sbuf_tensor("tb21", [128, 1024], F32, side="right"))
    ln1_tok = emit_ln(xT, ln1, xb1, sqb1, tb21)[1]
    ln_es.close()

    p2ps = ExitStack()
    pr_a = p2ps.enter_context(nc.psum_tensor("pr_a", [128, 512], F32))
    pr_b = p2ps.enter_context(nc.psum_tensor("pr_b", [128, 512], F32))

    def load_w(g, buf_slices, srcs):
        toks = []
        for sl, src in zip(buf_slices, srcs):
            toks.append(dW.inc(g.dma_start(out=sl, in_=src), 16))
        return toks

    aw1_es = ExitStack()
    aw1 = aw1_es.enter_context(nc.sbuf_tensor("aw1", [128, 8192], BF16, side="right"))
    t_aw1 = []
    blk.gpsimd(lambda g: (g.wait_ge(vG.h, ln1_tok),
                          t_aw1.extend(load_w(
                              g, [aw1[:, 0:4096], aw1[:, 4096:8192]], [aw_dkv, aw_dq]))))

    def cp_evac(dst_fn):
        def f(sc, mi, ni, ps):
            return sc.copy(dst_fn(mi, ni), ps)
        return f

    emit_proj(8, 4, 4, 512,
              lambda kc, mi: aw1[:, kc * 512 + mi * 128: kc * 512 + mi * 128 + 128],
              lambda kc, ni: ln1[:, kc * 2048 + ni * 512: kc * 2048 + ni * 512 + 512],
              cp_evac(lambda mi, ni: ckv_i[:, mi * 2048 + ni * 512: mi * 2048 + ni * 512 + 512]),
              pr_a, pr_b, dma_toks=[(dW, t_aw1[0]), (vG, ln1_tok)])
    ckv_done = aFE.n
    emit_proj(8, 4, 4, 512,
              lambda kc, mi: aw1[:, 4096 + kc * 512 + mi * 128: 4096 + kc * 512 + mi * 128 + 128],
              lambda kc, ni: ln1[:, kc * 2048 + ni * 512: kc * 2048 + ni * 512 + 512],
              cp_evac(lambda mi, ni: q2_i[:, mi * 2048 + ni * 512: mi * 2048 + ni * 512 + 512]),
              pr_a, pr_b, dma_toks=[(dW, t_aw1[1]), (vG, ln1_tok)])
    q2_done = aFE.n
    aw1_free_tok = pe.n
    aw1_es.close()
    ln1_es.close()

    aw2_es = ExitStack()
    aw2 = aw2_es.enter_context(nc.sbuf_tensor("aw2", [128, 8192], BF16, side="right"))
    B_es = ExitStack()
    B_i = B_es.enter_context(nc.sbuf_tensor("B_i", [128, 8192], BF16, side="right"))
    t_aw2 = []
    blk.gpsimd(lambda g: (g.wait_ge(pe.h, aw1_free_tok),
                          t_aw2.extend(load_w(
                              g, [aw2[:, 0:4096], aw2[:, 4096:8192]], [aw_uq, aw_dqn]))))
    emit_proj(4, 8, 2, 512,
              lambda kc, mi: aw2[:, kc * 1024 + mi * 128: kc * 1024 + mi * 128 + 128],
              lambda kc, ni: aw2[:, 4096 + kc * 1024 + ni * 512: 4096 + kc * 1024 + ni * 512 + 512],
              cp_evac(lambda mi, ni: B_i[:, mi * 1024 + ni * 512: mi * 1024 + ni * 512 + 512]),
              pr_a, pr_b, dma_toks=[(dW, t_aw2[1])])
    B_done = aFE.n

    aw3_es = ExitStack()
    aw3 = aw3_es.enter_context(nc.sbuf_tensor("aw3", [128, 8192], BF16, side="right"))
    t_aw3 = []
    blk.gpsimd(lambda g: t_aw3.extend(load_w(
        g, [aw3[:, 0:4096], aw3[:, 4096:8192]], [aw_uk, aw_uv])))
    emit_proj(8, 4, 2, 512,
              lambda kc, mi: aw3[:, kc * 512 + mi * 128: kc * 512 + mi * 128 + 128],
              lambda kc, ni: B_i[:, kc * 1024 + ni * 512: kc * 1024 + ni * 512 + 512],
              cp_evac(lambda mi, ni: keT_i[:, mi * 1024 + ni * 512: mi * 1024 + ni * 512 + 512]),
              pr_a, pr_b, dma_toks=[(dW, t_aw3[0]), (aFE, B_done)])
    keT_done = aFE.n
    aw4_es = ExitStack()
    aw4 = aw4_es.enter_context(nc.sbuf_tensor("aw4", [128, 8192], BF16, side="right"))
    t_aw4 = []
    blk.gpsimd(lambda g: t_aw4.extend(load_w(g, [aw4[:, 0:8192]], [aw_o])))
    emit_proj(8, 4, 2, 512,
              lambda kc, mi: aw3[:, 4096 + kc * 512 + mi * 128: 4096 + kc * 512 + mi * 128 + 128],
              lambda kc, ni: aw4[:, kc * 1024 + ni * 512: kc * 1024 + ni * 512 + 512],
              cp_evac(lambda mi, ni: veT_i[:, mi * 1024 + ni * 512: mi * 1024 + ni * 512 + 512]),
              pr_a, pr_b, dma_toks=[(dW, t_aw4[0])])
    veT_done = aFE.n
    aw4_es.close()

    q_i = lsb("q_i", (128, 16384), BF16)
    emit_proj(4, 8, 4, 512,
              lambda kc, mi: aw2[:, kc * 1024 + mi * 128: kc * 1024 + mi * 128 + 128],
              lambda kc, ni: q2_i[:, kc * 2048 + ni * 512: kc * 2048 + ni * 512 + 512],
              cp_evac(lambda mi, ni: q_i[:, mi * 2048 + ni * 512: mi * 2048 + ni * 512 + 512]),
              pr_a, pr_b, dma_toks=[(aFE, q2_done)])
    qT_free = pe.n
    aw3_es.close()
    B_es.close()
    aw2_es.close()
    p2_es.close()

    keys_i = lsb("keys_i", (128, 16384), BF16)
    u_i = lsb("u_i", (128, 16640), BF16)
    qT_done = aFE.n
    u_ones_tok = [0]
    blk.vector(lambda v: (v.wait_ge(pe.h, qT_free),
                          u_ones_tok.__setitem__(0, vG.inc(v.memset(
                              u_i.ap().rearrange("p (kc h s) -> p kc h s",
                                                 kc=16, h=16)[:, :, :, 64:65], 1.0)))))

    fillers = ([("kf", 0, ni) for ni in range(4)]
               + [("uf", kc, 0) for kc in range(16)]
               + [("kf", 1, ni) for ni in range(4)]
               + [("uf", kc, 1) for kc in range(16)]
               + [("kf", mi, ni) for mi in range(2, 8) for ni in range(4)])
    vFE_b3 = vFE.n
    kf_tok = {}
    uf_tok = {}
    for ft, it in enumerate(fillers):
        if it[0] == "kf":
            kf_tok[(it[1], it[2])] = vFE_b3 + ft + 1
        else:
            uf_tok[(it[1], it[2])] = vFE_b3 + ft + 1

    p3_es = ExitStack()

    def r3(name, shape, dtype):
        return p3_es.enter_context(nc.sbuf_tensor(name, list(shape), dtype, side="right"))

    E2 = r3("E2", (128, 2048), BF16)
    nrm = r3("nrm", (128, 2048), F32)
    o_ev = r3("o_ev", (128, 2048), F32)
    xr = r3("xr", (128, 4096), F32)
    xt = r3("xt", (128, 2048), F32)
    p3ps = ExitStack()
    qk_ps = p3ps.enter_context(nc.psum_tensor("qk_ps", [128, 2048], F32))
    pv_ps = p3ps.enter_context(nc.psum_tensor("pv_ps", [65, 1024], F32))

    tiles3 = []
    for h in range(H):
        for lbh in (0, 1):
            for rc in range(8 * (lbh + 1)):
                tiles3.append((h, lbh, rc))
    G3 = len(tiles3)
    masked3 = [(rc >= 8 * lbh) for (h, lbh, rc) in tiles3]
    nmask3 = np.cumsum(masked3).tolist()
    first3 = [g for g in range(G3) if tiles3[g][2] == 0]
    last3 = [g - 1 for g in first3[1:]] + [G3 - 1]
    idx_of = {}
    for i, g in enumerate(first3):
        for gg in range(g, last3[i] + 1):
            idx_of[gg] = i
    aEX_b, aEV_b, vRC_b, vNM_b, pPV_b, gMK_b, dO_b, dX_b = (
        aEX.n, aEV.n, vRC.n, vNM.n, pPV.n, gMK.n, dO.n, dX.n)

    def acc_ranges(lo, lo_next, last):
        rs = []
        if lo < 512:
            rs.append((lo, 512, last or lo_next >= 512))
        rs.append((max(lo, 512), 1024, last))
        return rs

    sched3 = []
    fi = 0
    for _ in range(4):
        sched3.append(fillers[fi])
        fi += 1
    for g in range(G3):
        sched3.append(("qk", g))
        for _ in range(2):
            if fi < len(fillers):
                sched3.append(fillers[fi])
                fi += 1
        if g >= 2:
            sched3.append(("pv", g - 2))
        if tiles3[g][2] == 1 and idx_of[g] > 0:
            sched3.append(("bc", idx_of[g] - 1))
    sched3.append(("pv", G3 - 2))
    sched3.append(("pv", G3 - 1))
    sched3.append(("bc", 2 * H - 1))
    assert fi == len(fillers)

    def f3_sync(sy):
        sy.wait_ge(pe.h, qT_free)

        def store(idx):
            h, lbh = idx // 2, idx % 2
            hc, rb = h // 2, (h % 2) * 64
            sy.wait_ge(vNM.h, vNM_b + idx + 1)
            sy.dma_start(
                out=parkA[rb:rb + 64, hc * 2048 + lbh * 1024: hc * 2048 + lbh * 1024 + 1024],
                in_=xt[0:64, (idx % 2) * 1024:(idx % 2) * 1024 + 1024]).then_inc(dO.h, 16)
            dO.n += 16
        for idx in range(2 * H):
            h, lbh = idx // 2, idx % 2
            hc, rb = h // 2, (h % 2) * 64
            if idx >= 4:
                sy.wait_ge(vNM.h, vNM_b + idx - 3)
            sy.dma_start(
                out=xr[0:64, (h % 2) * 2048 + lbh * 1024:(h % 2) * 2048 + lbh * 1024 + 1024],
                in_=xT[rb:rb + 64, hc * 2048 + lbh * 1024: hc * 2048 + lbh * 1024 + 1024]
            ).then_inc(dX.h, 16)
            dX.n += 16
            if idx >= 2:
                store(idx - 2)
        store(2 * H - 2)
        store(2 * H - 1)
    blk.sync(f3_sync)

    qk_tok = [0] * G3
    bc_tok = [0] * (2 * H)
    f_pe_tok = {}

    def f3_pe(t):
        t.wait_ge(aFE.h, qT_done)
        t.wait_ge(vG.h, u_ones_tok[0])
        hw = [vFE_b3]
        nfill = [0]

        def need_vfe(tok):
            assert tok <= vFE_b3 + nfill[0], "filler dep not yet scheduled"
            if tok > hw[0]:
                t.wait_ge(vFE.h, tok)
                hw[0] = tok

        def emit_fill(item):
            ft = nfill[0]
            ps = (pr_a, pr_b)[ft % 2]
            if ft >= 2:
                need_vfe(vFE_b3 + ft - 1)
            if item[0] == "kf":
                _, mi, ni = item
                for kc in range(4):
                    ins = t.matmul(
                        ps[:, :],
                        keT_i[:, kc * 1024 + mi * 128: kc * 1024 + mi * 128 + 128],
                        ckv_i[:, kc * 2048 + ni * 512: kc * 2048 + ni * 512 + 512],
                        start=(kc == 0), stop=(kc == 3))
            else:
                _, kc, nb = item
                for lc in range(4):
                    ins = t.matmul(
                        ps[:, :],
                        ckv_i[:, lc * 2048 + kc * 128: lc * 2048 + kc * 128 + 128],
                        veT_i[:, lc * 1024 + nb * 512: lc * 1024 + nb * 512 + 512],
                        start=(lc == 0), stop=(lc == 3))
            f_pe_tok[ft] = pe.inc(ins)
            nfill[0] += 1

        def emit_qk(g):
            h, lbh, rc = tiles3[g]
            hc, rb = h // 2, (h % 2) * 64
            lo = max(0, 128 * (rc - 8 * lbh))
            need_vfe(kf_tok[(hc, rc // 4)])
            if aEX_b + g - 1 > 0:
                t.wait_ge(aEX.h, aEX_b + g - 1)
            for c0, c1 in (((lo, 512), (512, 1024)) if lo < 512 else ((lo, 1024),)):
                ins = t.matmul(
                    qk_ps[:, (g % 2) * 1024 + c0:(g % 2) * 1024 + c1],
                    keys_i[rb:rb + 64, hc * 2048 + rc * 128: hc * 2048 + rc * 128 + 128],
                    q_i[rb:rb + 64, hc * 2048 + lbh * 1024 + c0:
                        hc * 2048 + lbh * 1024 + c1],
                    start=True, stop=True)
            qk_tok[g] = pe.inc(ins)

        def emit_pv(g):
            h, lbh, rc = tiles3[g]
            idx = idx_of[g]
            lo = max(0, 128 * (rc - 8 * lbh))
            need_vfe(uf_tok[(rc, h // 8)])
            t.wait_ge(aEX.h, aEX_b + g + 1)
            if masked3[g]:
                t.wait_ge(gMK.h, gMK_b + nmask3[g])
            if rc == 0 and idx > 0:
                t.wait_ge(aEV.h, aEV_b + idx)
                t.wait_ge(vNM.h, vNM_b + idx)
            lo_next = max(0, 128 * (rc + 1 - 8 * lbh))
            for c0, c1, st in acc_ranges(lo, lo_next, rc == 8 * (lbh + 1) - 1):
                ins = t.matmul(
                    pv_ps[:, c0:c1],
                    u_i[:, rc * 1040 + h * 65: rc * 1040 + h * 65 + 65],
                    E2[:, (g % 2) * 1024 + c0:(g % 2) * 1024 + c1],
                    start=(rc == 0), stop=st)
            pPV.inc(ins)

        def emit_bc(idx):
            t.wait_ge(aEV.h, aEV_b + idx + 1)
            if idx > 0:
                t.wait_ge(vNM.h, vNM_b + idx)
            for cb in (0, 1):
                ins = t.matmul(
                    pv_ps[0:64, cb * 512:cb * 512 + 512], ones_f[64:65, 0:64],
                    nrm[64:65, (idx % 2) * 1024 + cb * 512:(idx % 2) * 1024 + cb * 512 + 512],
                    start=True, stop=True)
            bc_tok[idx] = pe.inc(ins)

        for item in sched3:
            if item[0] == "qk":
                emit_qk(item[1])
            elif item[0] == "pv":
                emit_pv(item[1])
            elif item[0] == "bc":
                emit_bc(item[1])
            else:
                emit_fill(item)
    blk.tensor(f3_pe)

    ev_tok = [0] * (2 * H)

    def f3_act(sc):
        for g in range(G3):
            h, lbh, rc = tiles3[g]
            idx = idx_of[g]
            lo = max(0, 128 * (rc - 8 * lbh))
            sc.wait_ge(pe.h, qk_tok[g])
            if pPV_b + g - 1 > 0:
                sc.wait_ge(pPV.h, pPV_b + g - 1)
            aEX.inc(sc.activation(E2[:, (g % 2) * 1024 + lo:(g % 2) * 1024 + 1024],
                                  qk_ps[:, (g % 2) * 1024 + lo:(g % 2) * 1024 + 1024],
                                  AF.Exp, scale=0.125))
            if g == last3[idx]:
                sc.wait_ge(pPV.h, pPV_b + g + 1)
                if idx >= 2:
                    sc.wait_ge(vNM.h, vNM_b + idx - 1)
                sc.copy(o_ev[0:64, (idx % 2) * 1024:(idx % 2) * 1024 + 1024],
                        pv_ps[0:64, :])
                sc.activation(nrm[64:65, (idx % 2) * 1024:(idx % 2) * 1024 + 1024],
                              pv_ps[64:65, :], AF.Ln)
                ins = sc.activation(nrm[64:65, (idx % 2) * 1024:(idx % 2) * 1024 + 1024],
                                    nrm[64:65, (idx % 2) * 1024:(idx % 2) * 1024 + 1024],
                                    AF.Exp, scale=-1.0)
                ev_tok[idx] = aEV.inc(ins)
    blk.scalar(f3_act)

    def f3_pool(g_eng):
        for g in range(G3):
            if not masked3[g]:
                continue
            h, lbh, rc = tiles3[g]
            lo = max(0, 128 * (rc - 8 * lbh))
            g_eng.wait_ge(aEX.h, aEX_b + g + 1)
            ins = g_eng.affine_select(
                E2[:, (g % 2) * 1024 + lo:(g % 2) * 1024 + 1024],
                E2[:, (g % 2) * 1024 + lo:(g % 2) * 1024 + 1024],
                [[1, 1024 - lo]], ALU.is_ge, 0.0,
                base=lbh * 1024 - rc * 128 + lo, channel_multiplier=-1)
            gMK.inc(ins)
    blk.gpsimd(f3_pool)

    def f3_vec(v):
        ft = 0
        for item in sched3:
            if item[0] in ("kf", "uf"):
                v.wait_ge(pe.h, f_pe_tok[ft])
                ps = (pr_a, pr_b)[ft % 2]
                if item[0] == "kf":
                    _, mi, ni = item
                    ins = v.tensor_copy(
                        keys_i[:, mi * 2048 + ni * 512: mi * 2048 + ni * 512 + 512],
                        ps[:, :])
                else:
                    _, kc, nb = item
                    dst = u_i.ap().rearrange("p (kc h s) -> p kc h s",
                                             kc=16, h=16)[:, kc, nb * 8:(nb + 1) * 8, 0:64]
                    ins = v.tensor_copy(dst, ps.ap().rearrange("p (h d) -> p h d", h=8))
                vFE.inc(ins)
                ft += 1
            elif item[0] == "bc":
                idx = item[1]
                h, lbh = idx // 2, idx % 2
                v.wait_ge(pe.h, bc_tok[idx])
                v.wait_ge(dX.h, dX_b + 16 * (idx + 1))
                osl = o_ev[0:64, (idx % 2) * 1024:(idx % 2) * 1024 + 1024]
                v.tensor_tensor(osl, osl, pv_ps[0:64, :], ALU.mult)
                if dO_b + 16 * (idx - 1) > 0:
                    v.wait_ge(dO.h, dO_b + 16 * (idx - 1))
                ins = v.tensor_tensor(
                    xt[0:64, (idx % 2) * 1024:(idx % 2) * 1024 + 1024], osl,
                    xr[0:64, (h % 2) * 2048 + lbh * 1024:(h % 2) * 2048 + lbh * 1024 + 1024],
                    ALU.add)
                vNM.inc(ins)
    blk.vector(f3_vec)

    x1_tok = dO.n

    p3ps.close()
    p2ps.close()
    p3_es.close()
    a_es.close()

    ffn_ctr = [0]

    def emit_ffn(ln_src, park_src, dst, first, x_gate, ln_tok):
        ffn_ctr[0] += 1
        fu = str(ffn_ctr[0])
        f_es = ExitStack()

        def rf(name, shape, dtype):
            return f_es.enter_context(nc.sbuf_tensor(name + fu, list(shape), dtype, side="right"))

        HA = rf("HA", (128, 32768), BF16)
        HB = rf("HB", (128, 32768), BF16)
        wb = rf("wb", (128, 8192), BF16)
        xold = rf("xold", (128, 1024), F32)
        xn = rf("xn", (128, 1024), F32)
        tAB = rf("tAB", (128, 1024), BF16)
        fps = ExitStack()
        psA = [fps.enter_context(nc.psum_tensor(f"fpsA{i}" + fu, [128, 512], F32)) for i in range(2)]
        psB = [fps.enter_context(nc.psum_tensor(f"fpsB{i}" + fu, [128, 512], F32)) for i in range(2)]

        stages_d = [
            ("A", 32, 8, (f_w1, f_w2), ("w1", "w2"), 1024),
            ("B", 32, 32, (f_h1,), ("h1",), 4096),
            ("C", 32, 32, (f_h2,), ("h2",), 4096),
            ("D", 8, 32, (f_w3,), ("w3",), 4096),
        ]
        prev_stage_done = [pe.n, aFE.n, vFE.n]

        for th in (0, 1):
            for kind, n_og, n_ic, srcs, cks, wcols in stages_d:
                nsrc = len(srcs)
                use_cache = (not first) or th == 1
                dma_sem = dX if use_cache else dW
                dma_b = dma_sem.n
                pW_b = pW.n
                aFE_b = aFE.n
                vFE_b = vFE.n
                dO_b = dO.n
                dX_b = dX.n
                dXO_b_d = dXO.n
                pe_prev, aFE_prev, vFE_prev = prev_stage_done

                pe_tok = {}

                def f_pe(t, kind=kind, n_og=n_og, n_ic=n_ic, nsrc=nsrc, wcols=wcols,
                         dma_sem=dma_sem, dma_b=dma_b, pW_b=pW_b, aFE_b=aFE_b,
                         vFE_b=vFE_b, dX_b=dX_b, th=th, pe_tok=pe_tok,
                         aFE_prev=aFE_prev, vFE_prev=vFE_prev):
                    if kind == "A":
                        t.wait_ge(vG.h, ln_tok[th])
                    if aFE_prev:
                        t.wait_ge(aFE.h, aFE_prev)
                    if vFE_prev:
                        t.wait_ge(vFE.h, vFE_prev)
                    for og in range(n_og):
                        t.wait_ge(dma_sem.h, dma_b + 16 * nsrc * (og + 1))
                        sl = wb[:, (og % 2) * 4096:(og % 2) * 4096 + 4096]
                        for tb in (0, 1):
                            s = og * 2 + tb
                            rhs_off = th * 1024 + tb * 512
                            if kind == "A":
                                if aFE_b + s - 1 > 0:
                                    t.wait_ge(aFE.h, aFE_b + s - 1)
                                if vFE_b + s - 1 > 0:
                                    t.wait_ge(vFE.h, vFE_b + s - 1)
                                for w_i in (0, 1):
                                    ps = (psA, psB)[w_i][s % 2]
                                    for ic in range(8):
                                        ins = t.matmul(
                                            ps[:, :],
                                            sl[:, w_i * 1024 + ic * 128: w_i * 1024 + ic * 128 + 128],
                                            ln_src[:, ic * 2048 + rhs_off: ic * 2048 + rhs_off + 512],
                                            start=(ic == 0), stop=(ic == 7))
                                    if w_i == 0:
                                        pe_tok[(og, tb, 0)] = pe.inc(ins)
                                if tb == 1:
                                    pW.inc(ins)
                                    pe_tok[(og, tb, 1)] = None
                                else:
                                    pe_tok[(og, tb, 1)] = pe.inc(ins)
                            else:
                                if aFE_b + s - 1 > 0 and kind != "D":
                                    t.wait_ge(aFE.h, aFE_b + s - 1)
                                if vFE_b + s - 1 > 0 and kind == "D":
                                    t.wait_ge(vFE.h, vFE_b + s - 1)
                                src_h = HA if kind in ("B", "D") else HB
                                ps = psA[s % 2]
                                for ic in range(n_ic):
                                    ins = t.matmul(
                                        ps[:, :], sl[:, ic * 128: ic * 128 + 128],
                                        src_h[:, ic * 1024 + tb * 512: ic * 1024 + tb * 512 + 512],
                                        start=(ic == 0), stop=(ic == n_ic - 1))
                                if tb == 1:
                                    pW.inc(ins)
                                    pe_tok[(og, tb)] = None
                                else:
                                    pe_tok[(og, tb)] = pe.inc(ins)
                blk.tensor(f_pe)
                pe_stage_done = pe.n

                if kind in ("A", "B", "C"):
                    def f_act(sc, kind=kind, n_og=n_og, th=th, pe_tok=pe_tok,
                              pW_b=pW_b, vFE_b=vFE_b):
                        for og in range(n_og):
                            for tb in (0, 1):
                                s = og * 2 + tb
                                if kind == "A":
                                    tok = pe_tok[(og, tb, 0)]
                                else:
                                    tok = pe_tok[(og, tb)]
                                if tok is None:
                                    sc.wait_ge(pW.h, pW_b + og + 1)
                                else:
                                    sc.wait_ge(pe.h, tok)
                                if kind == "A":
                                    if vFE_b + s - 1 > 0:
                                        sc.wait_ge(vFE.h, vFE_b + s - 1)
                                    ins = sc.activation(
                                        tAB[:, (s % 2) * 512:(s % 2) * 512 + 512],
                                        psA[s % 2][:, :], AF.Silu, scale=s1s[:, og:og + 1])
                                else:
                                    scl = (sh1s if kind == "B" else sh2s)[:, og:og + 1]
                                    dstH = HB if kind == "B" else HA
                                    ins = sc.activation(
                                        dstH[:, og * 1024 + tb * 512: og * 1024 + tb * 512 + 512],
                                        psA[s % 2][:, :], AF.Gelu, scale=scl)
                                aFE.inc(ins)
                    blk.scalar(f_act)
                aFE_stage_done = aFE.n

                if kind in ("A", "D"):
                    def f_vec(v, kind=kind, n_og=n_og, th=th, pe_tok=pe_tok,
                              pW_b=pW_b, dO_b=dO_b, dX_b=dX_b, aFE_b=aFE_b):
                        for og in range(n_og):
                            for tb in (0, 1):
                                s = og * 2 + tb
                                if kind == "A":
                                    tok = pe_tok[(og, tb, 1)]
                                    if tok is None:
                                        v.wait_ge(pW.h, pW_b + og + 1)
                                    else:
                                        v.wait_ge(pe.h, tok)
                                    v.wait_ge(aFE.h, aFE_b + s + 1)
                                    ins = v.scalar_tensor_tensor(
                                        HA[:, og * 1024 + tb * 512: og * 1024 + tb * 512 + 512],
                                        psB[s % 2][:, :], s2s[:, og:og + 1],
                                        tAB[:, (s % 2) * 512:(s % 2) * 512 + 512],
                                        ALU.mult, ALU.mult)
                                    vFE.inc(ins)
                                else:
                                    tok = pe_tok[(og, tb)]
                                    if tok is None:
                                        v.wait_ge(pW.h, pW_b + og + 1)
                                    else:
                                        v.wait_ge(pe.h, tok)
                                    v.wait_ge(dXO.h, dXO_b_d + 16 * (og + 1))
                                    if dO_b + 16 * (s - 1) > 0:
                                        v.wait_ge(dO.h, dO_b + 16 * (s - 1))
                                    ins = v.scalar_tensor_tensor(
                                        xn[:, (s % 2) * 512:(s % 2) * 512 + 512],
                                        psA[s % 2][:, :], s3s[:, og:og + 1],
                                        xold[:, tb * 512: tb * 512 + 512],
                                        ALU.mult, ALU.add)
                                    vFE.inc(ins)
                    blk.vector(f_vec)
                vFE_stage_done = vFE.n

                wwid = 1024 if nsrc == 2 else 4096
                vFE_stage_b_d = vFE_b

                def emit_store(sy, og, tb, th=th):
                    s = og * 2 + tb
                    sy.wait_ge(vFE.h, vFE_stage_b_d + s + 1)
                    sy.dma_start(
                        out=dst[:, og * 2048 + th * 1024 + tb * 512:
                                og * 2048 + th * 1024 + tb * 512 + 512],
                        in_=xn[:, (s % 2) * 512:(s % 2) * 512 + 512]
                    ).then_inc(dO.h, 16)
                    dO.n += 16

                if not use_cache:
                    dCW_pre = dCW.n

                    def f_gp(g, n_og=n_og, nsrc=nsrc, srcs=srcs, pW_b=pW_b,
                             dCW_pre=dCW_pre, wwid=wwid):
                        g.wait_ge(dO.h, x_gate)
                        for og in range(n_og):
                            g.wait_ge(pW.h, pW_b + max(0, og - 1))
                            if og >= 2:
                                g.wait_ge(dCW.h, dCW_pre + 16 * nsrc * (og - 1))
                            sl = wb[:, (og % 2) * 4096:(og % 2) * 4096 + 4096]
                            for w_i, srcw in enumerate(srcs):
                                ins = g.dma_start(
                                    out=sl[:, w_i * wwid: w_i * wwid + wwid],
                                    in_=srcw[og])
                                dW.inc(ins, 16)

                    def f_syncw(sy, n_og=n_og, nsrc=nsrc, cks=cks, dma_b=dma_b,
                                wwid=wwid, kind=kind):
                        for og in range(n_og):
                            sy.wait_ge(dW.h, dma_b + 16 * nsrc * (og + 1))
                            sl = wb[:, (og % 2) * 4096:(og % 2) * 4096 + 4096]
                            for w_i, ck in enumerate(cks):
                                ins = sy.dma_start(
                                    out=cach[ck][og],
                                    in_=sl[:, w_i * wwid: w_i * wwid + wwid])
                                dCW.inc(ins, 16)
                            if kind == "D" and og >= 1:
                                emit_store(sy, og - 1, 0)
                                emit_store(sy, og - 1, 1)
                        if kind == "D":
                            emit_store(sy, n_og - 1, 0)
                            emit_store(sy, n_og - 1, 1)
                    blk.gpsimd(f_gp)
                    blk.sync(f_syncw)
                else:
                    def f_syncr(sy, n_og=n_og, nsrc=nsrc, cks=cks, pW_b=pW_b,
                                wwid=wwid, kind=kind):
                        sy.wait_ge(dO.h, x_gate)
                        for og in range(n_og):
                            sy.wait_ge(pW.h, pW_b + max(0, og - 1))
                            sl = wb[:, (og % 2) * 4096:(og % 2) * 4096 + 4096]
                            for w_i, ck in enumerate(cks):
                                ins = sy.dma_start(
                                    out=sl[:, w_i * wwid: w_i * wwid + wwid],
                                    in_=cach[ck][og])
                                dX.inc(ins, 16)
                            if kind == "D" and og >= 1:
                                emit_store(sy, og - 1, 0)
                                emit_store(sy, og - 1, 1)
                        if kind == "D":
                            emit_store(sy, n_og - 1, 0)
                            emit_store(sy, n_og - 1, 1)
                    blk.sync(f_syncr)

                if kind == "D":
                    dXO_b = dXO.n

                    def f_actd(sc, n_og=n_og, th=th, x_gate=x_gate):
                        sc.wait_ge(dO.h, x_gate)
                        for og in range(n_og):
                            if og >= 1:
                                sc.wait_ge(vFE.h, vFE_stage_b_d + 2 * og)
                            sc.dma_start(
                                out=xold[:, 0:1024],
                                in_=park_src[:, og * 2048 + th * 1024: og * 2048 + th * 1024 + 1024]
                            ).then_inc(dXO.h, 16)
                            dXO.n += 16
                    blk.scalar(f_actd)

                prev_stage_done = [pe_stage_done, aFE_stage_done, vFE_stage_done]
        fps.close()
        f_es.close()
        return dO.n

    if stages >= 2:
        f1_es = ExitStack()
        ln2 = f1_es.enter_context(nc.sbuf_tensor("ln2", [128, 16384], BF16, side="right"))
        ln2s = ExitStack()
        xb2 = ln2s.enter_context(nc.sbuf_tensor("xb2", [128, 8192], BF16, side="right"))
        sqb2 = ln2s.enter_context(nc.sbuf_tensor("sqb2", [128, 2048], BF16, side="right"))
        tb22 = ln2s.enter_context(nc.sbuf_tensor("tb22", [128, 1024], F32, side="right"))
        ln2_tok = emit_ln(parkA, ln2, xb2, sqb2, tb22,
                          gate=lambda g: g.wait_ge(dO.h, x1_tok))
        ln2s.close()
        x2_tok = emit_ffn(ln2, parkA, parkB, True, x1_tok, ln2_tok)
        f1_es.close()

    if stages >= 3:
        b_es = ExitStack()

        def rb_(name, shape, dtype):
            return b_es.enter_context(nc.sbuf_tensor(name, list(shape), dtype, side="right"))

        qb_i = rb_("qb_i", (128, 16384), BF16)
        kb_i = rb_("kb_i", (128, 16384), BF16)
        v_i = rb_("v_i", (128, 16640), BF16)
        rel2_s = rb_("rel2_s", (128, 4095), BF16)
        bwk_es = ExitStack()
        bwk = bwk_es.enter_context(nc.sbuf_tensor("bwk", [128, 8192], BF16, side="right"))
        ln3_es = ExitStack()
        ln3 = ln3_es.enter_context(nc.sbuf_tensor("ln3", [128, 16384], BF16, side="right"))
        ln3s = ExitStack()
        xb3 = ln3s.enter_context(nc.sbuf_tensor("xb3", [128, 8192], BF16, side="right"))
        sqb3 = ln3s.enter_context(nc.sbuf_tensor("sqb3", [128, 2048], BF16, side="right"))
        tb23 = ln3s.enter_context(nc.sbuf_tensor("tb23", [128, 1024], F32, side="right"))
        dcw_t, dxo_t = dCW.n, dXO.n
        ln3_done_t = emit_ln(parkB, ln3, xb3, sqb3, tb23,
                           gate=lambda g: (g.wait_ge(dO.h, x2_tok),
                                           g.wait_ge(dCW.h, dcw_t),
                                           g.wait_ge(dXO.h, dxo_t)))
        ln3_done = ln3_done_t[1]
        ln3_stats_pe = pe.n
        ln3s.close()
        vg_ln3 = vG.n

        def f_ln3st(sy):
            sy.wait_ge(vG.h, vg_ln3)
            sy.dma_start(out=ln3st, in_=ln3[:, :]).then_inc(dCW.h, 16)
            dCW.n += 16
        blk.sync(f_ln3st)
        ln3st_tok = dCW.n

        t_rel = []
        blk.gpsimd(lambda g: (g.wait_ge(vG.h, ln3_done), g.wait_ge(pe.h, ln3_stats_pe),
                              t_rel.append(dW.inc(
                                  g.dma_start(out=rel2_s[:], in_=rel2_d), 16))))

        bps = ExitStack()
        pb_a = bps.enter_context(nc.psum_tensor("pb_a", [128, 512], F32))
        pb_b = bps.enter_context(nc.psum_tensor("pb_b", [128, 512], F32))
        bw_es = ExitStack()
        bwb = bw_es.enter_context(nc.sbuf_tensor("bwb", [128, 8192], BF16, side="right"))
        t_bw = []
        blk.gpsimd(lambda g: (g.wait_ge(vG.h, ln3_done), g.wait_ge(pe.h, ln3_stats_pe),
                              t_bw.extend(load_w(
                                  g, [bwb[:, 0:8192], bwk[:, 0:8192]], [bw_q, bw_k]))))

        ln3_tok = vG.n
        emit_proj(8, 8, 4, 512,
                  lambda kc, mi: bwb[:, kc * 1024 + mi * 128: kc * 1024 + mi * 128 + 128],
                  lambda kc, ni: ln3[:, kc * 2048 + ni * 512: kc * 2048 + ni * 512 + 512],
                  cp_evac(lambda mi, ni: qb_i[:, mi * 2048 + ni * 512: mi * 2048 + ni * 512 + 512]),
                  pb_a, pb_b, dma_toks=[(dW, t_bw[0]), (vG, ln3_tok)],
                  bias_row=bq_row_s)
        qk_proj_done = aFE.n
        qb_pe = pe.n

        t_bwv = []
        blk.gpsimd(lambda g: (g.wait_ge(pe.h, qb_pe),
                              t_bwv.extend(load_w(g, [bwb[:, 0:8192]], [bw_v]))))
        v_ones_tok = [0]
        pe_gate_v = pe.n
        blk.vector(lambda v: (v.wait_ge(pe.h, pe_gate_v),
                              v_ones_tok.__setitem__(0, vG.inc(v.memset(
                                  v_i.ap().rearrange("p (kc h s) -> p kc h s",
                                                     kc=16, h=16)[:, :, :, 64:65], 1.0)))))
        v_pe_tok = []
        aFE_v_b = aFE.n

        def f_v_pe(t):
            t.wait_ge(dW.h, t_bwv[0])
            for kc in range(16):
                for nb in (0, 1):
                    kt = kc * 2 + nb
                    ps = (pb_a, pb_b)[kt % 2]
                    if aFE_v_b + kt - 1 > 0:
                        t.wait_ge(aFE.h, aFE_v_b + kt - 1)
                    for ic in range(8):
                        t.matmul(ps[:, :],
                                 ln3[:, ic * 2048 + kc * 128: ic * 2048 + kc * 128 + 128],
                                 bwb[:, ic * 1024 + nb * 512: ic * 1024 + nb * 512 + 512],
                                 start=(ic == 0), stop=False)
                    ins = t.matmul(ps[:, :], ones_b[0:1, 0:128],
                                   bvs[0:1, nb * 512:nb * 512 + 512],
                                   start=False, stop=True)
                    v_pe_tok.append(pe.inc(ins))

        def f_v_act(sc):
            sc.wait_ge(vG.h, v_ones_tok[0])
            for kc in range(16):
                for nb in (0, 1):
                    kt = kc * 2 + nb
                    ps = (pb_a, pb_b)[kt % 2]
                    sc.wait_ge(pe.h, v_pe_tok[kt])
                    dst = v_i.ap().rearrange("p (kc h s) -> p kc h s",
                                             kc=16, h=16)[:, kc, nb * 8:nb * 8 + 8, 0:64]
                    aFE.inc(sc.copy(dst, ps.ap().rearrange("p (h d) -> p h d", h=8)))
        blk.tensor(f_v_pe)
        blk.scalar(f_v_act)
        v_proj_done = aFE.n
        bw_es.close()
        ln3_es.close()
        if DEBUG_DUMP:
            dbgq = nc.dram_tensor("dbgq", [128, 16384], BF16, kind="Internal").ap()
            dbgl = nc.dram_tensor("dbgl", [128, 16384], BF16, kind="Internal").ap()

            def f_dbg(sy):
                sy.wait_ge(aFE.h, qk_proj_done)
                sy.dma_start(out=dbgq, in_=qb_i[:, :]).then_inc(dO.h, 16)
                sy.dma_start(out=dbgl, in_=ln3[:, :]).then_inc(dO.h, 16)
                dO.n += 32
            blk.sync(f_dbg)

        p2b_es = ExitStack()
        er = p2b_es.enter_context(nc.sbuf_tensor("er", [128, 6144], BF16, side="right"))
        t1b = p2b_es.enter_context(nc.sbuf_tensor("t1b", [128, 2048], BF16, side="right"))
        Eb = p2b_es.enter_context(nc.sbuf_tensor("Eb", [128, 2048], BF16, side="right"))
        nrm2 = p2b_es.enter_context(nc.sbuf_tensor("nrm2", [128, 2048], F32, side="right"))
        oev2 = p2b_es.enter_context(nc.sbuf_tensor("oev2", [128, 2048], F32, side="right"))
        o_t = p2b_es.enter_context(nc.sbuf_tensor("o_t", [128, 2048], BF16, side="right"))
        p1_es = ExitStack()
        Pst = p1_es.enter_context(nc.sbuf_tensor("Pst", [128, 8704], BF16, side="right"))
        lnst = p1_es.enter_context(nc.sbuf_tensor("lnst", [128, 4096], BF16, side="right"))
        p1ps = ExitStack()
        P01 = [p1ps.enter_context(nc.psum_tensor(f"P0{i}", [128, 1088], F32)) for i in (0, 1)]
        aEX_b1 = aEX.n
        dPX_b1 = dPX.n

        p1_items = [(h, lc, half) for h in range(H) for lc in range(16) for half in (0, 1)]
        proj_tiles = [("kb", mi, ni) for ni in range(4) for mi in range(8)]
        sched = []
        pi = 0
        for i, it in enumerate(p1_items):
            sched.append(("p1",) + it)
            if i % 16 == 15 and pi < len(proj_tiles):
                sched.append(proj_tiles[pi])
                pi += 1
        while pi < len(proj_tiles):
            sched.append(proj_tiles[pi])
            pi += 1

        p1_pe_tok = {}
        pt_pe_tok = {}
        dX_st_b = dX.n

        def f_lnst(sy):
            sy.wait_ge(dCW.h, ln3st_tok)
            for ni in range(4):
                if ni >= 1:
                    sy.wait_ge(aFE.h, v_proj_done + 8 * ni)
                sy.dma_start(
                    out=lnst[:, :],
                    in_=AP(ln3st.tensor, ni * 512, [[16384, 128], [2048, 8], [1, 512]])
                ).then_inc(dX.h, 16)
                dX.n += 16
        blk.sync(f_lnst)

        def f1_pe(t):
            t.wait_ge(aFE.h, v_proj_done)
            t.wait_ge(dCW.h, ln3st_tok)
            t.wait_ge(dW.h, t_rel[0])
            pt = 0
            for item in sched:
                if item[0] == "p1":
                    _, h, lc, half = item
                    hc, rbv = h // 2, (h % 2) * 64
                    s = (h * 16 + lc) * 2 + half
                    w = 1088 if half == 0 else 1087
                    s0 = 1920 - 128 * lc + half * 1088
                    if s > 1:
                        t.wait_ge(aEX.h, aEX_b1 + s - 1)
                    psx = P01[s % 2]
                    off = 0
                    for n in (512, 512, w - 1024):
                        ins = t.matmul(
                            psx[:, off:off + n],
                            qb_i[rbv:rbv + 64, hc * 2048 + lc * 128: hc * 2048 + lc * 128 + 128],
                            rel2_s[rbv:rbv + 64, s0 + off: s0 + off + n],
                            start=True, stop=True)
                        off += n
                    p1_pe_tok[s] = pe.inc(ins)
                else:
                    _, mi, ni = item
                    ps = (pb_a, pb_b)[pt % 2]
                    if v_proj_done + pt - 1 > 0:
                        t.wait_ge(aFE.h, v_proj_done + pt - 1)
                    t.wait_ge(dX.h, dX_st_b + 16 * (ni + 1))
                    for kc in range(8):
                        t.matmul(ps[:, :],
                                 bwk[:, kc * 1024 + mi * 128: kc * 1024 + mi * 128 + 128],
                                 lnst[:, kc * 512: kc * 512 + 512],
                                 start=(kc == 0), stop=False)
                    ins = t.matmul(ps[:, :], bk_row_s[0:1, mi * 128:mi * 128 + 128],
                                   ones_b[0:1, 0:512], start=False, stop=True)
                    pt_pe_tok[pt] = pe.inc(ins)
                    pt += 1
        blk.tensor(f1_pe)

        def f1_act(sc):
            pt = 0
            for item in sched:
                if item[0] == "p1":
                    _, h, lc, half = item
                    hl = h * 16 + lc
                    s = hl * 2 + half
                    w = 1088 if half == 0 else 1087
                    sc.wait_ge(pe.h, p1_pe_tok[s])
                    if hl >= 4 and half == 0:
                        sc.wait_ge(dPX.h, dPX_b1 + 16 * (hl - 3))
                    ins = sc.activation(
                        Pst[:, (hl % 4) * 2176 + half * 1088:
                            (hl % 4) * 2176 + half * 1088 + w],
                        P01[s % 2][:, 0:w], AF.Exp)
                    aEX.inc(ins)
                else:
                    _, mi, ni = item
                    ps = (pb_a, pb_b)[pt % 2]
                    sc.wait_ge(pe.h, pt_pe_tok[pt])
                    ins = sc.copy(kb_i[:, mi * 2048 + ni * 512: mi * 2048 + ni * 512 + 512],
                                  ps[:, :])
                    aFE.inc(ins)
                    pt += 1
        blk.scalar(f1_act)
        v_done = aFE.n

        def f1_gp(g):
            for h in range(H):
                for lc in range(16):
                    hl = h * 16 + lc
                    g.wait_ge(aEX.h, aEX_b1 + 2 * hl + 2)
                    ins = g.dma_start(
                        out=AP(pexp.tensor,
                               h * 2048 * 4095 + (lc * 128) * 4095 + (1920 - 128 * lc),
                               [[4095, 128], [1, 2175]]),
                        in_=Pst[:, (hl % 4) * 2176: (hl % 4) * 2176 + 2175])
                    dPX.inc(ins, 16)
        blk.gpsimd(f1_gp)
        pexp_all = dPX.n
        p1ps.close()
        p1_es.close()
        bps.close()
        p2bps = ExitStack()
        qk2 = p2bps.enter_context(nc.psum_tensor("qk2", [128, 2048], F32))
        pv2 = [p2bps.enter_context(nc.psum_tensor(f"pv2_{i}", [65, 1024], F32)) for i in (0, 1)]

        aEX_b2, aEV_b2, vRC_b2, vNM_b2, vEM_b2, pPV_b2, dK_b2, dOO_b2 = (
            aEX.n, aEV.n, vRC.n, vNM.n, vEM.n, pPV.n, dK.n, dOO.n)

        qk2_tok = {}
        bc2_tok = {}
        units = [(h, rc, lbh) for h in range(H) for rc in range(16)
                 for lbh in (0, 1)]
        NU = len(units)
        u_lo = [0] * NU
        last_of = {}
        vem_after_R = {}
        for j, (h, rc, lbh) in enumerate(units):
            last_of[(h, lbh)] = j
            vem_after_R[h * 16 + rc] = j + 1

        def f2_syncr(sy):

            def store_o(idx):
                h2, lbh2 = idx // 2, idx % 2
                hc2, rbv2 = h2 // 2, (h2 % 2) * 64
                sy.wait_ge(vNM.h, vNM_b2 + idx + 1)
                sy.dma_start(
                    out=osc_d[rbv2:rbv2 + 64,
                              hc2 * 2048 + lbh2 * 1024: hc2 * 2048 + lbh2 * 1024 + 1024],
                    in_=o_t[0:64, (idx % 2) * 1024:(idx % 2) * 1024 + 1024]
                ).then_inc(dOO.h, 16)
                dOO.n += 16
            for h in range(H):
                if h >= 1:
                    store_o(2 * (h - 1))
                    store_o(2 * (h - 1) + 1)
                sy.wait_ge(dPX.h, dPX_b1 + 256 * (h + 1))
                for rc in range(16):
                    R = h * 16 + rc
                    if R >= 3:
                        sy.wait_ge(vEM.h, vEM_b2 + vem_after_R[R - 3])
                    src = AP(pexp.tensor, h * 2048 * 4095 + 2047 + 128 * rc,
                             [[4094, 2048], [1, 128]])
                    sy.dma_start(out=er[:, (R % 3) * 2048:(R % 3) * 2048 + 2048],
                                 in_=src, transpose=True).then_inc(dK.h, 16)
                    dK.n += 16
            store_o(2 * H - 2)
            store_o(2 * H - 1)
        blk.sync(f2_syncr)


        def f2_pe(t):
            t.wait_ge(aFE.h, v_done)
            t.wait_ge(aEX.h, aEX_b2)
            t.wait_ge(vG.h, v_ones_tok[0])

            def emit_qk(j):
                h, rc, lbh = units[j]
                hc, rbv = h // 2, (h % 2) * 64
                lo = u_lo[j]
                if j > 0:
                    t.wait_ge(aEX.h, aEX_b2 + j - 1)
                for c0, c1 in (((lo, 512), (512, 1024)) if lo < 512 else ((lo, 1024),)):
                    ins = t.matmul(
                        qk2[:, (j % 2) * 1024 + c0:(j % 2) * 1024 + c1],
                        kb_i[rbv:rbv + 64, hc * 2048 + rc * 128: hc * 2048 + rc * 128 + 128],
                        qb_i[rbv:rbv + 64, hc * 2048 + lbh * 1024 + c0:
                             hc * 2048 + lbh * 1024 + c1],
                        start=True, stop=True)
                qk2_tok[j] = pe.inc(ins)

            def emit_pv(j):
                h, rc, lbh = units[j]
                idx = 2 * h + lbh
                lo = u_lo[j]
                t.wait_ge(vEM.h, vEM_b2 + j + 1)
                if rc == 0 and h > 0:
                    t.wait_ge(aEV.h, aEV_b2 + 2 * (h - 1) + lbh + 1)
                    t.wait_ge(vNM.h, vNM_b2 + 2 * (h - 1) + lbh + 1)
                for c0, c1, st in acc_ranges(lo, 0, j == last_of[(h, lbh)]):
                    ins = t.matmul(
                        pv2[lbh][:, c0:c1],
                        v_i[:, rc * 1040 + h * 65: rc * 1040 + h * 65 + 65],
                        Eb[:, (j % 2) * 1024 + c0:(j % 2) * 1024 + c1],
                        start=(rc == 0), stop=st)
                pPV.inc(ins)
                if j == last_of[(h, lbh)]:
                    t.wait_ge(aEV.h, aEV_b2 + idx + 1)
                    for cb in (0, 1):
                        ins = t.matmul(
                            pv2[lbh][0:64, cb * 512:cb * 512 + 512], ones_f[64:65, 0:64],
                            nrm2[64:65, (idx % 2) * 1024 + cb * 512:
                                 (idx % 2) * 1024 + cb * 512 + 512],
                            start=True, stop=True)
                    bc2_tok[idx] = pe.inc(ins)

            for j in range(NU):
                emit_qk(j)
                if j >= 2:
                    emit_pv(j - 2)
            emit_pv(NU - 2)
            emit_pv(NU - 1)
        blk.tensor(f2_pe)

        def f2_act(sc):
            for j in range(NU):
                h, rc, lbh = units[j]
                lo = u_lo[j]
                sc.wait_ge(pe.h, qk2_tok[j])
                if j >= 2:
                    sc.wait_ge(vEM.h, vEM_b2 + j - 1)
                aEX.inc(sc.activation(t1b[:, (j % 2) * 1024 + lo:(j % 2) * 1024 + 1024],
                                      qk2[:, (j % 2) * 1024 + lo:(j % 2) * 1024 + 1024],
                                      AF.Exp, scale=0.125))
                if j == last_of[(h, lbh)]:
                    idx = 2 * h + lbh
                    sc.wait_ge(pPV.h, pPV_b2 + j + 1)
                    if idx >= 2:
                        sc.wait_ge(vNM.h, vNM_b2 + idx - 1)
                    sc.copy(oev2[0:64, (idx % 2) * 1024:(idx % 2) * 1024 + 1024],
                            pv2[lbh][0:64, :])
                    sc.activation(nrm2[64:65, (idx % 2) * 1024:(idx % 2) * 1024 + 1024],
                                  pv2[lbh][64:65, :], AF.Ln)
                    ins = sc.activation(nrm2[64:65, (idx % 2) * 1024:(idx % 2) * 1024 + 1024],
                                        nrm2[64:65, (idx % 2) * 1024:(idx % 2) * 1024 + 1024],
                                        AF.Exp, scale=-1.0)
                    aEV.inc(ins)
        blk.scalar(f2_act)

        def f2_vec(v):
            for j in range(NU):
                h, rc, lbh = units[j]
                lo = u_lo[j]
                v.wait_ge(aEX.h, aEX_b2 + j + 1)
                v.wait_ge(dK.h, dK_b2 + 16 * (h * 16 + rc + 1))
                if j >= 2:
                    v.wait_ge(pPV.h, pPV_b2 + j - 1)
                R = h * 16 + rc
                ins = v.tensor_tensor(
                    Eb[:, (j % 2) * 1024 + lo:(j % 2) * 1024 + 1024],
                    t1b[:, (j % 2) * 1024 + lo:(j % 2) * 1024 + 1024],
                    er[:, (R % 3) * 2048 + lbh * 1024 + lo:
                       (R % 3) * 2048 + lbh * 1024 + 1024],
                    ALU.mult)
                vEM.inc(ins)
                if j == last_of[(h, lbh)]:
                    idx = 2 * h + lbh
                    v.wait_ge(pe.h, bc2_tok[idx])
                    if dOO_b2 + 16 * (idx - 1) > dOO_b2:
                        v.wait_ge(dOO.h, dOO_b2 + 16 * (idx - 1))
                    ins = v.tensor_tensor(
                        o_t[0:64, (idx % 2) * 1024:(idx % 2) * 1024 + 1024],
                        oev2[0:64, (idx % 2) * 1024:(idx % 2) * 1024 + 1024],
                        pv2[lbh][0:64, 0:1024], ALU.mult)
                    vNM.inc(ins)
        blk.vector(f2_vec)

        oscr_tok = dOO.n
        p2bps.close()
        p2b_es.close()
        bwk_es.close()

        wo_es = ExitStack()
        o_i = wo_es.enter_context(nc.sbuf_tensor("o_i", [128, 16384], BF16, side="right"))
        wo_s = wo_es.enter_context(nc.sbuf_tensor("wo_s", [128, 8192], BF16, side="right"))
        xr2 = wo_es.enter_context(nc.sbuf_tensor("xr2", [128, 4096], F32, side="right"))
        xt2 = wo_es.enter_context(nc.sbuf_tensor("xt2", [128, 1024], F32, side="right"))
        wops = ExitStack()
        wo_a = wops.enter_context(nc.psum_tensor("wo_a", [128, 512], F32))
        wo_b = wops.enter_context(nc.psum_tensor("wo_b", [128, 512], F32))
        t_wo = []
        blk.gpsimd(lambda g: (g.wait_ge(dOO.h, oscr_tok),
                              t_wo.extend(load_w(g, [wo_s[:, :]], [bw_o]))))
        oi_tok = [0]

        def fw_sync_oi(sy):
            sy.wait_ge(dOO.h, oscr_tok)
            oi_tok[0] = dX.inc(sy.dma_start(out=o_i[:, :], in_=osc_d[:, :]), 16)
        blk.sync(fw_sync_oi)

        wo_pe_tok = {}
        vFE_wo_b = vFE.n
        dX_wo_b = dX.n
        dO_wo_b = dO.n

        def fw_pe(t):
            t.wait_ge(dW.h, t_wo[0])
            t.wait_ge(dX.h, oi_tok[0])
            for oc in range(8):
                for tb in range(4):
                    s = oc * 4 + tb
                    if vFE_wo_b + s - 1 > vFE_wo_b - 1 and s >= 2:
                        t.wait_ge(vFE.h, vFE_wo_b + s - 1)
                    ps = (wo_a, wo_b)[s % 2]
                    for ic in range(8):
                        ins = t.matmul(
                            ps[:, :], wo_s[:, ic * 1024 + oc * 128: ic * 1024 + oc * 128 + 128],
                            o_i[:, ic * 2048 + tb * 512: ic * 2048 + tb * 512 + 512],
                            start=(ic == 0), stop=(ic == 7))
                    wo_pe_tok[s] = pe.inc(ins)
        blk.tensor(fw_pe)

        def fw_sync_x(sy):
            sy.wait_ge(dOO.h, oscr_tok)

            def store_x(oc2):
                for tb2 in range(4):
                    s2 = oc2 * 4 + tb2
                    sy.wait_ge(vFE.h, vFE_wo_b + s2 + 1)
                    sy.dma_start(out=parkA[:, oc2 * 2048 + tb2 * 512: oc2 * 2048 + tb2 * 512 + 512],
                                 in_=xt2[:, (s2 % 2) * 512:(s2 % 2) * 512 + 512]
                                 ).then_inc(dO.h, 16)
                    dO.n += 16
            for oc in range(8):
                if oc >= 2:
                    sy.wait_ge(vFE.h, vFE_wo_b + 4 * (oc - 1))
                sy.dma_start(out=xr2[:, (oc % 2) * 2048:(oc % 2) * 2048 + 2048],
                             in_=parkB[:, oc * 2048: oc * 2048 + 2048]).then_inc(dX.h, 16)
                dX.n += 16
                if oc >= 1:
                    store_x(oc - 1)
            store_x(7)
        blk.sync(fw_sync_x)

        def fw_vec(v):
            for oc in range(8):
                v.wait_ge(dX.h, dX_wo_b + 16 * (oc + 1))
                for tb in range(4):
                    s = oc * 4 + tb
                    v.wait_ge(pe.h, wo_pe_tok[s])
                    if dO_wo_b + 16 * (s - 1) > dO_wo_b:
                        v.wait_ge(dO.h, dO_wo_b + 16 * (s - 1))
                    ps = (wo_a, wo_b)[s % 2]
                    ins = v.scalar_tensor_tensor(
                        xt2[:, (s % 2) * 512:(s % 2) * 512 + 512], ps[:, :],
                        bos[:, oc:oc + 1],
                        xr2[:, (oc % 2) * 2048 + tb * 512:(oc % 2) * 2048 + tb * 512 + 512],
                        ALU.add, ALU.add)
                    vFE.inc(ins)
        blk.vector(fw_vec)

        x3_tok = dO.n
        wops.close()
        wo_es.close()
        b_es.close()

    if stages >= 4:
        f2_es = ExitStack()
        ln4 = f2_es.enter_context(nc.sbuf_tensor("ln4", [128, 16384], BF16, side="right"))
        ln4s = ExitStack()
        xb4 = ln4s.enter_context(nc.sbuf_tensor("xb4", [128, 8192], BF16, side="right"))
        sqb4 = ln4s.enter_context(nc.sbuf_tensor("sqb4", [128, 2048], BF16, side="right"))
        tb24 = ln4s.enter_context(nc.sbuf_tensor("tb24", [128, 1024], F32, side="right"))
        dcw_t4, dxo_t4, doo_t4 = dCW.n, dXO.n, dOO.n
        ln4_tok = emit_ln(parkA, ln4, xb4, sqb4, tb24,
                          gate=lambda g: (g.wait_ge(dO.h, x3_tok),
                                          g.wait_ge(dCW.h, dcw_t4),
                                          g.wait_ge(dXO.h, dxo_t4),
                                          g.wait_ge(dOO.h, doo_t4)))
        ln4s.close()
        emit_ffn(ln4, parkA, yO, False, x3_tok, ln4_tok)
        f2_es.close()

    if stages < 4:
        src_park = parkA if stages in (1, 3) else parkB

        def f_tail(sy):
            sy.wait_ge(dO.h, dO.n)
            sy.dma_start(out=yO, in_=src_park).then_inc(dO.h, 16)
            dO.n += 16
        blk.sync(f_tail)

    fins = [(s, s.n) for s in (dW, dXB, dPX, dX, dK, dO, dCW, dOO, dXO)]

    def f_fin(sy):
        for s, n in fins:
            if n:
                sy.wait_ge(s.h, n)
    blk.sync(f_fin)

    es.close()
    nc.finalize()
    return nc


_BUILT = {}


def _prep_inputs(inputs):
    f32 = np.float32
    g = {k: np.asarray(v, dtype=f32) for k, v in inputs.items()}
    d = {}
    d["aw_dkv"] = img128(g["L0_attn_a_W_dkv"].T.copy())
    d["aw_dq"] = img128(g["L0_attn_a_W_dq"].T.copy())
    d["aw_dqn"] = img128(g["L0_attn_a_W_dq"].copy())
    d["aw_uq"] = img128(g["L0_attn_a_W_uq"].T.copy())
    d["aw_uk"] = img128(g["L0_attn_a_W_uk"].copy())
    d["aw_uv"] = img128(g["L0_attn_a_W_uv"].copy())
    d["aw_o"] = img128(g["L0_attn_a_W_o"].T.copy())
    d["bw_q"] = img128(g["L0_attn_b_Wq"].T.copy())
    d["bw_k"] = img128(g["L0_attn_b_Wk"].T.copy())
    d["bw_v"] = img128(g["L0_attn_b_Wv"].T.copy())
    d["bw_o"] = img128(g["L0_attn_b_Wo"].T.copy())
    d["bq_r"] = g["L0_attn_b_bq"].reshape(1, 1024).copy()
    d["bk_r"] = g["L0_attn_b_bk"].reshape(1, 1024).copy()
    d["bo_i"] = colimg(g["L0_attn_b_bo"])
    d["bv_r"] = g["L0_attn_b_bv"].reshape(1, 1024).copy()
    re_t = np.ascontiguousarray(g["L0_attn_b_rel_emb"].T)
    d["rel2"] = np.concatenate([re_t, re_t], axis=0).copy()
    d["f_w1"] = ogblocks(g["L0_ffn_w1_w"].T.copy())
    d["f_w2"] = ogblocks(g["L0_ffn_w2_w"].T.copy())
    d["f_h1"] = ogblocks(g["L0_ffn_h1_w"].T.copy())
    d["f_h2"] = ogblocks(g["L0_ffn_h2_w"].T.copy())
    d["f_w3"] = ogblocks(g["L0_ffn_w3_w"].T.copy())
    d["s1_i"] = colimg(g["L0_ffn_w1_s"])
    d["s2_i"] = colimg(g["L0_ffn_w2_s"])
    d["sh1_i"] = colimg(g["L0_ffn_h1_s"])
    d["sh2_i"] = colimg(g["L0_ffn_h2_s"])
    d["s3_i"] = colimg(g["L0_ffn_w3_s"])
    x = g["L0_attn_a_x"]
    in_maps = []
    for b in range(NCORE):
        m = dict(d)
        m["xT"] = img128(np.ascontiguousarray(x[b].T))
        in_maps.append(m)
    return in_maps


def kernel(**inputs):
    if STAGES not in _BUILT:
        _BUILT[STAGES] = build(STAGES)
    nc = _BUILT[STAGES]
    in_maps = _prep_inputs(inputs)
    res = run_bass_kernel_spmd(nc, in_maps, core_ids=list(range(NCORE)))
    outs = []
    for b in range(NCORE):
        yi = np.asarray(res.results[b]["y"])
        outs.append(yi.reshape(128, 8, 2048).transpose(1, 0, 2).reshape(1024, 2048).T)
    return np.stack(outs, axis=0).astype(np.float32)
